# version 02
# baseline (speedup 1.0000x reference)
import numpy as np
import ml_dtypes
import concourse.bass as bass
import concourse.mybir as mybir
from concourse.bass import ds
from concourse.bass_utils import run_bass_kernel_spmd

F32 = mybir.dt.float32
BF16 = mybir.dt.bfloat16
I32 = mybir.dt.int32
AF = mybir.ActivationFunctionType
ALU = mybir.AluOpType
AX = mybir.AxisListType
BF16NP = ml_dtypes.bfloat16

NCORES = 8
T = 1024
NT = 8
D = 4096
NDC = 32
GROUPS = [[0, 1, 2, 3], [4, 5, 6, 7]]

ENGINES = ("sync", "scalar", "vector", "tensor", "gpsimd")



def w_stream(WT, os_width=512, dcg=8):
    d, o = WT.shape
    ndc = d // 128
    ng = ndc // dcg
    nos = o // os_width
    a = WT.reshape(ng, dcg, 128, nos, os_width)
    a = a.transpose(3, 0, 2, 1, 4)
    return np.ascontiguousarray(a.reshape(nos * ng, 128, dcg * os_width))


def shard_tokens(x_full):
    out = []
    for c in range(NCORES):
        b, r = divmod(c, 4)
        out.append(np.ascontiguousarray(x_full[b, r * T:(r + 1) * T, :]))
    return out


def unshard_tokens(shards):
    x = np.empty((2, 4096, D), dtype=shards[0].dtype)
    for c in range(NCORES):
        b, r = divmod(c, 4)
        x[b, r * T:(r + 1) * T, :] = shards[c]
    return x



class Sem:
    __slots__ = ("h", "cnt", "name")

    def __init__(self, h, name):
        self.h = h
        self.cnt = 0
        self.name = name


class MK:
    def __init__(self, nc):
        self.nc = nc
        self.q = {e: [] for e in ENGINES}
        self.sems = {}

    def sem(self, name):
        if name not in self.sems:
            self.sems[name] = Sem(self.nc.alloc_semaphore(name), name)
        return self.sems[name]

    def _push(self, eng, fn):
        self.q[eng].append(fn)

    def wait(self, eng, sem, val=None):
        v = sem.cnt if val is None else val
        self._push(eng, lambda e, s=sem, v=v: e.wait_ge(s.h, v))

    def op(self, eng, builder, inc=None, inc_n=1):
        if inc is not None:
            inc.cnt += inc_n
            self._push(eng, lambda e, b=builder, s=inc, n=inc_n: b(e).then_inc(s.h, n))
        else:
            self._push(eng, lambda e, b=builder: b(e))

    def dma(self, eng, out, in_, sem, **kw):
        self.op(eng, lambda e, o=out, i=in_, k=dict(kw): e.dma_start(out=o, in_=i, **k),
                inc=sem, inc_n=16)
        return sem.cnt

    def barrier(self, name="bar", engines=None):
        engs = [e for e in (engines or ENGINES) if self.q[e]]
        sems = {e: self.sem(f"bar_{e}") for e in engs}
        for e in engs:
            s = sems[e]
            s.cnt += 1
            self._push(e, lambda eng, s=s: eng.drain().then_inc(s.h, 1))
        for e in engs:
            for e2 in engs:
                if e2 != e:
                    self.wait(e, sems[e2])

    def build(self):
        nc = self.nc
        with nc.Block() as block:
            for eng_name in ENGINES:
                ops = self.q[eng_name]
                if not ops:
                    continue
                def runner(e, ops=ops):
                    for f in ops:
                        f(e)
                getattr(block, eng_name)(runner)
        return nc


import os as _os
DYN_SCRATCH = int(_os.environ.get("MK_DYN_SCRATCH", "4096"))


def make_nc(name="mk"):
    return bass.Bass("TRN2", target_bir_lowering=False, name=name,
                     dynamic_dma_scratch_size=DYN_SCRATCH)



class Ctx:
    pass


def alloc_common(nc, K):
    c = Ctx()
    c.nc = nc
    c.x_res = nc.alloc_sbuf_tensor("x_res", [128, NT * D], F32)
    c.x_fm = nc.alloc_sbuf_tensor("x_fm", [128, NDC * T], BF16)
    c.wbuf = nc.alloc_sbuf_tensor("wbuf", [128, 2 * 4096], BF16)
    c.stats = nc.alloc_sbuf_tensor("stats", [128, 8 * 8], F32)
    c.ident = nc.alloc_sbuf_tensor("ident", [128, 128], F32)
    c.onesr = nc.alloc_sbuf_tensor("onesr", [1, 128], F32)
    c.hw = nc.alloc_sbuf_tensor("hw", [128, 1024], BF16)
    c.ps = [nc.alloc_psum_tensor(f"ps{i}", [128, 512], F32) for i in range(8)]
    c.identb = nc.alloc_sbuf_tensor("identb", [128, 128], BF16)
    c.rs = nc.alloc_sbuf_tensor("rs", [128, 512], F32)
    c.ob = nc.alloc_sbuf_tensor("ob", [128, 1024], BF16)
    c.onescol = nc.alloc_sbuf_tensor("onescol", [128, 1], BF16)
    c.onesrb = nc.alloc_sbuf_tensor("onesrb", [1, 128], BF16)
    c.bmT01 = nc.alloc_sbuf_tensor("bmT01", [128, 128], BF16)
    c.ident_np = np.eye(128, dtype=np.float32)
    c.ones_np = np.ones((1, 128), dtype=np.float32)
    identc = nc.inline_tensor(c.ident_np, "identc").ap()
    identbc = nc.inline_tensor(c.ident_np.astype(BF16NP), "identbc").ap()
    onesc = nc.inline_tensor(c.ones_np, "onesc").ap()
    s = K.sem("s_const")
    K.dma("sync", c.ident[:], identc, s)
    K.dma("sync", c.identb[:], identbc, s)
    K.dma("sync", c.onesr[:], onesc, s)
    onescolc = nc.inline_tensor(np.ones((128, 1), np.float32).astype(BF16NP), "onescolc").ap()
    K.dma("sync", c.onescol[:], onescolc, s)
    onesrbc = nc.inline_tensor(c.ones_np.astype(BF16NP), "onesrbc").ap()
    K.dma("sync", c.onesrb[:], onesrbc, s)
    c.s_const = s
    for e in ("sync", "scalar", "vector", "tensor", "gpsimd"):
        K.wait(e, s)
    return c


def st_col(c, slot, tt):
    return c.stats[:, slot * 8 + tt: slot * 8 + tt + 1]


def st_all(c, slot):
    return c.stats[:, slot * 8: slot * 8 + 8]



def emit_load_x(K, c, x_in_ap):
    s = K.sem("s_xload")
    for tt in range(NT):
        K.dma("sync", c.x_res[:, tt * D:(tt + 1) * D],
              x_in_ap[tt * 128:(tt + 1) * 128, :], s)
    return s


def emit_store_x(K, c, x_out_ap):
    s = K.sem("s_xstore")
    for tt in range(NT):
        K.dma("sync", x_out_ap[tt * 128:(tt + 1) * 128, :],
              c.x_res[:, tt * D:(tt + 1) * D], s)
    K.wait("sync", s)
    return s


def emit_ln_xfm(K, c, eps=1e-5, want_xfm=True, tag=""):
    sv = K.sem("s_ln_v" + tag)
    sa = K.sem("s_ln_a" + tag)
    sp = K.sem("s_ln_pe" + tag)
    for tt in range(NT):
        xt = c.x_res[:, tt * D:(tt + 1) * D]
        K.op("vector", lambda e, o=st_col(c, 0, tt), i=xt:
             e.reduce_sum(o, i, axis=AX.X), inc=sv)
        K.op("scalar", lambda e, o=c.x_fm[:, tt * 4096:(tt + 1) * 4096], i=xt,
             a=st_col(c, 1, tt):
             e.activation(o, i, AF.Square, accum_out=a), inc=sa)
    K.wait("vector", sa)
    K.wait("vector", sv)
    inv_d = 1.0 / D
    K.op("vector", lambda e: e.tensor_scalar(st_all(c, 2), st_all(c, 0), inv_d, None, ALU.mult), inc=sv)
    K.op("vector", lambda e: e.tensor_scalar(st_all(c, 3), st_all(c, 1), inv_d, None, ALU.mult), inc=sv)
    K.wait("vector", sv)
    K.op("vector", lambda e: e.tensor_tensor(st_all(c, 6), st_all(c, 2), st_all(c, 2), ALU.mult), inc=sv)
    K.wait("vector", sv)
    K.op("vector", lambda e: e.tensor_tensor(st_all(c, 3), st_all(c, 3), st_all(c, 6), ALU.subtract), inc=sv)
    K.wait("vector", sv)
    K.op("vector", lambda e: e.tensor_scalar(st_all(c, 3), st_all(c, 3), float(eps), None, ALU.add), inc=sv)
    K.wait("scalar", sv)
    K.op("scalar", lambda e: e.activation(st_all(c, 6), st_all(c, 3), AF.Sqrt), inc=sa)
    K.wait("vector", sa)
    K.op("vector", lambda e: e.reciprocal(st_all(c, 4), st_all(c, 6)), inc=sv)
    K.wait("vector", sv)
    K.op("vector", lambda e: e.scalar_tensor_tensor(st_all(c, 5), st_all(c, 2), -1.0, st_all(c, 4), ALU.mult, ALU.mult), inc=sv)
    K.wait("vector", sv)
    if not want_xfm:
        return sv, sa, sp
    xfm3 = c.x_fm[:].rearrange("p (dc t) -> p dc t", t=T)
    psT = [c.ps[6][:].bitcast(BF16), c.ps[7][:].bitcast(BF16)]
    K.wait("tensor", c.s_const)
    for tt in range(NT):
        half = tt % 2
        xln = c.wbuf[:, half * 4096: half * 4096 + 4096]
        if tt >= 2:
            K.wait("vector", sp)
        K.op("vector", lambda e, o=xln, i=c.x_res[:, tt * D:(tt + 1) * D],
             r=st_col(c, 4, tt), m=st_col(c, 5, tt):
             e.tensor_scalar(o, i, r, m, ALU.mult, ALU.add), inc=sv)
        K.wait("tensor", sv)
        for g in range(4):
            bank = psT[g % 2]
            if tt > 0 or g >= 2:
                K.wait("tensor", sv)
            for i in range(8):
                dc = g * 8 + i
                src = xln[:, dc * 128:(dc + 1) * 128]
                reg = bank[:, i * 128:(i + 1) * 128]
                K.op("tensor", lambda e, r=reg, x=src: e.transpose(r, x, c.identb[:]),
                     inc=(sp if i == 7 else None))
            K.wait("vector", sp)
            dst = xfm3[:, g * 8:(g + 1) * 8, tt * 128:(tt + 1) * 128]
            K.op("vector", lambda e, d=dst,
                 s=bank[:].rearrange("p (i t) -> p i t", t=128):
                 e.tensor_copy(d, s), inc=sv)
    return sv, sa, sp


def emit_proj_tok(K, c, wst_ap, n_os, dest_fn, *, name, bias_ap=None,
                  n_dc=NDC, lhsT_fn=None, pe_waits=(), dma_waits=(),
                  rhs_cols=512, prequant=False, dcg=4):
    ng = n_dc // dcg
    npc = max(1, dcg // 2)
    s_pw = [K.sem(f"s_pw_{name}0"), K.sem(f"s_pw_{name}1")]
    s_cast = K.sem(f"s_cast_{name}")
    s_wf = K.sem(f"s_wf_{name}")
    s_mm = K.sem(f"s_mm_{name}")
    s_ev = K.sem(f"s_ev_{name}")
    s_b = K.sem(f"s_b_{name}")
    base_wf = s_wf.cnt
    base_mm = s_mm.cnt
    if lhsT_fn is None:
        def lhsT_fn(dc, tt):
            return c.x_fm[:, dc * T + tt * 128: dc * T + (tt + 1) * 128]
    for w in pe_waits:
        K.wait("tensor", w)
    for w in dma_waits:
        K.wait("sync", w)
        K.wait("vector", w)
    wb32 = c.wbuf[:].bitcast(F32)
    n_slab = n_os * ng
    slab_ready = []
    piece_cast = []
    for os in range(n_os):
        if bias_ap is not None:
            if os >= 1:
                K.wait("sync", s_mm, base_mm + 8 * os)
            K.dma("sync", c.rs[0:1, 0:rhs_cols],
                  bias_ap[0:1, os * rhs_cols:(os + 1) * rhs_cols], s_b)
        for g in range(ng):
            slab = os * ng + g
            spar = slab % 2
            if prequant:
                if slab >= 2:
                    K.wait("sync", s_wf, base_wf + slab - 1)
                K.dma("sync", c.wbuf[:, spar * 2048:(spar + 1) * 2048],
                      wst_ap[slab], s_pw[spar])
                slab_ready.append((s_pw[spar], s_pw[spar].cnt))
            else:
                for j in range(npc):
                    p = npc * slab + j
                    ppar = p % 2
                    if p >= 2:
                        K.wait("sync", s_cast, piece_cast[p - 2])
                    K.dma("sync", wb32[:, 2048 + ppar * 1024: 2048 + (ppar + 1) * 1024],
                          wst_ap[slab][:, j * 1024:(j + 1) * 1024], s_pw[ppar])
                    if slab >= 2 and j == 0:
                        K.wait("vector", s_wf, base_wf + slab - 1)
                    K.wait("vector", s_pw[ppar])
                    K.op("vector", lambda e,
                         o=c.wbuf[:, spar * 2048 + j * 1024: spar * 2048 + (j + 1) * 1024],
                         i=wb32[:, 2048 + ppar * 1024: 2048 + (ppar + 1) * 1024]:
                         e.tensor_copy(o, i), inc=s_cast)
                    piece_cast.append(s_cast.cnt)
                slab_ready.append((s_cast, s_cast.cnt))
        if os > 0:
            K.wait("tensor", s_ev)
        for g in range(ng):
            slab = os * ng + g
            spar = slab % 2
            sem_, val_ = slab_ready[slab]
            K.wait("tensor", sem_, val_)
            for dcs in range(dcg):
                dc = g * dcg + dcs
                w_sl = c.wbuf[:, spar * 2048 + dcs * rhs_cols: spar * 2048 + (dcs + 1) * rhs_cols]
                last = (dc == n_dc - 1) and bias_ap is None
                for tt in range(NT):
                    K.op("tensor", lambda e, p=c.ps[tt][:, 0:rhs_cols], l=lhsT_fn(dc, tt),
                         w=w_sl, st=(dc == 0), sp=(dc == n_dc - 1):
                         e.matmul(p, l, w, start=st, stop=sp),
                         inc=(s_mm if last else None))
            K.op("tensor", lambda e: e.drain(), inc=s_wf)
        if bias_ap is not None:
            K.wait("tensor", s_b)
            for tt in range(NT):
                K.op("tensor", lambda e, p=c.ps[tt][:, 0:rhs_cols]:
                     e.matmul(p, c.onesr[:], c.rs[0:1, 0:rhs_cols], start=False,
                              stop=True, skip_group_check=True), inc=s_mm)
        K.wait("vector", s_mm)
        K.wait("scalar", s_mm)
        for tt in range(NT):
            dest_fn(K, c, tt, os, c.ps[tt][:, 0:rhs_cols], s_ev)
    return s_ev, s_mm, s_wf


def dest_resid(K, c, tt, os, psum, s_ev, col0=0):
    dst = c.x_res[:, tt * D + col0 + os * 512: tt * D + col0 + os * 512 + 512]
    K.op("vector", lambda e, d=dst, a=psum: e.tensor_tensor(d, d, a, ALU.add), inc=s_ev)



def emit_allgather(K, in_ap, out_ap, name, wait_sems=()):
    sc = K.sem("s_ag_" + name)
    for w in wait_sems:
        K.wait("gpsimd", w)
    K.op("gpsimd", lambda e: e.collective_compute(
        "AllGather", mybir.AluOpType.bypass, replica_groups=GROUPS,
        ins=[in_ap.opt()], outs=[out_ap.opt()]), inc=sc, inc_n=1)
    return sc


def l0ab_consts(nc):
    sel4 = np.zeros((128, 4), np.float32)
    for p in range(128):
        sel4[p, p // 32] = 1.0
    ind = np.zeros((128, 4096), np.float32)
    for n in range(4096):
        ind[n // 32, n] = 1.0
    a = nc.inline_tensor(sel4, "sel4c").ap()
    b = nc.inline_tensor(ind.astype(BF16NP), "indc").ap()
    return a, b


def emit_l0ab_maskprep(K, c, maskT_ap, sel4_ap, wait_wbuf=()):
    s_mc = [K.sem("s_l0ab_mc0"), K.sem("s_l0ab_mc1")]
    s_rd = K.sem("s_l0ab_rd")
    s_pe = K.sem("s_l0ab_pe")
    s_cst = K.sem("s_l0ab_cst")
    K.dma("sync", c.rs[:, 0:4], sel4_ap, s_cst)
    mbuf = c.wbuf[:].bitcast(I32)
    for w in wait_wbuf:
        K.wait("sync", w)
    K.wait("tensor", s_cst)
    red_done = []
    for kc in range(NDC):
        par = kc % 2
        rs_acc = c.rs[:, 128 + par * 128: 128 + par * 128 + 128]
        if kc >= 2:
            K.wait("vector", s_pe, kc - 1)
        for h in range(2):
            m = kc * 2 + h
            if m >= 2:
                K.wait("sync", s_rd, red_done[m - 2])
            K.dma("sync", mbuf[:, (m % 2) * 2048: (m % 2) * 2048 + 2048],
                  maskT_ap[kc * 128:(kc + 1) * 128, h * 2048:(h + 1) * 2048], s_mc[m % 2])
            K.wait("vector", s_mc[m % 2])
            dst = rs_acc[:, h * 64:(h + 1) * 64]
            K.op("vector", lambda e, d=dst, i=mbuf[:, (m % 2) * 2048:(m % 2) * 2048 + 2048]:
                 e.reduce_sum(d.rearrange("p (a b) -> p a b", b=1),
                              i.rearrange("p (a b) -> p a b", b=32), axis=AX.X), inc=s_rd)
            red_done.append(s_rd.cnt)
        K.wait("vector", s_rd)
        K.op("vector", lambda e: e.engine_nop(), inc=s_rd)
        K.wait("tensor", s_rd)
        K.op("tensor", lambda e, l=rs_acc, kc=kc: e.matmul(
            c.ps[0][:, kc * 4:kc * 4 + 4], l, c.rs[:, 0:4], start=True, stop=True),
            inc=s_pe)
    K.wait("vector", s_pe)
    K.op("vector", lambda e: e.tensor_scalar(c.bmT01[:], c.ps[0][:, 0:128], 0.0, None, ALU.is_gt), inc=s_rd)
    K.wait("vector", s_rd)
    return s_rd


def emit_l0ab_wm(K, c, wst_ap, wm_ap, ind_ap, s_bm):
    s_w = K.sem("s_l0ab_w")
    s_ind = K.sem("s_l0ab_ind")
    s_em = K.sem("s_l0ab_em")
    s_tt = K.sem("s_l0ab_wmtt")
    s_out = [K.sem("s_l0ab_out0"), K.sem("s_l0ab_out1")]
    s_cp = K.sem("s_l0ab_cp")
    bmE = c.wbuf[:, 0:4096]
    K.wait("vector", s_bm)
    K.wait("sync", s_bm)
    for j in range(32):
        K.op("vector", lambda e, j=j: e.tensor_copy(c.wbuf[:, j:4096:32], c.bmT01[:]),
             inc=s_cp)
    K.wait("vector", s_cp)
    K.op("vector", lambda e: e.engine_nop(), inc=s_cp)
    K.wait("tensor", s_cp)
    K.wait("sync", s_cp)
    indb = c.ob[:, 0:512]
    wq32 = c.wbuf[:].bitcast(F32)[:, 2048:3072]
    n_slab = wst_ap.shape[0]
    ng = 8
    tt_done = []
    for qq in range(2 * n_slab):
        slab, jq = divmod(qq, 2)
        os, g = divmod(slab, ng)
        qpar = qq % 2
        if g == 0 and jq == 0:
            K.wait("sync", s_em)
            K.dma("sync", indb[:], ind_ap[:, os * 512:(os + 1) * 512], s_ind)
        if qq >= 1:
            K.wait("sync", s_tt, tt_done[qq - 1])
        K.dma("sync", wq32, wst_ap[slab][:, jq * 1024:(jq + 1) * 1024], s_w)
        banks = (c.ps[0], c.ps[1]) if qpar == 0 else (c.ps[2], c.ps[3])
        K.wait("tensor", s_ind, 16 * (os + 1))
        if qq >= 2:
            K.wait("tensor", s_tt, tt_done[qq - 2])
        for q in range(2):
            dc = g * 4 + jq * 2 + q
            K.op("tensor", lambda e, b=banks[q][:, 0:512],
                 l=bmE[:, dc * 128:(dc + 1) * 128]:
                 e.matmul(b, l, indb[:], start=True, stop=True),
                 inc=(s_em if q == 1 else None))
        K.wait("vector", s_em)
        K.wait("vector", s_w)
        if qq >= 2:
            K.wait("vector", s_out[qpar])
        outq = c.wbuf[:, 6144 + qpar * 1024: 6144 + (qpar + 1) * 1024]
        for q in range(2):
            K.op("vector", lambda e, o=outq[:, q * 512:(q + 1) * 512],
                 w=wq32[:, q * 512:(q + 1) * 512], b=banks[q][:, 0:512]:
                 e.tensor_tensor(o, w, b, ALU.mult), inc=s_tt)
        tt_done.append(s_tt.cnt)
        K.wait("sync", s_tt)
        if qq >= 2:
            K.wait("sync", s_out[qpar])
        K.dma("sync", wm_ap[slab][:, jq * 1024:(jq + 1) * 1024], outq, s_out[qpar])
    K.wait("sync", s_out[0])
    K.wait("sync", s_out[1])
    return s_out


def emit_l0ab(K, c, maskT_ap, wst_ap, wm_ap, sel4_ap, ind_ap, ln_sems):
    sv, sa, sp = ln_sems
    s_bm = emit_l0ab_maskprep(K, c, maskT_ap, sel4_ap, wait_wbuf=(sp,))
    s_out = emit_l0ab_wm(K, c, wst_ap, wm_ap, ind_ap, s_bm)
    s_ev, s_mm, s_wf = emit_proj_tok(
        K, c, wm_ap, 8, dest_resid, name="pj", prequant=True,
        pe_waits=(sv,), dma_waits=tuple(s_out))
    return s_ev



def l0ffn_consts(nc):
    e8 = np.zeros((8, 128), np.float32)
    for er in range(128):
        e8[er // 16, er] = 2.0
    return nc.inline_tensor(e8.astype(BF16NP), "e8c").ap()


def emit_l0ffn(K, c, wg9_ap, ast_ap, wbst_ap, bias_ap, bthr_ap, e8_ap, ln_sems, tag=""):
    sv, sa, sp = ln_sems
    s_h = K.sem("s_ff_h" + tag)
    base_h = s_h.cnt
    s_ha = K.sem("s_ff_ha" + tag)
    s_hc = K.sem("s_ff_hc" + tag)
    s_g = K.sem("s_ff_g" + tag)
    s_gv = K.sem("s_ff_gv" + tag)
    s_gs = K.sem("s_ff_gs" + tag)
    s_w9 = K.sem("s_ff_w9" + tag)
    obf = c.ob[:].bitcast(F32)
    abf = c.rs[:, 0:128].bitcast(BF16)
    ast32 = obf[:, 0:128]
    col = lambda i: c.rs[:, 288 + i: 288 + i + 1]
    e9 = c.rs[:, 304:312]
    ad = c.rs[:, 312:320]
    wv = c.rs[:, 320:328]
    wnf = c.rs[:, 328:336]
    e8_sb = c.rs[:, 336:400].bitcast(BF16)
    wg9b = obf[:, 128:272].bitcast(BF16)
    K.wait("tensor", sv)
    cast_done = []
    for dc in range(NDC):
        par = dc % 2
        if dc >= 1:
            K.wait("sync", s_hc, cast_done[dc - 1])
        K.dma("sync", ast32, ast_ap[dc], s_ha)
        K.wait("vector", s_ha)
        if dc >= 2:
            K.wait("vector", s_h, base_h + dc - 1)
        K.op("vector", lambda e, o=abf[:, par * 128:(par + 1) * 128]:
             e.tensor_copy(o, ast32), inc=s_hc)
        cast_done.append(s_hc.cnt)
        K.wait("tensor", s_hc)
        for th in range(2):
            K.op("tensor", lambda e, b=c.ps[4 + th][:, 0:512],
                 l=abf[:, par * 128:(par + 1) * 128],
                 r=c.x_fm[:, dc * T + th * 512: dc * T + th * 512 + 512],
                 st=(dc == 0), spp=(dc == NDC - 1):
                 e.matmul(b, l, r, start=st, stop=spp),
                 inc=(s_h if th == 1 else None))
    K.wait("sync", s_h)
    K.dma("sync", c.rs[:, 0:288], wg9_ap, s_w9)
    K.dma("sync", e8_sb[0:8, 0:128], e8_ap, s_w9)
    K.dma("sync", c.rs[0:1, 298:299], bthr_ap, s_w9)
    K.wait("vector", s_w9)
    K.op("vector", lambda e: e.tensor_copy(wg9b, c.rs[:, 0:288]), inc=s_gv)
    K.wait("tensor", s_w9)
    K.op("tensor", lambda e: e.matmul(c.ps[7][:, 0:1], c.onesr[:], c.rs[0:1, 298:299],
         start=True, stop=True), inc=s_g)
    K.wait("vector", s_g)
    K.op("vector", lambda e: e.tensor_copy(col(15), c.ps[7][:, 0:1]), inc=s_gv)
    K.wait("scalar", s_gv)
    K.wait("vector", s_gv)
    K.wait("tensor", s_gv)
    for tt in range(NT):
        if tt > 0:
            K.wait("tensor", s_gs)
        for dc in range(NDC):
            K.op("tensor", lambda e, l=c.x_fm[:, dc * T + tt * 128: dc * T + (tt + 1) * 128],
                 r=wg9b[:, dc * 9: dc * 9 + 9], st=(dc == 0), spp=(dc == NDC - 1):
                 e.matmul(c.ps[6][:, 0:9], l, r, start=st, stop=spp),
                 inc=(s_g if dc == NDC - 1 else None))
        K.wait("vector", s_g)
        K.op("vector", lambda e: e.reduce_max(col(0), c.ps[6][:, 0:8], axis=AX.X), inc=s_gv)
        K.wait("vector", s_gv)
        K.op("vector", lambda e: e.tensor_scalar(col(1), col(0), -1.0, None, ALU.mult), inc=s_gv)
        K.wait("scalar", s_gv)
        K.op("scalar", lambda e: e.activation(e9, c.ps[6][:, 0:8], AF.Exp,
             bias=col(1), accum_out=col(2)), inc=s_gs)
        K.op("scalar", lambda e: e.activation(col(3), c.ps[6][:, 8:9], AF.Sigmoid,
             bias=col(15)), inc=s_gs)
        K.wait("vector", s_gs)
        K.op("vector", lambda e: e.reciprocal(col(4), col(2)), inc=s_gv)
        K.op("vector", lambda e: e.tensor_scalar(col(5), col(3), -0.125, None, ALU.mult), inc=s_gv)
        K.wait("vector", s_gv)
        K.op("vector", lambda e: e.tensor_scalar(ad, e9, col(4), col(5), ALU.mult, ALU.add), inc=s_gv)
        K.wait("vector", s_gv)
        K.op("vector", lambda e: e.tensor_scalar(wv, ad, 0.0, None, ALU.max), inc=s_gv)
        K.wait("vector", s_gv)
        K.op("vector", lambda e: e.reduce_sum(col(6), wv, axis=AX.X), inc=s_gv)
        K.wait("vector", s_gv)
        K.op("vector", lambda e: e.tensor_scalar(col(7), col(6), 0.0, None, ALU.is_equal), inc=s_gv)
        K.wait("vector", s_gv)
        K.op("vector", lambda e: e.tensor_add(col(8), col(6), col(7)), inc=s_gv)
        K.wait("vector", s_gv)
        K.op("vector", lambda e: e.reciprocal(col(9), col(8)), inc=s_gv)
        K.wait("vector", s_gv)
        K.op("vector", lambda e: e.tensor_scalar(wnf, wv, col(9), None, ALU.mult), inc=s_gv)
        K.wait("tensor", s_gv)
        K.op("tensor", lambda e: e.transpose(c.ps[7][0:8, 0:128], wnf, c.ident[:]), inc=s_g)
        K.wait("vector", s_g)
        K.op("vector", lambda e: e.tensor_copy(c.bmT01[0:8, 0:128], c.ps[7][0:8, 0:128]), inc=s_gv)
        K.wait("tensor", s_gv)
        K.op("tensor", lambda e: e.matmul(c.ps[7][:, 0:128], e8_sb[0:8, 0:128],
             c.bmT01[0:8, 0:128], start=True, stop=True), inc=s_g)
        K.wait("vector", s_g)
        K.op("vector", lambda e: e.tensor_copy(c.bmT01[:, 0:128], c.ps[7][:, 0:128]), inc=s_gv)
        K.wait("vector", s_gv)
        hb = c.ps[4 + tt // 4][:, (tt % 4) * 128:(tt % 4) * 128 + 128]
        K.op("vector", lambda e, h=hb, o=c.hw[:, tt * 128:(tt + 1) * 128]:
             e.tensor_tensor(o, h, c.bmT01[:, 0:128], ALU.mult), inc=s_gv)
        K.wait("vector", s_gv)
    def lhsT_fn(dc, tt):
        if dc < NDC:
            return c.x_fm[:, dc * T + tt * 128: dc * T + (tt + 1) * 128]
        return c.hw[:, tt * 128:(tt + 1) * 128]
    s_ev, s_mm, s_wf = emit_proj_tok(
        K, c, wbst_ap, 8, dest_resid, name="pj", bias_ap=bias_ap,
        n_dc=34, dcg=2, lhsT_fn=lhsT_fn, pe_waits=(s_gv,), dma_waits=(s_gv,))
    return s_ev


def l0ffn_host(W_base, W_gate, W_thr, A_w, B_w):
    Wg9 = np.concatenate([W_gate.T, W_thr[:, None]], axis=1)
    wg9 = np.ascontiguousarray(Wg9.reshape(32, 128, 9).transpose(1, 0, 2)
                               .reshape(128, 288)).astype(np.float32)
    A_all = A_w.transpose(2, 0, 1).reshape(4096, 128)
    ast = np.ascontiguousarray(A_all.reshape(32, 128, 128)).astype(np.float32)
    B_all = B_w.transpose(0, 2, 1).reshape(128, 4096)
    WT_aug = np.concatenate(
        [W_base.T, B_all, np.zeros((128, 4096), np.float32)], axis=0)
    wbst = w_stream(WT_aug, dcg=2)
    return wg9, ast, wbst


def l0a_host(Wq, bq, Wk, bk, Wv, bv, Wo, bo):
    WT_kv = np.concatenate([Wk.T, Wv.T], axis=1)
    wkvst = w_stream(WT_kv.astype(np.float32), dcg=4)
    kvbias = np.concatenate([bk, bv])[None, :].astype(np.float32)
    WqT = Wq.T.astype(np.float32)
    sl = []
    for h in range(32):
        wh = WqT[:, h * 128:(h + 1) * 128].reshape(32, 128, 128)
        for half in range(2):
            sl.append(np.ascontiguousarray(
                wh[half * 16:(half + 1) * 16].transpose(1, 0, 2).reshape(128, 2048)))
    wqst = np.stack(sl)
    bqa = np.ascontiguousarray(bq.reshape(32, 1, 128)).astype(np.float32)
    wost = w_stream(Wo.T.astype(np.float32), dcg=4)
    bo_ = bo[None, :].astype(np.float32)
    return wkvst, kvbias, wqst, bqa, wost, bo_


def emit_l0a_kvproj(K, c, wkvst_ap, kvbias_ap, kv_dram, ln_sems):
    sv, sa, sp = ln_sems
    s_kvs = K.sem("s_l0a_kvs")
    s_kvo = [K.sem("s_l0a_kvo0"), K.sem("s_l0a_kvo1")]
    hwf = c.hw[:].bitcast(F32)

    def dest(K, c, tt, os, psum, s_ev):
        idx = tt + os * NT
        par = idx % 2
        if idx >= 2:
            K.wait("vector", s_kvo[par])
        ob = c.ob[:, par * 512:(par + 1) * 512]
        if os < 2:
            K.wait("scalar", s_ev)
            K.op("scalar", lambda e, p=psum: e.activation(hwf, p, AF.Exp), inc=s_kvs)
            K.wait("vector", s_kvs)
            K.op("vector", lambda e: e.tensor_scalar(hwf, hwf, 1.0, None, ALU.min), inc=s_ev)
            K.wait("vector", s_ev)
            K.op("vector", lambda e, p=psum, o=ob: e.scalar_tensor_tensor(
                o, p, 0.0, hwf, ALU.max, ALU.add), inc=s_ev)
        else:
            K.op("vector", lambda e, p=psum, o=ob: e.tensor_copy(o, p), inc=s_ev)
        K.wait("sync", s_ev)
        K.dma("sync", kv_dram[tt][:, os * 512:(os + 1) * 512], ob, s_kvo[par])

    s_ev, s_mm, s_wf = emit_proj_tok(
        K, c, wkvst_ap, 4, dest, name="pj", bias_ap=kvbias_ap,
        pe_waits=(sv,), dma_waits=(sv,))
    return s_ev, s_kvo


def emit_l0a_kvstate(K, c, kv_dram, s_kvo, kvout_ap):
    s_t = [K.sem("s_l0a_t0"), K.sem("s_l0a_t1")]
    s_pe = K.sem("s_l0a_kvpe")
    s_kv = K.sem("s_l0a_kve")
    s_ko = K.sem("s_l0a_ko")
    hwf = c.hw[:].bitcast(F32)
    ks_acc = c.rs[:, 144:152]
    for w in s_kvo:
        K.wait("sync", w)
    K.op("vector", lambda e: e.memset(ks_acc, 0.0), inc=s_kv)
    K.wait("vector", s_kv)
    tld = 0
    for wave in range(2):
        for tt in range(NT):
            par = tld % 2
            sl = c.wbuf[:, par * 4096: par * 4096 + 2048]
            if tld >= 2:
                K.wait("sync", s_pe, s_pe.cnt - 1)
            K.dma("sync", sl, kv_dram[tt], s_t[par])
            K.wait("tensor", s_t[par])
            if tt > 0:
                K.wait("tensor", s_kv)
            for q in range(4):
                kvh = wave * 4 + q
                ksl = sl[:, kvh * 128:(kvh + 1) * 128]
                vsl = sl[:, 1024 + kvh * 128: 1024 + (kvh + 1) * 128]
                K.op("tensor", lambda e, k=ksl, v=vsl, b=c.ps[q][:, 0:128],
                     st=(tt == 0), spp=(tt == NT - 1):
                     e.matmul(b, k, v, start=st, stop=spp), inc=None)
                K.op("tensor", lambda e, k=ksl, b=c.ps[4][:, q:q + 1]:
                     e.matmul(b, k, c.onescol[:], start=True, stop=True),
                     inc=(s_pe if q == 3 else None))
            tld += 1
            K.wait("vector", s_pe)
            K.op("vector", lambda e, o=ks_acc[:, wave * 4:(wave + 1) * 4]:
                 e.tensor_tensor(o, o, c.ps[4][:, 0:4], ALU.add), inc=s_kv)
        K.wait("vector", s_pe)
        for q in range(4):
            if wave > 0 or q > 0:
                K.wait("vector", s_ko)
            K.op("vector", lambda e, b=c.ps[q][:, 0:128], o=hwf[:, 0:128]:
                 e.tensor_copy(o, b), inc=s_kv)
            K.wait("sync", s_kv)
            K.dma("sync", kvout_ap[:, (wave * 4 + q) * 128:(wave * 4 + q) * 128 + 128],
                  hwf[:, 0:128], s_ko)
    K.wait("vector", s_kv)
    K.op("vector", lambda e: e.engine_nop(), inc=s_kv)
    K.wait("sync", s_kv)
    K.dma("sync", kvout_ap[:, 1024:1032], ks_acc, s_ko)
    K.wait("sync", s_ko)
    return s_ko


def emit_l0a_kvreduce_in(K, c, kvparts_ap):
    s_l = K.sem("s_l0a_kl")
    s_v = K.sem("s_l0a_kvv")
    wb32 = c.wbuf[:].bitcast(F32)
    for piece in range(3):
        w = 512 if piece < 2 else 8
        if piece > 0:
            K.wait("sync", s_v)
        for r in range(4):
            K.dma("sync", wb32[:, r * 512: r * 512 + w],
                  kvparts_ap[r][:, piece * 512: piece * 512 + w], s_l)
        K.wait("vector", s_l)
        K.op("vector", lambda e, w=w: e.tensor_add(wb32[:, 0:w], wb32[:, 0:w], wb32[:, 512:512 + w]), inc=s_v)
        K.op("vector", lambda e, w=w: e.tensor_add(wb32[:, 1024:1024 + w], wb32[:, 1024:1024 + w], wb32[:, 1536:1536 + w]), inc=s_v)
        K.wait("vector", s_v)
        K.op("vector", lambda e, w=w: e.tensor_add(wb32[:, 0:w], wb32[:, 0:w], wb32[:, 1024:1024 + w]), inc=s_v)
        K.wait("vector", s_v)
        if piece < 2:
            K.op("vector", lambda e, piece=piece: e.tensor_copy(
                c.ob[:, piece * 512:(piece + 1) * 512], wb32[:, 0:512]), inc=s_v)
        else:
            K.op("vector", lambda e: e.tensor_scalar(wb32[:, 0:8], wb32[:, 0:8], 1e-10, None, ALU.add), inc=s_v)
            K.wait("vector", s_v)
            K.op("vector", lambda e: e.reciprocal(c.rs[:, 128:136], wb32[:, 0:8]), inc=s_v)
        K.wait("vector", s_v)
    K.op("vector", lambda e: e.engine_nop(), inc=s_v)
    return s_v


def emit_l0a_attn(K, c, wqst_ap, bqa_ap, attn_dram, s_kvready, ln_sems):
    sv, sa, sp = ln_sems
    s_w = [K.sem("s_l0a_qw0"), K.sem("s_l0a_qw1")]
    s_c = K.sem("s_l0a_qc")
    s_q = K.sem("s_l0a_qpe")
    s_qv = K.sem("s_l0a_qv")
    s_qs = K.sem("s_l0a_qs")
    s_b = K.sem("s_l0a_qb")
    s_ao = [K.sem("s_l0a_ao0"), K.sem("s_l0a_ao1")]
    wb32 = c.wbuf[:].bitcast(F32)
    hwf = c.hw[:].bitcast(F32)
    q_phi = c.wbuf[:, 6144:7168]
    bqrow = c.rs[0:1, 256:384]
    for e in ("vector", "tensor", "sync", "scalar"):
        K.wait(e, s_kvready)
    K.wait("tensor", sv)
    cast_done = []
    qdone = []
    for h in range(32):
        for half in range(2):
            hs = h * 2 + half
            for j in range(2):
                p = hs * 2 + j
                if p >= 1:
                    K.wait("sync", s_c, cast_done[p - 1])
                K.dma("sync", wb32[:, 2048:3072],
                      wqst_ap[hs][:, j * 1024:(j + 1) * 1024], s_w[0])
                if hs >= 2 and j == 0:
                    K.wait("vector", s_q, qdone[hs - 2])
                K.wait("vector", s_w[0])
                K.op("vector", lambda e,
                     o=c.wbuf[:, (hs % 2) * 2048 + j * 1024: (hs % 2) * 2048 + (j + 1) * 1024],
                     i=wb32[:, 2048:3072]:
                     e.tensor_copy(o, i), inc=s_c)
                cast_done.append(s_c.cnt)
        if h >= 1:
            K.wait("sync", s_q)
        K.dma("sync", bqrow, bqa_ap[h], s_b)
        if h > 0:
            K.wait("tensor", s_qv)
        K.wait("tensor", s_b)
        for half in range(2):
            hs = h * 2 + half
            K.wait("tensor", s_c, cast_done[hs * 2 + 1])
            for i in range(16):
                dc = half * 16 + i
                l = c.wbuf[:, (hs % 2) * 2048 + i * 128: (hs % 2) * 2048 + (i + 1) * 128]
                for ts in range(2):
                    K.op("tensor", lambda e, l=l, b=c.ps[5 + ts][:, 0:512],
                         r=c.x_fm[:, dc * T + ts * 512: dc * T + ts * 512 + 512],
                         st=(dc == 0), spp=(dc == NDC - 1):
                         e.matmul(b, l, r, start=st, stop=spp), inc=None)
            K.op("tensor", lambda e: e.drain(), inc=s_q)
            qdone.append(s_q.cnt)
        for ts in range(2):
            for seg in range(4):
                K.op("tensor", lambda e, b=c.ps[5 + ts][:, seg * 128:(seg + 1) * 128]:
                     e.matmul(b, bqrow, c.onesr[:], start=False, stop=True,
                              skip_group_check=True),
                     inc=(s_q if (ts == 1 and seg == 3) else None))
        for ts in range(2):
            K.wait("scalar", s_q)
            K.wait("scalar", s_qv)
            K.op("scalar", lambda e, p=c.ps[5 + ts][:, 0:512]: e.activation(hwf, p, AF.Exp), inc=s_qs)
            K.wait("vector", s_qs)
            K.op("vector", lambda e: e.tensor_scalar(hwf, hwf, 1.0, None, ALU.min), inc=s_qv)
            K.wait("vector", s_qv)
            K.op("vector", lambda e, p=c.ps[5 + ts][:, 0:512], o=q_phi[:, ts * 512:(ts + 1) * 512]:
                 e.scalar_tensor_tensor(o, p, 0.0, hwf, ALU.max, ALU.add), inc=s_qv)
            K.wait("vector", s_qv)
        kvh = h // 4
        K.wait("tensor", s_qv)
        for ts in range(2):
            apar = ts
            K.op("tensor", lambda e, b=c.ps[3 + apar][:, 0:512],
                 l=c.ob[:, kvh * 128:(kvh + 1) * 128],
                 r=q_phi[:, ts * 512:(ts + 1) * 512]:
                 e.matmul(b, l, r, start=True, stop=True), inc=s_q)
            K.wait("vector", s_q)
            idx = h * 2 + ts
            if idx >= 2:
                K.wait("vector", s_ao[apar])
            obf = c.wbuf[:, 7168 + apar * 512: 7168 + (apar + 1) * 512]
            K.op("vector", lambda e, b=c.ps[3 + apar][:, 0:512], o=obf, kvh=kvh:
                 e.tensor_scalar(o, b, c.rs[:, 128 + kvh:129 + kvh], None, ALU.mult), inc=s_qv)
            K.wait("sync", s_qv)
            K.dma("sync", attn_dram[0:128, h * 1024 + ts * 512: h * 1024 + (ts + 1) * 512],
                  obf, s_ao[apar])
    K.wait("sync", s_ao[0])
    K.wait("sync", s_ao[1])
    return s_ao


def emit_l0a_oproj(K, c, attn_dram, wost_ap, bo_ap, s_ao, prequant=False, name="l0ao",
                   attn_list=None):
    s_l = K.sem("s_al_" + name)
    for w in s_ao:
        K.wait("sync", w)
    if attn_list is not None:
        for dc in range(NDC):
            K.dma("sync", c.x_fm[:, dc * T:(dc + 1) * T], attn_list[dc], s_l)
    else:
        K.dma("sync", c.x_fm[:], attn_dram, s_l)
    s_ev, s_mm, s_wf = emit_proj_tok(
        K, c, wost_ap, 8, dest_resid, name=name, bias_ap=bo_ap, prequant=prequant,
        pe_waits=(s_l,), dma_waits=(s_l,))
    return s_ev


def l1ffn_host(W, A, B, router_W):
    rw8 = np.ascontiguousarray(router_W.T.reshape(32, 128, 8).transpose(1, 0, 2)
                               .reshape(128, 256)).astype(np.float32)
    A_all = A.transpose(2, 0, 1).reshape(4096, 128)
    ast = np.ascontiguousarray(A_all.reshape(32, 128, 128)).astype(np.float32)
    B_all = B.transpose(0, 2, 1).reshape(128, 4096)
    WT_aug = np.concatenate(
        [W.T, B_all, np.zeros((128, 4096), np.float32)], axis=0)
    wbst = w_stream(WT_aug, dcg=2)
    return rw8, ast, wbst


def sel_host(core):
    b, r = divmod(core, 4)
    pos = np.arange(r * T, (r + 1) * T)
    sel = ((pos >= 611) & (pos <= 4094)).astype(np.float32)
    return np.ascontiguousarray(sel.reshape(8, 128).T).astype(BF16NP)


def emit_l1ffn_router_partial(K, c, rw8_ap, selb_ap, rpart_out_ap, ln_sems, tag=""):
    sv, sa, sp = ln_sems
    s_r = K.sem("s_rt_r" + tag)
    s_rv = K.sem("s_rt_v" + tag)
    s_w = K.sem("s_rt_w" + tag)
    obf = c.ob[:].bitcast(F32)
    rw8b = obf[:, 256:384].bitcast(BF16)
    selb = obf[:, 384:388].bitcast(BF16)
    K.dma("sync", obf[:, 0:256], rw8_ap, s_w)
    K.dma("sync", selb, selb_ap, s_w)
    K.wait("vector", s_w)
    K.op("vector", lambda e: e.tensor_copy(rw8b, obf[:, 0:256]), inc=s_rv)
    K.wait("vector", s_rv)
    K.wait("tensor", s_rv)
    K.wait("tensor", sv)
    for tt in range(NT):
        if tt > 0:
            K.wait("tensor", s_rv)
        for dc in range(NDC):
            K.op("tensor", lambda e, l=c.x_fm[:, dc * T + tt * 128: dc * T + (tt + 1) * 128],
                 r=rw8b[:, dc * 8: dc * 8 + 8], st=(dc == 0), spp=(dc == NDC - 1):
                 e.matmul(c.ps[6][:, 0:8], l, r, start=st, stop=spp),
                 inc=(s_r if dc == NDC - 1 else None))
        K.wait("vector", s_r)
        K.op("vector", lambda e: e.tensor_copy(c.rs[:, 256:260].bitcast(BF16), c.ps[6][:, 0:8]), inc=s_rv)
        K.wait("vector", s_rv)
        K.op("vector", lambda e: e.engine_nop(), inc=s_rv)
        K.wait("tensor", s_rv)
        K.op("tensor", lambda e, l=selb[:, tt:tt + 1], st=(tt == 0), spp=(tt == NT - 1):
             e.matmul(c.ps[7][0:1, 0:8], l, c.rs[:, 256:260].bitcast(BF16), start=st, stop=spp),
             inc=s_r)
    K.wait("vector", s_r)
    if rpart_out_ap is not None:
        K.op("vector", lambda e: e.tensor_copy(c.rs[0:1, 264:272], c.ps[7][0:1, 0:8]), inc=s_rv)
        K.wait("sync", s_rv)
        s_o = K.sem("s_rt_o" + tag)
        K.dma("sync", rpart_out_ap, c.rs[0:1, 264:272], s_o)
        K.wait("sync", s_o)
        return s_o
    return s_r


def emit_l1ffn_routing(K, c, rparts_ap, rb_ap, e8_ap, tag=""):
    s_l = K.sem("s_ro_l" + tag)
    s_v = K.sem("s_ro_v" + tag)
    s_s = K.sem("s_ro_s" + tag)
    s_p = K.sem("s_ro_p" + tag)
    row = lambda i: c.rs[0:1, 280 + i * 8: 280 + (i + 1) * 8]
    e8_sb = c.rs[:, 336:400].bitcast(BF16)
    col1 = lambda i: c.rs[0:1, 408 + i: 409 + i]
    rowb = c.rs[0:1, 404:408].bitcast(BF16)[0:1, 0:8]
    colb = c.rs[:, 274:275].bitcast(BF16)[0:8, 0:1]
    for r in range(4):
        K.dma("sync", c.rs[0:1, 280 + 8 * r:288 + 8 * r], rparts_ap[r], s_l)
    K.dma("sync", c.rs[0:1, 312:320], rb_ap, s_l)
    K.dma("sync", e8_sb[0:8, 0:128], e8_ap, s_l)
    K.wait("vector", s_l)
    K.op("vector", lambda e: e.tensor_add(row(0), row(0), row(1)), inc=s_v)
    K.op("vector", lambda e: e.tensor_add(row(2), row(2), row(3)), inc=s_v)
    K.wait("vector", s_v)
    K.op("vector", lambda e: e.tensor_add(row(0), row(0), row(2)), inc=s_v)
    K.wait("vector", s_v)
    K.op("vector", lambda e: e.scalar_tensor_tensor(
        row(0), row(0), 1.0 / 3484.0, c.rs[0:1, 312:320], ALU.mult, ALU.add), inc=s_v)
    K.wait("vector", s_v)
    K.op("vector", lambda e: e.reduce_max(col1(0), row(0), axis=AX.X), inc=s_v)
    K.wait("vector", s_v)
    K.op("vector", lambda e: e.tensor_scalar(col1(1), col1(0), -1.0, None, ALU.mult), inc=s_v)
    K.wait("scalar", s_v)
    K.op("scalar", lambda e: e.activation(row(5), row(0), AF.Exp,
         bias=col1(1), accum_out=col1(2)), inc=s_s)
    K.wait("vector", s_s)
    K.op("vector", lambda e: e.reciprocal(col1(3), col1(2)), inc=s_v)
    K.wait("vector", s_v)
    K.op("vector", lambda e: e.tensor_scalar(row(6), row(5), col1(3), None, ALU.mult), inc=s_v)
    K.wait("vector", s_v)
    K.op("vector", lambda e: e.tensor_copy(rowb, row(6)), inc=s_v)
    K.wait("tensor", s_v)
    K.op("tensor", lambda e: e.transpose(c.ps[6][:].bitcast(BF16)[0:8, 0:1], rowb, c.identb[0:1, 0:1]), inc=s_p)
    K.wait("vector", s_p)
    K.op("vector", lambda e: e.tensor_copy(colb, c.ps[6][:].bitcast(BF16)[0:8, 0:1]), inc=s_v)
    K.wait("vector", s_v)
    K.op("vector", lambda e: e.engine_nop(), inc=s_v)
    K.wait("tensor", s_v)
    K.op("tensor", lambda e: e.matmul(c.ps[7][:, 0:1], e8_sb[0:8, 0:128],
         colb, start=True, stop=True), inc=s_p)
    K.wait("vector", s_p)
    K.op("vector", lambda e: e.tensor_copy(c.rs[:, 272:273], c.ps[7][:, 0:1]), inc=s_v)
    K.wait("vector", s_v)
    K.op("vector", lambda e: e.engine_nop(), inc=s_v)
    return s_v


def emit_l1ffn_main(K, c, ast_ap, wbst_ap, bias_ap, s_route, ln_sems, tag=""):
    sv, sa, sp = ln_sems
    s_h = K.sem("s_lf_h" + tag)
    base_h = s_h.cnt
    s_ha = K.sem("s_lf_ha" + tag)
    s_hc = K.sem("s_lf_hc" + tag)
    s_hv = K.sem("s_lf_hv" + tag)
    obf = c.ob[:].bitcast(F32)
    abf = c.rs[:, 0:128].bitcast(BF16)
    ast32 = obf[:, 0:128]
    K.wait("tensor", sv)
    K.wait("vector", s_route)
    cast_done = []
    for dc in range(NDC):
        par = dc % 2
        if dc >= 1:
            K.wait("sync", s_hc, cast_done[dc - 1])
        K.dma("sync", ast32, ast_ap[dc], s_ha)
        K.wait("vector", s_ha)
        if dc >= 2:
            K.wait("vector", s_h, base_h + dc - 1)
        K.op("vector", lambda e, o=abf[:, par * 128:(par + 1) * 128]:
             e.tensor_copy(o, ast32), inc=s_hc)
        cast_done.append(s_hc.cnt)
        K.wait("tensor", s_hc)
        for th in range(2):
            K.op("tensor", lambda e, b=c.ps[4 + th][:, 0:512],
                 l=abf[:, par * 128:(par + 1) * 128],
                 r=c.x_fm[:, dc * T + th * 512: dc * T + th * 512 + 512],
                 st=(dc == 0), spp=(dc == NDC - 1):
                 e.matmul(b, l, r, start=st, stop=spp),
                 inc=(s_h if th == 1 else None))
    K.wait("vector", s_h)
    for th in range(2):
        K.op("vector", lambda e, b=c.ps[4 + th][:, 0:512], o=c.hw[:, th * 512:(th + 1) * 512]:
             e.tensor_scalar(o, b, c.rs[:, 272:273], None, ALU.mult), inc=s_hv)
    K.wait("vector", s_hv)
    K.op("vector", lambda e: e.engine_nop(), inc=s_hv)

    def lhsT_fn(dc, tt):
        if dc < NDC:
            return c.x_fm[:, dc * T + tt * 128: dc * T + (tt + 1) * 128]
        return c.hw[:, tt * 128:(tt + 1) * 128]
    s_ev, s_mm, s_wf = emit_proj_tok(
        K, c, wbst_ap, 8, dest_resid, name="pj", bias_ap=bias_ap,
        n_dc=34, dcg=2, lhsT_fn=lhsT_fn, pe_waits=(s_hv,), dma_waits=(s_hv,))
    return s_ev


def l1a_host(Wq_d, bq_d, W_qk, b_qk, Wkv_d, bkv_d, Wv_u, bv_u, Wo, bo):
    Wc16 = np.concatenate([Wq_d, Wkv_d], axis=1)
    wc16 = np.ascontiguousarray(Wc16.reshape(32, 128, 16).transpose(1, 0, 2)
                                .reshape(128, 512)).astype(np.float32)
    cbias = np.concatenate([bq_d, bkv_d])[None, :].astype(np.float32)
    wqk = W_qk[None if False else slice(None), :].astype(np.float32)
    bqk4 = np.ascontiguousarray(b_qk.reshape(32, 4).T).astype(np.float32)
    wvu = Wv_u.astype(np.float32)
    bvu = np.ascontiguousarray(bv_u.reshape(32, 128, 1)).astype(np.float32)
    wost = w_stream(Wo.astype(np.float32), dcg=4)
    boo = bo[None, :].astype(np.float32)
    return wc16, cbias, wqk, bqk4, wvu, bvu, wost, boo


def emit_l1a_cproj(K, c, wc16_ap, cbias_ap, ckv_own_ap, ln_sems, want_cq=True, tag=""):
    sv, sa, sp = ln_sems
    s_w = K.sem("s_ca_w" + tag)
    s_pe = K.sem("s_ca_pe" + tag)
    s_v = K.sem("s_ca_v" + tag)
    s_o = K.sem("s_ca_o" + tag)
    wbf = c.wbuf[:].bitcast(F32)
    wc16b = c.ob[:, 0:512]
    K.wait("sync", sp)
    K.dma("sync", wbf[:, 0:512], wc16_ap, s_w)
    K.dma("sync", c.rs[0:1, 0:16], cbias_ap, s_w)
    K.wait("vector", s_w)
    K.op("vector", lambda e: e.tensor_copy(wc16b, wbf[:, 0:512]), inc=s_v)
    K.wait("vector", s_v)
    K.wait("tensor", s_v)
    K.wait("tensor", sv)
    cqb = c.rs[:, 16:24].bitcast(BF16)
    ckvb = c.rs[:, 24:26].bitcast(BF16)
    for tt in range(NT):
        if tt > 0:
            K.wait("tensor", s_v)
        for dc in range(NDC):
            K.op("tensor", lambda e, l=c.x_fm[:, dc * T + tt * 128: dc * T + (tt + 1) * 128],
                 r=wc16b[:, dc * 16: dc * 16 + 16], st=(dc == 0), spp=(dc == NDC - 1):
                 e.matmul(c.ps[6][:, 0:16], l, r, start=st, stop=spp), inc=None)
        K.op("tensor", lambda e: e.matmul(c.ps[6][:, 0:16], c.onesr[:], c.rs[0:1, 0:16],
             start=False, stop=True, skip_group_check=True), inc=s_pe)
        K.wait("vector", s_pe)
        if tt >= 1:
            K.wait("vector", s_o)
        K.op("vector", lambda e: e.tensor_copy(ckvb, c.ps[6][:, 12:16]), inc=s_v)
        if want_cq:
            K.op("vector", lambda e: e.tensor_copy(cqb[:, 0:12], c.ps[6][:, 0:12]), inc=s_v)
        K.wait("sync", s_v)
        if tt >= 1:
            K.wait("sync", s_o)
        K.dma("sync", ckv_own_ap[tt * 128:(tt + 1) * 128, :], ckvb, s_o)
        if want_cq:
            K.wait("tensor", s_v)
            K.op("tensor", lambda e: e.transpose(c.ps[7][:].bitcast(BF16)[0:12, 0:128],
                 cqb[:, 0:12], c.identb[:]), inc=s_pe)
            K.wait("vector", s_pe)
            K.op("vector", lambda e, tt=tt: e.tensor_copy(
                c.wbuf[0:12, 6144 + tt * 128: 6144 + (tt + 1) * 128],
                c.ps[7][:].bitcast(BF16)[0:12, 0:128]), inc=s_v)
        K.wait("vector", s_v)
        K.op("vector", lambda e: e.engine_nop(), inc=s_v)
    K.wait("sync", s_o)
    return s_v, s_o


def emit_l1a_attn(K, c, ckv_full_ap, wqk_ap, bqk4_ap, wvu_ap, bvu_ap, attn_dram,
                  s_cq, tag=""):
    s_w = K.sem("s_la_w" + tag)
    s_pe = K.sem("s_la_pe" + tag)
    s_v = K.sem("s_la_v" + tag)
    s_s = K.sem("s_la_s" + tag)
    s_b = K.sem("s_la_b" + tag)
    s_t = K.sem("s_la_t" + tag)
    s_ao = [K.sem("s_la_ao0" + tag), K.sem("s_la_ao1" + tag)]
    obf = c.ob[:].bitcast(F32)
    xff = c.x_fm[:].bitcast(F32)
    hwf = c.hw[:].bitcast(F32)
    ckv_fm = c.wbuf[0:4, 0:4096]
    q4h_sb = c.wbuf[0:4, 4096:5120]
    P = [c.wbuf[:, 5120:5632], c.wbuf[:, 5632:6144]]
    cq_fm = c.wbuf[0:12, 6144:7168]
    tn_sb = c.wbuf[0:4, 7168:7680]
    wqk_b = c.ob[0:12, 640:768]
    bqk4 = obf[0:4, 448:480]
    bvu_col = obf[:, 480:481]
    wvu_b = c.x_fm[0:4, 0:4096]
    aout = [c.ob[:, 128:640], c.wbuf[:, 7680:8192]]
    K.wait("sync", s_cq)
    for g in range(8):
        K.dma("sync", c.ob[:, g * 16:(g + 1) * 16].rearrange("p (a b) -> p a b", b=4),
              ckv_full_ap[g * 512:(g + 1) * 512, :].rearrange("(a p) b -> p a b", p=128),
              s_w, allow_slow_non_contiguous=True)
    K.dma("sync", xff[0:4, 4096:8192], wvu_ap, s_w)
    K.dma("sync", hwf[0:12, 0:128], wqk_ap, s_w)
    K.dma("sync", bqk4, bqk4_ap, s_w)
    K.wait("vector", s_w)
    K.op("vector", lambda e: e.tensor_copy(wvu_b, xff[0:4, 4096:8192]), inc=s_v)
    K.op("vector", lambda e: e.tensor_copy(wqk_b, hwf[0:12, 0:128]), inc=s_v)
    K.wait("vector", s_v)
    K.op("vector", lambda e: e.engine_nop(), inc=s_v)
    K.wait("tensor", s_v)
    for kc in range(NDC):
        if kc >= 2:
            K.wait("tensor", s_v, s_v.cnt)
        K.op("tensor", lambda e, kc=kc: e.transpose(
            c.ps[6 + kc % 2][:].bitcast(BF16)[0:4, 0:128],
            c.ob[:, kc * 4:(kc + 1) * 4], c.identb[:]), inc=s_pe)
        K.wait("vector", s_pe)
        K.op("vector", lambda e, kc=kc: e.tensor_copy(
            ckv_fm[0:4, kc * 128:(kc + 1) * 128],
            c.ps[6 + kc % 2][:].bitcast(BF16)[0:4, 0:128]), inc=s_v)
    K.wait("vector", s_v)
    K.op("vector", lambda e: e.engine_nop(), inc=s_v)
    K.wait("tensor", s_v)
    for h in range(32):
        if h >= 1:
            K.wait("sync", s_v)
        K.dma("sync", bvu_col, bvu_ap[h], s_b)
        K.wait("tensor", s_b)
        if h >= 1:
            K.wait("tensor", s_v)
        for ts in range(2):
            K.op("tensor", lambda e, ts=ts, h=h: e.matmul(
                c.ps[6 + ts][0:4, 0:512], wqk_b[0:12, 4 * h:4 * h + 4],
                cq_fm[0:12, ts * 512:(ts + 1) * 512], start=True, stop=True), inc=s_pe)
        K.wait("vector", s_pe)
        for ts in range(2):
            K.op("vector", lambda e, ts=ts, h=h: e.tensor_scalar(
                q4h_sb[0:4, ts * 512:(ts + 1) * 512], c.ps[6 + ts][0:4, 0:512],
                bqk4[0:4, h:h + 1], 0.5, ALU.add, ALU.mult), inc=s_v)
        K.wait("vector", s_v)
        K.op("vector", lambda e: e.engine_nop(), inc=s_v)
        K.wait("tensor", s_v)
        K.wait("scalar", s_v)
        for qs in range(2):
            for kc in range(NDC):
                par = kc % 2
                if kc >= 2:
                    K.wait("tensor", s_s)
                K.op("tensor", lambda e, kc=kc, par=par, qs=qs: e.matmul(
                    c.ps[par][:, 0:512], ckv_fm[0:4, kc * 128:(kc + 1) * 128],
                    q4h_sb[0:4, qs * 512:(qs + 1) * 512], start=True, stop=True), inc=s_pe)
                K.wait("scalar", s_pe)
                if kc >= 2:
                    K.wait("scalar", s_pe, s_pe.cnt)
                if kc >= 2:
                    K.wait("scalar", s_pe)
                K.op("scalar", lambda e, par=par: e.activation(P[par], c.ps[par][:, 0:512], AF.Exp), inc=s_s)
                K.wait("tensor", s_s)
                K.op("tensor", lambda e, kc=kc, par=par: e.matmul(
                    c.ps[2][0:4, 0:512], c.ob[:, kc * 4:(kc + 1) * 4], P[par],
                    start=(kc == 0), stop=(kc == NDC - 1)), inc=None)
                K.op("tensor", lambda e, kc=kc, par=par: e.matmul(
                    c.ps[3][0:1, 0:512], c.onescol[:], P[par],
                    start=(kc == 0), stop=(kc == NDC - 1)), inc=(s_pe if kc == NDC - 1 else None))
            K.wait("vector", s_pe)
            K.op("vector", lambda e: e.tensor_copy(c.rs[0:4, 0:512], c.ps[2][0:4, 0:512]), inc=s_v)
            K.op("vector", lambda e: e.reciprocal(hwf[0:1, 0:512], c.ps[3][0:1, 0:512]), inc=s_v)
            K.wait("vector", s_v)
            K.op("vector", lambda e: e.engine_nop(), inc=s_v)
            K.wait("tensor", s_v)
            K.op("tensor", lambda e: e.matmul(c.ps[4][0:4, 0:512], c.onesr[0:1, 0:4],
                 hwf[0:1, 0:512], start=True, stop=True), inc=s_pe)
            K.wait("vector", s_pe)
            K.op("vector", lambda e: e.tensor_tensor(tn_sb, c.rs[0:4, 0:512],
                 c.ps[4][0:4, 0:512], ALU.mult), inc=s_v)
            K.wait("vector", s_v)
            K.op("vector", lambda e: e.engine_nop(), inc=s_v)
            K.wait("tensor", s_v)
            K.op("tensor", lambda e, h=h: e.matmul(c.ps[5][:, 0:512],
                 wvu_b[0:4, h * 128:(h + 1) * 128], tn_sb, start=True, stop=True), inc=s_pe)
            K.wait("vector", s_pe)
            idx = h * 2 + qs
            if idx >= 2:
                K.wait("vector", s_ao[qs])
            K.op("vector", lambda e, qs=qs: e.tensor_scalar(
                aout[qs], c.ps[5][:, 0:512], bvu_col, None, ALU.add), inc=s_v)
            K.wait("sync", s_v)
            K.dma("sync", attn_dram[0:128, h * 1024 + qs * 512: h * 1024 + (qs + 1) * 512],
                  aout[qs], s_ao[qs])
    K.wait("sync", s_ao[0])
    K.wait("sync", s_ao[1])
    return s_ao


def l1b_host_one(codes, scales):
    cst = w_stream(codes.astype(np.int32), dcg=4)
    n_slab = cst.shape[0]
    ng = codes.shape[0] // 512
    rows = []
    for s in range(n_slab):
        os_, g = divmod(s, ng)
        for j in range(2):
            for q in range(2):
                dcs = g * 4 + j * 2 + q
                rows.append(scales[dcs, os_ * 512:(os_ + 1) * 512])
    srow = np.ascontiguousarray(np.stack(rows)[:, None, :]).astype(np.float32)
    return cst, srow


def l1b_host(q_codes, q_scales, k_codes, k_scales, v_codes, v_scales, o_codes, o_scales):
    mk_stream = l1b_host_one
    qc, qs_ = mk_stream(q_codes, q_scales)
    kc_, ks_ = mk_stream(k_codes, k_scales)
    vc, vs_ = mk_stream(v_codes, v_scales)
    oc, os_ = mk_stream(o_codes, o_scales)
    return qc, qs_, kc_, ks_, vc, vs_, oc, os_


def rope_host(core):
    b, r = divmod(core, 4)
    pos = np.arange(r * T, (r + 1) * T, dtype=np.float32)
    inv = (1.0 / 10000.0 ** (np.arange(0, 128, 2, dtype=np.float32) / 128.0))
    f = pos[:, None] * inv[None, :]
    cos = np.cos(f).astype(np.float32)
    sin = np.sin(f).astype(np.float32)
    cosq = np.tile(cos, (1, 4)).reshape(8, 128, 256)
    sinq = np.tile(sin, (1, 4)).reshape(8, 128, 256)
    return np.ascontiguousarray(cosq), np.ascontiguousarray(sinq)


def emit_l1b_dequant(K, c, cst_ap, srow_ap, wdq_ap, tag=""):
    s_cd = K.sem("s_dq_c" + tag)
    s_sr = K.sem("s_dq_s" + tag)
    s_pe = K.sem("s_dq_p" + tag)
    s_v = K.sem("s_dq_v" + tag)
    s_o = [K.sem("s_dq_o0" + tag), K.sem("s_dq_o1" + tag)]
    wbi = c.wbuf[:].bitcast(I32)
    cpiece = wbi[:, 1024:2048]
    n_slab = cst_ap.shape[0]
    stt_done = []
    for qq in range(2 * n_slab):
        slab, j = divmod(qq, 2)
        if qq >= 1:
            K.wait("sync", s_v, stt_done[2 * qq - 1])
        K.dma("sync", cpiece, cst_ap[slab][:, j * 1024:(j + 1) * 1024], s_cd)
        for q in range(2):
            p2 = (qq * 2 + q) % 2
            K.wait("sync", s_pe)
            K.dma("sync", c.rs[0:1, 0:512], srow_ap[qq * 2 + q][0:1, :], s_sr)
            K.wait("tensor", s_sr)
            K.wait("tensor", s_v, stt_done[qq * 2 + q - 2] if qq * 2 + q >= 2 else 0)
            K.op("tensor", lambda e, b=c.ps[p2][:, 0:512]: e.matmul(
                b, c.onesr[:], c.rs[0:1, 0:512], start=True, stop=True), inc=s_pe)
            K.wait("vector", s_pe)
            K.wait("vector", s_cd)
            if qq * 2 + q >= 2:
                K.wait("vector", s_o[p2])
            K.op("vector", lambda e, q=q, p2=p2, b=c.ps[p2][:, 0:512]:
                 e.scalar_tensor_tensor(
                     c.wbuf[:, 6144 + p2 * 512: 6144 + (p2 + 1) * 512],
                     cpiece[:, q * 512:(q + 1) * 512], -8.0, b, ALU.add, ALU.mult),
                 inc=s_v)
            stt_done.append(s_v.cnt)
            K.wait("sync", s_v)
            K.dma("sync", wdq_ap[slab][:, j * 1024 + q * 512: j * 1024 + (q + 1) * 512],
                  c.wbuf[:, 6144 + p2 * 512: 6144 + (p2 + 1) * 512], s_o[p2])
    K.wait("sync", s_o[0])
    K.wait("sync", s_o[1])
    return s_o


def emit_l1b_qkv(K, c, wq_dq_ap, wkv_dq_ap, cosq_ap, sinq_ap, q_own_ap, kv_own_ap,
                 ln_sems, tag=""):
    sv, sa, sp = ln_sems
    hwf = c.hw[:].bitcast(F32)
    s_cs = K.sem("s_qb_cs" + tag)
    s_oo = K.sem("s_qb_oo" + tag)
    cosb = c.ob[:, 0:256]
    sinb = c.ob[:, 256:512]
    ob_i = c.ob[:, 512:1024]

    def rope_dest(out_ap, do_rope):
        def dest(K, c, tt, os, psum, s_ev):
            K.wait("vector", s_ev)
            if do_rope(os):
                K.wait("sync", s_ev)
                K.dma("sync", cosb, cosq_ap[tt], s_cs)
                K.dma("sync", sinb, sinq_ap[tt], s_cs)
                pe_ = psum[:, 0:512:2]
                po_ = psum[:, 1:512:2]
                t1 = hwf[:, 0:256]
                t2 = hwf[:, 256:512]
                K.wait("vector", s_cs)
                K.op("vector", lambda e: e.tensor_tensor(t1, pe_, cosb, ALU.mult), inc=s_ev)
                K.op("vector", lambda e: e.tensor_tensor(t2, po_, sinb, ALU.mult), inc=s_ev)
                K.wait("vector", s_ev)
                K.op("vector", lambda e: e.tensor_sub(ob_i[:, 0:512:2], t1, t2), inc=s_ev)
                K.wait("vector", s_ev)
                K.op("vector", lambda e: e.tensor_tensor(t1, po_, cosb, ALU.mult), inc=s_ev)
                K.op("vector", lambda e: e.tensor_tensor(t2, pe_, sinb, ALU.mult), inc=s_ev)
                K.wait("vector", s_ev)
                K.op("vector", lambda e: e.tensor_add(ob_i[:, 1:512:2], t1, t2), inc=s_ev)
            else:
                K.op("vector", lambda e, p=psum: e.tensor_copy(ob_i, p), inc=s_ev)
            pb = psum.bitcast(BF16)
            stg = c.rs[:].bitcast(BF16)
            K.wait("tensor", s_ev)
            for j in range(4):
                K.op("tensor", lambda e, j=j: e.transpose(
                    pb[0:128, j * 128:(j + 1) * 128],
                    ob_i[:, j * 128:(j + 1) * 128], c.identb[:]), inc=s_ev)
            K.wait("vector", s_ev)
            K.wait("vector", s_oo)
            for j in range(4):
                K.op("vector", lambda e, j=j: e.tensor_copy(
                    stg[:, j * 128:(j + 1) * 128], pb[0:128, j * 128:(j + 1) * 128]),
                    inc=s_ev)
            K.wait("sync", s_ev)
            for j in range(4):
                K.dma("sync", out_ap[os * 512 + j * 128:os * 512 + (j + 1) * 128,
                                     tt * 128:(tt + 1) * 128],
                      stg[:, j * 128:(j + 1) * 128], s_oo)
        return dest

    s_ev1, _, _ = emit_proj_tok(K, c, wq_dq_ap, 8, rope_dest(q_own_ap, lambda os: True),
                                name="pj", prequant=True,
                                pe_waits=(sv,), dma_waits=(sv,))
    s_ev2, _, _ = emit_proj_tok(K, c, wkv_dq_ap, 4, rope_dest(kv_own_ap, lambda os: os < 2),
                                name="pj", prequant=True,
                                pe_waits=(s_ev1,), dma_waits=(s_ev1,))
    K.wait("sync", s_oo)
    return s_ev2, s_oo


def l1b_masks(nc):
    m = np.zeros((4, 128, 512), np.float32)
    for j in range(4):
        for kl in range(128):
            m[j, kl, :] = np.where(j * 128 + kl <= np.arange(512), 0.0, -1e30)
    mt = np.zeros((4, 128, 512), np.float32)
    for i in range(4):
        for ql in range(128):
            mt[i, ql, :] = np.where(np.arange(512) <= i * 128 + ql, 0.0, -1e30)
    return (nc.inline_tensor(m.astype(BF16NP), "l1bmasks").ap(),
            nc.inline_tensor(mt.astype(BF16NP), "l1bmaskst").ap())


def emit_l1b_attn(K, c, qloc_ap, kloc_ap, vloc_ap, masks_aps, attn_own_ap, tag=""):
    masks_ap, maskst_ap = masks_aps
    """Head-sharded causal attention, 2 units x 4 qheads. VARIANT B:
    pass1 rowmax (S_tok), pass2 S^T + exp + mask-add + den + attn@v; divide at evac.
    Inputs are feature-major per-core slices: qloc [4,1024,1024] (g, qdim-window,
    tok), kloc/vloc [4,256,1024] (g, unit*128+feat, tok). Output attn_own_b
    [4096, 1024]: rows (qs//2)*1024 + u*512 + qh*128 + feat, cols token-in-block.
    All big buffers alias the x_fm region."""
    xf = c.x_fm
    K_fm = xf[:, 0:4096]
    V_sb = xf[:, 4096:8192]
    q_fm = xf[:, 8192:12288]
    P = [xf[:, 12288:12800], xf[:, 12800:13312]]
    maskb = [xf[:, 13312 + j * 512: 13312 + (j + 1) * 512] for j in range(4)]
    masktb = [xf[:, 16896 + j * 512: 16896 + (j + 1) * 512] for j in range(4)]
    nm_sb = xf[0:1, 15360:15872]
    aout = [xf[:, 15872:16384], xf[:, 16384:16896]]
    hwf = c.hw[:].bitcast(F32)
    s_kv = K.sem("s_ab_kv" + tag)
    s_q = K.sem("s_ab_q" + tag)
    s_pe = K.sem("s_ab_pe" + tag)
    s_v = K.sem("s_ab_v" + tag)
    s_s = K.sem("s_ab_s" + tag)
    s_t = K.sem("s_ab_t" + tag)
    s_m = K.sem("s_ab_m" + tag)
    s_ao = [K.sem("s_ab_ao0" + tag), K.sem("s_ab_ao1" + tag)]
    s_mk = K.sem("s_ab_mk" + tag)
    s_kf = K.sem("s_ab_kf" + tag)
    s_ksc = K.sem("s_ab_ksc" + tag)
    RSQ = float(1.0 / np.sqrt(128.0))
    for j in range(4):
        K.dma("sync", maskb[j], masks_ap[j], s_mk)
        K.dma("sync", masktb[j], maskst_ap[j], s_mk)
    K.wait("vector", s_mk)
    s_vt = K.sem("s_ab_vt" + tag)
    for u in range(2):
        if u > 0:
            K.wait("sync", s_v)
            K.wait("sync", s_pe)
            K.wait("vector", s_pe)
            K.wait("tensor", s_v)
        for g in range(4):
            K.dma("sync", K_fm[:, g * 1024:(g + 1) * 1024],
                  kloc_ap[g][u * 128:(u + 1) * 128, :], s_kf)
            K.dma("sync", q_fm[:, g * 1024:(g + 1) * 1024],
                  vloc_ap[g][u * 128:(u + 1) * 128, :], s_kv)
        K.wait("vector", s_kf)
        for j in range(8):
            K.op("vector", lambda e, j=j: e.tensor_scalar(
                K_fm[:, j * 512:(j + 1) * 512], K_fm[:, j * 512:(j + 1) * 512],
                RSQ, None, ALU.mult), inc=s_ksc)
        K.wait("tensor", s_kv)
        for kc in range(NDC):
            par = kc % 2
            if kc >= 2:
                K.wait("tensor", s_vt)
            K.op("tensor", lambda e, kc=kc, par=par: e.transpose(
                c.ps[6 + par][:].bitcast(BF16)[0:128, 0:128],
                q_fm[:, kc * 128:(kc + 1) * 128], c.identb[:]), inc=s_pe)
            K.wait("vector", s_pe)
            K.op("vector", lambda e, kc=kc, par=par: e.tensor_copy(
                V_sb[:, kc * 128:(kc + 1) * 128],
                c.ps[6 + par][:].bitcast(BF16)[0:128, 0:128]), inc=s_vt)
        K.wait("tensor", s_vt)
        K.wait("sync", s_vt)
        for qh in range(4):
            if u > 0 or qh > 0:
                K.wait("sync", s_pe)
            for g in range(4):
                K.dma("sync", q_fm[:, g * 1024:(g + 1) * 1024],
                      qloc_ap[g][u * 512 + qh * 128:u * 512 + (qh + 1) * 128, :], s_q)
            K.wait("tensor", s_q)
            K.wait("tensor", s_kv)
            K.wait("tensor", s_ksc)
            K.wait("tensor", s_mk)
            for qs in range(8):
                for i in range(4):
                    mcol = c.rs[:, 200 + i: 201 + i]
                    for ck in range(qs + 1):
                        par = ck % 2
                        if ck >= 2:
                            K.wait("tensor", s_m)
                        K.op("tensor", lambda e, i=i, ck=ck, par=par, qs=qs: e.matmul(
                            c.ps[par][:, 0:512],
                            q_fm[:, (qs * 4 + i) * 128:(qs * 4 + i + 1) * 128],
                            K_fm[:, ck * 512:(ck + 1) * 512], start=True, stop=True),
                            inc=s_pe)
                        K.wait("vector", s_pe)
                        dst = c.rs[:, 204:205] if ck > 0 else mcol
                        if ck == qs:
                            K.op("vector", lambda e, par=par, i=i: e.tensor_tensor(
                                hwf[:, 0:512], c.ps[par][:, 0:512], masktb[i], ALU.add),
                                inc=s_m)
                            K.wait("vector", s_m)
                            K.op("vector", lambda e, d=dst: e.reduce_max(
                                d, hwf[:, 0:512], axis=AX.X), inc=s_m)
                        else:
                            K.op("vector", lambda e, d=dst, par=par: e.reduce_max(
                                d, c.ps[par][:, 0:512], axis=AX.X), inc=s_m)
                        if ck > 0:
                            K.wait("vector", s_m)
                            K.op("vector", lambda e, m=mcol: e.tensor_max(
                                m, m, c.rs[:, 204:205]), inc=s_m)
                        K.wait("vector", s_m)
                        K.op("vector", lambda e: e.engine_nop(), inc=s_m)
                    K.op("vector", lambda e, m=mcol, i=i: e.tensor_scalar(
                        c.rs[:, 208 + i:209 + i].bitcast(BF16)[:, 0:1], m, -1.0, None,
                        ALU.mult), inc=s_m)
                    K.wait("vector", s_m)
                    K.op("vector", lambda e: e.engine_nop(), inc=s_m)
                    K.wait("tensor", s_m)
                    K.op("tensor", lambda e, i=i: e.transpose(
                        c.ps[6][:].bitcast(BF16)[0:1, 0:128],
                        c.rs[:, 208 + i:209 + i].bitcast(BF16)[:, 0:1], c.identb[:]),
                        inc=s_pe)
                    K.wait("vector", s_pe)
                    K.op("vector", lambda e, i=i: e.tensor_copy(
                        nm_sb[0:1, i * 128:(i + 1) * 128], c.ps[6][:].bitcast(BF16)[0:1, 0:128]),
                        inc=s_m)
                    K.wait("vector", s_m)
                    K.op("vector", lambda e: e.engine_nop(), inc=s_m)
                K.wait("tensor", s_m)
                sread = {}
                tdone = {}
                nkt = 4 * (qs + 1)
                for kt in range(nkt):
                    par = kt % 2
                    if kt >= 2:
                        K.wait("tensor", s_s, sread[kt - 2])
                    K.op("tensor", lambda e, kt=kt, par=par, qs=qs: e.matmul(
                        c.ps[2 + par][:, 0:512], K_fm[:, kt * 128:(kt + 1) * 128],
                        q_fm[:, qs * 512:(qs + 1) * 512], start=True, stop=True), inc=None)
                    K.op("tensor", lambda e, par=par: e.matmul(
                        c.ps[2 + par][:, 0:512], c.onesrb[:], nm_sb,
                        start=False, stop=True, skip_group_check=True), inc=s_pe)
                    mmv = s_pe.cnt
                    if kt >= 2:
                        K.wait("scalar", s_t, tdone[kt - 2])
                    if kt >= 4 * qs:
                        j = kt - 4 * qs
                        K.wait("vector", s_pe, mmv)
                        K.wait("vector", s_s)
                        K.op("vector", lambda e, par=par, j=j: e.tensor_tensor(
                            hwf[:, 0:512], c.ps[2 + par][:, 0:512], maskb[j], ALU.add),
                            inc=s_s)
                        K.wait("scalar", s_s)
                        K.op("scalar", lambda e, par=par: e.activation(
                            P[par], hwf[:, 0:512], AF.Exp), inc=s_s)
                    else:
                        K.wait("scalar", s_pe, mmv)
                        K.op("scalar", lambda e, par=par: e.activation(
                            P[par], c.ps[2 + par][:, 0:512], AF.Exp), inc=s_s)
                    sv_ = s_s.cnt
                    sread[kt] = sv_
                    K.wait("tensor", s_s, sv_)
                    K.op("tensor", lambda e, par=par, st=(kt == 0), sp=(kt == nkt - 1):
                         e.matmul(c.ps[4][0:1, 0:512], c.onescol[:], P[par],
                                  start=st, stop=sp), inc=None)
                    K.op("tensor", lambda e, kt=kt, par=par, st=(kt == 0), sp=(kt == nkt - 1):
                         e.matmul(c.ps[5][:, 0:512], V_sb[:, kt * 128:(kt + 1) * 128],
                                  P[par], start=st, stop=sp), inc=s_t)
                    tdone[kt] = s_t.cnt
                K.wait("vector", s_t)
                K.op("vector", lambda e: e.reciprocal(c.rs[0:1, 216:728] if False else hwf[0:1, 0:512], c.ps[4][0:1, 0:512]), inc=s_v)
                K.wait("vector", s_v)
                K.op("vector", lambda e: e.engine_nop(), inc=s_v)
                K.wait("tensor", s_v)
                K.op("tensor", lambda e: e.matmul(c.ps[6][:, 0:512], c.onesr[:],
                     hwf[0:1, 0:512], start=True, stop=True), inc=s_pe)
                K.wait("vector", s_pe)
                K.op("vector", lambda e: e.tensor_copy(hwf[:, 0:512] if False else c.hw[:, 512:1024].bitcast(F32) if False else hwf[:, 0:512], c.ps[5][:, 0:512]), inc=s_v)
                K.wait("vector", s_v)
                apar = qs % 2
                idx = ((u * 4 + qh) * 8 + qs)
                if idx >= 2:
                    K.wait("vector", s_ao[apar])
                K.op("vector", lambda e, apar=apar: e.tensor_tensor(
                    aout[apar], hwf[:, 0:512], c.ps[6][:, 0:512], ALU.mult), inc=s_v)
                K.wait("sync", s_v)
                r0 = (qs // 2) * 1024 + u * 512 + qh * 128
                c0 = (qs % 2) * 512
                K.dma("sync", attn_own_ap[r0:r0 + 128, c0:c0 + 512],
                      aout[apar], s_ao[apar])
    K.wait("sync", s_ao[0])
    K.wait("sync", s_ao[1])
    K.wait("vector", s_ao[0], s_ao[0].cnt)
    K.wait("vector", s_ao[1], s_ao[1].cnt)
    return s_ao



def _pre(nc, K, x_in):
    c = alloc_common(nc, K)
    s_x = emit_load_x(K, c, x_in)
    for e in ("vector", "scalar", "tensor"):
        K.wait(e, s_x)
    ln = emit_ln_xfm(K, c)
    return c, ln


def _build_l0a_A(nc):
    K = MK(nc)
    x_in = nc.dram_tensor("x_in", [T, D], F32, kind="ExternalInput").ap()
    wkvst = nc.dram_tensor("wkvst", [32, 128, 2048], F32, kind="ExternalInput").ap()
    kvbias = nc.dram_tensor("kvbias", [1, 2048], F32, kind="ExternalInput").ap()
    kvout = nc.dram_tensor("kvout", [128, 1032], F32, kind="ExternalOutput").ap()
    kv_dram = nc.dram_tensor("kv_dram", [8, 128, 2048], BF16, kind="Internal").ap()
    c, ln = _pre(nc, K, x_in)
    s_ev, s_kvo = emit_l0a_kvproj(K, c, wkvst, kvbias, kv_dram, ln)
    emit_l0a_kvstate(K, c, kv_dram, s_kvo, kvout)
    return K.build()


def _build_l0a_B(nc):
    K = MK(nc)
    x_in = nc.dram_tensor("x_in", [T, D], F32, kind="ExternalInput").ap()
    kvparts = nc.dram_tensor("kvparts", [4, 128, 1032], F32, kind="ExternalInput").ap()
    wqst = nc.dram_tensor("wqst", [64, 128, 2048], F32, kind="ExternalInput").ap()
    bqa = nc.dram_tensor("bqa", [32, 1, 128], F32, kind="ExternalInput").ap()
    wost = nc.dram_tensor("wost", [64, 128, 2048], F32, kind="ExternalInput").ap()
    bo = nc.dram_tensor("bo", [1, 4096], F32, kind="ExternalInput").ap()
    x_out = nc.dram_tensor("x_out", [T, D], F32, kind="ExternalOutput").ap()
    attn_dram = nc.dram_tensor("attn_dram", [128, 32768], BF16, kind="Internal").ap()
    c, ln = _pre(nc, K, x_in)
    sv, sa, sp = ln
    K.wait("sync", sp)
    s_kv = emit_l0a_kvreduce_in(K, c, kvparts)
    s_ao = emit_l0a_attn(K, c, wqst, bqa, attn_dram, s_kv, ln)
    s_ev = emit_l0a_oproj(K, c, attn_dram, wost, bo, s_ao)
    K.wait("sync", s_ev)
    emit_store_x(K, c, x_out)
    return K.build()


def _build_l0ffn(nc):
    K = MK(nc)
    x_in = nc.dram_tensor("x_in", [T, D], F32, kind="ExternalInput").ap()
    wg9 = nc.dram_tensor("wg9", [128, 288], F32, kind="ExternalInput").ap()
    ast = nc.dram_tensor("ast", [32, 128, 128], F32, kind="ExternalInput").ap()
    wbst = nc.dram_tensor("wbst", [136, 128, 1024], F32, kind="ExternalInput").ap()
    bias = nc.dram_tensor("bias", [1, 4096], F32, kind="ExternalInput").ap()
    bthr = nc.dram_tensor("bthr", [1, 1], F32, kind="ExternalInput").ap()
    x_out = nc.dram_tensor("x_out", [T, D], F32, kind="ExternalOutput").ap()
    c = alloc_common(nc, K)
    e8_ap = l0ffn_consts(nc)
    s_x = emit_load_x(K, c, x_in)
    for e in ("vector", "scalar", "tensor"):
        K.wait(e, s_x)
    ln = emit_ln_xfm(K, c)
    s_ev = emit_l0ffn(K, c, wg9, ast, wbst, bias, bthr, e8_ap, ln)
    K.wait("sync", s_ev)
    emit_store_x(K, c, x_out)
    return K.build()


def _build_l0ab(nc):
    K = MK(nc)
    x_in = nc.dram_tensor("x_in", [T, D], F32, kind="ExternalInput").ap()
    maskT = nc.dram_tensor("maskT", [D, D], I32, kind="ExternalInput").ap()
    wst = nc.dram_tensor("wst", [64, 128, 2048], F32, kind="ExternalInput").ap()
    x_out = nc.dram_tensor("x_out", [T, D], F32, kind="ExternalOutput").ap()
    wm = nc.dram_tensor("wm", [64, 128, 2048], BF16, kind="Internal").ap()
    c = alloc_common(nc, K)
    sel4_ap, ind_ap = l0ab_consts(nc)
    s_x = emit_load_x(K, c, x_in)
    for e in ("vector", "scalar", "tensor"):
        K.wait(e, s_x)
    ln_sems = emit_ln_xfm(K, c)
    s_ev = emit_l0ab(K, c, maskT, wst, wm, sel4_ap, ind_ap, ln_sems)
    K.wait("sync", s_ev)
    emit_store_x(K, c, x_out)
    return K.build()


def _build_l1a_A(nc):
    K = MK(nc)
    x_in = nc.dram_tensor("x_in", [T, D], F32, kind="ExternalInput").ap()
    wc16 = nc.dram_tensor("wc16", [128, 512], F32, kind="ExternalInput").ap()
    cbias = nc.dram_tensor("cbias", [1, 16], F32, kind="ExternalInput").ap()
    ckv_own = nc.dram_tensor("ckv_own", [T, 4], BF16, kind="ExternalOutput").ap()
    c, ln = _pre(nc, K, x_in)
    emit_l1a_cproj(K, c, wc16, cbias, ckv_own, ln, want_cq=False)
    return K.build()


def _build_l1a_B(nc):
    K = MK(nc)
    x_in = nc.dram_tensor("x_in", [T, D], F32, kind="ExternalInput").ap()
    ckv_full = nc.dram_tensor("ckv_full", [4096, 4], BF16, kind="ExternalInput").ap()
    wc16 = nc.dram_tensor("wc16", [128, 512], F32, kind="ExternalInput").ap()
    cbias = nc.dram_tensor("cbias", [1, 16], F32, kind="ExternalInput").ap()
    wqk = nc.dram_tensor("wqk", [12, 128], F32, kind="ExternalInput").ap()
    bqk4 = nc.dram_tensor("bqk4", [4, 32], F32, kind="ExternalInput").ap()
    wvu = nc.dram_tensor("wvu", [4, 4096], F32, kind="ExternalInput").ap()
    bvu = nc.dram_tensor("bvu", [32, 128, 1], F32, kind="ExternalInput").ap()
    wost = nc.dram_tensor("wost", [64, 128, 2048], F32, kind="ExternalInput").ap()
    bo = nc.dram_tensor("bo", [1, 4096], F32, kind="ExternalInput").ap()
    x_out = nc.dram_tensor("x_out", [T, D], F32, kind="ExternalOutput").ap()
    ckv_own2 = nc.dram_tensor("ckv_own2", [T, 4], BF16, kind="Internal").ap()
    attn_dram = nc.dram_tensor("attn_dram", [128, 32768], BF16, kind="Internal").ap()
    c, ln = _pre(nc, K, x_in)
    s_v, s_o = emit_l1a_cproj(K, c, wc16, cbias, ckv_own2, ln, want_cq=True)
    s_ao = emit_l1a_attn(K, c, ckv_full, wqk, bqk4, wvu, bvu, attn_dram, s_o)
    s_ev = emit_l0a_oproj(K, c, attn_dram, wost, bo, s_ao)
    K.wait("sync", s_ev)
    emit_store_x(K, c, x_out)
    return K.build()


def _build_l1ffn_A(nc):
    K = MK(nc)
    x_in = nc.dram_tensor("x_in", [T, D], F32, kind="ExternalInput").ap()
    rw8 = nc.dram_tensor("rw8", [128, 256], F32, kind="ExternalInput").ap()
    selb = nc.dram_tensor("selb", [128, 8], BF16, kind="ExternalInput").ap()
    rpart = nc.dram_tensor("rpart", [1, 8], F32, kind="ExternalOutput").ap()
    c, ln = _pre(nc, K, x_in)
    emit_l1ffn_router_partial(K, c, rw8, selb, rpart, ln)
    return K.build()


def _build_l1ffn_B(nc):
    K = MK(nc)
    x_in = nc.dram_tensor("x_in", [T, D], F32, kind="ExternalInput").ap()
    rparts = nc.dram_tensor("rparts", [4, 1, 8], F32, kind="ExternalInput").ap()
    rb = nc.dram_tensor("rb", [1, 8], F32, kind="ExternalInput").ap()
    ast = nc.dram_tensor("ast", [32, 128, 128], F32, kind="ExternalInput").ap()
    wbst = nc.dram_tensor("wbst", [136, 128, 1024], F32, kind="ExternalInput").ap()
    bias = nc.dram_tensor("bias", [1, 4096], F32, kind="ExternalInput").ap()
    x_out = nc.dram_tensor("x_out", [T, D], F32, kind="ExternalOutput").ap()
    c = alloc_common(nc, K)
    e8_ap = l0ffn_consts(nc)
    s_x = emit_load_x(K, c, x_in)
    for e in ("vector", "scalar", "tensor"):
        K.wait(e, s_x)
    ln = emit_ln_xfm(K, c)
    s_route = emit_l1ffn_routing(K, c, rparts, rb, e8_ap)
    s_ev = emit_l1ffn_main(K, c, ast, wbst, bias, s_route, ln)
    K.wait("sync", s_ev)
    emit_store_x(K, c, x_out)
    return K.build()


def _build_l1b_A(nc):
    K = MK(nc)
    x_in = nc.dram_tensor("x_in", [T, D], F32, kind="ExternalInput").ap()
    qcst = nc.dram_tensor("qcst", [64, 128, 2048], I32, kind="ExternalInput").ap()
    qsrow = nc.dram_tensor("qsrow", [256, 1, 512], F32, kind="ExternalInput").ap()
    kvcst = nc.dram_tensor("kvcst", [32, 128, 2048], I32, kind="ExternalInput").ap()
    kvsrow = nc.dram_tensor("kvsrow", [128, 1, 512], F32, kind="ExternalInput").ap()
    cosq = nc.dram_tensor("cosq", [8, 128, 256], BF16, kind="ExternalInput").ap()
    sinq = nc.dram_tensor("sinq", [8, 128, 256], BF16, kind="ExternalInput").ap()
    q_own = nc.dram_tensor("q_own", [T, 4096], BF16, kind="ExternalOutput").ap()
    kv_own = nc.dram_tensor("kv_own", [T, 2048], BF16, kind="ExternalOutput").ap()
    wq_dq = nc.dram_tensor("wq_dq", [64, 128, 2048], BF16, kind="Internal").ap()
    wkv_dq = nc.dram_tensor("wkv_dq", [32, 128, 2048], BF16, kind="Internal").ap()
    c, ln = _pre(nc, K, x_in)
    sv, sa, sp = ln
    K.wait("sync", sp)
    K.wait("vector", sp)
    s_o1 = emit_l1b_dequant(K, c, qcst, qsrow, wq_dq, tag="dq")
    s_o2 = emit_l1b_dequant(K, c, kvcst, kvsrow, wkv_dq, tag="dq")
    for w in s_o1 + s_o2:
        K.wait("sync", w)
        K.wait("vector", w)
    emit_l1b_qkv(K, c, wq_dq, wkv_dq, cosq, sinq, q_own, kv_own, ln)
    return K.build()


def _build_l1b_B(nc):
    K = MK(nc)
    qu = nc.dram_tensor("qu", [2, 4096, 512], BF16, kind="ExternalInput").ap()
    ku = nc.dram_tensor("ku", [2, 4096, 128], BF16, kind="ExternalInput").ap()
    vu = nc.dram_tensor("vu", [2, 4096, 128], BF16, kind="ExternalInput").ap()
    attn_own = nc.dram_tensor("attn_own", [1024, 4096], BF16, kind="ExternalOutput").ap()
    c = alloc_common(nc, K)
    masks_aps = l1b_masks(nc)
    K.wait("sync", c.s_const)
    K.wait("vector", c.s_const)
    K.wait("tensor", c.s_const)
    emit_l1b_attn(K, c, qu, ku, vu, masks_aps, attn_own)
    return K.build()


def _build_l1b_C(nc):
    K = MK(nc)
    x_in = nc.dram_tensor("x_in", [T, D], F32, kind="ExternalInput").ap()
    ocst = nc.dram_tensor("ocst", [64, 128, 2048], I32, kind="ExternalInput").ap()
    osrow = nc.dram_tensor("osrow", [256, 1, 512], F32, kind="ExternalInput").ap()
    attn_fm = nc.dram_tensor("attn_fm", [128, 32768], BF16, kind="ExternalInput").ap()
    x_out = nc.dram_tensor("x_out", [T, D], F32, kind="ExternalOutput").ap()
    wo_dq = nc.dram_tensor("wo_dq", [64, 128, 2048], BF16, kind="Internal").ap()
    c = alloc_common(nc, K)
    s_x = emit_load_x(K, c, x_in)
    s_o1 = emit_l1b_dequant(K, c, ocst, osrow, wo_dq, tag="dq")
    for w in s_o1:
        K.wait("sync", w)
        K.wait("vector", w)
    s_ev = emit_l0a_oproj(K, c, attn_fm, wo_dq, None, [], prequant=True, name="l1bo")
    K.wait("sync", s_ev)
    emit_store_x(K, c, x_out)
    return K.build()


def _bar(K, sems, engines=("sync", "scalar", "vector", "tensor", "gpsimd")):
    for e in engines:
        for s in sems:
            K.wait(e, s)


def _build_fused(nc):
    K = MK(nc)
    EI, INT = "ExternalInput", "Internal"
    dt = nc.dram_tensor
    x_in = dt("x_in", [T, D], F32, kind=EI).ap()
    x_out = dt("x_out", [T, D], F32, kind="ExternalOutput").ap()
    wkvst = dt("wkvst", [32, 128, 2048], F32, kind=EI).ap()
    kvbias = dt("kvbias", [1, 2048], F32, kind=EI).ap()
    wqst = dt("wqst", [64, 128, 2048], F32, kind=EI).ap()
    bqa = dt("bqa", [32, 1, 128], F32, kind=EI).ap()
    wost0 = dt("wost0", [64, 128, 2048], F32, kind=EI).ap()
    bo0 = dt("bo0", [1, 4096], F32, kind=EI).ap()
    wg9 = dt("wg9", [128, 288], F32, kind=EI).ap()
    ast0 = dt("ast0", [32, 128, 128], F32, kind=EI).ap()
    wbst0 = dt("wbst0", [136, 128, 1024], F32, kind=EI).ap()
    bias0 = dt("bias0", [1, 4096], F32, kind=EI).ap()
    bthr0 = dt("bthr0", [1, 1], F32, kind=EI).ap()
    maskT = dt("maskT", [D, D], I32, kind=EI).ap()
    wst_ab = dt("wst_ab", [64, 128, 2048], F32, kind=EI).ap()
    wc16 = dt("wc16", [128, 512], F32, kind=EI).ap()
    cbias = dt("cbias", [1, 16], F32, kind=EI).ap()
    wqk = dt("wqk", [12, 128], F32, kind=EI).ap()
    bqk4 = dt("bqk4", [4, 32], F32, kind=EI).ap()
    wvu = dt("wvu", [4, 4096], F32, kind=EI).ap()
    bvu = dt("bvu", [32, 128, 1], F32, kind=EI).ap()
    wost1 = dt("wost1", [64, 128, 2048], F32, kind=EI).ap()
    bo1 = dt("bo1", [1, 4096], F32, kind=EI).ap()
    rw8 = dt("rw8", [128, 256], F32, kind=EI).ap()
    selb = dt("selb", [128, 8], BF16, kind=EI).ap()
    rb = dt("rb", [1, 8], F32, kind=EI).ap()
    ast1 = dt("ast1", [32, 128, 128], F32, kind=EI).ap()
    wbst1 = dt("wbst1", [136, 128, 1024], F32, kind=EI).ap()
    bias1 = dt("bias1", [1, 4096], F32, kind=EI).ap()
    qcst = dt("qcst", [64, 128, 2048], I32, kind=EI).ap()
    qsrow = dt("qsrow", [256, 1, 512], F32, kind=EI).ap()
    kvcst = dt("kvcst", [32, 128, 2048], I32, kind=EI).ap()
    kvsrow = dt("kvsrow", [128, 1, 512], F32, kind=EI).ap()
    ocst = dt("ocst", [64, 128, 2048], I32, kind=EI).ap()
    osrow = dt("osrow", [256, 1, 512], F32, kind=EI).ap()
    cosq = dt("cosq", [8, 128, 256], BF16, kind=EI).ap()
    sinq = dt("sinq", [8, 128, 256], BF16, kind=EI).ap()
    kv_dram = dt("kv_dram", [8, 128, 2048], BF16, kind=INT).ap()
    kvout_own = dt("kvout_own", [128, 1032], F32, kind=INT).ap()
    kvg = dt("kvg", [512, 1032], F32, kind=INT).ap()
    attn_dram = dt("attn_dram", [128, 32768], BF16, kind=INT).ap()
    ckv_own = dt("ckv_own", [T, 4], BF16, kind=INT).ap()
    ckvg = dt("ckvg", [4096, 4], BF16, kind=INT).ap()
    rpart1 = dt("rpart1", [1, 8], F32, kind=INT).ap()
    rg1 = dt("rg1", [4, 8], F32, kind=INT).ap()
    rpart2 = dt("rpart2", [1, 8], F32, kind=INT).ap()
    rg2 = dt("rg2", [4, 8], F32, kind=INT).ap()
    wq_dq = dt("wq_dq", [64, 128, 2048], BF16, kind=INT).ap()
    wkv_dq = dt("wkv_dq", [32, 128, 2048], BF16, kind=INT).ap()
    wo_dq = dt("wo_dq", [64, 128, 2048], BF16, kind=INT).ap()
    q_own = dt("q_own", [4096, 1024], BF16, kind=INT).ap()
    kv_own = dt("kv_own", [2048, 1024], BF16, kind=INT).ap()
    qg = dt("qg", [16384, 1024], BF16, kind=INT).ap()
    kvg2 = dt("kvg2", [8192, 1024], BF16, kind=INT).ap()
    qloc = dt("qloc", [4, 1024, 1024], BF16, kind=INT).ap()
    kloc = dt("kloc", [4, 256, 1024], BF16, kind=INT).ap()
    vloc = dt("vloc", [4, 256, 1024], BF16, kind=INT).ap()
    attn_own = dt("attn_own", [4096, 1024], BF16, kind=INT).ap()
    ag = dt("ag", [16384, 1024], BF16, kind=INT).ap()
    aloc = dt("aloc", [4, 1024, 1024], BF16, kind=INT).ap()

    c = alloc_common(nc, K)
    e8_ap = l0ffn_consts(nc)
    sel4_ap, ind_ap = l0ab_consts(nc)
    masks_aps = l1b_masks(nc)
    s_x = emit_load_x(K, c, x_in)
    for e in ("vector", "scalar", "tensor"):
        K.wait(e, s_x)

    ln = emit_ln_xfm(K, c, tag="")
    s_ev, s_kvo = emit_l0a_kvproj(K, c, wkvst, kvbias, kv_dram, ln)
    s_ko = emit_l0a_kvstate(K, c, kv_dram, s_kvo, kvout_own)
    s_ag1 = emit_allgather(K, kvout_own, kvg, "ag", wait_sems=(s_ko,))
    _bar(K, [s_ev, s_ko])
    K.wait("sync", s_ag1)
    K.wait("vector", s_ag1)
    kvparts = [kvg[r * 128:(r + 1) * 128, :] for r in range(4)]
    s_kv = emit_l0a_kvreduce_in(K, c, kvparts)
    s_ao = emit_l0a_attn(K, c, wqst, bqa, attn_dram, s_kv, ln)
    s_ev0 = emit_l0a_oproj(K, c, attn_dram, wost0, bo0, s_ao, name="pj")
    _bar(K, [s_ev0])

    ln = emit_ln_xfm(K, c, tag="")
    s_ev1 = emit_l0ffn(K, c, wg9, ast0, wbst0, bias0, bthr0, e8_ap, ln, tag="")
    _bar(K, [s_ev1])

    ln = emit_ln_xfm(K, c, tag="")
    wm = dt("wm", [64, 128, 2048], BF16, kind=INT).ap()
    s_ev2 = emit_l0ab(K, c, maskT, wst_ab, wm, sel4_ap, ind_ap, ln)
    _bar(K, [s_ev2])

    ln = emit_ln_xfm(K, c, tag="")
    s_ev3 = emit_l0ffn(K, c, wg9, ast0, wbst0, bias0, bthr0, e8_ap, ln, tag="")
    _bar(K, [s_ev3])

    ln = emit_ln_xfm(K, c, tag="")
    s_v, s_o = emit_l1a_cproj(K, c, wc16, cbias, ckv_own, ln, want_cq=True)
    s_ag2 = emit_allgather(K, ckv_own, ckvg, "ag", wait_sems=(s_o,))
    K.wait("sync", s_ag2)
    s_ao = emit_l1a_attn(K, c, ckvg, wqk, bqk4, wvu, bvu, attn_dram, s_o)
    s_ev4 = emit_l0a_oproj(K, c, attn_dram, wost1, bo1, s_ao, name="pj")
    _bar(K, [s_ev4])

    ln = emit_ln_xfm(K, c, tag="")
    s_rp = emit_l1ffn_router_partial(K, c, rw8, selb, rpart1, ln, tag="")
    s_ag3 = emit_allgather(K, rpart1, rg1, "ag", wait_sems=(s_rp,))
    K.wait("sync", s_ag3)
    rg1l = [rg1[r:r + 1, :] for r in range(4)]
    s_route = emit_l1ffn_routing(K, c, rg1l, rb, e8_ap, tag="")
    s_ev5 = emit_l1ffn_main(K, c, ast1, wbst1, bias1, s_route, ln, tag="")
    _bar(K, [s_ev5])

    ln = emit_ln_xfm(K, c, tag="")
    sv6, sa6, sp6 = ln
    K.wait("sync", sp6)
    K.wait("vector", sp6)
    s_o1 = emit_l1b_dequant(K, c, qcst, qsrow, wq_dq, tag="dq")
    s_o2 = emit_l1b_dequant(K, c, kvcst, kvsrow, wkv_dq, tag="dq")
    for w in s_o1 + s_o2:
        K.wait("sync", w)
        K.wait("vector", w)
    s_ev6, s_oo = emit_l1b_qkv(K, c, wq_dq, wkv_dq, cosq, sinq, q_own, kv_own, ln)
    s_ag4 = emit_allgather(K, q_own, qg, "ag", wait_sems=(s_oo, s_ev6))
    s_ag5 = emit_allgather(K, kv_own, kvg2, "ag")
    _bar(K, [s_ev6, s_oo], engines=("sync", "vector", "scalar", "tensor"))
    s_o3 = emit_l1b_dequant(K, c, ocst, osrow, wo_dq, tag="dq")
    K.wait("gpsimd", s_ag4)
    K.wait("gpsimd", s_ag5)
    s_sl = K.sem("s_slice")
    pid = nc.partition_id()
    off = pid % 4
    for g in range(4):
        K.dma("gpsimd", qloc[g], qg[ds(g * 4096 + off * 1024, 1024), 0:1024], s_sl)
        K.dma("gpsimd", kloc[g], kvg2[ds(g * 2048 + off * 256, 256), 0:1024], s_sl)
        K.dma("gpsimd", vloc[g], kvg2[ds(g * 2048 + 1024 + off * 256, 256), 0:1024], s_sl)
    for w in s_o3:
        K.wait("sync", w)
        K.wait("vector", w)
        K.wait("tensor", w)
    K.wait("sync", s_sl)
    K.wait("tensor", s_sl)
    s_ao2 = emit_l1b_attn(K, c, qloc, kloc, vloc, masks_aps, attn_own)
    s_ag6 = emit_allgather(K, attn_own, ag, "ag", wait_sems=list(s_ao2))
    K.wait("gpsimd", s_ag6)
    for g in range(4):
        K.dma("gpsimd", aloc[g], ag[ds(g * 4096 + off * 1024, 1024), 0:1024], s_sl)
    K.wait("sync", s_sl)
    attn_list = [aloc[dc // 8][(dc % 8) * 128:(dc % 8) * 128 + 128, :]
                 for dc in range(NDC)]
    s_ev7 = emit_l0a_oproj(K, c, None, wo_dq, None, [], prequant=True,
                           name="pj", attn_list=attn_list)
    _bar(K, [s_ev7])

    ln = emit_ln_xfm(K, c, tag="")
    s_rp2 = emit_l1ffn_router_partial(K, c, rw8, selb, rpart2, ln, tag="")
    s_ag7 = emit_allgather(K, rpart2, rg2, "ag", wait_sems=(s_rp2,))
    K.wait("sync", s_ag7)
    rg2l = [rg2[r:r + 1, :] for r in range(4)]
    s_route2 = emit_l1ffn_routing(K, c, rg2l, rb, e8_ap, tag="")
    s_ev8 = emit_l1ffn_main(K, c, ast1, wbst1, bias1, s_route2, ln, tag="")
    K.wait("sync", s_ev8)
    emit_store_x(K, c, x_out)
    return K.build()


def kernel(**inputs):
    inp = {k: np.asarray(v) for k, v in inputs.items()}
    C = list(range(8))
    x_full = np.ascontiguousarray(inp["L0_attn_a_hidden_states"].astype(np.float32))
    shards = shard_tokens(x_full)

    wkvst, kvbias, wqst, bqa, wost0, bo0 = l0a_host(
        inp["L0_attn_a_Wq"], inp["L0_attn_a_bq"], inp["L0_attn_a_Wk"],
        inp["L0_attn_a_bk"], inp["L0_attn_a_Wv"], inp["L0_attn_a_bv"],
        inp["L0_attn_a_Wo"], inp["L0_attn_a_bo"])
    wg9, ast0, wbst0 = l0ffn_host(
        inp["L0_ffn_W_base"], inp["L0_ffn_W_gate"], inp["L0_ffn_W_thr"],
        inp["L0_ffn_A_w"], inp["L0_ffn_B_w"])
    bias0 = np.ascontiguousarray(inp["L0_ffn_b_base"][None, :]).astype(np.float32)
    bthr0 = np.ascontiguousarray(inp["L0_ffn_b_thr"][None, :]).astype(np.float32)
    wst_ab = w_stream(np.ascontiguousarray(
        inp["L0_attn_b_weight"].astype(np.float32).T), dcg=4)
    maskT = np.ascontiguousarray(inp["L0_attn_b_mask"].astype(np.int32).T)
    wc16, cbias, wqk, bqk4, wvu, bvu, wost1, bo1 = l1a_host(
        inp["L1_attn_a_Wq_d"], inp["L1_attn_a_bq_d"], inp["L1_attn_a_W_qk"],
        inp["L1_attn_a_b_qk"], inp["L1_attn_a_Wkv_d"], inp["L1_attn_a_bkv_d"],
        inp["L1_attn_a_Wv_u"], inp["L1_attn_a_bv_u"], inp["L1_attn_a_Wo"],
        inp["L1_attn_a_bo"])
    rw8, ast1, wbst1 = l1ffn_host(
        inp["L1_ffn_W"], inp["L1_ffn_A"], inp["L1_ffn_B"], inp["L1_ffn_router_W"])
    bias1 = np.ascontiguousarray(inp["L1_ffn_b"][None, :]).astype(np.float32)
    rbr = np.ascontiguousarray(inp["L1_ffn_router_b"][None, :]).astype(np.float32)
    kvc = np.concatenate([inp["L1_attn_b_k_codes"], inp["L1_attn_b_v_codes"]],
                         axis=1).astype(np.int32)
    kvs = np.concatenate([inp["L1_attn_b_k_scales"], inp["L1_attn_b_v_scales"]],
                         axis=1).astype(np.float32)
    qcst, qsrow = l1b_host_one(inp["L1_attn_b_q_codes"].astype(np.int32),
                               inp["L1_attn_b_q_scales"].astype(np.float32))
    kvcst, kvsrow = l1b_host_one(kvc, kvs)
    ocst, osrow = l1b_host_one(inp["L1_attn_b_o_codes"].astype(np.int32),
                               inp["L1_attn_b_o_scales"].astype(np.float32))
    ropes = [rope_host(cc) for cc in C]

    nc = make_nc("fused")
    _build_fused(nc)
    in_maps = []
    for cc in C:
        in_maps.append({
            "x_in": shards[cc],
            "wkvst": wkvst, "kvbias": kvbias, "wqst": wqst, "bqa": bqa,
            "wost0": wost0, "bo0": bo0,
            "wg9": wg9, "ast0": ast0, "wbst0": wbst0, "bias0": bias0,
            "bthr0": bthr0,
            "maskT": maskT, "wst_ab": wst_ab,
            "wc16": wc16, "cbias": cbias, "wqk": wqk, "bqk4": bqk4,
            "wvu": wvu, "bvu": bvu, "wost1": wost1, "bo1": bo1,
            "rw8": rw8, "selb": sel_host(cc), "rb": rbr,
            "ast1": ast1, "wbst1": wbst1, "bias1": bias1,
            "qcst": qcst, "qsrow": qsrow, "kvcst": kvcst, "kvsrow": kvsrow,
            "ocst": ocst, "osrow": osrow,
            "cosq": ropes[cc][0].astype(BF16NP),
            "sinq": ropes[cc][1].astype(BF16NP),
        })
    res = run_bass_kernel_spmd(nc, in_maps, core_ids=C)
    shards = [res.results[cc]["x_out"] for cc in C]
    return unshard_tokens(shards).astype(np.float32)


# version 03
# speedup vs baseline: 1.1574x; 1.1574x over previous
import numpy as np
import ml_dtypes
import concourse.bass as bass
import concourse.mybir as mybir
from concourse.bass import ds
from concourse.bass_utils import run_bass_kernel_spmd

F32 = mybir.dt.float32
BF16 = mybir.dt.bfloat16
I32 = mybir.dt.int32
AF = mybir.ActivationFunctionType
ALU = mybir.AluOpType
AX = mybir.AxisListType
BF16NP = ml_dtypes.bfloat16

NCORES = 8
T = 1024
NT = 8
D = 4096
NDC = 32
GROUPS = [[0, 1, 2, 3], [4, 5, 6, 7]]

ENGINES = ("sync", "scalar", "vector", "tensor", "gpsimd")



def w_stream(WT, os_width=512, dcg=8):
    d, o = WT.shape
    ndc = d // 128
    ng = ndc // dcg
    nos = o // os_width
    a = WT.reshape(ng, dcg, 128, nos, os_width)
    a = a.transpose(3, 0, 2, 1, 4)
    return np.ascontiguousarray(a.reshape(nos * ng, 128, dcg * os_width))


def shard_tokens(x_full):
    out = []
    for c in range(NCORES):
        b, r = divmod(c, 4)
        out.append(np.ascontiguousarray(x_full[b, r * T:(r + 1) * T, :]))
    return out


def unshard_tokens(shards):
    x = np.empty((2, 4096, D), dtype=shards[0].dtype)
    for c in range(NCORES):
        b, r = divmod(c, 4)
        x[b, r * T:(r + 1) * T, :] = shards[c]
    return x



class Sem:
    __slots__ = ("h", "cnt", "name")

    def __init__(self, h, name):
        self.h = h
        self.cnt = 0
        self.name = name


class MK:
    def __init__(self, nc):
        self.nc = nc
        self.q = {e: [] for e in ENGINES}
        self.sems = {}
        self.cur_scope = "init"

    def scope(self, name):
        self.cur_scope = name

    def sem(self, name):
        if name not in self.sems:
            self.sems[name] = Sem(self.nc.alloc_semaphore(name), name)
        return self.sems[name]

    def _push(self, eng, fn):
        self.q[eng].append((self.cur_scope, fn))

    def wait(self, eng, sem, val=None):
        v = sem.cnt if val is None else val
        self._push(eng, lambda e, s=sem, v=v: e.wait_ge(s.h, v))

    def op(self, eng, builder, inc=None, inc_n=1):
        if inc is not None:
            inc.cnt += inc_n
            self._push(eng, lambda e, b=builder, s=inc, n=inc_n: b(e).then_inc(s.h, n))
        else:
            self._push(eng, lambda e, b=builder: b(e))

    def dma(self, eng, out, in_, sem, **kw):
        self.op(eng, lambda e, o=out, i=in_, k=dict(kw): e.dma_start(out=o, in_=i, **k),
                inc=sem, inc_n=16)
        return sem.cnt

    def barrier(self, name="bar", engines=None):
        engs = [e for e in (engines or ENGINES) if self.q[e]]
        sems = {e: self.sem(f"bar_{e}") for e in engs}
        for e in engs:
            s = sems[e]
            s.cnt += 1
            self._push(e, lambda eng, s=s: eng.drain().then_inc(s.h, 1))
        for e in engs:
            for e2 in engs:
                if e2 != e:
                    self.wait(e, sems[e2])

    def build(self):
        nc = self.nc
        with nc.Block() as block:
            for eng_name in ENGINES:
                ops = self.q[eng_name]
                if not ops:
                    continue
                def runner(e, ops=ops):
                    i = 0
                    while i < len(ops):
                        sc = ops[i][0]
                        j = i
                        with nc.named_scope(sc):
                            while j < len(ops) and ops[j][0] == sc:
                                ops[j][1](e)
                                j += 1
                        i = j
                getattr(block, eng_name)(runner)
        return nc


import os as _os
DYN_SCRATCH = int(_os.environ.get("MK_DYN_SCRATCH", "4096"))


def make_nc(name="mk"):
    return bass.Bass("TRN2", target_bir_lowering=False, name=name,
                     dynamic_dma_scratch_size=DYN_SCRATCH)



class Ctx:
    pass


def alloc_common(nc, K):
    c = Ctx()
    c.nc = nc
    c.x_res = nc.alloc_sbuf_tensor("x_res", [128, NT * D], F32)
    c.x_fm = nc.alloc_sbuf_tensor("x_fm", [128, NDC * T], BF16)
    c.wbuf = nc.alloc_sbuf_tensor("wbuf", [128, 2 * 4096], BF16)
    c.stats = nc.alloc_sbuf_tensor("stats", [128, 8 * 8], F32)
    c.ident = nc.alloc_sbuf_tensor("ident", [128, 128], F32)
    c.onesr = nc.alloc_sbuf_tensor("onesr", [1, 128], F32)
    c.hw = nc.alloc_sbuf_tensor("hw", [128, 1024], BF16)
    c.ps = [nc.alloc_psum_tensor(f"ps{i}", [128, 512], F32) for i in range(8)]
    c.identb = nc.alloc_sbuf_tensor("identb", [128, 128], BF16)
    c.rs = nc.alloc_sbuf_tensor("rs", [128, 512], F32)
    c.ob = nc.alloc_sbuf_tensor("ob", [128, 1024], BF16)
    c.onescol = nc.alloc_sbuf_tensor("onescol", [128, 1], BF16)
    c.onesrb = nc.alloc_sbuf_tensor("onesrb", [1, 128], BF16)
    c.bmT01 = nc.alloc_sbuf_tensor("bmT01", [128, 128], BF16)
    c.ident_np = np.eye(128, dtype=np.float32)
    c.ones_np = np.ones((1, 128), dtype=np.float32)
    identc = nc.inline_tensor(c.ident_np, "identc").ap()
    identbc = nc.inline_tensor(c.ident_np.astype(BF16NP), "identbc").ap()
    onesc = nc.inline_tensor(c.ones_np, "onesc").ap()
    s = K.sem("s_const")
    K.dma("sync", c.ident[:], identc, s)
    K.dma("sync", c.identb[:], identbc, s)
    K.dma("sync", c.onesr[:], onesc, s)
    onescolc = nc.inline_tensor(np.ones((128, 1), np.float32).astype(BF16NP), "onescolc").ap()
    K.dma("sync", c.onescol[:], onescolc, s)
    onesrbc = nc.inline_tensor(c.ones_np.astype(BF16NP), "onesrbc").ap()
    K.dma("sync", c.onesrb[:], onesrbc, s)
    c.s_const = s
    for e in ("sync", "scalar", "vector", "tensor", "gpsimd"):
        K.wait(e, s)
    return c


def st_col(c, slot, tt):
    return c.stats[:, slot * 8 + tt: slot * 8 + tt + 1]


def st_all(c, slot):
    return c.stats[:, slot * 8: slot * 8 + 8]



def emit_load_x(K, c, x_in_ap):
    s = K.sem("s_xload")
    for tt in range(NT):
        K.dma("sync", c.x_res[:, tt * D:(tt + 1) * D],
              x_in_ap[tt * 128:(tt + 1) * 128, :], s)
    return s


def emit_store_x(K, c, x_out_ap):
    s = K.sem("s_xstore")
    for tt in range(NT):
        K.dma("sync", x_out_ap[tt * 128:(tt + 1) * 128, :],
              c.x_res[:, tt * D:(tt + 1) * D], s)
    K.wait("sync", s)
    return s


def emit_ln_xfm(K, c, eps=1e-5, want_xfm=True, tag=""):
    sv = K.sem("s_ln_v" + tag)
    sa = K.sem("s_ln_a" + tag)
    sp = K.sem("s_ln_pe" + tag)
    for tt in range(NT):
        xt = c.x_res[:, tt * D:(tt + 1) * D]
        K.op("vector", lambda e, o=st_col(c, 0, tt), i=xt:
             e.reduce_sum(o, i, axis=AX.X), inc=sv)
        K.op("scalar", lambda e, o=c.x_fm[:, tt * 4096:(tt + 1) * 4096], i=xt,
             a=st_col(c, 1, tt):
             e.activation(o, i, AF.Square, accum_out=a), inc=sa)
    K.wait("vector", sa)
    K.wait("vector", sv)
    inv_d = 1.0 / D
    K.op("vector", lambda e: e.tensor_scalar(st_all(c, 2), st_all(c, 0), inv_d, None, ALU.mult), inc=sv)
    K.op("vector", lambda e: e.tensor_scalar(st_all(c, 3), st_all(c, 1), inv_d, None, ALU.mult), inc=sv)
    K.wait("vector", sv)
    K.op("vector", lambda e: e.tensor_tensor(st_all(c, 6), st_all(c, 2), st_all(c, 2), ALU.mult), inc=sv)
    K.wait("vector", sv)
    K.op("vector", lambda e: e.tensor_tensor(st_all(c, 3), st_all(c, 3), st_all(c, 6), ALU.subtract), inc=sv)
    K.wait("vector", sv)
    K.op("vector", lambda e: e.tensor_scalar(st_all(c, 3), st_all(c, 3), float(eps), None, ALU.add), inc=sv)
    K.wait("scalar", sv)
    K.op("scalar", lambda e: e.activation(st_all(c, 6), st_all(c, 3), AF.Sqrt), inc=sa)
    K.wait("vector", sa)
    K.op("vector", lambda e: e.reciprocal(st_all(c, 4), st_all(c, 6)), inc=sv)
    K.wait("vector", sv)
    K.op("vector", lambda e: e.scalar_tensor_tensor(st_all(c, 5), st_all(c, 2), -1.0, st_all(c, 4), ALU.mult, ALU.mult), inc=sv)
    K.wait("vector", sv)
    if not want_xfm:
        return sv, sa, sp
    xfm3 = c.x_fm[:].rearrange("p (dc t) -> p dc t", t=T)
    psT = [c.ps[6][:].bitcast(BF16), c.ps[7][:].bitcast(BF16)]
    K.wait("tensor", c.s_const)
    for tt in range(NT):
        half = tt % 2
        xln = c.wbuf[:, half * 4096: half * 4096 + 4096]
        if tt >= 2:
            K.wait("vector", sp)
        K.op("vector", lambda e, o=xln, i=c.x_res[:, tt * D:(tt + 1) * D],
             r=st_col(c, 4, tt), m=st_col(c, 5, tt):
             e.tensor_scalar(o, i, r, m, ALU.mult, ALU.add), inc=sv)
        K.wait("tensor", sv)
        for g in range(4):
            bank = psT[g % 2]
            if tt > 0 or g >= 2:
                K.wait("tensor", sv)
            for i in range(8):
                dc = g * 8 + i
                src = xln[:, dc * 128:(dc + 1) * 128]
                reg = bank[:, i * 128:(i + 1) * 128]
                K.op("tensor", lambda e, r=reg, x=src: e.transpose(r, x, c.identb[:]),
                     inc=(sp if i == 7 else None))
            K.wait("vector", sp)
            dst = xfm3[:, g * 8:(g + 1) * 8, tt * 128:(tt + 1) * 128]
            K.op("vector", lambda e, d=dst,
                 s=bank[:].rearrange("p (i t) -> p i t", t=128):
                 e.tensor_copy(d, s), inc=sv)
    return sv, sa, sp


def emit_proj_tok(K, c, wst_ap, n_os, dest_fn, *, name, bias_ap=None,
                  n_dc=NDC, lhsT_fn=None, pe_waits=(), dma_waits=(),
                  rhs_cols=512, prequant=False, dcg=4):
    ng = n_dc // dcg
    npc = max(1, dcg // 2)
    s_pw = [K.sem(f"s_pw_{name}0"), K.sem(f"s_pw_{name}1")]
    s_cast = K.sem(f"s_cast_{name}")
    s_wf = K.sem(f"s_wf_{name}")
    s_mm = K.sem(f"s_mm_{name}")
    s_ev = K.sem(f"s_ev_{name}")
    s_b = K.sem(f"s_b_{name}")
    base_wf = s_wf.cnt
    base_mm = s_mm.cnt
    if lhsT_fn is None:
        def lhsT_fn(dc, tt):
            return c.x_fm[:, dc * T + tt * 128: dc * T + (tt + 1) * 128]
    for w in pe_waits:
        K.wait("tensor", w)
    for w in dma_waits:
        K.wait("sync", w)
        K.wait("vector", w)
    wb32 = c.wbuf[:].bitcast(F32)
    n_slab = n_os * ng
    slab_ready = []
    piece_cast = []
    for os in range(n_os):
        if bias_ap is not None:
            if os >= 1:
                K.wait("sync", s_mm, base_mm + 8 * os)
            K.dma("sync", c.rs[0:1, 0:rhs_cols],
                  bias_ap[0:1, os * rhs_cols:(os + 1) * rhs_cols], s_b)
        for g in range(ng):
            slab = os * ng + g
            spar = slab % 2
            if prequant:
                if slab >= 2:
                    K.wait("sync", s_wf, base_wf + slab - 1)
                K.dma("sync", c.wbuf[:, spar * 2048:(spar + 1) * 2048],
                      wst_ap[slab], s_pw[spar])
                slab_ready.append((s_pw[spar], s_pw[spar].cnt))
            else:
                for j in range(npc):
                    p = npc * slab + j
                    ppar = p % 2
                    if p >= 2:
                        K.wait("sync", s_cast, piece_cast[p - 2])
                    K.dma("sync", wb32[:, 2048 + ppar * 1024: 2048 + (ppar + 1) * 1024],
                          wst_ap[slab][:, j * 1024:(j + 1) * 1024], s_pw[ppar])
                    if slab >= 2 and j == 0:
                        K.wait("vector", s_wf, base_wf + slab - 1)
                    K.wait("vector", s_pw[ppar])
                    K.op("vector", lambda e,
                         o=c.wbuf[:, spar * 2048 + j * 1024: spar * 2048 + (j + 1) * 1024],
                         i=wb32[:, 2048 + ppar * 1024: 2048 + (ppar + 1) * 1024]:
                         e.tensor_copy(o, i), inc=s_cast)
                    piece_cast.append(s_cast.cnt)
                slab_ready.append((s_cast, s_cast.cnt))
        if os > 0:
            K.wait("tensor", s_ev)
        for g in range(ng):
            slab = os * ng + g
            spar = slab % 2
            sem_, val_ = slab_ready[slab]
            K.wait("tensor", sem_, val_)
            for dcs in range(dcg):
                dc = g * dcg + dcs
                w_sl = c.wbuf[:, spar * 2048 + dcs * rhs_cols: spar * 2048 + (dcs + 1) * rhs_cols]
                last = (dc == n_dc - 1) and bias_ap is None
                for tt in range(NT):
                    K.op("tensor", lambda e, p=c.ps[tt][:, 0:rhs_cols], l=lhsT_fn(dc, tt),
                         w=w_sl, st=(dc == 0), sp=(dc == n_dc - 1):
                         e.matmul(p, l, w, start=st, stop=sp),
                         inc=(s_mm if last else None))
            K.op("tensor", lambda e: e.drain(), inc=s_wf)
        if bias_ap is not None:
            K.wait("tensor", s_b)
            for tt in range(NT):
                K.op("tensor", lambda e, p=c.ps[tt][:, 0:rhs_cols]:
                     e.matmul(p, c.onesr[:], c.rs[0:1, 0:rhs_cols], start=False,
                              stop=True, skip_group_check=True), inc=s_mm)
        K.wait("vector", s_mm)
        K.wait("scalar", s_mm)
        for tt in range(NT):
            dest_fn(K, c, tt, os, c.ps[tt][:, 0:rhs_cols], s_ev)
    return s_ev, s_mm, s_wf


def dest_resid(K, c, tt, os, psum, s_ev, col0=0):
    dst = c.x_res[:, tt * D + col0 + os * 512: tt * D + col0 + os * 512 + 512]
    K.op("vector", lambda e, d=dst, a=psum: e.tensor_tensor(d, d, a, ALU.add), inc=s_ev)



def emit_allgather(K, in_ap, out_ap, name, wait_sems=()):
    sc = K.sem("s_ag_" + name)
    for w in wait_sems:
        K.wait("gpsimd", w)
    K.op("gpsimd", lambda e: e.collective_compute(
        "AllGather", mybir.AluOpType.bypass, replica_groups=GROUPS,
        ins=[in_ap.opt()], outs=[out_ap.opt()]), inc=sc, inc_n=1)
    return sc


def l0ab_consts(nc):
    sel4 = np.zeros((128, 4), np.float32)
    for p in range(128):
        sel4[p, p // 32] = 1.0
    ind = np.zeros((128, 4096), np.float32)
    for n in range(4096):
        ind[n // 32, n] = 1.0
    a = nc.inline_tensor(sel4, "sel4c").ap()
    b = nc.inline_tensor(ind.astype(BF16NP), "indc").ap()
    return a, b


def emit_l0ab_maskprep(K, c, maskT_ap, sel4_ap, wait_wbuf=()):
    s_mc = [K.sem("s_l0ab_mc0"), K.sem("s_l0ab_mc1")]
    s_rd = K.sem("s_l0ab_rd")
    s_pe = K.sem("s_l0ab_pe")
    s_cst = K.sem("s_l0ab_cst")
    K.dma("sync", c.rs[:, 0:4], sel4_ap, s_cst)
    mbuf = c.wbuf[:].bitcast(I32)
    for w in wait_wbuf:
        K.wait("sync", w)
    K.wait("tensor", s_cst)
    red_done = []
    for kc in range(NDC):
        par = kc % 2
        rs_acc = c.rs[:, 128 + par * 128: 128 + par * 128 + 128]
        if kc >= 2:
            K.wait("vector", s_pe, kc - 1)
        for h in range(2):
            m = kc * 2 + h
            if m >= 2:
                K.wait("sync", s_rd, red_done[m - 2])
            K.dma("sync", mbuf[:, (m % 2) * 2048: (m % 2) * 2048 + 2048],
                  maskT_ap[kc * 128:(kc + 1) * 128, h * 2048:(h + 1) * 2048], s_mc[m % 2])
            K.wait("vector", s_mc[m % 2])
            dst = rs_acc[:, h * 64:(h + 1) * 64]
            K.op("vector", lambda e, d=dst, i=mbuf[:, (m % 2) * 2048:(m % 2) * 2048 + 2048]:
                 e.reduce_sum(d.rearrange("p (a b) -> p a b", b=1),
                              i.rearrange("p (a b) -> p a b", b=32), axis=AX.X), inc=s_rd)
            red_done.append(s_rd.cnt)
        K.wait("vector", s_rd)
        K.op("vector", lambda e: e.engine_nop(), inc=s_rd)
        K.wait("tensor", s_rd)
        K.op("tensor", lambda e, l=rs_acc, kc=kc: e.matmul(
            c.ps[0][:, kc * 4:kc * 4 + 4], l, c.rs[:, 0:4], start=True, stop=True),
            inc=s_pe)
    K.wait("vector", s_pe)
    K.op("vector", lambda e: e.tensor_scalar(c.bmT01[:], c.ps[0][:, 0:128], 0.0, None, ALU.is_gt), inc=s_rd)
    K.wait("vector", s_rd)
    return s_rd


def emit_l0ab_wm(K, c, wst_ap, wm_ap, ind_ap, s_bm):
    s_w = K.sem("s_l0ab_w")
    s_ind = K.sem("s_l0ab_ind")
    s_em = K.sem("s_l0ab_em")
    s_tt = K.sem("s_l0ab_wmtt")
    s_out = [K.sem("s_l0ab_out0"), K.sem("s_l0ab_out1")]
    s_cp = K.sem("s_l0ab_cp")
    bmE = c.wbuf[:, 0:4096]
    K.wait("vector", s_bm)
    K.wait("sync", s_bm)
    for j in range(32):
        K.op("vector", lambda e, j=j: e.tensor_copy(c.wbuf[:, j:4096:32], c.bmT01[:]),
             inc=s_cp)
    K.wait("vector", s_cp)
    K.op("vector", lambda e: e.engine_nop(), inc=s_cp)
    K.wait("tensor", s_cp)
    K.wait("sync", s_cp)
    indb = c.ob[:, 0:512]
    wq32 = c.wbuf[:].bitcast(F32)[:, 2048:3072]
    n_slab = wst_ap.shape[0]
    ng = 8
    tt_done = []
    for qq in range(2 * n_slab):
        slab, jq = divmod(qq, 2)
        os, g = divmod(slab, ng)
        qpar = qq % 2
        if g == 0 and jq == 0:
            K.wait("sync", s_em)
            K.dma("sync", indb[:], ind_ap[:, os * 512:(os + 1) * 512], s_ind)
        if qq >= 1:
            K.wait("sync", s_tt, tt_done[qq - 1])
        K.dma("sync", wq32, wst_ap[slab][:, jq * 1024:(jq + 1) * 1024], s_w)
        banks = (c.ps[0], c.ps[1]) if qpar == 0 else (c.ps[2], c.ps[3])
        K.wait("tensor", s_ind, 16 * (os + 1))
        if qq >= 2:
            K.wait("tensor", s_tt, tt_done[qq - 2])
        for q in range(2):
            dc = g * 4 + jq * 2 + q
            K.op("tensor", lambda e, b=banks[q][:, 0:512],
                 l=bmE[:, dc * 128:(dc + 1) * 128]:
                 e.matmul(b, l, indb[:], start=True, stop=True),
                 inc=(s_em if q == 1 else None))
        K.wait("vector", s_em)
        K.wait("vector", s_w)
        if qq >= 2:
            K.wait("vector", s_out[qpar])
        outq = c.wbuf[:, 6144 + qpar * 1024: 6144 + (qpar + 1) * 1024]
        for q in range(2):
            K.op("vector", lambda e, o=outq[:, q * 512:(q + 1) * 512],
                 w=wq32[:, q * 512:(q + 1) * 512], b=banks[q][:, 0:512]:
                 e.tensor_tensor(o, w, b, ALU.mult), inc=s_tt)
        tt_done.append(s_tt.cnt)
        K.wait("sync", s_tt)
        if qq >= 2:
            K.wait("sync", s_out[qpar])
        K.dma("sync", wm_ap[slab][:, jq * 1024:(jq + 1) * 1024], outq, s_out[qpar])
    K.wait("sync", s_out[0])
    K.wait("sync", s_out[1])
    return s_out


def emit_l0ab(K, c, maskT_ap, wst_ap, wm_ap, sel4_ap, ind_ap, ln_sems):
    sv, sa, sp = ln_sems
    s_bm = emit_l0ab_maskprep(K, c, maskT_ap, sel4_ap, wait_wbuf=(sp,))
    s_out = emit_l0ab_wm(K, c, wst_ap, wm_ap, ind_ap, s_bm)
    s_ev, s_mm, s_wf = emit_proj_tok(
        K, c, wm_ap, 8, dest_resid, name="pj", prequant=True,
        pe_waits=(sv,), dma_waits=tuple(s_out))
    return s_ev



def l0ffn_consts(nc):
    e8 = np.zeros((8, 128), np.float32)
    for er in range(128):
        e8[er // 16, er] = 2.0
    return nc.inline_tensor(e8.astype(BF16NP), "e8c").ap()


def emit_l0ffn(K, c, wg9_ap, ast_ap, wbst_ap, bias_ap, bthr_ap, e8_ap, ln_sems, tag=""):
    sv, sa, sp = ln_sems
    s_h = K.sem("s_ff_h" + tag)
    base_h = s_h.cnt
    s_ha = K.sem("s_ff_ha" + tag)
    s_hc = K.sem("s_ff_hc" + tag)
    s_g = K.sem("s_ff_g" + tag)
    s_gv = K.sem("s_ff_gv" + tag)
    s_gs = K.sem("s_ff_gs" + tag)
    s_w9 = K.sem("s_ff_w9" + tag)
    obf = c.ob[:].bitcast(F32)
    abf = c.rs[:, 0:128].bitcast(BF16)
    ast32 = obf[:, 0:128]
    col = lambda i: c.rs[:, 288 + i: 288 + i + 1]
    e9 = c.rs[:, 304:312]
    ad = c.rs[:, 312:320]
    wv = c.rs[:, 320:328]
    wnf = c.rs[:, 328:336]
    e8_sb = c.rs[:, 336:400].bitcast(BF16)
    wg9b = obf[:, 128:272].bitcast(BF16)
    K.wait("tensor", sv)
    cast_done = []
    for dc in range(NDC):
        par = dc % 2
        if dc >= 1:
            K.wait("sync", s_hc, cast_done[dc - 1])
        K.dma("sync", ast32, ast_ap[dc], s_ha)
        K.wait("vector", s_ha)
        if dc >= 2:
            K.wait("vector", s_h, base_h + dc - 1)
        K.op("vector", lambda e, o=abf[:, par * 128:(par + 1) * 128]:
             e.tensor_copy(o, ast32), inc=s_hc)
        cast_done.append(s_hc.cnt)
        K.wait("tensor", s_hc)
        for th in range(2):
            K.op("tensor", lambda e, b=c.ps[4 + th][:, 0:512],
                 l=abf[:, par * 128:(par + 1) * 128],
                 r=c.x_fm[:, dc * T + th * 512: dc * T + th * 512 + 512],
                 st=(dc == 0), spp=(dc == NDC - 1):
                 e.matmul(b, l, r, start=st, stop=spp),
                 inc=(s_h if th == 1 else None))
    K.wait("sync", s_h)
    K.dma("sync", c.rs[:, 0:288], wg9_ap, s_w9)
    K.dma("sync", e8_sb[0:8, 0:128], e8_ap, s_w9)
    K.dma("sync", c.rs[0:1, 298:299], bthr_ap, s_w9)
    K.wait("vector", s_w9)
    K.op("vector", lambda e: e.tensor_copy(wg9b, c.rs[:, 0:288]), inc=s_gv)
    K.wait("tensor", s_w9)
    K.op("tensor", lambda e: e.matmul(c.ps[7][:, 0:1], c.onesr[:], c.rs[0:1, 298:299],
         start=True, stop=True), inc=s_g)
    K.wait("vector", s_g)
    K.op("vector", lambda e: e.tensor_copy(col(15), c.ps[7][:, 0:1]), inc=s_gv)
    K.wait("scalar", s_gv)
    K.wait("vector", s_gv)
    K.wait("tensor", s_gv)
    for tt in range(NT):
        if tt > 0:
            K.wait("tensor", s_gs)
        for dc in range(NDC):
            K.op("tensor", lambda e, l=c.x_fm[:, dc * T + tt * 128: dc * T + (tt + 1) * 128],
                 r=wg9b[:, dc * 9: dc * 9 + 9], st=(dc == 0), spp=(dc == NDC - 1):
                 e.matmul(c.ps[6][:, 0:9], l, r, start=st, stop=spp),
                 inc=(s_g if dc == NDC - 1 else None))
        K.wait("vector", s_g)
        K.op("vector", lambda e: e.reduce_max(col(0), c.ps[6][:, 0:8], axis=AX.X), inc=s_gv)
        K.wait("vector", s_gv)
        K.op("vector", lambda e: e.tensor_scalar(col(1), col(0), -1.0, None, ALU.mult), inc=s_gv)
        K.wait("scalar", s_gv)
        K.op("scalar", lambda e: e.activation(e9, c.ps[6][:, 0:8], AF.Exp,
             bias=col(1), accum_out=col(2)), inc=s_gs)
        K.op("scalar", lambda e: e.activation(col(3), c.ps[6][:, 8:9], AF.Sigmoid,
             bias=col(15)), inc=s_gs)
        K.wait("vector", s_gs)
        K.op("vector", lambda e: e.reciprocal(col(4), col(2)), inc=s_gv)
        K.op("vector", lambda e: e.tensor_scalar(col(5), col(3), -0.125, None, ALU.mult), inc=s_gv)
        K.wait("vector", s_gv)
        K.op("vector", lambda e: e.tensor_scalar(ad, e9, col(4), col(5), ALU.mult, ALU.add), inc=s_gv)
        K.wait("vector", s_gv)
        K.op("vector", lambda e: e.tensor_scalar(wv, ad, 0.0, None, ALU.max), inc=s_gv)
        K.wait("vector", s_gv)
        K.op("vector", lambda e: e.reduce_sum(col(6), wv, axis=AX.X), inc=s_gv)
        K.wait("vector", s_gv)
        K.op("vector", lambda e: e.tensor_scalar(col(7), col(6), 0.0, None, ALU.is_equal), inc=s_gv)
        K.wait("vector", s_gv)
        K.op("vector", lambda e: e.tensor_add(col(8), col(6), col(7)), inc=s_gv)
        K.wait("vector", s_gv)
        K.op("vector", lambda e: e.reciprocal(col(9), col(8)), inc=s_gv)
        K.wait("vector", s_gv)
        K.op("vector", lambda e: e.tensor_scalar(wnf, wv, col(9), None, ALU.mult), inc=s_gv)
        K.wait("tensor", s_gv)
        K.op("tensor", lambda e: e.transpose(c.ps[7][0:8, 0:128], wnf, c.ident[:]), inc=s_g)
        K.wait("vector", s_g)
        K.op("vector", lambda e: e.tensor_copy(c.bmT01[0:8, 0:128], c.ps[7][0:8, 0:128]), inc=s_gv)
        K.wait("tensor", s_gv)
        K.op("tensor", lambda e: e.matmul(c.ps[7][:, 0:128], e8_sb[0:8, 0:128],
             c.bmT01[0:8, 0:128], start=True, stop=True), inc=s_g)
        K.wait("vector", s_g)
        K.op("vector", lambda e: e.tensor_copy(c.bmT01[:, 0:128], c.ps[7][:, 0:128]), inc=s_gv)
        K.wait("vector", s_gv)
        hb = c.ps[4 + tt // 4][:, (tt % 4) * 128:(tt % 4) * 128 + 128]
        K.op("vector", lambda e, h=hb, o=c.hw[:, tt * 128:(tt + 1) * 128]:
             e.tensor_tensor(o, h, c.bmT01[:, 0:128], ALU.mult), inc=s_gv)
        K.wait("vector", s_gv)
    def lhsT_fn(dc, tt):
        if dc < NDC:
            return c.x_fm[:, dc * T + tt * 128: dc * T + (tt + 1) * 128]
        return c.hw[:, tt * 128:(tt + 1) * 128]
    s_ev, s_mm, s_wf = emit_proj_tok(
        K, c, wbst_ap, 8, dest_resid, name="pj", bias_ap=bias_ap,
        n_dc=34, dcg=2, lhsT_fn=lhsT_fn, pe_waits=(s_gv,), dma_waits=(s_gv,))
    return s_ev


def l0ffn_host(W_base, W_gate, W_thr, A_w, B_w):
    Wg9 = np.concatenate([W_gate.T, W_thr[:, None]], axis=1)
    wg9 = np.ascontiguousarray(Wg9.reshape(32, 128, 9).transpose(1, 0, 2)
                               .reshape(128, 288)).astype(np.float32)
    A_all = A_w.transpose(2, 0, 1).reshape(4096, 128)
    ast = np.ascontiguousarray(A_all.reshape(32, 128, 128)).astype(np.float32)
    B_all = B_w.transpose(0, 2, 1).reshape(128, 4096)
    WT_aug = np.concatenate(
        [W_base.T, B_all, np.zeros((128, 4096), np.float32)], axis=0)
    wbst = w_stream(WT_aug, dcg=2)
    return wg9, ast, wbst


def l0a_host(Wq, bq, Wk, bk, Wv, bv, Wo, bo):
    WT_kv = np.concatenate([Wk.T, Wv.T], axis=1)
    wkvst = w_stream(WT_kv.astype(np.float32), dcg=4)
    kvbias = np.concatenate([bk, bv])[None, :].astype(np.float32)
    WqT = Wq.T.astype(np.float32)
    sl = []
    for h in range(32):
        wh = WqT[:, h * 128:(h + 1) * 128].reshape(32, 128, 128)
        for half in range(2):
            sl.append(np.ascontiguousarray(
                wh[half * 16:(half + 1) * 16].transpose(1, 0, 2).reshape(128, 2048)))
    wqst = np.stack(sl)
    bqa = np.ascontiguousarray(bq.reshape(32, 1, 128)).astype(np.float32)
    wost = w_stream(Wo.T.astype(np.float32), dcg=4)
    bo_ = bo[None, :].astype(np.float32)
    return wkvst, kvbias, wqst, bqa, wost, bo_


def emit_l0a_kvproj(K, c, wkvst_ap, kvbias_ap, kv_dram, ln_sems):
    sv, sa, sp = ln_sems
    s_kvs = K.sem("s_l0a_kvs")
    s_kvo = [K.sem("s_l0a_kvo0"), K.sem("s_l0a_kvo1")]
    hwf = c.hw[:].bitcast(F32)

    def dest(K, c, tt, os, psum, s_ev):
        idx = tt + os * NT
        par = idx % 2
        if idx >= 2:
            K.wait("vector", s_kvo[par])
        ob = c.ob[:, par * 512:(par + 1) * 512]
        if os < 2:
            K.wait("scalar", s_ev)
            K.op("scalar", lambda e, p=psum: e.activation(hwf, p, AF.Exp), inc=s_kvs)
            K.wait("vector", s_kvs)
            K.op("vector", lambda e: e.tensor_scalar(hwf, hwf, 1.0, None, ALU.min), inc=s_ev)
            K.wait("vector", s_ev)
            K.op("vector", lambda e, p=psum, o=ob: e.scalar_tensor_tensor(
                o, p, 0.0, hwf, ALU.max, ALU.add), inc=s_ev)
        else:
            K.op("vector", lambda e, p=psum, o=ob: e.tensor_copy(o, p), inc=s_ev)
        K.wait("sync", s_ev)
        K.dma("sync", kv_dram[tt][:, os * 512:(os + 1) * 512], ob, s_kvo[par])

    s_ev, s_mm, s_wf = emit_proj_tok(
        K, c, wkvst_ap, 4, dest, name="pj", bias_ap=kvbias_ap,
        pe_waits=(sv,), dma_waits=(sv,))
    return s_ev, s_kvo


def emit_l0a_kvstate(K, c, kv_dram, s_kvo, kvout_ap):
    s_t = [K.sem("s_l0a_t0"), K.sem("s_l0a_t1")]
    s_pe = K.sem("s_l0a_kvpe")
    s_kv = K.sem("s_l0a_kve")
    s_ko = K.sem("s_l0a_ko")
    hwf = c.hw[:].bitcast(F32)
    ks_acc = c.rs[:, 144:152]
    for w in s_kvo:
        K.wait("sync", w)
    K.op("vector", lambda e: e.memset(ks_acc, 0.0), inc=s_kv)
    K.wait("vector", s_kv)
    tld = 0
    for wave in range(2):
        for tt in range(NT):
            par = tld % 2
            sl = c.wbuf[:, par * 4096: par * 4096 + 2048]
            if tld >= 2:
                K.wait("sync", s_pe, s_pe.cnt - 1)
            K.dma("sync", sl, kv_dram[tt], s_t[par])
            K.wait("tensor", s_t[par])
            if tt > 0:
                K.wait("tensor", s_kv)
            for q in range(4):
                kvh = wave * 4 + q
                ksl = sl[:, kvh * 128:(kvh + 1) * 128]
                vsl = sl[:, 1024 + kvh * 128: 1024 + (kvh + 1) * 128]
                K.op("tensor", lambda e, k=ksl, v=vsl, b=c.ps[q][:, 0:128],
                     st=(tt == 0), spp=(tt == NT - 1):
                     e.matmul(b, k, v, start=st, stop=spp), inc=None)
                K.op("tensor", lambda e, k=ksl, b=c.ps[4][:, q:q + 1]:
                     e.matmul(b, k, c.onescol[:], start=True, stop=True),
                     inc=(s_pe if q == 3 else None))
            tld += 1
            K.wait("vector", s_pe)
            K.op("vector", lambda e, o=ks_acc[:, wave * 4:(wave + 1) * 4]:
                 e.tensor_tensor(o, o, c.ps[4][:, 0:4], ALU.add), inc=s_kv)
        K.wait("vector", s_pe)
        for q in range(4):
            if wave > 0 or q > 0:
                K.wait("vector", s_ko)
            K.op("vector", lambda e, b=c.ps[q][:, 0:128], o=hwf[:, 0:128]:
                 e.tensor_copy(o, b), inc=s_kv)
            K.wait("sync", s_kv)
            K.dma("sync", kvout_ap[:, (wave * 4 + q) * 128:(wave * 4 + q) * 128 + 128],
                  hwf[:, 0:128], s_ko)
    K.wait("vector", s_kv)
    K.op("vector", lambda e: e.engine_nop(), inc=s_kv)
    K.wait("sync", s_kv)
    K.dma("sync", kvout_ap[:, 1024:1032], ks_acc, s_ko)
    K.wait("sync", s_ko)
    return s_ko


def emit_l0a_kvreduce_in(K, c, kvparts_ap):
    s_l = K.sem("s_l0a_kl")
    s_v = K.sem("s_l0a_kvv")
    wb32 = c.wbuf[:].bitcast(F32)
    for piece in range(3):
        w = 512 if piece < 2 else 8
        if piece > 0:
            K.wait("sync", s_v)
        for r in range(4):
            K.dma("sync", wb32[:, r * 512: r * 512 + w],
                  kvparts_ap[r][:, piece * 512: piece * 512 + w], s_l)
        K.wait("vector", s_l)
        K.op("vector", lambda e, w=w: e.tensor_add(wb32[:, 0:w], wb32[:, 0:w], wb32[:, 512:512 + w]), inc=s_v)
        K.op("vector", lambda e, w=w: e.tensor_add(wb32[:, 1024:1024 + w], wb32[:, 1024:1024 + w], wb32[:, 1536:1536 + w]), inc=s_v)
        K.wait("vector", s_v)
        K.op("vector", lambda e, w=w: e.tensor_add(wb32[:, 0:w], wb32[:, 0:w], wb32[:, 1024:1024 + w]), inc=s_v)
        K.wait("vector", s_v)
        if piece < 2:
            K.op("vector", lambda e, piece=piece: e.tensor_copy(
                c.ob[:, piece * 512:(piece + 1) * 512], wb32[:, 0:512]), inc=s_v)
        else:
            K.op("vector", lambda e: e.tensor_scalar(wb32[:, 0:8], wb32[:, 0:8], 1e-10, None, ALU.add), inc=s_v)
            K.wait("vector", s_v)
            K.op("vector", lambda e: e.reciprocal(c.rs[:, 128:136], wb32[:, 0:8]), inc=s_v)
        K.wait("vector", s_v)
    K.op("vector", lambda e: e.engine_nop(), inc=s_v)
    return s_v


def emit_l0a_attn(K, c, wqst_ap, bqa_ap, attn_dram, s_kvready, ln_sems):
    sv, sa, sp = ln_sems
    s_w = [K.sem("s_l0a_qw0"), K.sem("s_l0a_qw1")]
    s_c = K.sem("s_l0a_qc")
    s_q = K.sem("s_l0a_qpe")
    s_qv = K.sem("s_l0a_qv")
    s_qs = K.sem("s_l0a_qs")
    s_b = K.sem("s_l0a_qb")
    s_ao = [K.sem("s_l0a_ao0"), K.sem("s_l0a_ao1")]
    wb32 = c.wbuf[:].bitcast(F32)
    hwf = c.hw[:].bitcast(F32)
    q_phi = c.wbuf[:, 6144:7168]
    bqrow = c.rs[0:1, 256:384]
    for e in ("vector", "tensor", "sync", "scalar"):
        K.wait(e, s_kvready)
    K.wait("tensor", sv)
    cast_done = []
    qdone = []
    for h in range(32):
        for half in range(2):
            hs = h * 2 + half
            for j in range(2):
                p = hs * 2 + j
                if p >= 1:
                    K.wait("sync", s_c, cast_done[p - 1])
                K.dma("sync", wb32[:, 2048:3072],
                      wqst_ap[hs][:, j * 1024:(j + 1) * 1024], s_w[0])
                if hs >= 2 and j == 0:
                    K.wait("vector", s_q, qdone[hs - 2])
                K.wait("vector", s_w[0])
                K.op("vector", lambda e,
                     o=c.wbuf[:, (hs % 2) * 2048 + j * 1024: (hs % 2) * 2048 + (j + 1) * 1024],
                     i=wb32[:, 2048:3072]:
                     e.tensor_copy(o, i), inc=s_c)
                cast_done.append(s_c.cnt)
        if h >= 1:
            K.wait("sync", s_q)
        K.dma("sync", bqrow, bqa_ap[h], s_b)
        if h > 0:
            K.wait("tensor", s_qv)
        K.wait("tensor", s_b)
        for half in range(2):
            hs = h * 2 + half
            K.wait("tensor", s_c, cast_done[hs * 2 + 1])
            for i in range(16):
                dc = half * 16 + i
                l = c.wbuf[:, (hs % 2) * 2048 + i * 128: (hs % 2) * 2048 + (i + 1) * 128]
                for ts in range(2):
                    K.op("tensor", lambda e, l=l, b=c.ps[5 + ts][:, 0:512],
                         r=c.x_fm[:, dc * T + ts * 512: dc * T + ts * 512 + 512],
                         st=(dc == 0), spp=(dc == NDC - 1):
                         e.matmul(b, l, r, start=st, stop=spp), inc=None)
            K.op("tensor", lambda e: e.drain(), inc=s_q)
            qdone.append(s_q.cnt)
        for ts in range(2):
            for seg in range(4):
                K.op("tensor", lambda e, b=c.ps[5 + ts][:, seg * 128:(seg + 1) * 128]:
                     e.matmul(b, bqrow, c.onesr[:], start=False, stop=True,
                              skip_group_check=True),
                     inc=(s_q if (ts == 1 and seg == 3) else None))
        for ts in range(2):
            K.wait("scalar", s_q)
            K.wait("scalar", s_qv)
            K.op("scalar", lambda e, p=c.ps[5 + ts][:, 0:512]: e.activation(hwf, p, AF.Exp), inc=s_qs)
            K.wait("vector", s_qs)
            K.op("vector", lambda e: e.tensor_scalar(hwf, hwf, 1.0, None, ALU.min), inc=s_qv)
            K.wait("vector", s_qv)
            K.op("vector", lambda e, p=c.ps[5 + ts][:, 0:512], o=q_phi[:, ts * 512:(ts + 1) * 512]:
                 e.scalar_tensor_tensor(o, p, 0.0, hwf, ALU.max, ALU.add), inc=s_qv)
            K.wait("vector", s_qv)
        kvh = h // 4
        K.wait("tensor", s_qv)
        for ts in range(2):
            apar = ts
            K.op("tensor", lambda e, b=c.ps[3 + apar][:, 0:512],
                 l=c.ob[:, kvh * 128:(kvh + 1) * 128],
                 r=q_phi[:, ts * 512:(ts + 1) * 512]:
                 e.matmul(b, l, r, start=True, stop=True), inc=s_q)
            K.wait("vector", s_q)
            idx = h * 2 + ts
            if idx >= 2:
                K.wait("vector", s_ao[apar])
            obf = c.wbuf[:, 7168 + apar * 512: 7168 + (apar + 1) * 512]
            K.op("vector", lambda e, b=c.ps[3 + apar][:, 0:512], o=obf, kvh=kvh:
                 e.tensor_scalar(o, b, c.rs[:, 128 + kvh:129 + kvh], None, ALU.mult), inc=s_qv)
            K.wait("sync", s_qv)
            K.dma("sync", attn_dram[0:128, h * 1024 + ts * 512: h * 1024 + (ts + 1) * 512],
                  obf, s_ao[apar])
    K.wait("sync", s_ao[0])
    K.wait("sync", s_ao[1])
    return s_ao


def emit_l0a_oproj(K, c, attn_dram, wost_ap, bo_ap, s_ao, prequant=False, name="l0ao",
                   attn_list=None):
    s_l = K.sem("s_al_" + name)
    for w in s_ao:
        K.wait("sync", w)
    if attn_list is not None:
        for dc in range(NDC):
            K.dma("sync", c.x_fm[:, dc * T:(dc + 1) * T], attn_list[dc], s_l)
    else:
        K.dma("sync", c.x_fm[:], attn_dram, s_l)
    s_ev, s_mm, s_wf = emit_proj_tok(
        K, c, wost_ap, 8, dest_resid, name=name, bias_ap=bo_ap, prequant=prequant,
        pe_waits=(s_l,), dma_waits=(s_l,))
    return s_ev


def l1ffn_host(W, A, B, router_W):
    rw8 = np.ascontiguousarray(router_W.T.reshape(32, 128, 8).transpose(1, 0, 2)
                               .reshape(128, 256)).astype(np.float32)
    A_all = A.transpose(2, 0, 1).reshape(4096, 128)
    ast = np.ascontiguousarray(A_all.reshape(32, 128, 128)).astype(np.float32)
    B_all = B.transpose(0, 2, 1).reshape(128, 4096)
    WT_aug = np.concatenate(
        [W.T, B_all, np.zeros((128, 4096), np.float32)], axis=0)
    wbst = w_stream(WT_aug, dcg=2)
    return rw8, ast, wbst


def sel_host(core):
    b, r = divmod(core, 4)
    pos = np.arange(r * T, (r + 1) * T)
    sel = ((pos >= 611) & (pos <= 4094)).astype(np.float32)
    return np.ascontiguousarray(sel.reshape(8, 128).T).astype(BF16NP)


def emit_l1ffn_router_partial(K, c, rw8_ap, selb_ap, rpart_out_ap, ln_sems, tag=""):
    sv, sa, sp = ln_sems
    s_r = K.sem("s_rt_r" + tag)
    s_rv = K.sem("s_rt_v" + tag)
    s_w = K.sem("s_rt_w" + tag)
    obf = c.ob[:].bitcast(F32)
    rw8b = obf[:, 256:384].bitcast(BF16)
    selb = obf[:, 384:388].bitcast(BF16)
    K.dma("sync", obf[:, 0:256], rw8_ap, s_w)
    K.dma("sync", selb, selb_ap, s_w)
    K.wait("vector", s_w)
    K.op("vector", lambda e: e.tensor_copy(rw8b, obf[:, 0:256]), inc=s_rv)
    K.wait("vector", s_rv)
    K.wait("tensor", s_rv)
    K.wait("tensor", sv)
    for tt in range(NT):
        if tt > 0:
            K.wait("tensor", s_rv)
        for dc in range(NDC):
            K.op("tensor", lambda e, l=c.x_fm[:, dc * T + tt * 128: dc * T + (tt + 1) * 128],
                 r=rw8b[:, dc * 8: dc * 8 + 8], st=(dc == 0), spp=(dc == NDC - 1):
                 e.matmul(c.ps[6][:, 0:8], l, r, start=st, stop=spp),
                 inc=(s_r if dc == NDC - 1 else None))
        K.wait("vector", s_r)
        K.op("vector", lambda e: e.tensor_copy(c.rs[:, 256:260].bitcast(BF16), c.ps[6][:, 0:8]), inc=s_rv)
        K.wait("vector", s_rv)
        K.op("vector", lambda e: e.engine_nop(), inc=s_rv)
        K.wait("tensor", s_rv)
        K.op("tensor", lambda e, l=selb[:, tt:tt + 1], st=(tt == 0), spp=(tt == NT - 1):
             e.matmul(c.ps[7][0:1, 0:8], l, c.rs[:, 256:260].bitcast(BF16), start=st, stop=spp),
             inc=s_r)
    K.wait("vector", s_r)
    if rpart_out_ap is not None:
        K.op("vector", lambda e: e.tensor_copy(c.rs[0:1, 264:272], c.ps[7][0:1, 0:8]), inc=s_rv)
        K.wait("sync", s_rv)
        s_o = K.sem("s_rt_o" + tag)
        K.dma("sync", rpart_out_ap, c.rs[0:1, 264:272], s_o)
        K.wait("sync", s_o)
        return s_o
    return s_r


def emit_l1ffn_routing(K, c, rparts_ap, rb_ap, e8_ap, tag=""):
    s_l = K.sem("s_ro_l" + tag)
    s_v = K.sem("s_ro_v" + tag)
    s_s = K.sem("s_ro_s" + tag)
    s_p = K.sem("s_ro_p" + tag)
    row = lambda i: c.rs[0:1, 280 + i * 8: 280 + (i + 1) * 8]
    e8_sb = c.rs[:, 336:400].bitcast(BF16)
    col1 = lambda i: c.rs[0:1, 408 + i: 409 + i]
    rowb = c.rs[0:1, 404:408].bitcast(BF16)[0:1, 0:8]
    colb = c.rs[:, 274:275].bitcast(BF16)[0:8, 0:1]
    for r in range(4):
        K.dma("sync", c.rs[0:1, 280 + 8 * r:288 + 8 * r], rparts_ap[r], s_l)
    K.dma("sync", c.rs[0:1, 312:320], rb_ap, s_l)
    K.dma("sync", e8_sb[0:8, 0:128], e8_ap, s_l)
    K.wait("vector", s_l)
    K.op("vector", lambda e: e.tensor_add(row(0), row(0), row(1)), inc=s_v)
    K.op("vector", lambda e: e.tensor_add(row(2), row(2), row(3)), inc=s_v)
    K.wait("vector", s_v)
    K.op("vector", lambda e: e.tensor_add(row(0), row(0), row(2)), inc=s_v)
    K.wait("vector", s_v)
    K.op("vector", lambda e: e.scalar_tensor_tensor(
        row(0), row(0), 1.0 / 3484.0, c.rs[0:1, 312:320], ALU.mult, ALU.add), inc=s_v)
    K.wait("vector", s_v)
    K.op("vector", lambda e: e.reduce_max(col1(0), row(0), axis=AX.X), inc=s_v)
    K.wait("vector", s_v)
    K.op("vector", lambda e: e.tensor_scalar(col1(1), col1(0), -1.0, None, ALU.mult), inc=s_v)
    K.wait("scalar", s_v)
    K.op("scalar", lambda e: e.activation(row(5), row(0), AF.Exp,
         bias=col1(1), accum_out=col1(2)), inc=s_s)
    K.wait("vector", s_s)
    K.op("vector", lambda e: e.reciprocal(col1(3), col1(2)), inc=s_v)
    K.wait("vector", s_v)
    K.op("vector", lambda e: e.tensor_scalar(row(6), row(5), col1(3), None, ALU.mult), inc=s_v)
    K.wait("vector", s_v)
    K.op("vector", lambda e: e.tensor_copy(rowb, row(6)), inc=s_v)
    K.wait("tensor", s_v)
    K.op("tensor", lambda e: e.transpose(c.ps[6][:].bitcast(BF16)[0:8, 0:1], rowb, c.identb[0:1, 0:1]), inc=s_p)
    K.wait("vector", s_p)
    K.op("vector", lambda e: e.tensor_copy(colb, c.ps[6][:].bitcast(BF16)[0:8, 0:1]), inc=s_v)
    K.wait("vector", s_v)
    K.op("vector", lambda e: e.engine_nop(), inc=s_v)
    K.wait("tensor", s_v)
    K.op("tensor", lambda e: e.matmul(c.ps[7][:, 0:1], e8_sb[0:8, 0:128],
         colb, start=True, stop=True), inc=s_p)
    K.wait("vector", s_p)
    K.op("vector", lambda e: e.tensor_copy(c.rs[:, 272:273], c.ps[7][:, 0:1]), inc=s_v)
    K.wait("vector", s_v)
    K.op("vector", lambda e: e.engine_nop(), inc=s_v)
    return s_v


def emit_l1ffn_main(K, c, ast_ap, wbst_ap, bias_ap, s_route, ln_sems, tag=""):
    sv, sa, sp = ln_sems
    s_h = K.sem("s_lf_h" + tag)
    base_h = s_h.cnt
    s_ha = K.sem("s_lf_ha" + tag)
    s_hc = K.sem("s_lf_hc" + tag)
    s_hv = K.sem("s_lf_hv" + tag)
    obf = c.ob[:].bitcast(F32)
    abf = c.rs[:, 0:128].bitcast(BF16)
    ast32 = obf[:, 0:128]
    K.wait("tensor", sv)
    K.wait("vector", s_route)
    cast_done = []
    for dc in range(NDC):
        par = dc % 2
        if dc >= 1:
            K.wait("sync", s_hc, cast_done[dc - 1])
        K.dma("sync", ast32, ast_ap[dc], s_ha)
        K.wait("vector", s_ha)
        if dc >= 2:
            K.wait("vector", s_h, base_h + dc - 1)
        K.op("vector", lambda e, o=abf[:, par * 128:(par + 1) * 128]:
             e.tensor_copy(o, ast32), inc=s_hc)
        cast_done.append(s_hc.cnt)
        K.wait("tensor", s_hc)
        for th in range(2):
            K.op("tensor", lambda e, b=c.ps[4 + th][:, 0:512],
                 l=abf[:, par * 128:(par + 1) * 128],
                 r=c.x_fm[:, dc * T + th * 512: dc * T + th * 512 + 512],
                 st=(dc == 0), spp=(dc == NDC - 1):
                 e.matmul(b, l, r, start=st, stop=spp),
                 inc=(s_h if th == 1 else None))
    K.wait("vector", s_h)
    for th in range(2):
        K.op("vector", lambda e, b=c.ps[4 + th][:, 0:512], o=c.hw[:, th * 512:(th + 1) * 512]:
             e.tensor_scalar(o, b, c.rs[:, 272:273], None, ALU.mult), inc=s_hv)
    K.wait("vector", s_hv)
    K.op("vector", lambda e: e.engine_nop(), inc=s_hv)

    def lhsT_fn(dc, tt):
        if dc < NDC:
            return c.x_fm[:, dc * T + tt * 128: dc * T + (tt + 1) * 128]
        return c.hw[:, tt * 128:(tt + 1) * 128]
    s_ev, s_mm, s_wf = emit_proj_tok(
        K, c, wbst_ap, 8, dest_resid, name="pj", bias_ap=bias_ap,
        n_dc=34, dcg=2, lhsT_fn=lhsT_fn, pe_waits=(s_hv,), dma_waits=(s_hv,))
    return s_ev


def l1a_host(Wq_d, bq_d, W_qk, b_qk, Wkv_d, bkv_d, Wv_u, bv_u, Wo, bo):
    Wc16 = np.concatenate([Wq_d, Wkv_d], axis=1)
    wc16 = np.ascontiguousarray(Wc16.reshape(32, 128, 16).transpose(1, 0, 2)
                                .reshape(128, 512)).astype(np.float32)
    cbias = np.concatenate([bq_d, bkv_d])[None, :].astype(np.float32)
    wqk = W_qk[None if False else slice(None), :].astype(np.float32)
    bqk4 = np.ascontiguousarray(b_qk.reshape(32, 4).T).astype(np.float32)
    wvu = Wv_u.astype(np.float32)
    bvu = np.ascontiguousarray(bv_u.reshape(32, 128, 1)).astype(np.float32)
    wost = w_stream(Wo.astype(np.float32), dcg=4)
    boo = bo[None, :].astype(np.float32)
    return wc16, cbias, wqk, bqk4, wvu, bvu, wost, boo


def emit_l1a_cproj(K, c, wc16_ap, cbias_ap, ckv_own_ap, ln_sems, want_cq=True, tag=""):
    sv, sa, sp = ln_sems
    s_w = K.sem("s_ca_w" + tag)
    s_pe = K.sem("s_ca_pe" + tag)
    s_v = K.sem("s_ca_v" + tag)
    s_o = K.sem("s_ca_o" + tag)
    wbf = c.wbuf[:].bitcast(F32)
    wc16b = c.ob[:, 0:512]
    K.wait("sync", sp)
    K.dma("sync", wbf[:, 0:512], wc16_ap, s_w)
    K.dma("sync", c.rs[0:1, 0:16], cbias_ap, s_w)
    K.wait("vector", s_w)
    K.op("vector", lambda e: e.tensor_copy(wc16b, wbf[:, 0:512]), inc=s_v)
    K.wait("vector", s_v)
    K.wait("tensor", s_v)
    K.wait("tensor", sv)
    cqb = c.rs[:, 16:24].bitcast(BF16)
    ckvb = c.rs[:, 24:26].bitcast(BF16)
    for tt in range(NT):
        if tt > 0:
            K.wait("tensor", s_v)
        for dc in range(NDC):
            K.op("tensor", lambda e, l=c.x_fm[:, dc * T + tt * 128: dc * T + (tt + 1) * 128],
                 r=wc16b[:, dc * 16: dc * 16 + 16], st=(dc == 0), spp=(dc == NDC - 1):
                 e.matmul(c.ps[6][:, 0:16], l, r, start=st, stop=spp), inc=None)
        K.op("tensor", lambda e: e.matmul(c.ps[6][:, 0:16], c.onesr[:], c.rs[0:1, 0:16],
             start=False, stop=True, skip_group_check=True), inc=s_pe)
        K.wait("vector", s_pe)
        if tt >= 1:
            K.wait("vector", s_o)
        K.op("vector", lambda e: e.tensor_copy(ckvb, c.ps[6][:, 12:16]), inc=s_v)
        if want_cq:
            K.op("vector", lambda e: e.tensor_copy(cqb[:, 0:12], c.ps[6][:, 0:12]), inc=s_v)
        K.wait("sync", s_v)
        if tt >= 1:
            K.wait("sync", s_o)
        K.dma("sync", ckv_own_ap[tt * 128:(tt + 1) * 128, :], ckvb, s_o)
        if want_cq:
            K.wait("tensor", s_v)
            K.op("tensor", lambda e: e.transpose(c.ps[7][:].bitcast(BF16)[0:12, 0:128],
                 cqb[:, 0:12], c.identb[:]), inc=s_pe)
            K.wait("vector", s_pe)
            K.op("vector", lambda e, tt=tt: e.tensor_copy(
                c.wbuf[0:12, 6144 + tt * 128: 6144 + (tt + 1) * 128],
                c.ps[7][:].bitcast(BF16)[0:12, 0:128]), inc=s_v)
        K.wait("vector", s_v)
        K.op("vector", lambda e: e.engine_nop(), inc=s_v)
    K.wait("sync", s_o)
    return s_v, s_o


def emit_l1a_attn(K, c, ckv_full_ap, wqk_ap, bqk4_ap, wvu_ap, bvu_ap, attn_dram,
                  s_cq, tag=""):
    s_w = K.sem("s_la_w" + tag)
    s_pe = K.sem("s_la_pe" + tag)
    s_v = K.sem("s_la_v" + tag)
    s_s = K.sem("s_la_s" + tag)
    s_b = K.sem("s_la_b" + tag)
    s_t = K.sem("s_la_t" + tag)
    s_ao = [K.sem("s_la_ao0" + tag), K.sem("s_la_ao1" + tag)]
    obf = c.ob[:].bitcast(F32)
    xff = c.x_fm[:].bitcast(F32)
    hwf = c.hw[:].bitcast(F32)
    ckv_fm = c.wbuf[0:4, 0:4096]
    q4h_sb = c.wbuf[0:4, 4096:5120]
    P = [c.wbuf[:, 5120:5632], c.wbuf[:, 5632:6144]]
    cq_fm = c.wbuf[0:12, 6144:7168]
    tn_sb = c.wbuf[0:4, 7168:7680]
    wqk_b = c.ob[0:12, 640:768]
    bqk4 = obf[0:4, 448:480]
    bvu_col = obf[:, 480:481]
    wvu_b = c.x_fm[0:4, 0:4096]
    aout = [c.ob[:, 128:640], c.wbuf[:, 7680:8192]]
    K.wait("sync", s_cq)
    for g in range(8):
        K.dma("sync", c.ob[:, g * 16:(g + 1) * 16].rearrange("p (a b) -> p a b", b=4),
              ckv_full_ap[g * 512:(g + 1) * 512, :].rearrange("(a p) b -> p a b", p=128),
              s_w, allow_slow_non_contiguous=True)
    K.dma("sync", xff[0:4, 4096:8192], wvu_ap, s_w)
    K.dma("sync", hwf[0:12, 0:128], wqk_ap, s_w)
    K.dma("sync", bqk4, bqk4_ap, s_w)
    K.wait("vector", s_w)
    K.op("vector", lambda e: e.tensor_copy(wvu_b, xff[0:4, 4096:8192]), inc=s_v)
    K.op("vector", lambda e: e.tensor_copy(wqk_b, hwf[0:12, 0:128]), inc=s_v)
    K.wait("vector", s_v)
    K.op("vector", lambda e: e.engine_nop(), inc=s_v)
    K.wait("tensor", s_v)
    for kc in range(NDC):
        if kc >= 2:
            K.wait("tensor", s_v, s_v.cnt)
        K.op("tensor", lambda e, kc=kc: e.transpose(
            c.ps[6 + kc % 2][:].bitcast(BF16)[0:4, 0:128],
            c.ob[:, kc * 4:(kc + 1) * 4], c.identb[:]), inc=s_pe)
        K.wait("vector", s_pe)
        K.op("vector", lambda e, kc=kc: e.tensor_copy(
            ckv_fm[0:4, kc * 128:(kc + 1) * 128],
            c.ps[6 + kc % 2][:].bitcast(BF16)[0:4, 0:128]), inc=s_v)
    K.wait("vector", s_v)
    K.op("vector", lambda e: e.engine_nop(), inc=s_v)
    K.wait("tensor", s_v)
    for h in range(32):
        if h >= 1:
            K.wait("sync", s_v)
        K.dma("sync", bvu_col, bvu_ap[h], s_b)
        K.wait("tensor", s_b)
        if h >= 1:
            K.wait("tensor", s_v)
        for ts in range(2):
            K.op("tensor", lambda e, ts=ts, h=h: e.matmul(
                c.ps[6 + ts][0:4, 0:512], wqk_b[0:12, 4 * h:4 * h + 4],
                cq_fm[0:12, ts * 512:(ts + 1) * 512], start=True, stop=True), inc=s_pe)
        K.wait("vector", s_pe)
        for ts in range(2):
            K.op("vector", lambda e, ts=ts, h=h: e.tensor_scalar(
                q4h_sb[0:4, ts * 512:(ts + 1) * 512], c.ps[6 + ts][0:4, 0:512],
                bqk4[0:4, h:h + 1], 0.5, ALU.add, ALU.mult), inc=s_v)
        K.wait("vector", s_v)
        K.op("vector", lambda e: e.engine_nop(), inc=s_v)
        K.wait("tensor", s_v)
        K.wait("scalar", s_v)
        for qs in range(2):
            K.wait("tensor", s_v)
            sread = {}
            tdone = {}
            SABANK = [0, 1, 4, 5]

            def tden(kc):
                p2 = kc % 2
                K.wait("tensor", s_s, sread[kc])
                K.op("tensor", lambda e, kc=kc, p2=p2: e.matmul(
                    c.ps[2][0:4, 0:512], c.ob[:, kc * 4:(kc + 1) * 4], P[p2],
                    start=(kc == 0), stop=(kc == NDC - 1)), inc=None)
                K.op("tensor", lambda e, kc=kc, p2=p2: e.matmul(
                    c.ps[3][0:1, 0:512], c.onescol[:], P[p2],
                    start=(kc == 0), stop=(kc == NDC - 1)), inc=s_t)
                tdone[kc] = s_t.cnt

            for kc in range(NDC):
                bk = SABANK[kc % 4]
                p2 = kc % 2
                if kc >= 4:
                    K.wait("tensor", s_s, sread[kc - 4])
                K.op("tensor", lambda e, kc=kc, bk=bk, qs=qs: e.matmul(
                    c.ps[bk][:, 0:512], ckv_fm[0:4, kc * 128:(kc + 1) * 128],
                    q4h_sb[0:4, qs * 512:(qs + 1) * 512], start=True, stop=True), inc=s_pe)
                mmv = s_pe.cnt
                if kc >= 1:
                    tden(kc - 1)
                if kc >= 2:
                    K.wait("scalar", s_t, tdone[kc - 2])
                K.wait("scalar", s_pe, mmv)
                K.op("scalar", lambda e, bk=bk, p2=p2: e.activation(
                    P[p2], c.ps[bk][:, 0:512], AF.Exp), inc=s_s)
                sread[kc] = s_s.cnt
            tden(NDC - 1)
            K.wait("vector", s_t)
            K.op("vector", lambda e: e.tensor_copy(c.rs[0:4, 0:512], c.ps[2][0:4, 0:512]), inc=s_v)
            K.op("vector", lambda e: e.reciprocal(hwf[0:1, 0:512], c.ps[3][0:1, 0:512]), inc=s_v)
            K.wait("vector", s_v)
            K.op("vector", lambda e: e.engine_nop(), inc=s_v)
            K.wait("tensor", s_v)
            K.op("tensor", lambda e: e.matmul(c.ps[4][0:4, 0:512], c.onesr[0:1, 0:4],
                 hwf[0:1, 0:512], start=True, stop=True), inc=s_pe)
            K.wait("vector", s_pe)
            K.op("vector", lambda e: e.tensor_tensor(tn_sb, c.rs[0:4, 0:512],
                 c.ps[4][0:4, 0:512], ALU.mult), inc=s_v)
            K.wait("vector", s_v)
            K.op("vector", lambda e: e.engine_nop(), inc=s_v)
            K.wait("tensor", s_v)
            K.op("tensor", lambda e, h=h: e.matmul(c.ps[5][:, 0:512],
                 wvu_b[0:4, h * 128:(h + 1) * 128], tn_sb, start=True, stop=True), inc=s_pe)
            K.wait("vector", s_pe)
            idx = h * 2 + qs
            if idx >= 2:
                K.wait("vector", s_ao[qs])
            K.op("vector", lambda e, qs=qs: e.tensor_scalar(
                aout[qs], c.ps[5][:, 0:512], bvu_col, None, ALU.add), inc=s_v)
            K.wait("sync", s_v)
            K.dma("sync", attn_dram[0:128, h * 1024 + qs * 512: h * 1024 + (qs + 1) * 512],
                  aout[qs], s_ao[qs])
    K.wait("sync", s_ao[0])
    K.wait("sync", s_ao[1])
    return s_ao


def l1b_host_one(codes, scales):
    cst = w_stream(codes.astype(np.int32), dcg=4)
    n_slab = cst.shape[0]
    ng = codes.shape[0] // 512
    rows = []
    for s in range(n_slab):
        os_, g = divmod(s, ng)
        for j in range(2):
            for q in range(2):
                dcs = g * 4 + j * 2 + q
                rows.append(scales[dcs, os_ * 512:(os_ + 1) * 512])
    srow = np.ascontiguousarray(np.stack(rows)[:, None, :]).astype(np.float32)
    return cst, srow


def l1b_host(q_codes, q_scales, k_codes, k_scales, v_codes, v_scales, o_codes, o_scales):
    mk_stream = l1b_host_one
    qc, qs_ = mk_stream(q_codes, q_scales)
    kc_, ks_ = mk_stream(k_codes, k_scales)
    vc, vs_ = mk_stream(v_codes, v_scales)
    oc, os_ = mk_stream(o_codes, o_scales)
    return qc, qs_, kc_, ks_, vc, vs_, oc, os_


def rope_host(core):
    b, r = divmod(core, 4)
    pos = np.arange(r * T, (r + 1) * T, dtype=np.float32)
    inv = (1.0 / 10000.0 ** (np.arange(0, 128, 2, dtype=np.float32) / 128.0))
    f = pos[:, None] * inv[None, :]
    cos = np.cos(f).astype(np.float32)
    sin = np.sin(f).astype(np.float32)
    cosq = np.tile(cos, (1, 4)).reshape(8, 128, 256)
    sinq = np.tile(sin, (1, 4)).reshape(8, 128, 256)
    return np.ascontiguousarray(cosq), np.ascontiguousarray(sinq)


def emit_l1b_dequant(K, c, cst_ap, srow_ap, wdq_ap, tag=""):
    s_cd = K.sem("s_dq_c" + tag)
    s_sr = K.sem("s_dq_s" + tag)
    s_pe = K.sem("s_dq_p" + tag)
    s_v = K.sem("s_dq_v" + tag)
    s_o = [K.sem("s_dq_o0" + tag), K.sem("s_dq_o1" + tag)]
    wbi = c.wbuf[:].bitcast(I32)
    cpiece = wbi[:, 1024:2048]
    n_slab = cst_ap.shape[0]
    stt_done = []
    for qq in range(2 * n_slab):
        slab, j = divmod(qq, 2)
        if qq >= 1:
            K.wait("sync", s_v, stt_done[2 * qq - 1])
        K.dma("sync", cpiece, cst_ap[slab][:, j * 1024:(j + 1) * 1024], s_cd)
        for q in range(2):
            p2 = (qq * 2 + q) % 2
            K.wait("sync", s_pe)
            K.dma("sync", c.rs[0:1, 0:512], srow_ap[qq * 2 + q][0:1, :], s_sr)
            K.wait("tensor", s_sr)
            K.wait("tensor", s_v, stt_done[qq * 2 + q - 2] if qq * 2 + q >= 2 else 0)
            K.op("tensor", lambda e, b=c.ps[p2][:, 0:512]: e.matmul(
                b, c.onesr[:], c.rs[0:1, 0:512], start=True, stop=True), inc=s_pe)
            K.wait("vector", s_pe)
            K.wait("vector", s_cd)
            if qq * 2 + q >= 2:
                K.wait("vector", s_o[p2])
            K.op("vector", lambda e, q=q, p2=p2, b=c.ps[p2][:, 0:512]:
                 e.scalar_tensor_tensor(
                     c.wbuf[:, 6144 + p2 * 512: 6144 + (p2 + 1) * 512],
                     cpiece[:, q * 512:(q + 1) * 512], -8.0, b, ALU.add, ALU.mult),
                 inc=s_v)
            stt_done.append(s_v.cnt)
            K.wait("sync", s_v)
            K.dma("sync", wdq_ap[slab][:, j * 1024 + q * 512: j * 1024 + (q + 1) * 512],
                  c.wbuf[:, 6144 + p2 * 512: 6144 + (p2 + 1) * 512], s_o[p2])
    K.wait("sync", s_o[0])
    K.wait("sync", s_o[1])
    return s_o


def emit_l1b_qkv(K, c, wq_dq_ap, wkv_dq_ap, cosq_ap, sinq_ap, q_own_ap, kv_own_ap,
                 ln_sems, tag=""):
    sv, sa, sp = ln_sems
    hwf = c.hw[:].bitcast(F32)
    s_cs = K.sem("s_qb_cs" + tag)
    s_oo = K.sem("s_qb_oo" + tag)
    cosb = c.ob[:, 0:256]
    sinb = c.ob[:, 256:512]
    ob_i = c.ob[:, 512:1024]

    def rope_dest(out_ap, do_rope):
        def dest(K, c, tt, os, psum, s_ev):
            K.wait("vector", s_ev)
            if do_rope(os):
                K.wait("sync", s_ev)
                K.dma("sync", cosb, cosq_ap[tt], s_cs)
                K.dma("sync", sinb, sinq_ap[tt], s_cs)
                pe_ = psum[:, 0:512:2]
                po_ = psum[:, 1:512:2]
                t1 = hwf[:, 0:256]
                t2 = hwf[:, 256:512]
                K.wait("vector", s_cs)
                K.op("vector", lambda e: e.tensor_tensor(t1, pe_, cosb, ALU.mult), inc=s_ev)
                K.op("vector", lambda e: e.tensor_tensor(t2, po_, sinb, ALU.mult), inc=s_ev)
                K.wait("vector", s_ev)
                K.op("vector", lambda e: e.tensor_sub(ob_i[:, 0:512:2], t1, t2), inc=s_ev)
                K.wait("vector", s_ev)
                K.op("vector", lambda e: e.tensor_tensor(t1, po_, cosb, ALU.mult), inc=s_ev)
                K.op("vector", lambda e: e.tensor_tensor(t2, pe_, sinb, ALU.mult), inc=s_ev)
                K.wait("vector", s_ev)
                K.op("vector", lambda e: e.tensor_add(ob_i[:, 1:512:2], t1, t2), inc=s_ev)
            else:
                K.op("vector", lambda e, p=psum: e.tensor_copy(ob_i, p), inc=s_ev)
            pb = psum.bitcast(BF16)
            stg = c.rs[:].bitcast(BF16)
            K.wait("tensor", s_ev)
            for j in range(4):
                K.op("tensor", lambda e, j=j: e.transpose(
                    pb[0:128, j * 128:(j + 1) * 128],
                    ob_i[:, j * 128:(j + 1) * 128], c.identb[:]), inc=s_ev)
            K.wait("vector", s_ev)
            K.wait("vector", s_oo)
            for j in range(4):
                K.op("vector", lambda e, j=j: e.tensor_copy(
                    stg[:, j * 128:(j + 1) * 128], pb[0:128, j * 128:(j + 1) * 128]),
                    inc=s_ev)
            K.wait("sync", s_ev)
            for j in range(4):
                K.dma("sync", out_ap[os * 512 + j * 128:os * 512 + (j + 1) * 128,
                                     tt * 128:(tt + 1) * 128],
                      stg[:, j * 128:(j + 1) * 128], s_oo)
        return dest

    s_ev1, _, _ = emit_proj_tok(K, c, wq_dq_ap, 8, rope_dest(q_own_ap, lambda os: True),
                                name="pj", prequant=True,
                                pe_waits=(sv,), dma_waits=(sv,))
    s_ev2, _, _ = emit_proj_tok(K, c, wkv_dq_ap, 4, rope_dest(kv_own_ap, lambda os: os < 2),
                                name="pj", prequant=True,
                                pe_waits=(s_ev1,), dma_waits=(s_ev1,))
    K.wait("sync", s_oo)
    return s_ev2, s_oo


def l1b_masks(nc):
    m = np.zeros((4, 128, 512), np.float32)
    for j in range(4):
        for kl in range(128):
            m[j, kl, :] = np.where(j * 128 + kl <= np.arange(512), 0.0, -1e30)
    mt = np.zeros((4, 128, 512), np.float32)
    for i in range(4):
        for ql in range(128):
            mt[i, ql, :] = np.where(np.arange(512) <= i * 128 + ql, 0.0, -1e30)
    return (nc.inline_tensor(m.astype(BF16NP), "l1bmasks").ap(),
            nc.inline_tensor(mt.astype(BF16NP), "l1bmaskst").ap())


def emit_l1b_attn(K, c, qloc_ap, kloc_ap, vloc_ap, masks_aps, attn_own_ap, tag=""):
    masks_ap, maskst_ap = masks_aps
    """Head-sharded causal attention, 2 units x 4 qheads. VARIANT B:
    pass1 rowmax (S_tok), pass2 S^T + exp + mask-add + den + attn@v; divide at evac.
    Inputs are feature-major per-core slices: qloc [4,1024,1024] (g, qdim-window,
    tok), kloc/vloc [4,256,1024] (g, unit*128+feat, tok). Output attn_own_b
    [4096, 1024]: rows (qs//2)*1024 + u*512 + qh*128 + feat, cols token-in-block.
    All big buffers alias the x_fm region."""
    xf = c.x_fm
    K_fm = xf[:, 0:4096]
    V_sb = xf[:, 4096:8192]
    q_fm = xf[:, 8192:12288]
    P = [xf[:, 12288:12800], xf[:, 12800:13312]]
    maskb = [xf[:, 13312 + j * 512: 13312 + (j + 1) * 512] for j in range(4)]
    masktb = [xf[:, 16896 + j * 512: 16896 + (j + 1) * 512] for j in range(4)]
    nm_sb = xf[0:1, 15360:15872]
    aout = [xf[:, 15872:16384], xf[:, 16384:16896]]
    hwf = c.hw[:].bitcast(F32)
    s_kv = K.sem("s_ab_kv" + tag)
    s_q = K.sem("s_ab_q" + tag)
    s_pe = K.sem("s_ab_pe" + tag)
    s_v = K.sem("s_ab_v" + tag)
    s_s = K.sem("s_ab_s" + tag)
    s_t = K.sem("s_ab_t" + tag)
    s_m = K.sem("s_ab_m" + tag)
    s_ao = [K.sem("s_ab_ao0" + tag), K.sem("s_ab_ao1" + tag)]
    s_mk = K.sem("s_ab_mk" + tag)
    s_kf = K.sem("s_ab_kf" + tag)
    s_ksc = K.sem("s_ab_ksc" + tag)
    RSQ = float(1.0 / np.sqrt(128.0))
    for j in range(4):
        K.dma("sync", maskb[j], masks_ap[j], s_mk)
        K.dma("sync", masktb[j], maskst_ap[j], s_mk)
    K.wait("vector", s_mk)
    s_vt = K.sem("s_ab_vt" + tag)
    for u in range(2):
        if u > 0:
            K.wait("sync", s_v)
            K.wait("sync", s_pe)
            K.wait("vector", s_pe)
            K.wait("tensor", s_v)
        for g in range(4):
            K.dma("sync", K_fm[:, g * 1024:(g + 1) * 1024],
                  kloc_ap[g][u * 128:(u + 1) * 128, :], s_kf)
            K.dma("sync", q_fm[:, g * 1024:(g + 1) * 1024],
                  vloc_ap[g][u * 128:(u + 1) * 128, :], s_kv)
        K.wait("vector", s_kf)
        for j in range(8):
            K.op("vector", lambda e, j=j: e.tensor_scalar(
                K_fm[:, j * 512:(j + 1) * 512], K_fm[:, j * 512:(j + 1) * 512],
                RSQ, None, ALU.mult), inc=s_ksc)
        K.wait("tensor", s_kv)
        for kc in range(NDC):
            par = kc % 2
            if kc >= 2:
                K.wait("tensor", s_vt)
            K.op("tensor", lambda e, kc=kc, par=par: e.transpose(
                c.ps[6 + par][:].bitcast(BF16)[0:128, 0:128],
                q_fm[:, kc * 128:(kc + 1) * 128], c.identb[:]), inc=s_pe)
            K.wait("vector", s_pe)
            K.op("vector", lambda e, kc=kc, par=par: e.tensor_copy(
                V_sb[:, kc * 128:(kc + 1) * 128],
                c.ps[6 + par][:].bitcast(BF16)[0:128, 0:128]), inc=s_vt)
        K.wait("tensor", s_vt)
        K.wait("sync", s_vt)
        for qh in range(4):
            if u > 0 or qh > 0:
                K.wait("sync", s_pe)
            for g in range(4):
                K.dma("sync", q_fm[:, g * 1024:(g + 1) * 1024],
                      qloc_ap[g][u * 512 + qh * 128:u * 512 + (qh + 1) * 128, :], s_q)
            K.wait("tensor", s_q)
            K.wait("tensor", s_kv)
            K.wait("tensor", s_ksc)
            K.wait("tensor", s_mk)
            for qs in range(8):
                K.wait("tensor", s_v)
                rd_done = {}
                mcol = lambda i: c.rs[:, 200 + i:201 + i]
                tmp = lambda i: c.rs[:, 204 + i:205 + i]
                for ck in range(qs + 1):
                    if ck >= 1:
                        K.wait("tensor", s_m, rd_done[ck - 1])
                    for i in range(4):
                        K.op("tensor", lambda e, i=i, ck=ck, qs=qs: e.matmul(
                            c.ps[i][:, 0:512],
                            q_fm[:, (qs * 4 + i) * 128:(qs * 4 + i + 1) * 128],
                            K_fm[:, ck * 512:(ck + 1) * 512], start=True, stop=True),
                            inc=(s_pe if i == 3 else None))
                    K.wait("vector", s_pe)
                    if ck == qs:
                        for half in range(2):
                            if half == 1:
                                K.wait("vector", s_m)
                            for i2 in range(2):
                                i = half * 2 + i2
                                K.op("vector", lambda e, i=i, i2=i2: e.tensor_tensor(
                                    c.ps[4 + i2][:, 0:512], c.ps[i][:, 0:512],
                                    masktb[i], ALU.add), inc=s_m)
                            K.wait("vector", s_m)
                            for i2 in range(2):
                                i = half * 2 + i2
                                d = mcol(i) if ck == 0 else tmp(i)
                                K.op("vector", lambda e, d=d, i2=i2: e.reduce_max(
                                    d, c.ps[4 + i2][:, 0:512], axis=AX.X), inc=s_m)
                    else:
                        for i in range(4):
                            d = mcol(i) if ck == 0 else tmp(i)
                            K.op("vector", lambda e, d=d, i=i: e.reduce_max(
                                d, c.ps[i][:, 0:512], axis=AX.X), inc=s_m)
                    if ck > 0:
                        K.wait("vector", s_m)
                        for i in range(4):
                            K.op("vector", lambda e, m=mcol(i), t=tmp(i):
                                 e.tensor_max(m, m, t), inc=s_m)
                    rd_done[ck] = s_m.cnt
                K.wait("vector", s_m)
                nmb = c.rs[:, 208:210].bitcast(BF16)
                K.op("vector", lambda e, n=nmb: e.tensor_scalar(
                    n[:, 0:4], c.rs[:, 200:204], -1.0, None, ALU.mult), inc=s_m)
                K.wait("vector", s_m)
                K.wait("tensor", s_m)
                nm_done = {}
                for i in range(4):
                    if i >= 2:
                        K.wait("tensor", s_m, nm_done[i - 2])
                    K.op("tensor", lambda e, i=i, n=nmb: e.transpose(
                        c.ps[6 + i % 2][:].bitcast(BF16)[0:1, 0:128],
                        n[:, i:i + 1], c.identb[:]), inc=s_pe)
                    K.wait("vector", s_pe)
                    K.op("vector", lambda e, i=i: e.tensor_copy(
                        nm_sb[0:1, i * 128:(i + 1) * 128],
                        c.ps[6 + i % 2][:].bitcast(BF16)[0:1, 0:128]), inc=s_m)
                    nm_done[i] = s_m.cnt
                K.wait("tensor", s_m)
                sread = {}
                tdone = {}
                nkt = 4 * (qs + 1)

                def denpv(kt, nkt=nkt):
                    p2 = kt % 2
                    K.wait("tensor", s_s, sread[kt])
                    K.op("tensor", lambda e, p2=p2, st=(kt == 0), sp=(kt == nkt - 1):
                         e.matmul(c.ps[4][0:1, 0:512], c.onescol[:], P[p2],
                                  start=st, stop=sp), inc=None)
                    K.op("tensor", lambda e, kt=kt, p2=p2, st=(kt == 0),
                         sp=(kt == nkt - 1):
                         e.matmul(c.ps[5][:, 0:512], V_sb[:, kt * 128:(kt + 1) * 128],
                                  P[p2], start=st, stop=sp), inc=s_t)
                    tdone[kt] = s_t.cnt

                for kt in range(nkt):
                    par = kt % 4
                    p2 = kt % 2
                    if kt >= 4:
                        K.wait("tensor", s_s, sread[kt - 4])
                    K.op("tensor", lambda e, kt=kt, par=par, qs=qs: e.matmul(
                        c.ps[par][:, 0:512], K_fm[:, kt * 128:(kt + 1) * 128],
                        q_fm[:, qs * 512:(qs + 1) * 512], start=True, stop=True), inc=None)
                    K.op("tensor", lambda e, par=par: e.matmul(
                        c.ps[par][:, 0:512], c.onesrb[:], nm_sb,
                        start=False, stop=True, skip_group_check=True), inc=s_pe)
                    mmv = s_pe.cnt
                    if kt >= 1:
                        denpv(kt - 1)
                    if kt >= 2:
                        K.wait("scalar", s_t, tdone[kt - 2])
                    if kt >= 4 * qs:
                        j = kt - 4 * qs
                        K.wait("vector", s_pe, mmv)
                        K.wait("vector", s_s)
                        K.op("vector", lambda e, par=par, j=j: e.tensor_tensor(
                            hwf[:, 0:512], c.ps[par][:, 0:512], maskb[j], ALU.add),
                            inc=s_s)
                        K.wait("scalar", s_s)
                        K.op("scalar", lambda e, p2=p2: e.activation(
                            P[p2], hwf[:, 0:512], AF.Exp), inc=s_s)
                    else:
                        K.wait("scalar", s_pe, mmv)
                        K.op("scalar", lambda e, par=par, p2=p2: e.activation(
                            P[p2], c.ps[par][:, 0:512], AF.Exp), inc=s_s)
                    sread[kt] = s_s.cnt
                denpv(nkt - 1)
                K.wait("vector", s_t)
                K.op("vector", lambda e: e.reciprocal(c.rs[0:1, 216:728] if False else hwf[0:1, 0:512], c.ps[4][0:1, 0:512]), inc=s_v)
                K.wait("vector", s_v)
                K.op("vector", lambda e: e.engine_nop(), inc=s_v)
                K.wait("tensor", s_v)
                K.op("tensor", lambda e: e.matmul(c.ps[6][:, 0:512], c.onesr[:],
                     hwf[0:1, 0:512], start=True, stop=True), inc=s_pe)
                K.wait("vector", s_pe)
                K.op("vector", lambda e: e.tensor_copy(hwf[:, 0:512] if False else c.hw[:, 512:1024].bitcast(F32) if False else hwf[:, 0:512], c.ps[5][:, 0:512]), inc=s_v)
                K.wait("vector", s_v)
                apar = qs % 2
                idx = ((u * 4 + qh) * 8 + qs)
                if idx >= 2:
                    K.wait("vector", s_ao[apar])
                K.op("vector", lambda e, apar=apar: e.tensor_tensor(
                    aout[apar], hwf[:, 0:512], c.ps[6][:, 0:512], ALU.mult), inc=s_v)
                K.wait("sync", s_v)
                r0 = (qs // 2) * 1024 + u * 512 + qh * 128
                c0 = (qs % 2) * 512
                K.dma("sync", attn_own_ap[r0:r0 + 128, c0:c0 + 512],
                      aout[apar], s_ao[apar])
    K.wait("sync", s_ao[0])
    K.wait("sync", s_ao[1])
    K.wait("vector", s_ao[0], s_ao[0].cnt)
    K.wait("vector", s_ao[1], s_ao[1].cnt)
    return s_ao



def _pre(nc, K, x_in):
    c = alloc_common(nc, K)
    s_x = emit_load_x(K, c, x_in)
    for e in ("vector", "scalar", "tensor"):
        K.wait(e, s_x)
    ln = emit_ln_xfm(K, c)
    return c, ln


def _build_l0a_A(nc):
    K = MK(nc)
    x_in = nc.dram_tensor("x_in", [T, D], F32, kind="ExternalInput").ap()
    wkvst = nc.dram_tensor("wkvst", [32, 128, 2048], F32, kind="ExternalInput").ap()
    kvbias = nc.dram_tensor("kvbias", [1, 2048], F32, kind="ExternalInput").ap()
    kvout = nc.dram_tensor("kvout", [128, 1032], F32, kind="ExternalOutput").ap()
    kv_dram = nc.dram_tensor("kv_dram", [8, 128, 2048], BF16, kind="Internal").ap()
    c, ln = _pre(nc, K, x_in)
    s_ev, s_kvo = emit_l0a_kvproj(K, c, wkvst, kvbias, kv_dram, ln)
    emit_l0a_kvstate(K, c, kv_dram, s_kvo, kvout)
    return K.build()


def _build_l0a_B(nc):
    K = MK(nc)
    x_in = nc.dram_tensor("x_in", [T, D], F32, kind="ExternalInput").ap()
    kvparts = nc.dram_tensor("kvparts", [4, 128, 1032], F32, kind="ExternalInput").ap()
    wqst = nc.dram_tensor("wqst", [64, 128, 2048], F32, kind="ExternalInput").ap()
    bqa = nc.dram_tensor("bqa", [32, 1, 128], F32, kind="ExternalInput").ap()
    wost = nc.dram_tensor("wost", [64, 128, 2048], F32, kind="ExternalInput").ap()
    bo = nc.dram_tensor("bo", [1, 4096], F32, kind="ExternalInput").ap()
    x_out = nc.dram_tensor("x_out", [T, D], F32, kind="ExternalOutput").ap()
    attn_dram = nc.dram_tensor("attn_dram", [128, 32768], BF16, kind="Internal").ap()
    c, ln = _pre(nc, K, x_in)
    sv, sa, sp = ln
    K.wait("sync", sp)
    s_kv = emit_l0a_kvreduce_in(K, c, kvparts)
    s_ao = emit_l0a_attn(K, c, wqst, bqa, attn_dram, s_kv, ln)
    s_ev = emit_l0a_oproj(K, c, attn_dram, wost, bo, s_ao)
    K.wait("sync", s_ev)
    emit_store_x(K, c, x_out)
    return K.build()


def _build_l0ffn(nc):
    K = MK(nc)
    x_in = nc.dram_tensor("x_in", [T, D], F32, kind="ExternalInput").ap()
    wg9 = nc.dram_tensor("wg9", [128, 288], F32, kind="ExternalInput").ap()
    ast = nc.dram_tensor("ast", [32, 128, 128], F32, kind="ExternalInput").ap()
    wbst = nc.dram_tensor("wbst", [136, 128, 1024], F32, kind="ExternalInput").ap()
    bias = nc.dram_tensor("bias", [1, 4096], F32, kind="ExternalInput").ap()
    bthr = nc.dram_tensor("bthr", [1, 1], F32, kind="ExternalInput").ap()
    x_out = nc.dram_tensor("x_out", [T, D], F32, kind="ExternalOutput").ap()
    c = alloc_common(nc, K)
    e8_ap = l0ffn_consts(nc)
    s_x = emit_load_x(K, c, x_in)
    for e in ("vector", "scalar", "tensor"):
        K.wait(e, s_x)
    ln = emit_ln_xfm(K, c)
    s_ev = emit_l0ffn(K, c, wg9, ast, wbst, bias, bthr, e8_ap, ln)
    K.wait("sync", s_ev)
    emit_store_x(K, c, x_out)
    return K.build()


def _build_l0ab(nc):
    K = MK(nc)
    x_in = nc.dram_tensor("x_in", [T, D], F32, kind="ExternalInput").ap()
    maskT = nc.dram_tensor("maskT", [D, D], I32, kind="ExternalInput").ap()
    wst = nc.dram_tensor("wst", [64, 128, 2048], F32, kind="ExternalInput").ap()
    x_out = nc.dram_tensor("x_out", [T, D], F32, kind="ExternalOutput").ap()
    wm = nc.dram_tensor("wm", [64, 128, 2048], BF16, kind="Internal").ap()
    c = alloc_common(nc, K)
    sel4_ap, ind_ap = l0ab_consts(nc)
    s_x = emit_load_x(K, c, x_in)
    for e in ("vector", "scalar", "tensor"):
        K.wait(e, s_x)
    ln_sems = emit_ln_xfm(K, c)
    s_ev = emit_l0ab(K, c, maskT, wst, wm, sel4_ap, ind_ap, ln_sems)
    K.wait("sync", s_ev)
    emit_store_x(K, c, x_out)
    return K.build()


def _build_l1a_A(nc):
    K = MK(nc)
    x_in = nc.dram_tensor("x_in", [T, D], F32, kind="ExternalInput").ap()
    wc16 = nc.dram_tensor("wc16", [128, 512], F32, kind="ExternalInput").ap()
    cbias = nc.dram_tensor("cbias", [1, 16], F32, kind="ExternalInput").ap()
    ckv_own = nc.dram_tensor("ckv_own", [T, 4], BF16, kind="ExternalOutput").ap()
    c, ln = _pre(nc, K, x_in)
    emit_l1a_cproj(K, c, wc16, cbias, ckv_own, ln, want_cq=False)
    return K.build()


def _build_l1a_B(nc):
    K = MK(nc)
    x_in = nc.dram_tensor("x_in", [T, D], F32, kind="ExternalInput").ap()
    ckv_full = nc.dram_tensor("ckv_full", [4096, 4], BF16, kind="ExternalInput").ap()
    wc16 = nc.dram_tensor("wc16", [128, 512], F32, kind="ExternalInput").ap()
    cbias = nc.dram_tensor("cbias", [1, 16], F32, kind="ExternalInput").ap()
    wqk = nc.dram_tensor("wqk", [12, 128], F32, kind="ExternalInput").ap()
    bqk4 = nc.dram_tensor("bqk4", [4, 32], F32, kind="ExternalInput").ap()
    wvu = nc.dram_tensor("wvu", [4, 4096], F32, kind="ExternalInput").ap()
    bvu = nc.dram_tensor("bvu", [32, 128, 1], F32, kind="ExternalInput").ap()
    wost = nc.dram_tensor("wost", [64, 128, 2048], F32, kind="ExternalInput").ap()
    bo = nc.dram_tensor("bo", [1, 4096], F32, kind="ExternalInput").ap()
    x_out = nc.dram_tensor("x_out", [T, D], F32, kind="ExternalOutput").ap()
    ckv_own2 = nc.dram_tensor("ckv_own2", [T, 4], BF16, kind="Internal").ap()
    attn_dram = nc.dram_tensor("attn_dram", [128, 32768], BF16, kind="Internal").ap()
    c, ln = _pre(nc, K, x_in)
    s_v, s_o = emit_l1a_cproj(K, c, wc16, cbias, ckv_own2, ln, want_cq=True)
    s_ao = emit_l1a_attn(K, c, ckv_full, wqk, bqk4, wvu, bvu, attn_dram, s_o)
    s_ev = emit_l0a_oproj(K, c, attn_dram, wost, bo, s_ao)
    K.wait("sync", s_ev)
    emit_store_x(K, c, x_out)
    return K.build()


def _build_l1ffn_A(nc):
    K = MK(nc)
    x_in = nc.dram_tensor("x_in", [T, D], F32, kind="ExternalInput").ap()
    rw8 = nc.dram_tensor("rw8", [128, 256], F32, kind="ExternalInput").ap()
    selb = nc.dram_tensor("selb", [128, 8], BF16, kind="ExternalInput").ap()
    rpart = nc.dram_tensor("rpart", [1, 8], F32, kind="ExternalOutput").ap()
    c, ln = _pre(nc, K, x_in)
    emit_l1ffn_router_partial(K, c, rw8, selb, rpart, ln)
    return K.build()


def _build_l1ffn_B(nc):
    K = MK(nc)
    x_in = nc.dram_tensor("x_in", [T, D], F32, kind="ExternalInput").ap()
    rparts = nc.dram_tensor("rparts", [4, 1, 8], F32, kind="ExternalInput").ap()
    rb = nc.dram_tensor("rb", [1, 8], F32, kind="ExternalInput").ap()
    ast = nc.dram_tensor("ast", [32, 128, 128], F32, kind="ExternalInput").ap()
    wbst = nc.dram_tensor("wbst", [136, 128, 1024], F32, kind="ExternalInput").ap()
    bias = nc.dram_tensor("bias", [1, 4096], F32, kind="ExternalInput").ap()
    x_out = nc.dram_tensor("x_out", [T, D], F32, kind="ExternalOutput").ap()
    c = alloc_common(nc, K)
    e8_ap = l0ffn_consts(nc)
    s_x = emit_load_x(K, c, x_in)
    for e in ("vector", "scalar", "tensor"):
        K.wait(e, s_x)
    ln = emit_ln_xfm(K, c)
    s_route = emit_l1ffn_routing(K, c, rparts, rb, e8_ap)
    s_ev = emit_l1ffn_main(K, c, ast, wbst, bias, s_route, ln)
    K.wait("sync", s_ev)
    emit_store_x(K, c, x_out)
    return K.build()


def _build_l1b_A(nc):
    K = MK(nc)
    x_in = nc.dram_tensor("x_in", [T, D], F32, kind="ExternalInput").ap()
    qcst = nc.dram_tensor("qcst", [64, 128, 2048], I32, kind="ExternalInput").ap()
    qsrow = nc.dram_tensor("qsrow", [256, 1, 512], F32, kind="ExternalInput").ap()
    kvcst = nc.dram_tensor("kvcst", [32, 128, 2048], I32, kind="ExternalInput").ap()
    kvsrow = nc.dram_tensor("kvsrow", [128, 1, 512], F32, kind="ExternalInput").ap()
    cosq = nc.dram_tensor("cosq", [8, 128, 256], BF16, kind="ExternalInput").ap()
    sinq = nc.dram_tensor("sinq", [8, 128, 256], BF16, kind="ExternalInput").ap()
    q_own = nc.dram_tensor("q_own", [T, 4096], BF16, kind="ExternalOutput").ap()
    kv_own = nc.dram_tensor("kv_own", [T, 2048], BF16, kind="ExternalOutput").ap()
    wq_dq = nc.dram_tensor("wq_dq", [64, 128, 2048], BF16, kind="Internal").ap()
    wkv_dq = nc.dram_tensor("wkv_dq", [32, 128, 2048], BF16, kind="Internal").ap()
    c, ln = _pre(nc, K, x_in)
    sv, sa, sp = ln
    K.wait("sync", sp)
    K.wait("vector", sp)
    s_o1 = emit_l1b_dequant(K, c, qcst, qsrow, wq_dq, tag="dq")
    s_o2 = emit_l1b_dequant(K, c, kvcst, kvsrow, wkv_dq, tag="dq")
    for w in s_o1 + s_o2:
        K.wait("sync", w)
        K.wait("vector", w)
    emit_l1b_qkv(K, c, wq_dq, wkv_dq, cosq, sinq, q_own, kv_own, ln)
    return K.build()


def _build_l1b_B(nc):
    K = MK(nc)
    qu = nc.dram_tensor("qu", [2, 4096, 512], BF16, kind="ExternalInput").ap()
    ku = nc.dram_tensor("ku", [2, 4096, 128], BF16, kind="ExternalInput").ap()
    vu = nc.dram_tensor("vu", [2, 4096, 128], BF16, kind="ExternalInput").ap()
    attn_own = nc.dram_tensor("attn_own", [1024, 4096], BF16, kind="ExternalOutput").ap()
    c = alloc_common(nc, K)
    masks_aps = l1b_masks(nc)
    K.wait("sync", c.s_const)
    K.wait("vector", c.s_const)
    K.wait("tensor", c.s_const)
    emit_l1b_attn(K, c, qu, ku, vu, masks_aps, attn_own)
    return K.build()


def _build_l1b_C(nc):
    K = MK(nc)
    x_in = nc.dram_tensor("x_in", [T, D], F32, kind="ExternalInput").ap()
    ocst = nc.dram_tensor("ocst", [64, 128, 2048], I32, kind="ExternalInput").ap()
    osrow = nc.dram_tensor("osrow", [256, 1, 512], F32, kind="ExternalInput").ap()
    attn_fm = nc.dram_tensor("attn_fm", [128, 32768], BF16, kind="ExternalInput").ap()
    x_out = nc.dram_tensor("x_out", [T, D], F32, kind="ExternalOutput").ap()
    wo_dq = nc.dram_tensor("wo_dq", [64, 128, 2048], BF16, kind="Internal").ap()
    c = alloc_common(nc, K)
    s_x = emit_load_x(K, c, x_in)
    s_o1 = emit_l1b_dequant(K, c, ocst, osrow, wo_dq, tag="dq")
    for w in s_o1:
        K.wait("sync", w)
        K.wait("vector", w)
    s_ev = emit_l0a_oproj(K, c, attn_fm, wo_dq, None, [], prequant=True, name="l1bo")
    K.wait("sync", s_ev)
    emit_store_x(K, c, x_out)
    return K.build()


def _bar(K, sems, engines=("sync", "scalar", "vector", "tensor", "gpsimd")):
    for e in engines:
        for s in sems:
            K.wait(e, s)


def _build_fused(nc):
    K = MK(nc)
    EI, INT = "ExternalInput", "Internal"
    dt = nc.dram_tensor
    x_in = dt("x_in", [T, D], F32, kind=EI).ap()
    x_out = dt("x_out", [T, D], F32, kind="ExternalOutput").ap()
    wkvst = dt("wkvst", [32, 128, 2048], F32, kind=EI).ap()
    kvbias = dt("kvbias", [1, 2048], F32, kind=EI).ap()
    wqst = dt("wqst", [64, 128, 2048], F32, kind=EI).ap()
    bqa = dt("bqa", [32, 1, 128], F32, kind=EI).ap()
    wost0 = dt("wost0", [64, 128, 2048], F32, kind=EI).ap()
    bo0 = dt("bo0", [1, 4096], F32, kind=EI).ap()
    wg9 = dt("wg9", [128, 288], F32, kind=EI).ap()
    ast0 = dt("ast0", [32, 128, 128], F32, kind=EI).ap()
    wbst0 = dt("wbst0", [136, 128, 1024], F32, kind=EI).ap()
    bias0 = dt("bias0", [1, 4096], F32, kind=EI).ap()
    bthr0 = dt("bthr0", [1, 1], F32, kind=EI).ap()
    maskT = dt("maskT", [D, D], I32, kind=EI).ap()
    wst_ab = dt("wst_ab", [64, 128, 2048], F32, kind=EI).ap()
    wc16 = dt("wc16", [128, 512], F32, kind=EI).ap()
    cbias = dt("cbias", [1, 16], F32, kind=EI).ap()
    wqk = dt("wqk", [12, 128], F32, kind=EI).ap()
    bqk4 = dt("bqk4", [4, 32], F32, kind=EI).ap()
    wvu = dt("wvu", [4, 4096], F32, kind=EI).ap()
    bvu = dt("bvu", [32, 128, 1], F32, kind=EI).ap()
    wost1 = dt("wost1", [64, 128, 2048], F32, kind=EI).ap()
    bo1 = dt("bo1", [1, 4096], F32, kind=EI).ap()
    rw8 = dt("rw8", [128, 256], F32, kind=EI).ap()
    selb = dt("selb", [128, 8], BF16, kind=EI).ap()
    rb = dt("rb", [1, 8], F32, kind=EI).ap()
    ast1 = dt("ast1", [32, 128, 128], F32, kind=EI).ap()
    wbst1 = dt("wbst1", [136, 128, 1024], F32, kind=EI).ap()
    bias1 = dt("bias1", [1, 4096], F32, kind=EI).ap()
    qcst = dt("qcst", [64, 128, 2048], I32, kind=EI).ap()
    qsrow = dt("qsrow", [256, 1, 512], F32, kind=EI).ap()
    kvcst = dt("kvcst", [32, 128, 2048], I32, kind=EI).ap()
    kvsrow = dt("kvsrow", [128, 1, 512], F32, kind=EI).ap()
    ocst = dt("ocst", [64, 128, 2048], I32, kind=EI).ap()
    osrow = dt("osrow", [256, 1, 512], F32, kind=EI).ap()
    cosq = dt("cosq", [8, 128, 256], BF16, kind=EI).ap()
    sinq = dt("sinq", [8, 128, 256], BF16, kind=EI).ap()
    kv_dram = dt("kv_dram", [8, 128, 2048], BF16, kind=INT).ap()
    kvout_own = dt("kvout_own", [128, 1032], F32, kind=INT).ap()
    kvg = dt("kvg", [512, 1032], F32, kind=INT).ap()
    attn_dram = dt("attn_dram", [128, 32768], BF16, kind=INT).ap()
    ckv_own = dt("ckv_own", [T, 4], BF16, kind=INT).ap()
    ckvg = dt("ckvg", [4096, 4], BF16, kind=INT).ap()
    rpart1 = dt("rpart1", [1, 8], F32, kind=INT).ap()
    rg1 = dt("rg1", [4, 8], F32, kind=INT).ap()
    rpart2 = dt("rpart2", [1, 8], F32, kind=INT).ap()
    rg2 = dt("rg2", [4, 8], F32, kind=INT).ap()
    wq_dq = dt("wq_dq", [64, 128, 2048], BF16, kind=INT).ap()
    wkv_dq = dt("wkv_dq", [32, 128, 2048], BF16, kind=INT).ap()
    wo_dq = dt("wo_dq", [64, 128, 2048], BF16, kind=INT).ap()
    q_own = dt("q_own", [4096, 1024], BF16, kind=INT).ap()
    kv_own = dt("kv_own", [2048, 1024], BF16, kind=INT).ap()
    qg = dt("qg", [16384, 1024], BF16, kind=INT).ap()
    kvg2 = dt("kvg2", [8192, 1024], BF16, kind=INT).ap()
    qloc = dt("qloc", [4, 1024, 1024], BF16, kind=INT).ap()
    kloc = dt("kloc", [4, 256, 1024], BF16, kind=INT).ap()
    vloc = dt("vloc", [4, 256, 1024], BF16, kind=INT).ap()
    attn_own = dt("attn_own", [4096, 1024], BF16, kind=INT).ap()
    ag = dt("ag", [16384, 1024], BF16, kind=INT).ap()
    aloc = dt("aloc", [4, 1024, 1024], BF16, kind=INT).ap()

    c = alloc_common(nc, K)
    e8_ap = l0ffn_consts(nc)
    sel4_ap, ind_ap = l0ab_consts(nc)
    masks_aps = l1b_masks(nc)
    s_x = emit_load_x(K, c, x_in)
    for e in ("vector", "scalar", "tensor"):
        K.wait(e, s_x)

    K.scope("l0a")
    ln = emit_ln_xfm(K, c, tag="")
    s_ev, s_kvo = emit_l0a_kvproj(K, c, wkvst, kvbias, kv_dram, ln)
    s_ko = emit_l0a_kvstate(K, c, kv_dram, s_kvo, kvout_own)
    s_ag1 = emit_allgather(K, kvout_own, kvg, "ag", wait_sems=(s_ko,))
    _bar(K, [s_ev, s_ko])
    K.wait("sync", s_ag1)
    K.wait("vector", s_ag1)
    kvparts = [kvg[r * 128:(r + 1) * 128, :] for r in range(4)]
    s_kv = emit_l0a_kvreduce_in(K, c, kvparts)
    s_ao = emit_l0a_attn(K, c, wqst, bqa, attn_dram, s_kv, ln)
    s_ev0 = emit_l0a_oproj(K, c, attn_dram, wost0, bo0, s_ao, name="pj")
    _bar(K, [s_ev0])

    K.scope("l0f1")
    ln = emit_ln_xfm(K, c, tag="")
    s_ev1 = emit_l0ffn(K, c, wg9, ast0, wbst0, bias0, bthr0, e8_ap, ln, tag="")
    _bar(K, [s_ev1])

    K.scope("l0ab")
    ln = emit_ln_xfm(K, c, tag="")
    wm = dt("wm", [64, 128, 2048], BF16, kind=INT).ap()
    s_ev2 = emit_l0ab(K, c, maskT, wst_ab, wm, sel4_ap, ind_ap, ln)
    _bar(K, [s_ev2])

    K.scope("l0f2")
    ln = emit_ln_xfm(K, c, tag="")
    s_ev3 = emit_l0ffn(K, c, wg9, ast0, wbst0, bias0, bthr0, e8_ap, ln, tag="")
    _bar(K, [s_ev3])

    K.scope("l1a")
    ln = emit_ln_xfm(K, c, tag="")
    s_v, s_o = emit_l1a_cproj(K, c, wc16, cbias, ckv_own, ln, want_cq=True)
    s_ag2 = emit_allgather(K, ckv_own, ckvg, "ag", wait_sems=(s_o,))
    K.wait("sync", s_ag2)
    s_ao = emit_l1a_attn(K, c, ckvg, wqk, bqk4, wvu, bvu, attn_dram, s_o)
    s_ev4 = emit_l0a_oproj(K, c, attn_dram, wost1, bo1, s_ao, name="pj")
    _bar(K, [s_ev4])

    K.scope("l1f1")
    ln = emit_ln_xfm(K, c, tag="")
    s_rp = emit_l1ffn_router_partial(K, c, rw8, selb, rpart1, ln, tag="")
    s_ag3 = emit_allgather(K, rpart1, rg1, "ag", wait_sems=(s_rp,))
    K.wait("sync", s_ag3)
    rg1l = [rg1[r:r + 1, :] for r in range(4)]
    s_route = emit_l1ffn_routing(K, c, rg1l, rb, e8_ap, tag="")
    s_ev5 = emit_l1ffn_main(K, c, ast1, wbst1, bias1, s_route, ln, tag="")
    _bar(K, [s_ev5])

    K.scope("l1b")
    ln = emit_ln_xfm(K, c, tag="")
    sv6, sa6, sp6 = ln
    K.wait("sync", sp6)
    K.wait("vector", sp6)
    s_o1 = emit_l1b_dequant(K, c, qcst, qsrow, wq_dq, tag="dq")
    s_o2 = emit_l1b_dequant(K, c, kvcst, kvsrow, wkv_dq, tag="dq")
    for w in s_o1 + s_o2:
        K.wait("sync", w)
        K.wait("vector", w)
    s_ev6, s_oo = emit_l1b_qkv(K, c, wq_dq, wkv_dq, cosq, sinq, q_own, kv_own, ln)
    s_ag4 = emit_allgather(K, q_own, qg, "ag", wait_sems=(s_oo, s_ev6))
    s_ag5 = emit_allgather(K, kv_own, kvg2, "ag")
    _bar(K, [s_ev6, s_oo], engines=("sync", "vector", "scalar", "tensor"))
    s_o3 = emit_l1b_dequant(K, c, ocst, osrow, wo_dq, tag="dq")
    K.wait("gpsimd", s_ag4)
    K.wait("gpsimd", s_ag5)
    s_sl = K.sem("s_slice")
    pid = nc.partition_id()
    off = pid % 4
    for g in range(4):
        K.dma("gpsimd", qloc[g], qg[ds(g * 4096 + off * 1024, 1024), 0:1024], s_sl)
        K.dma("gpsimd", kloc[g], kvg2[ds(g * 2048 + off * 256, 256), 0:1024], s_sl)
        K.dma("gpsimd", vloc[g], kvg2[ds(g * 2048 + 1024 + off * 256, 256), 0:1024], s_sl)
    for w in s_o3:
        K.wait("sync", w)
        K.wait("vector", w)
        K.wait("tensor", w)
    K.wait("sync", s_sl)
    K.wait("tensor", s_sl)
    s_ao2 = emit_l1b_attn(K, c, qloc, kloc, vloc, masks_aps, attn_own)
    s_ag6 = emit_allgather(K, attn_own, ag, "ag", wait_sems=list(s_ao2))
    K.wait("gpsimd", s_ag6)
    for g in range(4):
        K.dma("gpsimd", aloc[g], ag[ds(g * 4096 + off * 1024, 1024), 0:1024], s_sl)
    K.wait("sync", s_sl)
    attn_list = [aloc[dc // 8][(dc % 8) * 128:(dc % 8) * 128 + 128, :]
                 for dc in range(NDC)]
    s_ev7 = emit_l0a_oproj(K, c, None, wo_dq, None, [], prequant=True,
                           name="pj", attn_list=attn_list)
    _bar(K, [s_ev7])

    K.scope("l1f2")
    ln = emit_ln_xfm(K, c, tag="")
    s_rp2 = emit_l1ffn_router_partial(K, c, rw8, selb, rpart2, ln, tag="")
    s_ag7 = emit_allgather(K, rpart2, rg2, "ag", wait_sems=(s_rp2,))
    K.wait("sync", s_ag7)
    rg2l = [rg2[r:r + 1, :] for r in range(4)]
    s_route2 = emit_l1ffn_routing(K, c, rg2l, rb, e8_ap, tag="")
    s_ev8 = emit_l1ffn_main(K, c, ast1, wbst1, bias1, s_route2, ln, tag="")
    K.wait("sync", s_ev8)
    emit_store_x(K, c, x_out)
    return K.build()


def kernel(**inputs):
    inp = {k: np.asarray(v) for k, v in inputs.items()}
    C = list(range(8))
    x_full = np.ascontiguousarray(inp["L0_attn_a_hidden_states"].astype(np.float32))
    shards = shard_tokens(x_full)

    wkvst, kvbias, wqst, bqa, wost0, bo0 = l0a_host(
        inp["L0_attn_a_Wq"], inp["L0_attn_a_bq"], inp["L0_attn_a_Wk"],
        inp["L0_attn_a_bk"], inp["L0_attn_a_Wv"], inp["L0_attn_a_bv"],
        inp["L0_attn_a_Wo"], inp["L0_attn_a_bo"])
    wg9, ast0, wbst0 = l0ffn_host(
        inp["L0_ffn_W_base"], inp["L0_ffn_W_gate"], inp["L0_ffn_W_thr"],
        inp["L0_ffn_A_w"], inp["L0_ffn_B_w"])
    bias0 = np.ascontiguousarray(inp["L0_ffn_b_base"][None, :]).astype(np.float32)
    bthr0 = np.ascontiguousarray(inp["L0_ffn_b_thr"][None, :]).astype(np.float32)
    wst_ab = w_stream(np.ascontiguousarray(
        inp["L0_attn_b_weight"].astype(np.float32).T), dcg=4)
    maskT = np.ascontiguousarray(inp["L0_attn_b_mask"].astype(np.int32).T)
    wc16, cbias, wqk, bqk4, wvu, bvu, wost1, bo1 = l1a_host(
        inp["L1_attn_a_Wq_d"], inp["L1_attn_a_bq_d"], inp["L1_attn_a_W_qk"],
        inp["L1_attn_a_b_qk"], inp["L1_attn_a_Wkv_d"], inp["L1_attn_a_bkv_d"],
        inp["L1_attn_a_Wv_u"], inp["L1_attn_a_bv_u"], inp["L1_attn_a_Wo"],
        inp["L1_attn_a_bo"])
    rw8, ast1, wbst1 = l1ffn_host(
        inp["L1_ffn_W"], inp["L1_ffn_A"], inp["L1_ffn_B"], inp["L1_ffn_router_W"])
    bias1 = np.ascontiguousarray(inp["L1_ffn_b"][None, :]).astype(np.float32)
    rbr = np.ascontiguousarray(inp["L1_ffn_router_b"][None, :]).astype(np.float32)
    kvc = np.concatenate([inp["L1_attn_b_k_codes"], inp["L1_attn_b_v_codes"]],
                         axis=1).astype(np.int32)
    kvs = np.concatenate([inp["L1_attn_b_k_scales"], inp["L1_attn_b_v_scales"]],
                         axis=1).astype(np.float32)
    qcst, qsrow = l1b_host_one(inp["L1_attn_b_q_codes"].astype(np.int32),
                               inp["L1_attn_b_q_scales"].astype(np.float32))
    kvcst, kvsrow = l1b_host_one(kvc, kvs)
    ocst, osrow = l1b_host_one(inp["L1_attn_b_o_codes"].astype(np.int32),
                               inp["L1_attn_b_o_scales"].astype(np.float32))
    ropes = [rope_host(cc) for cc in C]

    nc = make_nc("fused")
    _build_fused(nc)
    in_maps = []
    for cc in C:
        in_maps.append({
            "x_in": shards[cc],
            "wkvst": wkvst, "kvbias": kvbias, "wqst": wqst, "bqa": bqa,
            "wost0": wost0, "bo0": bo0,
            "wg9": wg9, "ast0": ast0, "wbst0": wbst0, "bias0": bias0,
            "bthr0": bthr0,
            "maskT": maskT, "wst_ab": wst_ab,
            "wc16": wc16, "cbias": cbias, "wqk": wqk, "bqk4": bqk4,
            "wvu": wvu, "bvu": bvu, "wost1": wost1, "bo1": bo1,
            "rw8": rw8, "selb": sel_host(cc), "rb": rbr,
            "ast1": ast1, "wbst1": wbst1, "bias1": bias1,
            "qcst": qcst, "qsrow": qsrow, "kvcst": kvcst, "kvsrow": kvsrow,
            "ocst": ocst, "osrow": osrow,
            "cosq": ropes[cc][0].astype(BF16NP),
            "sinq": ropes[cc][1].astype(BF16NP),
        })
    res = run_bass_kernel_spmd(nc, in_maps, core_ids=C)
    shards = [res.results[cc]["x_out"] for cc in C]
    return unshard_tokens(shards).astype(np.float32)


# version 04
# speedup vs baseline: 1.2556x; 1.0849x over previous
import numpy as np
import ml_dtypes
import concourse.bass as bass
import concourse.mybir as mybir
from concourse.bass import ds
from concourse.bass_utils import run_bass_kernel_spmd

F32 = mybir.dt.float32
BF16 = mybir.dt.bfloat16
I32 = mybir.dt.int32
AF = mybir.ActivationFunctionType
ALU = mybir.AluOpType
AX = mybir.AxisListType
BF16NP = ml_dtypes.bfloat16

NCORES = 8
T = 1024
NT = 8
D = 4096
NDC = 32
GROUPS = [[0, 1, 2, 3], [4, 5, 6, 7]]

ENGINES = ("sync", "scalar", "vector", "tensor", "gpsimd")



def w_stream(WT, os_width=512, dcg=8):
    d, o = WT.shape
    ndc = d // 128
    ng = ndc // dcg
    nos = o // os_width
    a = WT.reshape(ng, dcg, 128, nos, os_width)
    a = a.transpose(3, 0, 2, 1, 4)
    return np.ascontiguousarray(a.reshape(nos * ng, 128, dcg * os_width))


def shard_tokens(x_full):
    out = []
    for c in range(NCORES):
        b, r = divmod(c, 4)
        out.append(np.ascontiguousarray(x_full[b, r * T:(r + 1) * T, :]))
    return out


def unshard_tokens(shards):
    x = np.empty((2, 4096, D), dtype=shards[0].dtype)
    for c in range(NCORES):
        b, r = divmod(c, 4)
        x[b, r * T:(r + 1) * T, :] = shards[c]
    return x



class Sem:
    __slots__ = ("h", "cnt", "name")

    def __init__(self, h, name):
        self.h = h
        self.cnt = 0
        self.name = name


class MK:
    def __init__(self, nc):
        self.nc = nc
        self.q = {e: [] for e in ENGINES}
        self.sems = {}
        self.cur_scope = "init"

    def scope(self, name):
        self.cur_scope = name

    def sem(self, name):
        if name not in self.sems:
            self.sems[name] = Sem(self.nc.alloc_semaphore(name), name)
        return self.sems[name]

    def _push(self, eng, fn):
        self.q[eng].append((self.cur_scope, fn))

    def wait(self, eng, sem, val=None):
        v = sem.cnt if val is None else val
        self._push(eng, lambda e, s=sem, v=v: e.wait_ge(s.h, v))

    def op(self, eng, builder, inc=None, inc_n=1):
        if inc is not None:
            inc.cnt += inc_n
            self._push(eng, lambda e, b=builder, s=inc, n=inc_n: b(e).then_inc(s.h, n))
        else:
            self._push(eng, lambda e, b=builder: b(e))

    def dma(self, eng, out, in_, sem, **kw):
        self.op(eng, lambda e, o=out, i=in_, k=dict(kw): e.dma_start(out=o, in_=i, **k),
                inc=sem, inc_n=16)
        return sem.cnt

    def barrier(self, name="bar", engines=None):
        engs = [e for e in (engines or ENGINES) if self.q[e]]
        sems = {e: self.sem(f"bar_{e}") for e in engs}
        for e in engs:
            s = sems[e]
            s.cnt += 1
            self._push(e, lambda eng, s=s: eng.drain().then_inc(s.h, 1))
        for e in engs:
            for e2 in engs:
                if e2 != e:
                    self.wait(e, sems[e2])

    def build(self):
        nc = self.nc
        with nc.Block() as block:
            for eng_name in ENGINES:
                ops = self.q[eng_name]
                if not ops:
                    continue
                def runner(e, ops=ops):
                    i = 0
                    while i < len(ops):
                        sc = ops[i][0]
                        j = i
                        with nc.named_scope(sc):
                            while j < len(ops) and ops[j][0] == sc:
                                ops[j][1](e)
                                j += 1
                        i = j
                getattr(block, eng_name)(runner)
        return nc


import os as _os
DYN_SCRATCH = int(_os.environ.get("MK_DYN_SCRATCH", "4096"))


def make_nc(name="mk"):
    return bass.Bass("TRN2", target_bir_lowering=False, name=name,
                     dynamic_dma_scratch_size=DYN_SCRATCH)



class Ctx:
    pass


def alloc_common(nc, K):
    c = Ctx()
    c.nc = nc
    c.x_res = nc.alloc_sbuf_tensor("x_res", [128, NT * D], F32)
    c.x_fm = nc.alloc_sbuf_tensor("x_fm", [128, NDC * T], BF16)
    c.wbuf = nc.alloc_sbuf_tensor("wbuf", [128, 2 * 4096], BF16)
    c.stats = nc.alloc_sbuf_tensor("stats", [128, 8 * 8], F32)
    c.ident = nc.alloc_sbuf_tensor("ident", [128, 128], F32)
    c.onesr = nc.alloc_sbuf_tensor("onesr", [1, 128], F32)
    c.hw = nc.alloc_sbuf_tensor("hw", [128, 1024], BF16)
    c.ps = [nc.alloc_psum_tensor(f"ps{i}", [128, 512], F32) for i in range(8)]
    c.identb = nc.alloc_sbuf_tensor("identb", [128, 128], BF16)
    c.rs = nc.alloc_sbuf_tensor("rs", [128, 512], F32)
    c.ob = nc.alloc_sbuf_tensor("ob", [128, 1024], BF16)
    c.onescol = nc.alloc_sbuf_tensor("onescol", [128, 1], BF16)
    c.onesrb = nc.alloc_sbuf_tensor("onesrb", [1, 128], BF16)
    c.bmT01 = nc.alloc_sbuf_tensor("bmT01", [128, 128], BF16)
    c.ident_np = np.eye(128, dtype=np.float32)
    c.ones_np = np.ones((1, 128), dtype=np.float32)
    identc = nc.inline_tensor(c.ident_np, "identc").ap()
    identbc = nc.inline_tensor(c.ident_np.astype(BF16NP), "identbc").ap()
    onesc = nc.inline_tensor(c.ones_np, "onesc").ap()
    s = K.sem("s_const")
    K.dma("sync", c.ident[:], identc, s)
    K.dma("sync", c.identb[:], identbc, s)
    K.dma("sync", c.onesr[:], onesc, s)
    onescolc = nc.inline_tensor(np.ones((128, 1), np.float32).astype(BF16NP), "onescolc").ap()
    K.dma("sync", c.onescol[:], onescolc, s)
    onesrbc = nc.inline_tensor(c.ones_np.astype(BF16NP), "onesrbc").ap()
    K.dma("sync", c.onesrb[:], onesrbc, s)
    c.s_const = s
    for e in ("sync", "scalar", "vector", "tensor", "gpsimd"):
        K.wait(e, s)
    return c


def st_col(c, slot, tt):
    return c.stats[:, slot * 8 + tt: slot * 8 + tt + 1]


def st_all(c, slot):
    return c.stats[:, slot * 8: slot * 8 + 8]



def emit_load_x(K, c, x_in_ap):
    s = K.sem("s_xload")
    for tt in range(NT):
        K.dma("sync", c.x_res[:, tt * D:(tt + 1) * D],
              x_in_ap[tt * 128:(tt + 1) * 128, :], s)
    return s


def emit_store_x(K, c, x_out_ap):
    s = K.sem("s_xstore")
    for tt in range(NT):
        K.dma("sync", x_out_ap[tt * 128:(tt + 1) * 128, :],
              c.x_res[:, tt * D:(tt + 1) * D], s)
    K.wait("sync", s)
    return s


def emit_ln_xfm(K, c, eps=1e-5, want_xfm=True, tag=""):
    sv = K.sem("s_ln_v" + tag)
    sa = K.sem("s_ln_a" + tag)
    sp = K.sem("s_ln_pe" + tag)
    for tt in range(NT):
        xt = c.x_res[:, tt * D:(tt + 1) * D]
        K.op("vector", lambda e, o=st_col(c, 0, tt), i=xt:
             e.reduce_sum(o, i, axis=AX.X), inc=sv)
        K.op("scalar", lambda e, o=c.x_fm[:, tt * 4096:(tt + 1) * 4096], i=xt,
             a=st_col(c, 1, tt):
             e.activation(o, i, AF.Square, accum_out=a), inc=sa)
    K.wait("vector", sa)
    K.wait("vector", sv)
    inv_d = 1.0 / D
    K.op("vector", lambda e: e.tensor_scalar(st_all(c, 2), st_all(c, 0), inv_d, None, ALU.mult), inc=sv)
    K.op("vector", lambda e: e.tensor_scalar(st_all(c, 3), st_all(c, 1), inv_d, None, ALU.mult), inc=sv)
    K.wait("vector", sv)
    K.op("vector", lambda e: e.tensor_tensor(st_all(c, 6), st_all(c, 2), st_all(c, 2), ALU.mult), inc=sv)
    K.wait("vector", sv)
    K.op("vector", lambda e: e.tensor_tensor(st_all(c, 3), st_all(c, 3), st_all(c, 6), ALU.subtract), inc=sv)
    K.wait("vector", sv)
    K.op("vector", lambda e: e.tensor_scalar(st_all(c, 3), st_all(c, 3), float(eps), None, ALU.add), inc=sv)
    K.wait("scalar", sv)
    K.op("scalar", lambda e: e.activation(st_all(c, 6), st_all(c, 3), AF.Sqrt), inc=sa)
    K.wait("vector", sa)
    K.op("vector", lambda e: e.reciprocal(st_all(c, 4), st_all(c, 6)), inc=sv)
    K.wait("vector", sv)
    K.op("vector", lambda e: e.scalar_tensor_tensor(st_all(c, 5), st_all(c, 2), -1.0, st_all(c, 4), ALU.mult, ALU.mult), inc=sv)
    K.wait("vector", sv)
    if not want_xfm:
        return sv, sa, sp
    xfm3 = c.x_fm[:].rearrange("p (dc t) -> p dc t", t=T)
    psT = [c.ps[6][:].bitcast(BF16), c.ps[7][:].bitcast(BF16)]
    K.wait("tensor", c.s_const)
    for tt in range(NT):
        half = tt % 2
        xln = c.wbuf[:, half * 4096: half * 4096 + 4096]
        if tt >= 2:
            K.wait("vector", sp)
        K.op("vector", lambda e, o=xln, i=c.x_res[:, tt * D:(tt + 1) * D],
             r=st_col(c, 4, tt), m=st_col(c, 5, tt):
             e.tensor_scalar(o, i, r, m, ALU.mult, ALU.add), inc=sv)
        K.wait("tensor", sv)
        for g in range(4):
            bank = psT[g % 2]
            if tt > 0 or g >= 2:
                K.wait("tensor", sv)
            for i in range(8):
                dc = g * 8 + i
                src = xln[:, dc * 128:(dc + 1) * 128]
                reg = bank[:, i * 128:(i + 1) * 128]
                K.op("tensor", lambda e, r=reg, x=src: e.transpose(r, x, c.identb[:]),
                     inc=(sp if i == 7 else None))
            K.wait("vector", sp)
            dst = xfm3[:, g * 8:(g + 1) * 8, tt * 128:(tt + 1) * 128]
            K.op("vector", lambda e, d=dst,
                 s=bank[:].rearrange("p (i t) -> p i t", t=128):
                 e.tensor_copy(d, s), inc=sv)
    return sv, sa, sp


def emit_proj_tok(K, c, wst_ap, n_os, dest_fn, *, name, bias_ap=None,
                  n_dc=NDC, lhsT_fn=None, pe_waits=(), dma_waits=(),
                  rhs_cols=512, prequant=False, dcg=4):
    ng = n_dc // dcg
    npc = max(1, dcg // 2)
    s_pw = [K.sem(f"s_pw_{name}0"), K.sem(f"s_pw_{name}1")]
    s_cast = K.sem(f"s_cast_{name}")
    s_wf = K.sem(f"s_wf_{name}")
    s_mm = K.sem(f"s_mm_{name}")
    s_ev = K.sem(f"s_ev_{name}")
    s_b = K.sem(f"s_b_{name}")
    base_wf = s_wf.cnt
    base_mm = s_mm.cnt
    if lhsT_fn is None:
        def lhsT_fn(dc, tt):
            return c.x_fm[:, dc * T + tt * 128: dc * T + (tt + 1) * 128]
    for w in pe_waits:
        K.wait("tensor", w)
    for w in dma_waits:
        K.wait("sync", w)
        K.wait("vector", w)
    wb32 = c.wbuf[:].bitcast(F32)
    n_slab = n_os * ng
    slab_ready = []
    piece_cast = []
    for os in range(n_os):
        if bias_ap is not None:
            if os >= 1:
                K.wait("sync", s_mm, base_mm + 8 * os)
            K.dma("sync", c.rs[0:1, 0:rhs_cols],
                  bias_ap[0:1, os * rhs_cols:(os + 1) * rhs_cols], s_b)
        for g in range(ng):
            slab = os * ng + g
            spar = slab % 2
            if prequant:
                if slab >= 2:
                    K.wait("sync", s_wf, base_wf + slab - 1)
                K.dma("sync", c.wbuf[:, spar * 2048:(spar + 1) * 2048],
                      wst_ap[slab], s_pw[spar])
                slab_ready.append((s_pw[spar], s_pw[spar].cnt))
            else:
                for j in range(npc):
                    p = npc * slab + j
                    ppar = p % 2
                    if p >= 2:
                        K.wait("sync", s_cast, piece_cast[p - 2])
                    K.dma("sync", wb32[:, 2048 + ppar * 1024: 2048 + (ppar + 1) * 1024],
                          wst_ap[slab][:, j * 1024:(j + 1) * 1024], s_pw[ppar])
                    if slab >= 2 and j == 0:
                        K.wait("vector", s_wf, base_wf + slab - 1)
                    K.wait("vector", s_pw[ppar])
                    K.op("vector", lambda e,
                         o=c.wbuf[:, spar * 2048 + j * 1024: spar * 2048 + (j + 1) * 1024],
                         i=wb32[:, 2048 + ppar * 1024: 2048 + (ppar + 1) * 1024]:
                         e.tensor_copy(o, i), inc=s_cast)
                    piece_cast.append(s_cast.cnt)
                slab_ready.append((s_cast, s_cast.cnt))
        if os > 0:
            K.wait("tensor", s_ev)
        for g in range(ng):
            slab = os * ng + g
            spar = slab % 2
            sem_, val_ = slab_ready[slab]
            K.wait("tensor", sem_, val_)
            for dcs in range(dcg):
                dc = g * dcg + dcs
                w_sl = c.wbuf[:, spar * 2048 + dcs * rhs_cols: spar * 2048 + (dcs + 1) * rhs_cols]
                last = (dc == n_dc - 1) and bias_ap is None
                for tt in range(NT):
                    K.op("tensor", lambda e, p=c.ps[tt][:, 0:rhs_cols], l=lhsT_fn(dc, tt),
                         w=w_sl, st=(dc == 0), sp=(dc == n_dc - 1):
                         e.matmul(p, l, w, start=st, stop=sp),
                         inc=(s_mm if last else None))
            K.op("tensor", lambda e: e.drain(), inc=s_wf)
        if bias_ap is not None:
            K.wait("tensor", s_b)
            for tt in range(NT):
                K.op("tensor", lambda e, p=c.ps[tt][:, 0:rhs_cols]:
                     e.matmul(p, c.onesr[:], c.rs[0:1, 0:rhs_cols], start=False,
                              stop=True, skip_group_check=True), inc=s_mm)
        K.wait("vector", s_mm)
        K.wait("scalar", s_mm)
        for tt in range(NT):
            dest_fn(K, c, tt, os, c.ps[tt][:, 0:rhs_cols], s_ev)
    return s_ev, s_mm, s_wf


def dest_resid(K, c, tt, os, psum, s_ev, col0=0):
    dst = c.x_res[:, tt * D + col0 + os * 512: tt * D + col0 + os * 512 + 512]
    K.op("vector", lambda e, d=dst, a=psum: e.tensor_tensor(d, d, a, ALU.add), inc=s_ev)



def emit_allgather(K, in_ap, out_ap, name, wait_sems=()):
    sc = K.sem("s_ag_" + name)
    for w in wait_sems:
        K.wait("gpsimd", w)
    K.op("gpsimd", lambda e: e.collective_compute(
        "AllGather", mybir.AluOpType.bypass, replica_groups=GROUPS,
        ins=[in_ap.opt()], outs=[out_ap.opt()]), inc=sc, inc_n=1)
    return sc


def l0ab_consts(nc):
    sel4 = np.zeros((128, 4), np.float32)
    for p in range(128):
        sel4[p, p // 32] = 1.0
    ind = np.zeros((128, 4096), np.float32)
    for n in range(4096):
        ind[n // 32, n] = 1.0
    a = nc.inline_tensor(sel4, "sel4c").ap()
    b = nc.inline_tensor(ind.astype(BF16NP), "indc").ap()
    return a, b


def emit_l0ab_maskprep(K, c, maskT_ap, sel4_ap, wait_wbuf=()):
    s_mc = [K.sem("s_l0ab_mc0"), K.sem("s_l0ab_mc1")]
    s_rd = K.sem("s_l0ab_rd")
    s_pe = K.sem("s_l0ab_pe")
    s_cst = K.sem("s_l0ab_cst")
    K.dma("sync", c.rs[:, 0:4], sel4_ap, s_cst)
    mbuf = c.wbuf[:].bitcast(I32)
    for w in wait_wbuf:
        K.wait("sync", w)
    K.wait("tensor", s_cst)
    red_done = []
    for kc in range(NDC):
        par = kc % 2
        rs_acc = c.rs[:, 128 + par * 128: 128 + par * 128 + 128]
        if kc >= 2:
            K.wait("vector", s_pe, kc - 1)
        for h in range(2):
            m = kc * 2 + h
            if m >= 2:
                K.wait("sync", s_rd, red_done[m - 2])
            K.dma("sync", mbuf[:, (m % 2) * 2048: (m % 2) * 2048 + 2048],
                  maskT_ap[kc * 128:(kc + 1) * 128, h * 2048:(h + 1) * 2048], s_mc[m % 2])
            K.wait("vector", s_mc[m % 2])
            dst = rs_acc[:, h * 64:(h + 1) * 64]
            K.op("vector", lambda e, d=dst, i=mbuf[:, (m % 2) * 2048:(m % 2) * 2048 + 2048]:
                 e.reduce_sum(d.rearrange("p (a b) -> p a b", b=1),
                              i.rearrange("p (a b) -> p a b", b=32), axis=AX.X), inc=s_rd)
            red_done.append(s_rd.cnt)
        K.wait("vector", s_rd)
        K.op("vector", lambda e: e.engine_nop(), inc=s_rd)
        K.wait("tensor", s_rd)
        K.op("tensor", lambda e, l=rs_acc, kc=kc: e.matmul(
            c.ps[0][:, kc * 4:kc * 4 + 4], l, c.rs[:, 0:4], start=True, stop=True),
            inc=s_pe)
    K.wait("vector", s_pe)
    K.op("vector", lambda e: e.tensor_scalar(c.bmT01[:], c.ps[0][:, 0:128], 0.0, None, ALU.is_gt), inc=s_rd)
    K.wait("vector", s_rd)
    return s_rd


def emit_l0ab_wm(K, c, wst_ap, wm_ap, ind_ap, s_bm):
    s_w = K.sem("s_l0ab_w")
    s_ind = K.sem("s_l0ab_ind")
    s_em = K.sem("s_l0ab_em")
    s_tt = K.sem("s_l0ab_wmtt")
    s_out = [K.sem("s_l0ab_out0"), K.sem("s_l0ab_out1")]
    s_cp = K.sem("s_l0ab_cp")
    bmE = c.wbuf[:, 0:4096]
    K.wait("vector", s_bm)
    K.wait("sync", s_bm)
    for j in range(32):
        K.op("vector", lambda e, j=j: e.tensor_copy(c.wbuf[:, j:4096:32], c.bmT01[:]),
             inc=s_cp)
    K.wait("vector", s_cp)
    K.op("vector", lambda e: e.engine_nop(), inc=s_cp)
    K.wait("tensor", s_cp)
    K.wait("sync", s_cp)
    indb = c.ob[:, 0:512]
    wq32 = c.wbuf[:].bitcast(F32)[:, 2048:3072]
    n_slab = wst_ap.shape[0]
    ng = 8
    tt_done = []
    for qq in range(2 * n_slab):
        slab, jq = divmod(qq, 2)
        os, g = divmod(slab, ng)
        qpar = qq % 2
        if g == 0 and jq == 0:
            K.wait("sync", s_em)
            K.dma("sync", indb[:], ind_ap[:, os * 512:(os + 1) * 512], s_ind)
        if qq >= 1:
            K.wait("sync", s_tt, tt_done[qq - 1])
        K.dma("sync", wq32, wst_ap[slab][:, jq * 1024:(jq + 1) * 1024], s_w)
        banks = (c.ps[0], c.ps[1]) if qpar == 0 else (c.ps[2], c.ps[3])
        K.wait("tensor", s_ind, 16 * (os + 1))
        if qq >= 2:
            K.wait("tensor", s_tt, tt_done[qq - 2])
        for q in range(2):
            dc = g * 4 + jq * 2 + q
            K.op("tensor", lambda e, b=banks[q][:, 0:512],
                 l=bmE[:, dc * 128:(dc + 1) * 128]:
                 e.matmul(b, l, indb[:], start=True, stop=True),
                 inc=(s_em if q == 1 else None))
        K.wait("vector", s_em)
        K.wait("vector", s_w)
        if qq >= 2:
            K.wait("vector", s_out[qpar])
        outq = c.wbuf[:, 6144 + qpar * 1024: 6144 + (qpar + 1) * 1024]
        for q in range(2):
            K.op("vector", lambda e, o=outq[:, q * 512:(q + 1) * 512],
                 w=wq32[:, q * 512:(q + 1) * 512], b=banks[q][:, 0:512]:
                 e.tensor_tensor(o, w, b, ALU.mult), inc=s_tt)
        tt_done.append(s_tt.cnt)
        K.wait("sync", s_tt)
        if qq >= 2:
            K.wait("sync", s_out[qpar])
        K.dma("sync", wm_ap[slab][:, jq * 1024:(jq + 1) * 1024], outq, s_out[qpar])
    K.wait("sync", s_out[0])
    K.wait("sync", s_out[1])
    return s_out


def emit_l0ab(K, c, maskT_ap, wst_ap, wm_ap, sel4_ap, ind_ap, ln_sems):
    sv, sa, sp = ln_sems
    s_bm = emit_l0ab_maskprep(K, c, maskT_ap, sel4_ap, wait_wbuf=(sp,))
    s_out = emit_l0ab_wm(K, c, wst_ap, wm_ap, ind_ap, s_bm)
    s_ev, s_mm, s_wf = emit_proj_tok(
        K, c, wm_ap, 8, dest_resid, name="pj", prequant=True,
        pe_waits=(sv,), dma_waits=tuple(s_out))
    return s_ev



def l0ffn_consts(nc):
    e8 = np.zeros((8, 128), np.float32)
    for er in range(128):
        e8[er // 16, er] = 2.0
    return nc.inline_tensor(e8.astype(BF16NP), "e8c").ap()


def emit_l0ffn(K, c, wg9_ap, ast_ap, wbst_ap, bias_ap, bthr_ap, e8_ap, ln_sems, tag=""):
    sv, sa, sp = ln_sems
    s_h = K.sem("s_ff_h" + tag)
    base_h = s_h.cnt
    s_ha = K.sem("s_ff_ha" + tag)
    s_hc = K.sem("s_ff_hc" + tag)
    s_g = K.sem("s_ff_g" + tag)
    s_gv = K.sem("s_ff_gv" + tag)
    s_gs = K.sem("s_ff_gs" + tag)
    s_w9 = K.sem("s_ff_w9" + tag)
    obf = c.ob[:].bitcast(F32)
    abf = c.rs[:, 0:128].bitcast(BF16)
    ast32 = obf[:, 0:128]
    col = lambda i: c.rs[:, 288 + i: 288 + i + 1]
    e9 = c.rs[:, 304:312]
    ad = c.rs[:, 312:320]
    wv = c.rs[:, 320:328]
    wnf = c.rs[:, 328:336]
    e8_sb = c.rs[:, 336:400].bitcast(BF16)
    wg9b = obf[:, 128:272].bitcast(BF16)
    K.wait("tensor", sv)
    cast_done = []
    for dc in range(NDC):
        par = dc % 2
        if dc >= 1:
            K.wait("sync", s_hc, cast_done[dc - 1])
        K.dma("sync", ast32, ast_ap[dc], s_ha)
        K.wait("vector", s_ha)
        if dc >= 2:
            K.wait("vector", s_h, base_h + dc - 1)
        K.op("vector", lambda e, o=abf[:, par * 128:(par + 1) * 128]:
             e.tensor_copy(o, ast32), inc=s_hc)
        cast_done.append(s_hc.cnt)
        K.wait("tensor", s_hc)
        for th in range(2):
            K.op("tensor", lambda e, b=c.ps[4 + th][:, 0:512],
                 l=abf[:, par * 128:(par + 1) * 128],
                 r=c.x_fm[:, dc * T + th * 512: dc * T + th * 512 + 512],
                 st=(dc == 0), spp=(dc == NDC - 1):
                 e.matmul(b, l, r, start=st, stop=spp),
                 inc=(s_h if th == 1 else None))
    K.wait("sync", s_h)
    K.dma("sync", c.rs[:, 0:288], wg9_ap, s_w9)
    K.dma("sync", e8_sb[0:8, 0:128], e8_ap, s_w9)
    K.dma("sync", c.rs[0:1, 298:299], bthr_ap, s_w9)
    K.wait("vector", s_w9)
    K.op("vector", lambda e: e.tensor_copy(wg9b, c.rs[:, 0:288]), inc=s_gv)
    K.wait("tensor", s_w9)
    K.op("tensor", lambda e: e.matmul(c.ps[7][:, 0:1], c.onesr[:], c.rs[0:1, 298:299],
         start=True, stop=True), inc=s_g)
    K.wait("vector", s_g)
    K.op("vector", lambda e: e.tensor_copy(col(15), c.ps[7][:, 0:1]), inc=s_gv)
    K.wait("scalar", s_gv)
    K.wait("vector", s_gv)
    K.wait("tensor", s_gv)
    for tt in range(NT):
        if tt > 0:
            K.wait("tensor", s_gs)
        for dc in range(NDC):
            K.op("tensor", lambda e, l=c.x_fm[:, dc * T + tt * 128: dc * T + (tt + 1) * 128],
                 r=wg9b[:, dc * 9: dc * 9 + 9], st=(dc == 0), spp=(dc == NDC - 1):
                 e.matmul(c.ps[6][:, 0:9], l, r, start=st, stop=spp),
                 inc=(s_g if dc == NDC - 1 else None))
        K.wait("vector", s_g)
        K.op("vector", lambda e: e.reduce_max(col(0), c.ps[6][:, 0:8], axis=AX.X), inc=s_gv)
        K.wait("vector", s_gv)
        K.op("vector", lambda e: e.tensor_scalar(col(1), col(0), -1.0, None, ALU.mult), inc=s_gv)
        K.wait("scalar", s_gv)
        K.op("scalar", lambda e: e.activation(e9, c.ps[6][:, 0:8], AF.Exp,
             bias=col(1), accum_out=col(2)), inc=s_gs)
        K.op("scalar", lambda e: e.activation(col(3), c.ps[6][:, 8:9], AF.Sigmoid,
             bias=col(15)), inc=s_gs)
        K.wait("vector", s_gs)
        K.op("vector", lambda e: e.reciprocal(col(4), col(2)), inc=s_gv)
        K.op("vector", lambda e: e.tensor_scalar(col(5), col(3), -0.125, None, ALU.mult), inc=s_gv)
        K.wait("vector", s_gv)
        K.op("vector", lambda e: e.tensor_scalar(ad, e9, col(4), col(5), ALU.mult, ALU.add), inc=s_gv)
        K.wait("vector", s_gv)
        K.op("vector", lambda e: e.tensor_scalar(wv, ad, 0.0, None, ALU.max), inc=s_gv)
        K.wait("vector", s_gv)
        K.op("vector", lambda e: e.reduce_sum(col(6), wv, axis=AX.X), inc=s_gv)
        K.wait("vector", s_gv)
        K.op("vector", lambda e: e.tensor_scalar(col(7), col(6), 0.0, None, ALU.is_equal), inc=s_gv)
        K.wait("vector", s_gv)
        K.op("vector", lambda e: e.tensor_add(col(8), col(6), col(7)), inc=s_gv)
        K.wait("vector", s_gv)
        K.op("vector", lambda e: e.reciprocal(col(9), col(8)), inc=s_gv)
        K.wait("vector", s_gv)
        K.op("vector", lambda e: e.tensor_scalar(wnf, wv, col(9), None, ALU.mult), inc=s_gv)
        K.wait("tensor", s_gv)
        K.op("tensor", lambda e: e.transpose(c.ps[7][0:8, 0:128], wnf, c.ident[:]), inc=s_g)
        K.wait("vector", s_g)
        K.op("vector", lambda e: e.tensor_copy(c.bmT01[0:8, 0:128], c.ps[7][0:8, 0:128]), inc=s_gv)
        K.wait("tensor", s_gv)
        K.op("tensor", lambda e: e.matmul(c.ps[7][:, 0:128], e8_sb[0:8, 0:128],
             c.bmT01[0:8, 0:128], start=True, stop=True), inc=s_g)
        K.wait("vector", s_g)
        K.op("vector", lambda e: e.tensor_copy(c.bmT01[:, 0:128], c.ps[7][:, 0:128]), inc=s_gv)
        K.wait("vector", s_gv)
        hb = c.ps[4 + tt // 4][:, (tt % 4) * 128:(tt % 4) * 128 + 128]
        K.op("vector", lambda e, h=hb, o=c.hw[:, tt * 128:(tt + 1) * 128]:
             e.tensor_tensor(o, h, c.bmT01[:, 0:128], ALU.mult), inc=s_gv)
        K.wait("vector", s_gv)
    def lhsT_fn(dc, tt):
        if dc < NDC:
            return c.x_fm[:, dc * T + tt * 128: dc * T + (tt + 1) * 128]
        return c.hw[:, tt * 128:(tt + 1) * 128]
    s_ev, s_mm, s_wf = emit_proj_tok(
        K, c, wbst_ap, 8, dest_resid, name="pj", bias_ap=bias_ap,
        n_dc=34, dcg=2, lhsT_fn=lhsT_fn, pe_waits=(s_gv,), dma_waits=(s_gv,))
    return s_ev


def l0ffn_host(W_base, W_gate, W_thr, A_w, B_w):
    Wg9 = np.concatenate([W_gate.T, W_thr[:, None]], axis=1)
    wg9 = np.ascontiguousarray(Wg9.reshape(32, 128, 9).transpose(1, 0, 2)
                               .reshape(128, 288)).astype(np.float32)
    A_all = A_w.transpose(2, 0, 1).reshape(4096, 128)
    ast = np.ascontiguousarray(A_all.reshape(32, 128, 128)).astype(np.float32)
    B_all = B_w.transpose(0, 2, 1).reshape(128, 4096)
    WT_aug = np.concatenate(
        [W_base.T, B_all, np.zeros((128, 4096), np.float32)], axis=0)
    wbst = w_stream(WT_aug, dcg=2)
    return wg9, ast, wbst


def l0a_host(Wq, bq, Wk, bk, Wv, bv, Wo, bo):
    WT_kv = np.concatenate([Wk.T, Wv.T], axis=1)
    wkvst = w_stream(WT_kv.astype(np.float32), dcg=4)
    kvbias = np.concatenate([bk, bv])[None, :].astype(np.float32)
    WqT = Wq.T.astype(np.float32)
    sl = []
    for h in range(32):
        wh = WqT[:, h * 128:(h + 1) * 128].reshape(32, 128, 128)
        for half in range(2):
            sl.append(np.ascontiguousarray(
                wh[half * 16:(half + 1) * 16].transpose(1, 0, 2).reshape(128, 2048)))
    wqst = np.stack(sl)
    bqa = np.ascontiguousarray(bq.reshape(32, 1, 128)).astype(np.float32)
    wost = w_stream(Wo.T.astype(np.float32), dcg=4)
    bo_ = bo[None, :].astype(np.float32)
    return wkvst, kvbias, wqst, bqa, wost, bo_


def emit_l0a_kvproj(K, c, wkvst_ap, kvbias_ap, kv_dram, ln_sems):
    sv, sa, sp = ln_sems
    s_kvs = K.sem("s_l0a_kvs")
    s_kvo = [K.sem("s_l0a_kvo0"), K.sem("s_l0a_kvo1")]
    hwf = c.hw[:].bitcast(F32)

    def dest(K, c, tt, os, psum, s_ev):
        idx = tt + os * NT
        par = idx % 2
        if idx >= 2:
            K.wait("vector", s_kvo[par])
        ob = c.ob[:, par * 512:(par + 1) * 512]
        if os < 2:
            K.wait("scalar", s_ev)
            K.op("scalar", lambda e, p=psum: e.activation(hwf, p, AF.Exp), inc=s_kvs)
            K.wait("vector", s_kvs)
            K.op("vector", lambda e: e.tensor_scalar(hwf, hwf, 1.0, None, ALU.min), inc=s_ev)
            K.wait("vector", s_ev)
            K.op("vector", lambda e, p=psum, o=ob: e.scalar_tensor_tensor(
                o, p, 0.0, hwf, ALU.max, ALU.add), inc=s_ev)
        else:
            K.op("vector", lambda e, p=psum, o=ob: e.tensor_copy(o, p), inc=s_ev)
        K.wait("sync", s_ev)
        K.dma("sync", kv_dram[tt][:, os * 512:(os + 1) * 512], ob, s_kvo[par])

    s_ev, s_mm, s_wf = emit_proj_tok(
        K, c, wkvst_ap, 4, dest, name="pj", bias_ap=kvbias_ap,
        pe_waits=(sv,), dma_waits=(sv,))
    return s_ev, s_kvo


def emit_l0a_kvstate(K, c, kv_dram, s_kvo, kvout_ap):
    s_t = [K.sem("s_l0a_t0"), K.sem("s_l0a_t1")]
    s_pe = K.sem("s_l0a_kvpe")
    s_kv = K.sem("s_l0a_kve")
    s_ko = K.sem("s_l0a_ko")
    hwf = c.hw[:].bitcast(F32)
    ks_acc = c.rs[:, 144:152]
    for w in s_kvo:
        K.wait("sync", w)
    K.op("vector", lambda e: e.memset(ks_acc, 0.0), inc=s_kv)
    K.wait("vector", s_kv)
    tld = 0
    for wave in range(2):
        for tt in range(NT):
            par = tld % 2
            sl = c.wbuf[:, par * 4096: par * 4096 + 2048]
            if tld >= 2:
                K.wait("sync", s_pe, s_pe.cnt - 1)
            K.dma("sync", sl, kv_dram[tt], s_t[par])
            K.wait("tensor", s_t[par])
            if tt > 0:
                K.wait("tensor", s_kv)
            for q in range(4):
                kvh = wave * 4 + q
                ksl = sl[:, kvh * 128:(kvh + 1) * 128]
                vsl = sl[:, 1024 + kvh * 128: 1024 + (kvh + 1) * 128]
                K.op("tensor", lambda e, k=ksl, v=vsl, b=c.ps[q][:, 0:128],
                     st=(tt == 0), spp=(tt == NT - 1):
                     e.matmul(b, k, v, start=st, stop=spp), inc=None)
                K.op("tensor", lambda e, k=ksl, b=c.ps[4][:, q:q + 1]:
                     e.matmul(b, k, c.onescol[:], start=True, stop=True),
                     inc=(s_pe if q == 3 else None))
            tld += 1
            K.wait("vector", s_pe)
            K.op("vector", lambda e, o=ks_acc[:, wave * 4:(wave + 1) * 4]:
                 e.tensor_tensor(o, o, c.ps[4][:, 0:4], ALU.add), inc=s_kv)
        K.wait("vector", s_pe)
        for q in range(4):
            if wave > 0 or q > 0:
                K.wait("vector", s_ko)
            K.op("vector", lambda e, b=c.ps[q][:, 0:128], o=hwf[:, 0:128]:
                 e.tensor_copy(o, b), inc=s_kv)
            K.wait("sync", s_kv)
            K.dma("sync", kvout_ap[:, (wave * 4 + q) * 128:(wave * 4 + q) * 128 + 128],
                  hwf[:, 0:128], s_ko)
    K.wait("vector", s_kv)
    K.op("vector", lambda e: e.engine_nop(), inc=s_kv)
    K.wait("sync", s_kv)
    K.dma("sync", kvout_ap[:, 1024:1032], ks_acc, s_ko)
    K.wait("sync", s_ko)
    return s_ko


def emit_l0a_kvreduce_in(K, c, kvparts_ap):
    s_l = K.sem("s_l0a_kl")
    s_v = K.sem("s_l0a_kvv")
    wb32 = c.wbuf[:].bitcast(F32)
    for piece in range(3):
        w = 512 if piece < 2 else 8
        if piece > 0:
            K.wait("sync", s_v)
        for r in range(4):
            K.dma("sync", wb32[:, r * 512: r * 512 + w],
                  kvparts_ap[r][:, piece * 512: piece * 512 + w], s_l)
        K.wait("vector", s_l)
        K.op("vector", lambda e, w=w: e.tensor_add(wb32[:, 0:w], wb32[:, 0:w], wb32[:, 512:512 + w]), inc=s_v)
        K.op("vector", lambda e, w=w: e.tensor_add(wb32[:, 1024:1024 + w], wb32[:, 1024:1024 + w], wb32[:, 1536:1536 + w]), inc=s_v)
        K.wait("vector", s_v)
        K.op("vector", lambda e, w=w: e.tensor_add(wb32[:, 0:w], wb32[:, 0:w], wb32[:, 1024:1024 + w]), inc=s_v)
        K.wait("vector", s_v)
        if piece < 2:
            K.op("vector", lambda e, piece=piece: e.tensor_copy(
                c.ob[:, piece * 512:(piece + 1) * 512], wb32[:, 0:512]), inc=s_v)
        else:
            K.op("vector", lambda e: e.tensor_scalar(wb32[:, 0:8], wb32[:, 0:8], 1e-10, None, ALU.add), inc=s_v)
            K.wait("vector", s_v)
            K.op("vector", lambda e: e.reciprocal(c.rs[:, 128:136], wb32[:, 0:8]), inc=s_v)
        K.wait("vector", s_v)
    K.op("vector", lambda e: e.engine_nop(), inc=s_v)
    return s_v


def emit_l0a_attn(K, c, wqst_ap, bqa_ap, attn_dram, s_kvready, ln_sems):
    sv, sa, sp = ln_sems
    s_w = [K.sem("s_l0a_qw0"), K.sem("s_l0a_qw1")]
    s_c = K.sem("s_l0a_qc")
    s_q = K.sem("s_l0a_qpe")
    s_qv = K.sem("s_l0a_qv")
    s_qs = K.sem("s_l0a_qs")
    s_b = K.sem("s_l0a_qb")
    s_ao = [K.sem("s_l0a_ao0"), K.sem("s_l0a_ao1")]
    wb32 = c.wbuf[:].bitcast(F32)
    hwf = c.hw[:].bitcast(F32)
    q_phi = c.wbuf[:, 6144:7168]
    bqrow = c.rs[0:1, 256:384]
    for e in ("vector", "tensor", "sync", "scalar"):
        K.wait(e, s_kvready)
    K.wait("tensor", sv)
    cast_done = []
    qdone = []
    for h in range(32):
        for half in range(2):
            hs = h * 2 + half
            for j in range(2):
                p = hs * 2 + j
                if p >= 1:
                    K.wait("sync", s_c, cast_done[p - 1])
                K.dma("sync", wb32[:, 2048:3072],
                      wqst_ap[hs][:, j * 1024:(j + 1) * 1024], s_w[0])
                if hs >= 2 and j == 0:
                    K.wait("vector", s_q, qdone[hs - 2])
                K.wait("vector", s_w[0])
                K.op("vector", lambda e,
                     o=c.wbuf[:, (hs % 2) * 2048 + j * 1024: (hs % 2) * 2048 + (j + 1) * 1024],
                     i=wb32[:, 2048:3072]:
                     e.tensor_copy(o, i), inc=s_c)
                cast_done.append(s_c.cnt)
        if h >= 1:
            K.wait("sync", s_q)
        K.dma("sync", bqrow, bqa_ap[h], s_b)
        if h > 0:
            K.wait("tensor", s_qv)
        K.wait("tensor", s_b)
        for half in range(2):
            hs = h * 2 + half
            K.wait("tensor", s_c, cast_done[hs * 2 + 1])
            for i in range(16):
                dc = half * 16 + i
                l = c.wbuf[:, (hs % 2) * 2048 + i * 128: (hs % 2) * 2048 + (i + 1) * 128]
                for ts in range(2):
                    K.op("tensor", lambda e, l=l, b=c.ps[5 + ts][:, 0:512],
                         r=c.x_fm[:, dc * T + ts * 512: dc * T + ts * 512 + 512],
                         st=(dc == 0), spp=(dc == NDC - 1):
                         e.matmul(b, l, r, start=st, stop=spp), inc=None)
            K.op("tensor", lambda e: e.drain(), inc=s_q)
            qdone.append(s_q.cnt)
        for ts in range(2):
            for seg in range(4):
                K.op("tensor", lambda e, b=c.ps[5 + ts][:, seg * 128:(seg + 1) * 128]:
                     e.matmul(b, bqrow, c.onesr[:], start=False, stop=True,
                              skip_group_check=True),
                     inc=(s_q if (ts == 1 and seg == 3) else None))
        for ts in range(2):
            K.wait("scalar", s_q)
            K.wait("scalar", s_qv)
            K.op("scalar", lambda e, p=c.ps[5 + ts][:, 0:512]: e.activation(hwf, p, AF.Exp), inc=s_qs)
            K.wait("vector", s_qs)
            K.op("vector", lambda e: e.tensor_scalar(hwf, hwf, 1.0, None, ALU.min), inc=s_qv)
            K.wait("vector", s_qv)
            K.op("vector", lambda e, p=c.ps[5 + ts][:, 0:512], o=q_phi[:, ts * 512:(ts + 1) * 512]:
                 e.scalar_tensor_tensor(o, p, 0.0, hwf, ALU.max, ALU.add), inc=s_qv)
            K.wait("vector", s_qv)
        kvh = h // 4
        K.wait("tensor", s_qv)
        for ts in range(2):
            apar = ts
            K.op("tensor", lambda e, b=c.ps[3 + apar][:, 0:512],
                 l=c.ob[:, kvh * 128:(kvh + 1) * 128],
                 r=q_phi[:, ts * 512:(ts + 1) * 512]:
                 e.matmul(b, l, r, start=True, stop=True), inc=s_q)
            K.wait("vector", s_q)
            idx = h * 2 + ts
            if idx >= 2:
                K.wait("vector", s_ao[apar])
            obf = c.wbuf[:, 7168 + apar * 512: 7168 + (apar + 1) * 512]
            K.op("vector", lambda e, b=c.ps[3 + apar][:, 0:512], o=obf, kvh=kvh:
                 e.tensor_scalar(o, b, c.rs[:, 128 + kvh:129 + kvh], None, ALU.mult), inc=s_qv)
            K.wait("sync", s_qv)
            K.dma("sync", attn_dram[0:128, h * 1024 + ts * 512: h * 1024 + (ts + 1) * 512],
                  obf, s_ao[apar])
    K.wait("sync", s_ao[0])
    K.wait("sync", s_ao[1])
    return s_ao


def emit_l0a_oproj(K, c, attn_dram, wost_ap, bo_ap, s_ao, prequant=False, name="l0ao",
                   attn_list=None):
    s_l = K.sem("s_al_" + name)
    for w in s_ao:
        K.wait("sync", w)
    if attn_list is not None:
        for dc in range(NDC):
            K.dma("sync", c.x_fm[:, dc * T:(dc + 1) * T], attn_list[dc], s_l)
    else:
        K.dma("sync", c.x_fm[:], attn_dram, s_l)
    s_ev, s_mm, s_wf = emit_proj_tok(
        K, c, wost_ap, 8, dest_resid, name=name, bias_ap=bo_ap, prequant=prequant,
        pe_waits=(s_l,), dma_waits=(s_l,))
    return s_ev


def l1ffn_host(W, A, B, router_W):
    rw8 = np.ascontiguousarray(router_W.T.reshape(32, 128, 8).transpose(1, 0, 2)
                               .reshape(128, 256)).astype(np.float32)
    A_all = A.transpose(2, 0, 1).reshape(4096, 128)
    ast = np.ascontiguousarray(A_all.reshape(32, 128, 128)).astype(np.float32)
    B_all = B.transpose(0, 2, 1).reshape(128, 4096)
    WT_aug = np.concatenate(
        [W.T, B_all, np.zeros((128, 4096), np.float32)], axis=0)
    wbst = w_stream(WT_aug, dcg=2)
    return rw8, ast, wbst


def sel_host(core):
    b, r = divmod(core, 4)
    pos = np.arange(r * T, (r + 1) * T)
    sel = ((pos >= 611) & (pos <= 4094)).astype(np.float32)
    return np.ascontiguousarray(sel.reshape(8, 128).T).astype(BF16NP)


def emit_l1ffn_router_partial(K, c, rw8_ap, selb_ap, rpart_out_ap, ln_sems, tag=""):
    sv, sa, sp = ln_sems
    s_r = K.sem("s_rt_r" + tag)
    s_rv = K.sem("s_rt_v" + tag)
    s_w = K.sem("s_rt_w" + tag)
    obf = c.ob[:].bitcast(F32)
    rw8b = obf[:, 256:384].bitcast(BF16)
    selb = obf[:, 384:388].bitcast(BF16)
    K.dma("sync", obf[:, 0:256], rw8_ap, s_w)
    K.dma("sync", selb, selb_ap, s_w)
    K.wait("vector", s_w)
    K.op("vector", lambda e: e.tensor_copy(rw8b, obf[:, 0:256]), inc=s_rv)
    K.wait("vector", s_rv)
    K.wait("tensor", s_rv)
    K.wait("tensor", sv)
    for tt in range(NT):
        if tt > 0:
            K.wait("tensor", s_rv)
        for dc in range(NDC):
            K.op("tensor", lambda e, l=c.x_fm[:, dc * T + tt * 128: dc * T + (tt + 1) * 128],
                 r=rw8b[:, dc * 8: dc * 8 + 8], st=(dc == 0), spp=(dc == NDC - 1):
                 e.matmul(c.ps[6][:, 0:8], l, r, start=st, stop=spp),
                 inc=(s_r if dc == NDC - 1 else None))
        K.wait("vector", s_r)
        K.op("vector", lambda e: e.tensor_copy(c.rs[:, 256:260].bitcast(BF16), c.ps[6][:, 0:8]), inc=s_rv)
        K.wait("vector", s_rv)
        K.op("vector", lambda e: e.engine_nop(), inc=s_rv)
        K.wait("tensor", s_rv)
        K.op("tensor", lambda e, l=selb[:, tt:tt + 1], st=(tt == 0), spp=(tt == NT - 1):
             e.matmul(c.ps[7][0:1, 0:8], l, c.rs[:, 256:260].bitcast(BF16), start=st, stop=spp),
             inc=s_r)
    K.wait("vector", s_r)
    if rpart_out_ap is not None:
        K.op("vector", lambda e: e.tensor_copy(c.rs[0:1, 264:272], c.ps[7][0:1, 0:8]), inc=s_rv)
        K.wait("sync", s_rv)
        s_o = K.sem("s_rt_o" + tag)
        K.dma("sync", rpart_out_ap, c.rs[0:1, 264:272], s_o)
        K.wait("sync", s_o)
        return s_o
    return s_r


def emit_l1ffn_routing(K, c, rparts_ap, rb_ap, e8_ap, tag=""):
    s_l = K.sem("s_ro_l" + tag)
    s_v = K.sem("s_ro_v" + tag)
    s_s = K.sem("s_ro_s" + tag)
    s_p = K.sem("s_ro_p" + tag)
    row = lambda i: c.rs[0:1, 280 + i * 8: 280 + (i + 1) * 8]
    e8_sb = c.rs[:, 336:400].bitcast(BF16)
    col1 = lambda i: c.rs[0:1, 408 + i: 409 + i]
    rowb = c.rs[0:1, 404:408].bitcast(BF16)[0:1, 0:8]
    colb = c.rs[:, 274:275].bitcast(BF16)[0:8, 0:1]
    for r in range(4):
        K.dma("sync", c.rs[0:1, 280 + 8 * r:288 + 8 * r], rparts_ap[r], s_l)
    K.dma("sync", c.rs[0:1, 312:320], rb_ap, s_l)
    K.dma("sync", e8_sb[0:8, 0:128], e8_ap, s_l)
    K.wait("vector", s_l)
    K.op("vector", lambda e: e.tensor_add(row(0), row(0), row(1)), inc=s_v)
    K.op("vector", lambda e: e.tensor_add(row(2), row(2), row(3)), inc=s_v)
    K.wait("vector", s_v)
    K.op("vector", lambda e: e.tensor_add(row(0), row(0), row(2)), inc=s_v)
    K.wait("vector", s_v)
    K.op("vector", lambda e: e.scalar_tensor_tensor(
        row(0), row(0), 1.0 / 3484.0, c.rs[0:1, 312:320], ALU.mult, ALU.add), inc=s_v)
    K.wait("vector", s_v)
    K.op("vector", lambda e: e.reduce_max(col1(0), row(0), axis=AX.X), inc=s_v)
    K.wait("vector", s_v)
    K.op("vector", lambda e: e.tensor_scalar(col1(1), col1(0), -1.0, None, ALU.mult), inc=s_v)
    K.wait("scalar", s_v)
    K.op("scalar", lambda e: e.activation(row(5), row(0), AF.Exp,
         bias=col1(1), accum_out=col1(2)), inc=s_s)
    K.wait("vector", s_s)
    K.op("vector", lambda e: e.reciprocal(col1(3), col1(2)), inc=s_v)
    K.wait("vector", s_v)
    K.op("vector", lambda e: e.tensor_scalar(row(6), row(5), col1(3), None, ALU.mult), inc=s_v)
    K.wait("vector", s_v)
    K.op("vector", lambda e: e.tensor_copy(rowb, row(6)), inc=s_v)
    K.wait("tensor", s_v)
    K.op("tensor", lambda e: e.transpose(c.ps[6][:].bitcast(BF16)[0:8, 0:1], rowb, c.identb[0:1, 0:1]), inc=s_p)
    K.wait("vector", s_p)
    K.op("vector", lambda e: e.tensor_copy(colb, c.ps[6][:].bitcast(BF16)[0:8, 0:1]), inc=s_v)
    K.wait("vector", s_v)
    K.op("vector", lambda e: e.engine_nop(), inc=s_v)
    K.wait("tensor", s_v)
    K.op("tensor", lambda e: e.matmul(c.ps[7][:, 0:1], e8_sb[0:8, 0:128],
         colb, start=True, stop=True), inc=s_p)
    K.wait("vector", s_p)
    K.op("vector", lambda e: e.tensor_copy(c.rs[:, 272:273], c.ps[7][:, 0:1]), inc=s_v)
    K.wait("vector", s_v)
    K.op("vector", lambda e: e.engine_nop(), inc=s_v)
    return s_v


def emit_l1ffn_main(K, c, ast_ap, wbst_ap, bias_ap, s_route, ln_sems, tag=""):
    sv, sa, sp = ln_sems
    s_h = K.sem("s_lf_h" + tag)
    base_h = s_h.cnt
    s_ha = K.sem("s_lf_ha" + tag)
    s_hc = K.sem("s_lf_hc" + tag)
    s_hv = K.sem("s_lf_hv" + tag)
    obf = c.ob[:].bitcast(F32)
    abf = c.rs[:, 0:128].bitcast(BF16)
    ast32 = obf[:, 0:128]
    K.wait("tensor", sv)
    K.wait("vector", s_route)
    cast_done = []
    for dc in range(NDC):
        par = dc % 2
        if dc >= 1:
            K.wait("sync", s_hc, cast_done[dc - 1])
        K.dma("sync", ast32, ast_ap[dc], s_ha)
        K.wait("vector", s_ha)
        if dc >= 2:
            K.wait("vector", s_h, base_h + dc - 1)
        K.op("vector", lambda e, o=abf[:, par * 128:(par + 1) * 128]:
             e.tensor_copy(o, ast32), inc=s_hc)
        cast_done.append(s_hc.cnt)
        K.wait("tensor", s_hc)
        for th in range(2):
            K.op("tensor", lambda e, b=c.ps[4 + th][:, 0:512],
                 l=abf[:, par * 128:(par + 1) * 128],
                 r=c.x_fm[:, dc * T + th * 512: dc * T + th * 512 + 512],
                 st=(dc == 0), spp=(dc == NDC - 1):
                 e.matmul(b, l, r, start=st, stop=spp),
                 inc=(s_h if th == 1 else None))
    K.wait("vector", s_h)
    for th in range(2):
        K.op("vector", lambda e, b=c.ps[4 + th][:, 0:512], o=c.hw[:, th * 512:(th + 1) * 512]:
             e.tensor_scalar(o, b, c.rs[:, 272:273], None, ALU.mult), inc=s_hv)
    K.wait("vector", s_hv)
    K.op("vector", lambda e: e.engine_nop(), inc=s_hv)

    def lhsT_fn(dc, tt):
        if dc < NDC:
            return c.x_fm[:, dc * T + tt * 128: dc * T + (tt + 1) * 128]
        return c.hw[:, tt * 128:(tt + 1) * 128]
    s_ev, s_mm, s_wf = emit_proj_tok(
        K, c, wbst_ap, 8, dest_resid, name="pj", bias_ap=bias_ap,
        n_dc=34, dcg=2, lhsT_fn=lhsT_fn, pe_waits=(s_hv,), dma_waits=(s_hv,))
    return s_ev


def l1a_host(Wq_d, bq_d, W_qk, b_qk, Wkv_d, bkv_d, Wv_u, bv_u, Wo, bo):
    Wc16 = np.concatenate([Wq_d, Wkv_d], axis=1)
    wc16 = np.ascontiguousarray(Wc16.reshape(32, 128, 16).transpose(1, 0, 2)
                                .reshape(128, 512)).astype(np.float32)
    cbias = np.concatenate([bq_d, bkv_d])[None, :].astype(np.float32)
    wqk = W_qk[None if False else slice(None), :].astype(np.float32)
    bqk4 = np.ascontiguousarray(b_qk.reshape(32, 4).T).astype(np.float32)
    wvu = Wv_u.astype(np.float32)
    bvu = np.ascontiguousarray(bv_u.reshape(32, 128, 1)).astype(np.float32)
    wost = w_stream(Wo.astype(np.float32), dcg=4)
    boo = bo[None, :].astype(np.float32)
    return wc16, cbias, wqk, bqk4, wvu, bvu, wost, boo


def emit_l1a_cproj(K, c, wc16_ap, cbias_ap, ckv_own_ap, ln_sems, want_cq=True, tag=""):
    sv, sa, sp = ln_sems
    s_w = K.sem("s_ca_w" + tag)
    s_pe = K.sem("s_ca_pe" + tag)
    s_v = K.sem("s_ca_v" + tag)
    s_o = K.sem("s_ca_o" + tag)
    wbf = c.wbuf[:].bitcast(F32)
    wc16b = c.ob[:, 0:512]
    K.wait("sync", sp)
    K.dma("sync", wbf[:, 0:512], wc16_ap, s_w)
    K.dma("sync", c.rs[0:1, 0:16], cbias_ap, s_w)
    K.wait("vector", s_w)
    K.op("vector", lambda e: e.tensor_copy(wc16b, wbf[:, 0:512]), inc=s_v)
    K.wait("vector", s_v)
    K.wait("tensor", s_v)
    K.wait("tensor", sv)
    cqb = c.rs[:, 16:24].bitcast(BF16)
    ckvb = c.rs[:, 24:26].bitcast(BF16)
    for tt in range(NT):
        if tt > 0:
            K.wait("tensor", s_v)
        for dc in range(NDC):
            K.op("tensor", lambda e, l=c.x_fm[:, dc * T + tt * 128: dc * T + (tt + 1) * 128],
                 r=wc16b[:, dc * 16: dc * 16 + 16], st=(dc == 0), spp=(dc == NDC - 1):
                 e.matmul(c.ps[6][:, 0:16], l, r, start=st, stop=spp), inc=None)
        K.op("tensor", lambda e: e.matmul(c.ps[6][:, 0:16], c.onesr[:], c.rs[0:1, 0:16],
             start=False, stop=True, skip_group_check=True), inc=s_pe)
        K.wait("vector", s_pe)
        if tt >= 1:
            K.wait("vector", s_o)
        K.op("vector", lambda e: e.tensor_copy(ckvb, c.ps[6][:, 12:16]), inc=s_v)
        if want_cq:
            K.op("vector", lambda e: e.tensor_copy(cqb[:, 0:12], c.ps[6][:, 0:12]), inc=s_v)
        K.wait("sync", s_v)
        if tt >= 1:
            K.wait("sync", s_o)
        K.dma("sync", ckv_own_ap[tt * 128:(tt + 1) * 128, :], ckvb, s_o)
        if want_cq:
            K.wait("tensor", s_v)
            K.op("tensor", lambda e: e.transpose(c.ps[7][:].bitcast(BF16)[0:12, 0:128],
                 cqb[:, 0:12], c.identb[:]), inc=s_pe)
            K.wait("vector", s_pe)
            K.op("vector", lambda e, tt=tt: e.tensor_copy(
                c.wbuf[0:12, 6144 + tt * 128: 6144 + (tt + 1) * 128],
                c.ps[7][:].bitcast(BF16)[0:12, 0:128]), inc=s_v)
        K.wait("vector", s_v)
        K.op("vector", lambda e: e.engine_nop(), inc=s_v)
    K.wait("sync", s_o)
    return s_v, s_o


def emit_l1a_attn(K, c, ckv_full_ap, wqk_ap, bqk4_ap, wvu_ap, bvu_ap, attn_dram,
                  s_cq, tag=""):
    s_w = K.sem("s_la_w" + tag)
    s_pe = K.sem("s_la_pe" + tag)
    s_v = K.sem("s_la_v" + tag)
    s_s = K.sem("s_la_s" + tag)
    s_b = K.sem("s_la_b" + tag)
    s_t = K.sem("s_la_t" + tag)
    s_ao = [K.sem("s_la_ao0" + tag), K.sem("s_la_ao1" + tag)]
    obf = c.ob[:].bitcast(F32)
    xff = c.x_fm[:].bitcast(F32)
    hwf = c.hw[:].bitcast(F32)
    ckv_fm = c.wbuf[0:4, 0:4096]
    q4h_sb = c.wbuf[0:4, 4096:5120]
    P = [c.wbuf[:, 5120:5632], c.wbuf[:, 5632:6144]]
    cq_fm = c.wbuf[0:12, 6144:7168]
    tn_sb = c.wbuf[0:4, 7168:7680]
    wqk_b = c.ob[0:12, 640:768]
    bqk4 = obf[0:4, 448:480]
    bvu_col = obf[:, 480:481]
    wvu_b = c.x_fm[0:4, 0:4096]
    aout = [c.ob[:, 128:640], c.wbuf[:, 7680:8192]]
    K.wait("sync", s_cq)
    for g in range(8):
        K.dma("sync", c.ob[:, g * 16:(g + 1) * 16].rearrange("p (a b) -> p a b", b=4),
              ckv_full_ap[g * 512:(g + 1) * 512, :].rearrange("(a p) b -> p a b", p=128),
              s_w, allow_slow_non_contiguous=True)
    K.dma("sync", xff[0:4, 4096:8192], wvu_ap, s_w)
    K.dma("sync", hwf[0:12, 0:128], wqk_ap, s_w)
    K.dma("sync", bqk4, bqk4_ap, s_w)
    K.wait("vector", s_w)
    K.op("vector", lambda e: e.tensor_copy(wvu_b, xff[0:4, 4096:8192]), inc=s_v)
    K.op("vector", lambda e: e.tensor_copy(wqk_b, hwf[0:12, 0:128]), inc=s_v)
    K.wait("vector", s_v)
    K.op("vector", lambda e: e.engine_nop(), inc=s_v)
    K.wait("tensor", s_v)
    for kc in range(NDC):
        if kc >= 2:
            K.wait("tensor", s_v, s_v.cnt)
        K.op("tensor", lambda e, kc=kc: e.transpose(
            c.ps[6 + kc % 2][:].bitcast(BF16)[0:4, 0:128],
            c.ob[:, kc * 4:(kc + 1) * 4], c.identb[:]), inc=s_pe)
        K.wait("vector", s_pe)
        K.op("vector", lambda e, kc=kc: e.tensor_copy(
            ckv_fm[0:4, kc * 128:(kc + 1) * 128],
            c.ps[6 + kc % 2][:].bitcast(BF16)[0:4, 0:128]), inc=s_v)
    K.wait("vector", s_v)
    K.op("vector", lambda e: e.engine_nop(), inc=s_v)
    K.wait("tensor", s_v)
    for h in range(32):
        if h >= 1:
            K.wait("sync", s_v)
        K.dma("sync", bvu_col, bvu_ap[h], s_b)
        K.wait("tensor", s_b)
        if h >= 1:
            K.wait("tensor", s_v)
        for ts in range(2):
            K.op("tensor", lambda e, ts=ts, h=h: e.matmul(
                c.ps[6 + ts][0:4, 0:512], wqk_b[0:12, 4 * h:4 * h + 4],
                cq_fm[0:12, ts * 512:(ts + 1) * 512], start=True, stop=True), inc=s_pe)
        K.wait("vector", s_pe)
        for ts in range(2):
            K.op("vector", lambda e, ts=ts, h=h: e.tensor_scalar(
                q4h_sb[0:4, ts * 512:(ts + 1) * 512], c.ps[6 + ts][0:4, 0:512],
                bqk4[0:4, h:h + 1], 0.5, ALU.add, ALU.mult), inc=s_v)
        K.wait("vector", s_v)
        K.op("vector", lambda e: e.engine_nop(), inc=s_v)
        K.wait("tensor", s_v)
        K.wait("scalar", s_v)
        for qs in range(2):
            K.wait("tensor", s_v)
            sread = {}
            tdone = {}
            SABANK = [0, 1, 4, 5]

            def tden(kc):
                p2 = kc % 2
                K.wait("tensor", s_s, sread[kc])
                K.op("tensor", lambda e, kc=kc, p2=p2: e.matmul(
                    c.ps[2][0:4, 0:512], c.ob[:, kc * 4:(kc + 1) * 4], P[p2],
                    start=(kc == 0), stop=(kc == NDC - 1)), inc=None)
                K.op("tensor", lambda e, kc=kc, p2=p2: e.matmul(
                    c.ps[3][0:1, 0:512], c.onescol[:], P[p2],
                    start=(kc == 0), stop=(kc == NDC - 1)), inc=s_t)
                tdone[kc] = s_t.cnt

            for kc in range(NDC):
                bk = SABANK[kc % 4]
                p2 = kc % 2
                if kc >= 4:
                    K.wait("tensor", s_s, sread[kc - 4])
                K.op("tensor", lambda e, kc=kc, bk=bk, qs=qs: e.matmul(
                    c.ps[bk][:, 0:512], ckv_fm[0:4, kc * 128:(kc + 1) * 128],
                    q4h_sb[0:4, qs * 512:(qs + 1) * 512], start=True, stop=True), inc=s_pe)
                mmv = s_pe.cnt
                if kc >= 1:
                    tden(kc - 1)
                if kc >= 2:
                    K.wait("scalar", s_t, tdone[kc - 2])
                K.wait("scalar", s_pe, mmv)
                K.op("scalar", lambda e, bk=bk, p2=p2: e.activation(
                    P[p2], c.ps[bk][:, 0:512], AF.Exp), inc=s_s)
                sread[kc] = s_s.cnt
            tden(NDC - 1)
            K.wait("vector", s_t)
            K.op("vector", lambda e: e.tensor_copy(c.rs[0:4, 0:512], c.ps[2][0:4, 0:512]), inc=s_v)
            K.op("vector", lambda e: e.reciprocal(hwf[0:1, 0:512], c.ps[3][0:1, 0:512]), inc=s_v)
            K.wait("vector", s_v)
            K.op("vector", lambda e: e.engine_nop(), inc=s_v)
            K.wait("tensor", s_v)
            K.op("tensor", lambda e: e.matmul(c.ps[4][0:4, 0:512], c.onesr[0:1, 0:4],
                 hwf[0:1, 0:512], start=True, stop=True), inc=s_pe)
            K.wait("vector", s_pe)
            K.op("vector", lambda e: e.tensor_tensor(tn_sb, c.rs[0:4, 0:512],
                 c.ps[4][0:4, 0:512], ALU.mult), inc=s_v)
            K.wait("vector", s_v)
            K.op("vector", lambda e: e.engine_nop(), inc=s_v)
            K.wait("tensor", s_v)
            K.op("tensor", lambda e, h=h: e.matmul(c.ps[5][:, 0:512],
                 wvu_b[0:4, h * 128:(h + 1) * 128], tn_sb, start=True, stop=True), inc=s_pe)
            K.wait("vector", s_pe)
            idx = h * 2 + qs
            if idx >= 2:
                K.wait("vector", s_ao[qs])
            K.op("vector", lambda e, qs=qs: e.tensor_scalar(
                aout[qs], c.ps[5][:, 0:512], bvu_col, None, ALU.add), inc=s_v)
            K.wait("sync", s_v)
            K.dma("sync", attn_dram[0:128, h * 1024 + qs * 512: h * 1024 + (qs + 1) * 512],
                  aout[qs], s_ao[qs])
    K.wait("sync", s_ao[0])
    K.wait("sync", s_ao[1])
    return s_ao


def l1b_host_one(codes, scales):
    cst = w_stream(codes.astype(np.int32), dcg=4)
    n_slab = cst.shape[0]
    ng = codes.shape[0] // 512
    rows = []
    for s in range(n_slab):
        os_, g = divmod(s, ng)
        for j in range(2):
            for q in range(2):
                dcs = g * 4 + j * 2 + q
                rows.append(scales[dcs, os_ * 512:(os_ + 1) * 512])
    srow = np.ascontiguousarray(np.stack(rows)).astype(np.float32)
    return cst, srow


def l1b_host(q_codes, q_scales, k_codes, k_scales, v_codes, v_scales, o_codes, o_scales):
    mk_stream = l1b_host_one
    qc, qs_ = mk_stream(q_codes, q_scales)
    kc_, ks_ = mk_stream(k_codes, k_scales)
    vc, vs_ = mk_stream(v_codes, v_scales)
    oc, os_ = mk_stream(o_codes, o_scales)
    return qc, qs_, kc_, ks_, vc, vs_, oc, os_


def rope_host(core):
    b, r = divmod(core, 4)
    pos = np.arange(r * T, (r + 1) * T, dtype=np.float32)
    inv = (1.0 / 10000.0 ** (np.arange(0, 128, 2, dtype=np.float32) / 128.0))
    f = pos[:, None] * inv[None, :]
    cos = np.cos(f).astype(np.float32)
    sin = np.sin(f).astype(np.float32)
    cosq = np.tile(cos, (1, 4)).reshape(8, 128, 256)
    sinq = np.tile(sin, (1, 4)).reshape(8, 128, 256)
    return np.ascontiguousarray(cosq), np.ascontiguousarray(sinq)


def emit_l1b_dequant(K, c, cst_ap, srow_ap, wdq_ap, tag=""):
    s_cd = [K.sem("s_dq_c0" + tag), K.sem("s_dq_c1" + tag)]
    s_sc = [K.sem("s_dq_s0" + tag), K.sem("s_dq_s1" + tag)]
    s_v = K.sem("s_dq_v" + tag)
    s_o = [K.sem("s_dq_o0" + tag), K.sem("s_dq_o1" + tag)]
    wbi = c.wbuf[:].bitcast(I32)
    cpiece = lambda p: wbi[:, 1024 + p * 1024: 2048 + p * 1024]
    scslab = [c.rs[:, 0:512], c.hw[:].bitcast(F32)[:, 0:512]]
    n_slab = cst_ap.shape[0]
    stt_done = []
    for qq in range(2 * n_slab):
        slab, j = divmod(qq, 2)
        cpar = qq % 2
        if qq >= 2:
            K.wait("sync", s_v, stt_done[2 * qq - 3])
        K.dma("sync", cpiece(cpar), cst_ap[slab][:, j * 1024:(j + 1) * 1024],
              s_cd[cpar])
        for q in range(2):
            r = qq * 2 + q
            p2 = r % 2
            if r >= 2:
                K.wait("scalar", s_v, stt_done[r - 2])
            K.dma("scalar", scslab[p2], srow_ap[r, :].partition_broadcast(128),
                  s_sc[p2])
            K.wait("vector", s_sc[p2])
            K.wait("vector", s_cd[cpar])
            if r >= 2:
                K.wait("vector", s_o[p2])
            K.op("vector", lambda e, q=q, p2=p2, cpar=cpar: e.scalar_tensor_tensor(
                c.wbuf[:, 6144 + p2 * 512: 6144 + (p2 + 1) * 512],
                cpiece(cpar)[:, q * 512:(q + 1) * 512], -8.0, scslab[p2],
                ALU.add, ALU.mult), inc=s_v)
            stt_done.append(s_v.cnt)
            K.wait("sync", s_v)
            K.dma("sync", wdq_ap[slab][:, j * 1024 + q * 512: j * 1024 + (q + 1) * 512],
                  c.wbuf[:, 6144 + p2 * 512: 6144 + (p2 + 1) * 512], s_o[p2])
    K.wait("sync", s_o[0])
    K.wait("sync", s_o[1])
    K.wait("scalar", s_v)
    return s_o


def emit_l1b_qkv(K, c, wq_dq_ap, wkv_dq_ap, cosq_ap, sinq_ap, q_own_ap, kv_own_ap,
                 ln_sems, tag=""):
    sv, sa, sp = ln_sems
    hwf = c.hw[:].bitcast(F32)
    s_cs = K.sem("s_qb_cs" + tag)
    s_oo = K.sem("s_qb_oo" + tag)
    cosb = c.ob[:, 0:256]
    sinb = c.ob[:, 256:512]
    ob_i = c.ob[:, 512:1024]

    def rope_dest(out_ap, do_rope):
        def dest(K, c, tt, os, psum, s_ev):
            K.wait("vector", s_ev)
            if do_rope(os):
                K.wait("sync", s_ev)
                K.dma("sync", cosb, cosq_ap[tt], s_cs)
                K.dma("sync", sinb, sinq_ap[tt], s_cs)
                pe_ = psum[:, 0:512:2]
                po_ = psum[:, 1:512:2]
                t1 = hwf[:, 0:256]
                t2 = hwf[:, 256:512]
                K.wait("vector", s_cs)
                K.op("vector", lambda e: e.tensor_tensor(t1, pe_, cosb, ALU.mult), inc=s_ev)
                K.op("vector", lambda e: e.tensor_tensor(t2, po_, sinb, ALU.mult), inc=s_ev)
                K.wait("vector", s_ev)
                K.op("vector", lambda e: e.tensor_sub(ob_i[:, 0:512:2], t1, t2), inc=s_ev)
                K.wait("vector", s_ev)
                K.op("vector", lambda e: e.tensor_tensor(t1, po_, cosb, ALU.mult), inc=s_ev)
                K.op("vector", lambda e: e.tensor_tensor(t2, pe_, sinb, ALU.mult), inc=s_ev)
                K.wait("vector", s_ev)
                K.op("vector", lambda e: e.tensor_add(ob_i[:, 1:512:2], t1, t2), inc=s_ev)
            else:
                K.op("vector", lambda e, p=psum: e.tensor_copy(ob_i, p), inc=s_ev)
            pb = psum.bitcast(BF16)
            stg = c.rs[:].bitcast(BF16)
            K.wait("tensor", s_ev)
            for j in range(4):
                K.op("tensor", lambda e, j=j: e.transpose(
                    pb[0:128, j * 128:(j + 1) * 128],
                    ob_i[:, j * 128:(j + 1) * 128], c.identb[:]), inc=s_ev)
            K.wait("vector", s_ev)
            K.wait("vector", s_oo)
            for j in range(4):
                K.op("vector", lambda e, j=j: e.tensor_copy(
                    stg[:, j * 128:(j + 1) * 128], pb[0:128, j * 128:(j + 1) * 128]),
                    inc=s_ev)
            K.wait("sync", s_ev)
            for j in range(4):
                K.dma("sync", out_ap[os * 512 + j * 128:os * 512 + (j + 1) * 128,
                                     tt * 128:(tt + 1) * 128],
                      stg[:, j * 128:(j + 1) * 128], s_oo)
        return dest

    s_ev1, _, _ = emit_proj_tok(K, c, wq_dq_ap, 8, rope_dest(q_own_ap, lambda os: True),
                                name="pj", prequant=True,
                                pe_waits=(sv,), dma_waits=(sv,))
    s_ev2, _, _ = emit_proj_tok(K, c, wkv_dq_ap, 4, rope_dest(kv_own_ap, lambda os: os < 2),
                                name="pj", prequant=True,
                                pe_waits=(s_ev1,), dma_waits=(s_ev1,))
    K.wait("sync", s_oo)
    return s_ev2, s_oo


def l1b_masks(nc):
    m = np.zeros((4, 128, 512), np.float32)
    for j in range(4):
        for kl in range(128):
            m[j, kl, :] = np.where(j * 128 + kl <= np.arange(512), 0.0, -1e30)
    mt = np.zeros((4, 128, 512), np.float32)
    for i in range(4):
        for ql in range(128):
            mt[i, ql, :] = np.where(np.arange(512) <= i * 128 + ql, 0.0, -1e30)
    return (nc.inline_tensor(m.astype(BF16NP), "l1bmasks").ap(),
            nc.inline_tensor(mt.astype(BF16NP), "l1bmaskst").ap())


def emit_l1b_attn(K, c, qloc_ap, kloc_ap, vloc_ap, masks_aps, attn_own_ap, tag=""):
    masks_ap, maskst_ap = masks_aps
    """Head-sharded causal attention, 2 units x 4 qheads. VARIANT B:
    pass1 rowmax (S_tok), pass2 S^T + exp + mask-add + den + attn@v; divide at evac.
    Inputs are feature-major per-core slices: qloc [4,1024,1024] (g, qdim-window,
    tok), kloc/vloc [4,256,1024] (g, unit*128+feat, tok). Output attn_own_b
    [4096, 1024]: rows (qs//2)*1024 + u*512 + qh*128 + feat, cols token-in-block.
    All big buffers alias the x_fm region."""
    xf = c.x_fm
    K_fm = xf[:, 0:4096]
    V_sb = xf[:, 4096:8192]
    q_fm = xf[:, 8192:12288]
    P = [xf[:, 12288:12800], xf[:, 12800:13312]]
    maskb = [xf[:, 13312 + j * 512: 13312 + (j + 1) * 512] for j in range(4)]
    masktb = [xf[:, 16896 + j * 512: 16896 + (j + 1) * 512] for j in range(4)]
    nm_sb = xf[0:1, 15360:15872]
    aout = [xf[:, 15872:16384], xf[:, 16384:16896]]
    hwf = c.hw[:].bitcast(F32)
    s_kv = K.sem("s_ab_kv" + tag)
    s_q = K.sem("s_ab_q" + tag)
    s_pe = K.sem("s_ab_pe" + tag)
    s_v = K.sem("s_ab_v" + tag)
    s_s = K.sem("s_ab_s" + tag)
    s_t = K.sem("s_ab_t" + tag)
    s_m = K.sem("s_ab_m" + tag)
    s_ao = [K.sem("s_ab_ao0" + tag), K.sem("s_ab_ao1" + tag)]
    s_mk = K.sem("s_ab_mk" + tag)
    s_kf = K.sem("s_ab_kf" + tag)
    s_ksc = K.sem("s_ab_ksc" + tag)
    RSQ = float(1.0 / np.sqrt(128.0))
    for j in range(4):
        K.dma("sync", maskb[j], masks_ap[j], s_mk)
        K.dma("sync", masktb[j], maskst_ap[j], s_mk)
    K.wait("vector", s_mk)
    s_vt = K.sem("s_ab_vt" + tag)
    for u in range(2):
        if u > 0:
            K.wait("sync", s_v)
            K.wait("sync", s_pe)
            K.wait("vector", s_pe)
            K.wait("tensor", s_v)
        for g in range(4):
            K.dma("sync", K_fm[:, g * 1024:(g + 1) * 1024],
                  kloc_ap[g][u * 128:(u + 1) * 128, :], s_kf)
            K.dma("sync", q_fm[:, g * 1024:(g + 1) * 1024],
                  vloc_ap[g][u * 128:(u + 1) * 128, :], s_kv)
        K.wait("vector", s_kf)
        for j in range(8):
            K.op("vector", lambda e, j=j: e.tensor_scalar(
                K_fm[:, j * 512:(j + 1) * 512], K_fm[:, j * 512:(j + 1) * 512],
                RSQ, None, ALU.mult), inc=s_ksc)
        K.wait("tensor", s_kv)
        for kc in range(NDC):
            par = kc % 2
            if kc >= 2:
                K.wait("tensor", s_vt)
            K.op("tensor", lambda e, kc=kc, par=par: e.transpose(
                c.ps[6 + par][:].bitcast(BF16)[0:128, 0:128],
                q_fm[:, kc * 128:(kc + 1) * 128], c.identb[:]), inc=s_pe)
            K.wait("vector", s_pe)
            K.op("vector", lambda e, kc=kc, par=par: e.tensor_copy(
                V_sb[:, kc * 128:(kc + 1) * 128],
                c.ps[6 + par][:].bitcast(BF16)[0:128, 0:128]), inc=s_vt)
        K.wait("tensor", s_vt)
        K.wait("sync", s_vt)
        for qh in range(4):
            if u > 0 or qh > 0:
                K.wait("sync", s_pe)
            for g in range(4):
                K.dma("sync", q_fm[:, g * 1024:(g + 1) * 1024],
                      qloc_ap[g][u * 512 + qh * 128:u * 512 + (qh + 1) * 128, :], s_q)
            K.wait("tensor", s_q)
            K.wait("tensor", s_kv)
            K.wait("tensor", s_ksc)
            K.wait("tensor", s_mk)
            for qs in range(8):
                K.wait("tensor", s_v)
                rd_done = {}
                mcol = lambda i: c.rs[:, 200 + i:201 + i]
                tmp = lambda i: c.rs[:, 204 + i:205 + i]
                for ck in range(qs + 1):
                    if ck >= 1:
                        K.wait("tensor", s_m, rd_done[ck - 1])
                    for i in range(4):
                        K.op("tensor", lambda e, i=i, ck=ck, qs=qs: e.matmul(
                            c.ps[i][:, 0:512],
                            q_fm[:, (qs * 4 + i) * 128:(qs * 4 + i + 1) * 128],
                            K_fm[:, ck * 512:(ck + 1) * 512], start=True, stop=True),
                            inc=(s_pe if i == 3 else None))
                    K.wait("vector", s_pe)
                    if ck == qs:
                        for half in range(2):
                            if half == 1:
                                K.wait("vector", s_m)
                            for i2 in range(2):
                                i = half * 2 + i2
                                K.op("vector", lambda e, i=i, i2=i2: e.tensor_tensor(
                                    c.ps[4 + i2][:, 0:512], c.ps[i][:, 0:512],
                                    masktb[i], ALU.add), inc=s_m)
                            K.wait("vector", s_m)
                            for i2 in range(2):
                                i = half * 2 + i2
                                d = mcol(i) if ck == 0 else tmp(i)
                                K.op("vector", lambda e, d=d, i2=i2: e.reduce_max(
                                    d, c.ps[4 + i2][:, 0:512], axis=AX.X), inc=s_m)
                    else:
                        for i in range(4):
                            d = mcol(i) if ck == 0 else tmp(i)
                            K.op("vector", lambda e, d=d, i=i: e.reduce_max(
                                d, c.ps[i][:, 0:512], axis=AX.X), inc=s_m)
                    if ck > 0:
                        K.wait("vector", s_m)
                        for i in range(4):
                            K.op("vector", lambda e, m=mcol(i), t=tmp(i):
                                 e.tensor_max(m, m, t), inc=s_m)
                    rd_done[ck] = s_m.cnt
                K.wait("vector", s_m)
                nmb = c.rs[:, 208:210].bitcast(BF16)
                K.op("vector", lambda e, n=nmb: e.tensor_scalar(
                    n[:, 0:4], c.rs[:, 200:204], -1.0, None, ALU.mult), inc=s_m)
                K.wait("vector", s_m)
                K.wait("tensor", s_m)
                nm_done = {}
                for i in range(4):
                    if i >= 2:
                        K.wait("tensor", s_m, nm_done[i - 2])
                    K.op("tensor", lambda e, i=i, n=nmb: e.transpose(
                        c.ps[6 + i % 2][:].bitcast(BF16)[0:1, 0:128],
                        n[:, i:i + 1], c.identb[:]), inc=s_pe)
                    K.wait("vector", s_pe)
                    K.op("vector", lambda e, i=i: e.tensor_copy(
                        nm_sb[0:1, i * 128:(i + 1) * 128],
                        c.ps[6 + i % 2][:].bitcast(BF16)[0:1, 0:128]), inc=s_m)
                    nm_done[i] = s_m.cnt
                K.wait("tensor", s_m)
                sread = {}
                tdone = {}
                nkt = 4 * (qs + 1)

                def denpv(kt, nkt=nkt):
                    p2 = kt % 2
                    K.wait("tensor", s_s, sread[kt])
                    K.op("tensor", lambda e, p2=p2, st=(kt == 0), sp=(kt == nkt - 1):
                         e.matmul(c.ps[4][0:1, 0:512], c.onescol[:], P[p2],
                                  start=st, stop=sp), inc=None)
                    K.op("tensor", lambda e, kt=kt, p2=p2, st=(kt == 0),
                         sp=(kt == nkt - 1):
                         e.matmul(c.ps[5][:, 0:512], V_sb[:, kt * 128:(kt + 1) * 128],
                                  P[p2], start=st, stop=sp), inc=s_t)
                    tdone[kt] = s_t.cnt

                for kt in range(nkt):
                    par = kt % 4
                    p2 = kt % 2
                    if kt >= 4:
                        K.wait("tensor", s_s, sread[kt - 4])
                    K.op("tensor", lambda e, kt=kt, par=par, qs=qs: e.matmul(
                        c.ps[par][:, 0:512], K_fm[:, kt * 128:(kt + 1) * 128],
                        q_fm[:, qs * 512:(qs + 1) * 512], start=True, stop=True), inc=None)
                    K.op("tensor", lambda e, par=par: e.matmul(
                        c.ps[par][:, 0:512], c.onesrb[:], nm_sb,
                        start=False, stop=True, skip_group_check=True), inc=s_pe)
                    mmv = s_pe.cnt
                    if kt >= 1:
                        denpv(kt - 1)
                    if kt >= 2:
                        K.wait("scalar", s_t, tdone[kt - 2])
                    if kt >= 4 * qs:
                        j = kt - 4 * qs
                        K.wait("vector", s_pe, mmv)
                        K.wait("vector", s_s)
                        K.op("vector", lambda e, par=par, j=j: e.tensor_tensor(
                            hwf[:, 0:512], c.ps[par][:, 0:512], maskb[j], ALU.add),
                            inc=s_s)
                        K.wait("scalar", s_s)
                        K.op("scalar", lambda e, p2=p2: e.activation(
                            P[p2], hwf[:, 0:512], AF.Exp), inc=s_s)
                    else:
                        K.wait("scalar", s_pe, mmv)
                        K.op("scalar", lambda e, par=par, p2=p2: e.activation(
                            P[p2], c.ps[par][:, 0:512], AF.Exp), inc=s_s)
                    sread[kt] = s_s.cnt
                denpv(nkt - 1)
                K.wait("vector", s_t)
                K.op("vector", lambda e: e.reciprocal(c.rs[0:1, 216:728] if False else hwf[0:1, 0:512], c.ps[4][0:1, 0:512]), inc=s_v)
                K.wait("vector", s_v)
                K.op("vector", lambda e: e.engine_nop(), inc=s_v)
                K.wait("tensor", s_v)
                K.op("tensor", lambda e: e.matmul(c.ps[6][:, 0:512], c.onesr[:],
                     hwf[0:1, 0:512], start=True, stop=True), inc=s_pe)
                K.wait("vector", s_pe)
                K.op("vector", lambda e: e.tensor_copy(hwf[:, 0:512] if False else c.hw[:, 512:1024].bitcast(F32) if False else hwf[:, 0:512], c.ps[5][:, 0:512]), inc=s_v)
                K.wait("vector", s_v)
                apar = qs % 2
                idx = ((u * 4 + qh) * 8 + qs)
                if idx >= 2:
                    K.wait("vector", s_ao[apar])
                K.op("vector", lambda e, apar=apar: e.tensor_tensor(
                    aout[apar], hwf[:, 0:512], c.ps[6][:, 0:512], ALU.mult), inc=s_v)
                K.wait("sync", s_v)
                r0 = (qs // 2) * 1024 + u * 512 + qh * 128
                c0 = (qs % 2) * 512
                K.dma("sync", attn_own_ap[r0:r0 + 128, c0:c0 + 512],
                      aout[apar], s_ao[apar])
    K.wait("sync", s_ao[0])
    K.wait("sync", s_ao[1])
    K.wait("vector", s_ao[0], s_ao[0].cnt)
    K.wait("vector", s_ao[1], s_ao[1].cnt)
    return s_ao



def _pre(nc, K, x_in):
    c = alloc_common(nc, K)
    s_x = emit_load_x(K, c, x_in)
    for e in ("vector", "scalar", "tensor"):
        K.wait(e, s_x)
    ln = emit_ln_xfm(K, c)
    return c, ln


def _build_l0a_A(nc):
    K = MK(nc)
    x_in = nc.dram_tensor("x_in", [T, D], F32, kind="ExternalInput").ap()
    wkvst = nc.dram_tensor("wkvst", [32, 128, 2048], F32, kind="ExternalInput").ap()
    kvbias = nc.dram_tensor("kvbias", [1, 2048], F32, kind="ExternalInput").ap()
    kvout = nc.dram_tensor("kvout", [128, 1032], F32, kind="ExternalOutput").ap()
    kv_dram = nc.dram_tensor("kv_dram", [8, 128, 2048], BF16, kind="Internal").ap()
    c, ln = _pre(nc, K, x_in)
    K.scope("l0a_kvp")
    s_ev, s_kvo = emit_l0a_kvproj(K, c, wkvst, kvbias, kv_dram, ln)
    emit_l0a_kvstate(K, c, kv_dram, s_kvo, kvout)
    return K.build()


def _build_l0a_B(nc):
    K = MK(nc)
    x_in = nc.dram_tensor("x_in", [T, D], F32, kind="ExternalInput").ap()
    kvparts = nc.dram_tensor("kvparts", [4, 128, 1032], F32, kind="ExternalInput").ap()
    wqst = nc.dram_tensor("wqst", [64, 128, 2048], F32, kind="ExternalInput").ap()
    bqa = nc.dram_tensor("bqa", [32, 1, 128], F32, kind="ExternalInput").ap()
    wost = nc.dram_tensor("wost", [64, 128, 2048], F32, kind="ExternalInput").ap()
    bo = nc.dram_tensor("bo", [1, 4096], F32, kind="ExternalInput").ap()
    x_out = nc.dram_tensor("x_out", [T, D], F32, kind="ExternalOutput").ap()
    attn_dram = nc.dram_tensor("attn_dram", [128, 32768], BF16, kind="Internal").ap()
    c, ln = _pre(nc, K, x_in)
    sv, sa, sp = ln
    K.wait("sync", sp)
    K.scope("l0a_red")
    s_kv = emit_l0a_kvreduce_in(K, c, kvparts)
    K.scope("l0a_att")
    s_ao = emit_l0a_attn(K, c, wqst, bqa, attn_dram, s_kv, ln)
    s_ev = emit_l0a_oproj(K, c, attn_dram, wost, bo, s_ao)
    K.wait("sync", s_ev)
    emit_store_x(K, c, x_out)
    return K.build()


def _build_l0ffn(nc):
    K = MK(nc)
    x_in = nc.dram_tensor("x_in", [T, D], F32, kind="ExternalInput").ap()
    wg9 = nc.dram_tensor("wg9", [128, 288], F32, kind="ExternalInput").ap()
    ast = nc.dram_tensor("ast", [32, 128, 128], F32, kind="ExternalInput").ap()
    wbst = nc.dram_tensor("wbst", [136, 128, 1024], F32, kind="ExternalInput").ap()
    bias = nc.dram_tensor("bias", [1, 4096], F32, kind="ExternalInput").ap()
    bthr = nc.dram_tensor("bthr", [1, 1], F32, kind="ExternalInput").ap()
    x_out = nc.dram_tensor("x_out", [T, D], F32, kind="ExternalOutput").ap()
    c = alloc_common(nc, K)
    e8_ap = l0ffn_consts(nc)
    s_x = emit_load_x(K, c, x_in)
    for e in ("vector", "scalar", "tensor"):
        K.wait(e, s_x)
    ln = emit_ln_xfm(K, c)
    s_ev = emit_l0ffn(K, c, wg9, ast, wbst, bias, bthr, e8_ap, ln)
    K.wait("sync", s_ev)
    emit_store_x(K, c, x_out)
    return K.build()


def _build_l0ab(nc):
    K = MK(nc)
    x_in = nc.dram_tensor("x_in", [T, D], F32, kind="ExternalInput").ap()
    maskT = nc.dram_tensor("maskT", [D, D], I32, kind="ExternalInput").ap()
    wst = nc.dram_tensor("wst", [64, 128, 2048], F32, kind="ExternalInput").ap()
    x_out = nc.dram_tensor("x_out", [T, D], F32, kind="ExternalOutput").ap()
    wm = nc.dram_tensor("wm", [64, 128, 2048], BF16, kind="Internal").ap()
    c = alloc_common(nc, K)
    sel4_ap, ind_ap = l0ab_consts(nc)
    s_x = emit_load_x(K, c, x_in)
    for e in ("vector", "scalar", "tensor"):
        K.wait(e, s_x)
    ln_sems = emit_ln_xfm(K, c)
    s_ev = emit_l0ab(K, c, maskT, wst, wm, sel4_ap, ind_ap, ln_sems)
    K.wait("sync", s_ev)
    emit_store_x(K, c, x_out)
    return K.build()


def _build_l1a_A(nc):
    K = MK(nc)
    x_in = nc.dram_tensor("x_in", [T, D], F32, kind="ExternalInput").ap()
    wc16 = nc.dram_tensor("wc16", [128, 512], F32, kind="ExternalInput").ap()
    cbias = nc.dram_tensor("cbias", [1, 16], F32, kind="ExternalInput").ap()
    ckv_own = nc.dram_tensor("ckv_own", [T, 4], BF16, kind="ExternalOutput").ap()
    c, ln = _pre(nc, K, x_in)
    emit_l1a_cproj(K, c, wc16, cbias, ckv_own, ln, want_cq=False)
    return K.build()


def _build_l1a_B(nc):
    K = MK(nc)
    x_in = nc.dram_tensor("x_in", [T, D], F32, kind="ExternalInput").ap()
    ckv_full = nc.dram_tensor("ckv_full", [4096, 4], BF16, kind="ExternalInput").ap()
    wc16 = nc.dram_tensor("wc16", [128, 512], F32, kind="ExternalInput").ap()
    cbias = nc.dram_tensor("cbias", [1, 16], F32, kind="ExternalInput").ap()
    wqk = nc.dram_tensor("wqk", [12, 128], F32, kind="ExternalInput").ap()
    bqk4 = nc.dram_tensor("bqk4", [4, 32], F32, kind="ExternalInput").ap()
    wvu = nc.dram_tensor("wvu", [4, 4096], F32, kind="ExternalInput").ap()
    bvu = nc.dram_tensor("bvu", [32, 128, 1], F32, kind="ExternalInput").ap()
    wost = nc.dram_tensor("wost", [64, 128, 2048], F32, kind="ExternalInput").ap()
    bo = nc.dram_tensor("bo", [1, 4096], F32, kind="ExternalInput").ap()
    x_out = nc.dram_tensor("x_out", [T, D], F32, kind="ExternalOutput").ap()
    ckv_own2 = nc.dram_tensor("ckv_own2", [T, 4], BF16, kind="Internal").ap()
    attn_dram = nc.dram_tensor("attn_dram", [128, 32768], BF16, kind="Internal").ap()
    c, ln = _pre(nc, K, x_in)
    s_v, s_o = emit_l1a_cproj(K, c, wc16, cbias, ckv_own2, ln, want_cq=True)
    s_ao = emit_l1a_attn(K, c, ckv_full, wqk, bqk4, wvu, bvu, attn_dram, s_o)
    s_ev = emit_l0a_oproj(K, c, attn_dram, wost, bo, s_ao)
    K.wait("sync", s_ev)
    emit_store_x(K, c, x_out)
    return K.build()


def _build_l1ffn_A(nc):
    K = MK(nc)
    x_in = nc.dram_tensor("x_in", [T, D], F32, kind="ExternalInput").ap()
    rw8 = nc.dram_tensor("rw8", [128, 256], F32, kind="ExternalInput").ap()
    selb = nc.dram_tensor("selb", [128, 8], BF16, kind="ExternalInput").ap()
    rpart = nc.dram_tensor("rpart", [1, 8], F32, kind="ExternalOutput").ap()
    c, ln = _pre(nc, K, x_in)
    emit_l1ffn_router_partial(K, c, rw8, selb, rpart, ln)
    return K.build()


def _build_l1ffn_B(nc):
    K = MK(nc)
    x_in = nc.dram_tensor("x_in", [T, D], F32, kind="ExternalInput").ap()
    rparts = nc.dram_tensor("rparts", [4, 1, 8], F32, kind="ExternalInput").ap()
    rb = nc.dram_tensor("rb", [1, 8], F32, kind="ExternalInput").ap()
    ast = nc.dram_tensor("ast", [32, 128, 128], F32, kind="ExternalInput").ap()
    wbst = nc.dram_tensor("wbst", [136, 128, 1024], F32, kind="ExternalInput").ap()
    bias = nc.dram_tensor("bias", [1, 4096], F32, kind="ExternalInput").ap()
    x_out = nc.dram_tensor("x_out", [T, D], F32, kind="ExternalOutput").ap()
    c = alloc_common(nc, K)
    e8_ap = l0ffn_consts(nc)
    s_x = emit_load_x(K, c, x_in)
    for e in ("vector", "scalar", "tensor"):
        K.wait(e, s_x)
    ln = emit_ln_xfm(K, c)
    s_route = emit_l1ffn_routing(K, c, rparts, rb, e8_ap)
    s_ev = emit_l1ffn_main(K, c, ast, wbst, bias, s_route, ln)
    K.wait("sync", s_ev)
    emit_store_x(K, c, x_out)
    return K.build()


def _build_l1b_A(nc):
    K = MK(nc)
    x_in = nc.dram_tensor("x_in", [T, D], F32, kind="ExternalInput").ap()
    qcst = nc.dram_tensor("qcst", [64, 128, 2048], I32, kind="ExternalInput").ap()
    qsrow = nc.dram_tensor("qsrow", [256, 1, 512], F32, kind="ExternalInput").ap()
    kvcst = nc.dram_tensor("kvcst", [32, 128, 2048], I32, kind="ExternalInput").ap()
    kvsrow = nc.dram_tensor("kvsrow", [128, 1, 512], F32, kind="ExternalInput").ap()
    cosq = nc.dram_tensor("cosq", [8, 128, 256], BF16, kind="ExternalInput").ap()
    sinq = nc.dram_tensor("sinq", [8, 128, 256], BF16, kind="ExternalInput").ap()
    q_own = nc.dram_tensor("q_own", [T, 4096], BF16, kind="ExternalOutput").ap()
    kv_own = nc.dram_tensor("kv_own", [T, 2048], BF16, kind="ExternalOutput").ap()
    wq_dq = nc.dram_tensor("wq_dq", [64, 128, 2048], BF16, kind="Internal").ap()
    wkv_dq = nc.dram_tensor("wkv_dq", [32, 128, 2048], BF16, kind="Internal").ap()
    c, ln = _pre(nc, K, x_in)
    sv, sa, sp = ln
    K.wait("sync", sp)
    K.wait("vector", sp)
    K.scope("l1b_dq")
    s_o1 = emit_l1b_dequant(K, c, qcst, qsrow, wq_dq, tag="dq")
    s_o2 = emit_l1b_dequant(K, c, kvcst, kvsrow, wkv_dq, tag="dq")
    for w in s_o1 + s_o2:
        K.wait("sync", w)
        K.wait("vector", w)
    emit_l1b_qkv(K, c, wq_dq, wkv_dq, cosq, sinq, q_own, kv_own, ln)
    return K.build()


def _build_l1b_B(nc):
    K = MK(nc)
    qu = nc.dram_tensor("qu", [2, 4096, 512], BF16, kind="ExternalInput").ap()
    ku = nc.dram_tensor("ku", [2, 4096, 128], BF16, kind="ExternalInput").ap()
    vu = nc.dram_tensor("vu", [2, 4096, 128], BF16, kind="ExternalInput").ap()
    attn_own = nc.dram_tensor("attn_own", [1024, 4096], BF16, kind="ExternalOutput").ap()
    c = alloc_common(nc, K)
    masks_aps = l1b_masks(nc)
    K.wait("sync", c.s_const)
    K.wait("vector", c.s_const)
    K.wait("tensor", c.s_const)
    emit_l1b_attn(K, c, qu, ku, vu, masks_aps, attn_own)
    return K.build()


def _build_l1b_C(nc):
    K = MK(nc)
    x_in = nc.dram_tensor("x_in", [T, D], F32, kind="ExternalInput").ap()
    ocst = nc.dram_tensor("ocst", [64, 128, 2048], I32, kind="ExternalInput").ap()
    osrow = nc.dram_tensor("osrow", [256, 1, 512], F32, kind="ExternalInput").ap()
    attn_fm = nc.dram_tensor("attn_fm", [128, 32768], BF16, kind="ExternalInput").ap()
    x_out = nc.dram_tensor("x_out", [T, D], F32, kind="ExternalOutput").ap()
    wo_dq = nc.dram_tensor("wo_dq", [64, 128, 2048], BF16, kind="Internal").ap()
    c = alloc_common(nc, K)
    s_x = emit_load_x(K, c, x_in)
    s_o1 = emit_l1b_dequant(K, c, ocst, osrow, wo_dq, tag="dq")
    for w in s_o1:
        K.wait("sync", w)
        K.wait("vector", w)
    s_ev = emit_l0a_oproj(K, c, attn_fm, wo_dq, None, [], prequant=True, name="l1bo")
    K.wait("sync", s_ev)
    emit_store_x(K, c, x_out)
    return K.build()


def _bar(K, sems, engines=("sync", "scalar", "vector", "tensor", "gpsimd")):
    for e in engines:
        for s in sems:
            K.wait(e, s)


def _build_fused(nc):
    K = MK(nc)
    EI, INT = "ExternalInput", "Internal"
    dt = nc.dram_tensor
    x_in = dt("x_in", [T, D], F32, kind=EI).ap()
    x_out = dt("x_out", [T, D], F32, kind="ExternalOutput").ap()
    wkvst = dt("wkvst", [32, 128, 2048], F32, kind=EI).ap()
    kvbias = dt("kvbias", [1, 2048], F32, kind=EI).ap()
    wqst = dt("wqst", [64, 128, 2048], F32, kind=EI).ap()
    bqa = dt("bqa", [32, 1, 128], F32, kind=EI).ap()
    wost0 = dt("wost0", [64, 128, 2048], F32, kind=EI).ap()
    bo0 = dt("bo0", [1, 4096], F32, kind=EI).ap()
    wg9 = dt("wg9", [128, 288], F32, kind=EI).ap()
    ast0 = dt("ast0", [32, 128, 128], F32, kind=EI).ap()
    wbst0 = dt("wbst0", [136, 128, 1024], F32, kind=EI).ap()
    bias0 = dt("bias0", [1, 4096], F32, kind=EI).ap()
    bthr0 = dt("bthr0", [1, 1], F32, kind=EI).ap()
    maskT = dt("maskT", [D, D], I32, kind=EI).ap()
    wst_ab = dt("wst_ab", [64, 128, 2048], F32, kind=EI).ap()
    wc16 = dt("wc16", [128, 512], F32, kind=EI).ap()
    cbias = dt("cbias", [1, 16], F32, kind=EI).ap()
    wqk = dt("wqk", [12, 128], F32, kind=EI).ap()
    bqk4 = dt("bqk4", [4, 32], F32, kind=EI).ap()
    wvu = dt("wvu", [4, 4096], F32, kind=EI).ap()
    bvu = dt("bvu", [32, 128, 1], F32, kind=EI).ap()
    wost1 = dt("wost1", [64, 128, 2048], F32, kind=EI).ap()
    bo1 = dt("bo1", [1, 4096], F32, kind=EI).ap()
    rw8 = dt("rw8", [128, 256], F32, kind=EI).ap()
    selb = dt("selb", [128, 8], BF16, kind=EI).ap()
    rb = dt("rb", [1, 8], F32, kind=EI).ap()
    ast1 = dt("ast1", [32, 128, 128], F32, kind=EI).ap()
    wbst1 = dt("wbst1", [136, 128, 1024], F32, kind=EI).ap()
    bias1 = dt("bias1", [1, 4096], F32, kind=EI).ap()
    qcst = dt("qcst", [64, 128, 2048], I32, kind=EI).ap()
    qsrow = dt("qsrow", [256, 512], F32, kind=EI).ap()
    kvcst = dt("kvcst", [32, 128, 2048], I32, kind=EI).ap()
    kvsrow = dt("kvsrow", [128, 512], F32, kind=EI).ap()
    ocst = dt("ocst", [64, 128, 2048], I32, kind=EI).ap()
    osrow = dt("osrow", [256, 512], F32, kind=EI).ap()
    cosq = dt("cosq", [8, 128, 256], BF16, kind=EI).ap()
    sinq = dt("sinq", [8, 128, 256], BF16, kind=EI).ap()
    kv_dram = dt("kv_dram", [8, 128, 2048], BF16, kind=INT).ap()
    kvout_own = dt("kvout_own", [128, 1032], F32, kind=INT).ap()
    kvg = dt("kvg", [512, 1032], F32, kind=INT).ap()
    attn_dram = dt("attn_dram", [128, 32768], BF16, kind=INT).ap()
    ckv_own = dt("ckv_own", [T, 4], BF16, kind=INT).ap()
    ckvg = dt("ckvg", [4096, 4], BF16, kind=INT).ap()
    rpart1 = dt("rpart1", [1, 8], F32, kind=INT).ap()
    rg1 = dt("rg1", [4, 8], F32, kind=INT).ap()
    rpart2 = dt("rpart2", [1, 8], F32, kind=INT).ap()
    rg2 = dt("rg2", [4, 8], F32, kind=INT).ap()
    wq_dq = dt("wq_dq", [64, 128, 2048], BF16, kind=INT).ap()
    wkv_dq = dt("wkv_dq", [32, 128, 2048], BF16, kind=INT).ap()
    wo_dq = dt("wo_dq", [64, 128, 2048], BF16, kind=INT).ap()
    q_own = dt("q_own", [4096, 1024], BF16, kind=INT).ap()
    kv_own = dt("kv_own", [2048, 1024], BF16, kind=INT).ap()
    qg = dt("qg", [16384, 1024], BF16, kind=INT).ap()
    kvg2 = dt("kvg2", [8192, 1024], BF16, kind=INT).ap()
    qloc = dt("qloc", [4, 1024, 1024], BF16, kind=INT).ap()
    kloc = dt("kloc", [4, 256, 1024], BF16, kind=INT).ap()
    vloc = dt("vloc", [4, 256, 1024], BF16, kind=INT).ap()
    attn_own = dt("attn_own", [4096, 1024], BF16, kind=INT).ap()
    ag = dt("ag", [16384, 1024], BF16, kind=INT).ap()
    aloc = dt("aloc", [4, 1024, 1024], BF16, kind=INT).ap()

    c = alloc_common(nc, K)
    e8_ap = l0ffn_consts(nc)
    sel4_ap, ind_ap = l0ab_consts(nc)
    masks_aps = l1b_masks(nc)
    s_x = emit_load_x(K, c, x_in)
    for e in ("vector", "scalar", "tensor"):
        K.wait(e, s_x)

    K.scope("l0a")
    ln = emit_ln_xfm(K, c, tag="")
    K.scope("l0a_kvp")
    s_ev, s_kvo = emit_l0a_kvproj(K, c, wkvst, kvbias, kv_dram, ln)
    K.scope("l0a_kvs")
    s_ko = emit_l0a_kvstate(K, c, kv_dram, s_kvo, kvout_own)
    s_ag1 = emit_allgather(K, kvout_own, kvg, "ag", wait_sems=(s_ko,))
    _bar(K, [s_ev, s_ko])
    K.wait("sync", s_ag1)
    K.wait("vector", s_ag1)
    kvparts = [kvg[r * 128:(r + 1) * 128, :] for r in range(4)]
    K.scope("l0a_red")
    s_kv = emit_l0a_kvreduce_in(K, c, kvparts)
    K.scope("l0a_att")
    s_ao = emit_l0a_attn(K, c, wqst, bqa, attn_dram, s_kv, ln)
    K.scope("l0a_opj")
    s_ev0 = emit_l0a_oproj(K, c, attn_dram, wost0, bo0, s_ao, name="pj")
    _bar(K, [s_ev0])

    K.scope("l0f1")
    ln = emit_ln_xfm(K, c, tag="")
    s_ev1 = emit_l0ffn(K, c, wg9, ast0, wbst0, bias0, bthr0, e8_ap, ln, tag="")
    _bar(K, [s_ev1])

    K.scope("l0ab")
    ln = emit_ln_xfm(K, c, tag="")
    wm = dt("wm", [64, 128, 2048], BF16, kind=INT).ap()
    s_ev2 = emit_l0ab(K, c, maskT, wst_ab, wm, sel4_ap, ind_ap, ln)
    _bar(K, [s_ev2])

    K.scope("l0f2")
    ln = emit_ln_xfm(K, c, tag="")
    s_ev3 = emit_l0ffn(K, c, wg9, ast0, wbst0, bias0, bthr0, e8_ap, ln, tag="")
    _bar(K, [s_ev3])

    K.scope("l1a")
    ln = emit_ln_xfm(K, c, tag="")
    K.scope("l1a_cp")
    s_v, s_o = emit_l1a_cproj(K, c, wc16, cbias, ckv_own, ln, want_cq=True)
    s_ag2 = emit_allgather(K, ckv_own, ckvg, "ag", wait_sems=(s_o,))
    K.wait("sync", s_ag2)
    K.scope("l1a_att")
    s_ao = emit_l1a_attn(K, c, ckvg, wqk, bqk4, wvu, bvu, attn_dram, s_o)
    K.scope("l1a_opj")
    s_ev4 = emit_l0a_oproj(K, c, attn_dram, wost1, bo1, s_ao, name="pj")
    _bar(K, [s_ev4])

    K.scope("l1f1")
    ln = emit_ln_xfm(K, c, tag="")
    s_rp = emit_l1ffn_router_partial(K, c, rw8, selb, rpart1, ln, tag="")
    s_ag3 = emit_allgather(K, rpart1, rg1, "ag", wait_sems=(s_rp,))
    K.wait("sync", s_ag3)
    rg1l = [rg1[r:r + 1, :] for r in range(4)]
    s_route = emit_l1ffn_routing(K, c, rg1l, rb, e8_ap, tag="")
    s_ev5 = emit_l1ffn_main(K, c, ast1, wbst1, bias1, s_route, ln, tag="")
    _bar(K, [s_ev5])

    K.scope("l1b")
    ln = emit_ln_xfm(K, c, tag="")
    sv6, sa6, sp6 = ln
    K.wait("sync", sp6)
    K.wait("vector", sp6)
    K.scope("l1b_dq")
    s_o1 = emit_l1b_dequant(K, c, qcst, qsrow, wq_dq, tag="dq")
    s_o2 = emit_l1b_dequant(K, c, kvcst, kvsrow, wkv_dq, tag="dq")
    for w in s_o1 + s_o2:
        K.wait("sync", w)
        K.wait("vector", w)
    K.scope("l1b_qkv")
    s_ev6, s_oo = emit_l1b_qkv(K, c, wq_dq, wkv_dq, cosq, sinq, q_own, kv_own, ln)
    s_ag4 = emit_allgather(K, q_own, qg, "ag", wait_sems=(s_oo, s_ev6))
    s_ag5 = emit_allgather(K, kv_own, kvg2, "ag")
    _bar(K, [s_ev6, s_oo], engines=("sync", "vector", "scalar", "tensor"))
    K.scope("l1b_dqo")
    s_o3 = emit_l1b_dequant(K, c, ocst, osrow, wo_dq, tag="dq")
    K.wait("gpsimd", s_ag4)
    K.wait("gpsimd", s_ag5)
    s_sl = K.sem("s_slice")
    pid = nc.partition_id()
    off = pid % 4
    for g in range(4):
        K.dma("gpsimd", qloc[g], qg[ds(g * 4096 + off * 1024, 1024), 0:1024], s_sl)
        K.dma("gpsimd", kloc[g], kvg2[ds(g * 2048 + off * 256, 256), 0:1024], s_sl)
        K.dma("gpsimd", vloc[g], kvg2[ds(g * 2048 + 1024 + off * 256, 256), 0:1024], s_sl)
    for w in s_o3:
        K.wait("sync", w)
        K.wait("vector", w)
        K.wait("tensor", w)
    K.wait("sync", s_sl)
    K.wait("tensor", s_sl)
    K.scope("l1b_att")
    s_ao2 = emit_l1b_attn(K, c, qloc, kloc, vloc, masks_aps, attn_own)
    s_ag6 = emit_allgather(K, attn_own, ag, "ag", wait_sems=list(s_ao2))
    K.wait("gpsimd", s_ag6)
    for g in range(4):
        K.dma("gpsimd", aloc[g], ag[ds(g * 4096 + off * 1024, 1024), 0:1024], s_sl)
    K.wait("sync", s_sl)
    attn_list = [aloc[dc // 8][(dc % 8) * 128:(dc % 8) * 128 + 128, :]
                 for dc in range(NDC)]
    K.scope("l1b_opj")
    s_ev7 = emit_l0a_oproj(K, c, None, wo_dq, None, [], prequant=True,
                           name="pj", attn_list=attn_list)
    _bar(K, [s_ev7])

    K.scope("l1f2")
    ln = emit_ln_xfm(K, c, tag="")
    s_rp2 = emit_l1ffn_router_partial(K, c, rw8, selb, rpart2, ln, tag="")
    s_ag7 = emit_allgather(K, rpart2, rg2, "ag", wait_sems=(s_rp2,))
    K.wait("sync", s_ag7)
    rg2l = [rg2[r:r + 1, :] for r in range(4)]
    s_route2 = emit_l1ffn_routing(K, c, rg2l, rb, e8_ap, tag="")
    s_ev8 = emit_l1ffn_main(K, c, ast1, wbst1, bias1, s_route2, ln, tag="")
    K.wait("sync", s_ev8)
    emit_store_x(K, c, x_out)
    return K.build()


def kernel(**inputs):
    inp = {k: np.asarray(v) for k, v in inputs.items()}
    C = list(range(8))
    x_full = np.ascontiguousarray(inp["L0_attn_a_hidden_states"].astype(np.float32))
    shards = shard_tokens(x_full)

    wkvst, kvbias, wqst, bqa, wost0, bo0 = l0a_host(
        inp["L0_attn_a_Wq"], inp["L0_attn_a_bq"], inp["L0_attn_a_Wk"],
        inp["L0_attn_a_bk"], inp["L0_attn_a_Wv"], inp["L0_attn_a_bv"],
        inp["L0_attn_a_Wo"], inp["L0_attn_a_bo"])
    wg9, ast0, wbst0 = l0ffn_host(
        inp["L0_ffn_W_base"], inp["L0_ffn_W_gate"], inp["L0_ffn_W_thr"],
        inp["L0_ffn_A_w"], inp["L0_ffn_B_w"])
    bias0 = np.ascontiguousarray(inp["L0_ffn_b_base"][None, :]).astype(np.float32)
    bthr0 = np.ascontiguousarray(inp["L0_ffn_b_thr"][None, :]).astype(np.float32)
    wst_ab = w_stream(np.ascontiguousarray(
        inp["L0_attn_b_weight"].astype(np.float32).T), dcg=4)
    maskT = np.ascontiguousarray(inp["L0_attn_b_mask"].astype(np.int32).T)
    wc16, cbias, wqk, bqk4, wvu, bvu, wost1, bo1 = l1a_host(
        inp["L1_attn_a_Wq_d"], inp["L1_attn_a_bq_d"], inp["L1_attn_a_W_qk"],
        inp["L1_attn_a_b_qk"], inp["L1_attn_a_Wkv_d"], inp["L1_attn_a_bkv_d"],
        inp["L1_attn_a_Wv_u"], inp["L1_attn_a_bv_u"], inp["L1_attn_a_Wo"],
        inp["L1_attn_a_bo"])
    rw8, ast1, wbst1 = l1ffn_host(
        inp["L1_ffn_W"], inp["L1_ffn_A"], inp["L1_ffn_B"], inp["L1_ffn_router_W"])
    bias1 = np.ascontiguousarray(inp["L1_ffn_b"][None, :]).astype(np.float32)
    rbr = np.ascontiguousarray(inp["L1_ffn_router_b"][None, :]).astype(np.float32)
    kvc = np.concatenate([inp["L1_attn_b_k_codes"], inp["L1_attn_b_v_codes"]],
                         axis=1).astype(np.int32)
    kvs = np.concatenate([inp["L1_attn_b_k_scales"], inp["L1_attn_b_v_scales"]],
                         axis=1).astype(np.float32)
    qcst, qsrow = l1b_host_one(inp["L1_attn_b_q_codes"].astype(np.int32),
                               inp["L1_attn_b_q_scales"].astype(np.float32))
    kvcst, kvsrow = l1b_host_one(kvc, kvs)
    ocst, osrow = l1b_host_one(inp["L1_attn_b_o_codes"].astype(np.int32),
                               inp["L1_attn_b_o_scales"].astype(np.float32))
    ropes = [rope_host(cc) for cc in C]

    nc = make_nc("fused")
    _build_fused(nc)
    in_maps = []
    for cc in C:
        in_maps.append({
            "x_in": shards[cc],
            "wkvst": wkvst, "kvbias": kvbias, "wqst": wqst, "bqa": bqa,
            "wost0": wost0, "bo0": bo0,
            "wg9": wg9, "ast0": ast0, "wbst0": wbst0, "bias0": bias0,
            "bthr0": bthr0,
            "maskT": maskT, "wst_ab": wst_ab,
            "wc16": wc16, "cbias": cbias, "wqk": wqk, "bqk4": bqk4,
            "wvu": wvu, "bvu": bvu, "wost1": wost1, "bo1": bo1,
            "rw8": rw8, "selb": sel_host(cc), "rb": rbr,
            "ast1": ast1, "wbst1": wbst1, "bias1": bias1,
            "qcst": qcst, "qsrow": qsrow, "kvcst": kvcst, "kvsrow": kvsrow,
            "ocst": ocst, "osrow": osrow,
            "cosq": ropes[cc][0].astype(BF16NP),
            "sinq": ropes[cc][1].astype(BF16NP),
        })
    res = run_bass_kernel_spmd(nc, in_maps, core_ids=C)
    shards = [res.results[cc]["x_out"] for cc in C]
    return unshard_tokens(shards).astype(np.float32)


# version 05 (final)
# speedup vs baseline: 1.2747x; 1.0152x over previous
import numpy as np
import ml_dtypes
import concourse.bass as bass
import concourse.mybir as mybir
from concourse.bass import ds
from concourse.bass_utils import run_bass_kernel_spmd

F32 = mybir.dt.float32
BF16 = mybir.dt.bfloat16
I32 = mybir.dt.int32
AF = mybir.ActivationFunctionType
ALU = mybir.AluOpType
AX = mybir.AxisListType
BF16NP = ml_dtypes.bfloat16

NCORES = 8
T = 1024
NT = 8
D = 4096
NDC = 32
GROUPS = [[0, 1, 2, 3], [4, 5, 6, 7]]

ENGINES = ("sync", "scalar", "vector", "tensor", "gpsimd")



def w_stream(WT, os_width=512, dcg=8):
    d, o = WT.shape
    ndc = d // 128
    ng = ndc // dcg
    nos = o // os_width
    a = WT.reshape(ng, dcg, 128, nos, os_width)
    a = a.transpose(3, 0, 2, 1, 4)
    return np.ascontiguousarray(a.reshape(nos * ng, 128, dcg * os_width))


def shard_tokens(x_full):
    out = []
    for c in range(NCORES):
        b, r = divmod(c, 4)
        out.append(np.ascontiguousarray(x_full[b, r * T:(r + 1) * T, :]))
    return out


def unshard_tokens(shards):
    x = np.empty((2, 4096, D), dtype=shards[0].dtype)
    for c in range(NCORES):
        b, r = divmod(c, 4)
        x[b, r * T:(r + 1) * T, :] = shards[c]
    return x



class Sem:
    __slots__ = ("h", "cnt", "name")

    def __init__(self, h, name):
        self.h = h
        self.cnt = 0
        self.name = name


class MK:
    def __init__(self, nc):
        self.nc = nc
        self.q = {e: [] for e in ENGINES}
        self.sems = {}
        self.cur_scope = "init"

    def scope(self, name):
        self.cur_scope = name

    def sem(self, name):
        if name not in self.sems:
            self.sems[name] = Sem(self.nc.alloc_semaphore(name), name)
        return self.sems[name]

    def _push(self, eng, fn):
        self.q[eng].append((self.cur_scope, fn))

    def wait(self, eng, sem, val=None):
        v = sem.cnt if val is None else val
        self._push(eng, lambda e, s=sem, v=v: e.wait_ge(s.h, v))

    def op(self, eng, builder, inc=None, inc_n=1):
        if inc is not None:
            inc.cnt += inc_n
            self._push(eng, lambda e, b=builder, s=inc, n=inc_n: b(e).then_inc(s.h, n))
        else:
            self._push(eng, lambda e, b=builder: b(e))

    def dma(self, eng, out, in_, sem, **kw):
        self.op(eng, lambda e, o=out, i=in_, k=dict(kw): e.dma_start(out=o, in_=i, **k),
                inc=sem, inc_n=16)
        return sem.cnt

    def barrier(self, name="bar", engines=None):
        engs = [e for e in (engines or ENGINES) if self.q[e]]
        sems = {e: self.sem(f"bar_{e}") for e in engs}
        for e in engs:
            s = sems[e]
            s.cnt += 1
            self._push(e, lambda eng, s=s: eng.drain().then_inc(s.h, 1))
        for e in engs:
            for e2 in engs:
                if e2 != e:
                    self.wait(e, sems[e2])

    def build(self):
        nc = self.nc
        with nc.Block() as block:
            for eng_name in ENGINES:
                ops = self.q[eng_name]
                if not ops:
                    continue
                def runner(e, ops=ops):
                    i = 0
                    while i < len(ops):
                        sc = ops[i][0]
                        j = i
                        with nc.named_scope(sc):
                            while j < len(ops) and ops[j][0] == sc:
                                ops[j][1](e)
                                j += 1
                        i = j
                getattr(block, eng_name)(runner)
        return nc


import os as _os
DYN_SCRATCH = int(_os.environ.get("MK_DYN_SCRATCH", "4096"))


def make_nc(name="mk"):
    return bass.Bass("TRN2", target_bir_lowering=False, name=name,
                     dynamic_dma_scratch_size=DYN_SCRATCH)



class Ctx:
    pass


def alloc_common(nc, K):
    c = Ctx()
    c.nc = nc
    c.x_res = nc.alloc_sbuf_tensor("x_res", [128, NT * D], F32)
    c.x_fm = nc.alloc_sbuf_tensor("x_fm", [128, NDC * T], BF16)
    c.wbuf = nc.alloc_sbuf_tensor("wbuf", [128, 2 * 4096], BF16)
    c.stats = nc.alloc_sbuf_tensor("stats", [128, 8 * 8], F32)
    c.ident = nc.alloc_sbuf_tensor("ident", [128, 128], F32)
    c.onesr = nc.alloc_sbuf_tensor("onesr", [1, 128], F32)
    c.hw = nc.alloc_sbuf_tensor("hw", [128, 1024], BF16)
    c.ps = [nc.alloc_psum_tensor(f"ps{i}", [128, 512], F32) for i in range(8)]
    c.identb = nc.alloc_sbuf_tensor("identb", [128, 128], BF16)
    c.rs = nc.alloc_sbuf_tensor("rs", [128, 512], F32)
    c.ob = nc.alloc_sbuf_tensor("ob", [128, 1024], BF16)
    c.onescol = nc.alloc_sbuf_tensor("onescol", [128, 1], BF16)
    c.onesrb = nc.alloc_sbuf_tensor("onesrb", [1, 128], BF16)
    c.bmT01 = nc.alloc_sbuf_tensor("bmT01", [128, 128], BF16)
    c.ident_np = np.eye(128, dtype=np.float32)
    c.ones_np = np.ones((1, 128), dtype=np.float32)
    identc = nc.inline_tensor(c.ident_np, "identc").ap()
    identbc = nc.inline_tensor(c.ident_np.astype(BF16NP), "identbc").ap()
    onesc = nc.inline_tensor(c.ones_np, "onesc").ap()
    s = K.sem("s_const")
    K.dma("sync", c.ident[:], identc, s)
    K.dma("sync", c.identb[:], identbc, s)
    K.dma("sync", c.onesr[:], onesc, s)
    onescolc = nc.inline_tensor(np.ones((128, 1), np.float32).astype(BF16NP), "onescolc").ap()
    K.dma("sync", c.onescol[:], onescolc, s)
    onesrbc = nc.inline_tensor(c.ones_np.astype(BF16NP), "onesrbc").ap()
    K.dma("sync", c.onesrb[:], onesrbc, s)
    c.s_const = s
    for e in ("sync", "scalar", "vector", "tensor", "gpsimd"):
        K.wait(e, s)
    return c


def st_col(c, slot, tt):
    return c.stats[:, slot * 8 + tt: slot * 8 + tt + 1]


def st_all(c, slot):
    return c.stats[:, slot * 8: slot * 8 + 8]



def emit_load_x(K, c, x_in_ap):
    s = K.sem("s_xload")
    for tt in range(NT):
        K.dma("sync", c.x_res[:, tt * D:(tt + 1) * D],
              x_in_ap[tt * 128:(tt + 1) * 128, :], s)
    return s


def emit_store_x(K, c, x_out_ap):
    s = K.sem("s_xstore")
    for tt in range(NT):
        K.dma("sync", x_out_ap[tt * 128:(tt + 1) * 128, :],
              c.x_res[:, tt * D:(tt + 1) * D], s)
    K.wait("sync", s)
    return s


def emit_ln_xfm(K, c, eps=1e-5, want_xfm=True, tag=""):
    sv = K.sem("s_ln_v" + tag)
    sa = K.sem("s_ln_a" + tag)
    sp = K.sem("s_ln_pe" + tag)
    for tt in range(NT):
        xt = c.x_res[:, tt * D:(tt + 1) * D]
        K.op("vector", lambda e, o=st_col(c, 0, tt), i=xt:
             e.reduce_sum(o, i, axis=AX.X), inc=sv)
        K.op("scalar", lambda e, o=c.x_fm[:, tt * 4096:(tt + 1) * 4096], i=xt,
             a=st_col(c, 1, tt):
             e.activation(o, i, AF.Square, accum_out=a), inc=sa)
    K.wait("vector", sa)
    K.wait("vector", sv)
    inv_d = 1.0 / D
    K.op("vector", lambda e: e.tensor_scalar(st_all(c, 2), st_all(c, 0), inv_d, None, ALU.mult), inc=sv)
    K.op("vector", lambda e: e.tensor_scalar(st_all(c, 3), st_all(c, 1), inv_d, None, ALU.mult), inc=sv)
    K.wait("vector", sv)
    K.op("vector", lambda e: e.tensor_tensor(st_all(c, 6), st_all(c, 2), st_all(c, 2), ALU.mult), inc=sv)
    K.wait("vector", sv)
    K.op("vector", lambda e: e.tensor_tensor(st_all(c, 3), st_all(c, 3), st_all(c, 6), ALU.subtract), inc=sv)
    K.wait("vector", sv)
    K.op("vector", lambda e: e.tensor_scalar(st_all(c, 3), st_all(c, 3), float(eps), None, ALU.add), inc=sv)
    K.wait("scalar", sv)
    K.op("scalar", lambda e: e.activation(st_all(c, 6), st_all(c, 3), AF.Sqrt), inc=sa)
    K.wait("vector", sa)
    K.op("vector", lambda e: e.reciprocal(st_all(c, 4), st_all(c, 6)), inc=sv)
    K.wait("vector", sv)
    K.op("vector", lambda e: e.scalar_tensor_tensor(st_all(c, 5), st_all(c, 2), -1.0, st_all(c, 4), ALU.mult, ALU.mult), inc=sv)
    K.wait("vector", sv)
    if not want_xfm:
        return sv, sa, sp
    xfm3 = c.x_fm[:].rearrange("p (dc t) -> p dc t", t=T)
    psT = [c.ps[6][:].bitcast(BF16), c.ps[7][:].bitcast(BF16)]
    K.wait("tensor", c.s_const)
    for tt in range(NT):
        half = tt % 2
        xln = c.wbuf[:, half * 4096: half * 4096 + 4096]
        if tt >= 2:
            K.wait("vector", sp)
        K.op("vector", lambda e, o=xln, i=c.x_res[:, tt * D:(tt + 1) * D],
             r=st_col(c, 4, tt), m=st_col(c, 5, tt):
             e.tensor_scalar(o, i, r, m, ALU.mult, ALU.add), inc=sv)
        K.wait("tensor", sv)
        for g in range(4):
            bank = psT[g % 2]
            if tt > 0 or g >= 2:
                K.wait("tensor", sv)
            for i in range(8):
                dc = g * 8 + i
                src = xln[:, dc * 128:(dc + 1) * 128]
                reg = bank[:, i * 128:(i + 1) * 128]
                K.op("tensor", lambda e, r=reg, x=src: e.transpose(r, x, c.identb[:]),
                     inc=(sp if i == 7 else None))
            K.wait("vector", sp)
            dst = xfm3[:, g * 8:(g + 1) * 8, tt * 128:(tt + 1) * 128]
            K.op("vector", lambda e, d=dst,
                 s=bank[:].rearrange("p (i t) -> p i t", t=128):
                 e.tensor_copy(d, s), inc=sv)
    return sv, sa, sp


def emit_proj_tok(K, c, wst_ap, n_os, dest_fn, *, name, bias_ap=None,
                  n_dc=NDC, lhsT_fn=None, pe_waits=(), dma_waits=(),
                  rhs_cols=512, prequant=False, dcg=4):
    ng = n_dc // dcg
    npc = max(1, dcg // 2)
    s_pw = [K.sem(f"s_pw_{name}0"), K.sem(f"s_pw_{name}1")]
    s_cast = K.sem(f"s_cast_{name}")
    s_wf = K.sem(f"s_wf_{name}")
    s_mm = K.sem(f"s_mm_{name}")
    s_ev = K.sem(f"s_ev_{name}")
    s_b = K.sem(f"s_b_{name}")
    base_wf = s_wf.cnt
    base_mm = s_mm.cnt
    if lhsT_fn is None:
        def lhsT_fn(dc, tt):
            return c.x_fm[:, dc * T + tt * 128: dc * T + (tt + 1) * 128]
    for w in pe_waits:
        K.wait("tensor", w)
    for w in dma_waits:
        K.wait("sync", w)
        K.wait("vector", w)
    wb32 = c.wbuf[:].bitcast(F32)
    n_slab = n_os * ng
    slab_ready = []
    piece_cast = []
    for os in range(n_os):
        if bias_ap is not None:
            if os >= 1:
                K.wait("sync", s_mm, base_mm + 8 * os)
            K.dma("sync", c.rs[0:1, 0:rhs_cols],
                  bias_ap[0:1, os * rhs_cols:(os + 1) * rhs_cols], s_b)
        for g in range(ng):
            slab = os * ng + g
            spar = slab % 2
            if prequant:
                if slab >= 2:
                    K.wait("sync", s_wf, base_wf + slab - 1)
                K.dma("sync", c.wbuf[:, spar * 2048:(spar + 1) * 2048],
                      wst_ap[slab], s_pw[spar])
                slab_ready.append((s_pw[spar], s_pw[spar].cnt))
            else:
                for j in range(npc):
                    p = npc * slab + j
                    ppar = p % 2
                    if p >= 2:
                        K.wait("sync", s_cast, piece_cast[p - 2])
                    K.dma("sync", wb32[:, 2048 + ppar * 1024: 2048 + (ppar + 1) * 1024],
                          wst_ap[slab][:, j * 1024:(j + 1) * 1024], s_pw[ppar])
                    if slab >= 2 and j == 0:
                        K.wait("vector", s_wf, base_wf + slab - 1)
                    K.wait("vector", s_pw[ppar])
                    K.op("vector", lambda e,
                         o=c.wbuf[:, spar * 2048 + j * 1024: spar * 2048 + (j + 1) * 1024],
                         i=wb32[:, 2048 + ppar * 1024: 2048 + (ppar + 1) * 1024]:
                         e.tensor_copy(o, i), inc=s_cast)
                    piece_cast.append(s_cast.cnt)
                slab_ready.append((s_cast, s_cast.cnt))
        if os > 0:
            K.wait("tensor", s_ev)
        for g in range(ng):
            slab = os * ng + g
            spar = slab % 2
            sem_, val_ = slab_ready[slab]
            K.wait("tensor", sem_, val_)
            for dcs in range(dcg):
                dc = g * dcg + dcs
                w_sl = c.wbuf[:, spar * 2048 + dcs * rhs_cols: spar * 2048 + (dcs + 1) * rhs_cols]
                last = (dc == n_dc - 1) and bias_ap is None
                for tt in range(NT):
                    K.op("tensor", lambda e, p=c.ps[tt][:, 0:rhs_cols], l=lhsT_fn(dc, tt),
                         w=w_sl, st=(dc == 0), sp=(dc == n_dc - 1):
                         e.matmul(p, l, w, start=st, stop=sp),
                         inc=(s_mm if last else None))
            K.op("tensor", lambda e: e.drain(), inc=s_wf)
        if bias_ap is not None:
            K.wait("tensor", s_b)
            for tt in range(NT):
                K.op("tensor", lambda e, p=c.ps[tt][:, 0:rhs_cols]:
                     e.matmul(p, c.onesr[:], c.rs[0:1, 0:rhs_cols], start=False,
                              stop=True, skip_group_check=True), inc=s_mm)
        K.wait("vector", s_mm)
        K.wait("scalar", s_mm)
        for tt in range(NT):
            dest_fn(K, c, tt, os, c.ps[tt][:, 0:rhs_cols], s_ev)
    return s_ev, s_mm, s_wf


def dest_resid(K, c, tt, os, psum, s_ev, col0=0):
    dst = c.x_res[:, tt * D + col0 + os * 512: tt * D + col0 + os * 512 + 512]
    K.op("vector", lambda e, d=dst, a=psum: e.tensor_tensor(d, d, a, ALU.add), inc=s_ev)



def emit_allgather(K, in_ap, out_ap, name, wait_sems=()):
    sc = K.sem("s_ag_" + name)
    for w in wait_sems:
        K.wait("gpsimd", w)
    K.op("gpsimd", lambda e: e.collective_compute(
        "AllGather", mybir.AluOpType.bypass, replica_groups=GROUPS,
        ins=[in_ap.opt()], outs=[out_ap.opt()]), inc=sc, inc_n=1)
    return sc


def l0ab_consts(nc):
    sel4 = np.zeros((128, 4), np.float32)
    for p in range(128):
        sel4[p, p // 32] = 1.0
    ind = np.zeros((128, 4096), np.float32)
    for n in range(4096):
        ind[n // 32, n] = 1.0
    a = nc.inline_tensor(sel4, "sel4c").ap()
    b = nc.inline_tensor(ind.astype(BF16NP), "indc").ap()
    return a, b


def emit_l0ab_maskprep(K, c, maskT_ap, sel4_ap, wait_wbuf=()):
    s_mc = [K.sem("s_l0ab_mc0"), K.sem("s_l0ab_mc1")]
    s_rd = K.sem("s_l0ab_rd")
    s_pe = K.sem("s_l0ab_pe")
    s_cst = K.sem("s_l0ab_cst")
    K.dma("sync", c.rs[:, 0:4], sel4_ap, s_cst)
    mbuf = c.wbuf[:].bitcast(I32)
    for w in wait_wbuf:
        K.wait("sync", w)
    K.wait("tensor", s_cst)
    red_done = []
    for kc in range(NDC):
        par = kc % 2
        rs_acc = c.rs[:, 128 + par * 128: 128 + par * 128 + 128]
        if kc >= 2:
            K.wait("vector", s_pe, kc - 1)
        for h in range(2):
            m = kc * 2 + h
            if m >= 2:
                K.wait("sync", s_rd, red_done[m - 2])
            K.dma("sync", mbuf[:, (m % 2) * 2048: (m % 2) * 2048 + 2048],
                  maskT_ap[kc * 128:(kc + 1) * 128, h * 2048:(h + 1) * 2048], s_mc[m % 2])
            K.wait("vector", s_mc[m % 2])
            dst = rs_acc[:, h * 64:(h + 1) * 64]
            K.op("vector", lambda e, d=dst, i=mbuf[:, (m % 2) * 2048:(m % 2) * 2048 + 2048]:
                 e.reduce_sum(d.rearrange("p (a b) -> p a b", b=1),
                              i.rearrange("p (a b) -> p a b", b=32), axis=AX.X), inc=s_rd)
            red_done.append(s_rd.cnt)
        K.wait("vector", s_rd)
        K.op("vector", lambda e: e.engine_nop(), inc=s_rd)
        K.wait("tensor", s_rd)
        K.op("tensor", lambda e, l=rs_acc, kc=kc: e.matmul(
            c.ps[0][:, kc * 4:kc * 4 + 4], l, c.rs[:, 0:4], start=True, stop=True),
            inc=s_pe)
    K.wait("vector", s_pe)
    K.op("vector", lambda e: e.tensor_scalar(c.bmT01[:], c.ps[0][:, 0:128], 0.0, None, ALU.is_gt), inc=s_rd)
    K.wait("vector", s_rd)
    return s_rd


def emit_l0ab_wm(K, c, wst_ap, wm_ap, ind_ap, s_bm):
    s_w = K.sem("s_l0ab_w")
    s_ind = K.sem("s_l0ab_ind")
    s_em = K.sem("s_l0ab_em")
    s_tt = K.sem("s_l0ab_wmtt")
    s_out = [K.sem("s_l0ab_out0"), K.sem("s_l0ab_out1")]
    s_cp = K.sem("s_l0ab_cp")
    bmE = c.wbuf[:, 0:4096]
    K.wait("vector", s_bm)
    K.wait("sync", s_bm)
    for j in range(32):
        K.op("vector", lambda e, j=j: e.tensor_copy(c.wbuf[:, j:4096:32], c.bmT01[:]),
             inc=s_cp)
    K.wait("vector", s_cp)
    K.op("vector", lambda e: e.engine_nop(), inc=s_cp)
    K.wait("tensor", s_cp)
    K.wait("sync", s_cp)
    indb = c.ob[:, 0:512]
    wq32 = c.wbuf[:].bitcast(F32)[:, 2048:3072]
    n_slab = wst_ap.shape[0]
    ng = 8
    tt_done = []
    for qq in range(2 * n_slab):
        slab, jq = divmod(qq, 2)
        os, g = divmod(slab, ng)
        qpar = qq % 2
        if g == 0 and jq == 0:
            K.wait("sync", s_em)
            K.dma("sync", indb[:], ind_ap[:, os * 512:(os + 1) * 512], s_ind)
        if qq >= 1:
            K.wait("sync", s_tt, tt_done[qq - 1])
        K.dma("sync", wq32, wst_ap[slab][:, jq * 1024:(jq + 1) * 1024], s_w)
        banks = (c.ps[0], c.ps[1]) if qpar == 0 else (c.ps[2], c.ps[3])
        K.wait("tensor", s_ind, 16 * (os + 1))
        if qq >= 2:
            K.wait("tensor", s_tt, tt_done[qq - 2])
        for q in range(2):
            dc = g * 4 + jq * 2 + q
            K.op("tensor", lambda e, b=banks[q][:, 0:512],
                 l=bmE[:, dc * 128:(dc + 1) * 128]:
                 e.matmul(b, l, indb[:], start=True, stop=True),
                 inc=(s_em if q == 1 else None))
        K.wait("vector", s_em)
        K.wait("vector", s_w)
        if qq >= 2:
            K.wait("vector", s_out[qpar])
        outq = c.wbuf[:, 6144 + qpar * 1024: 6144 + (qpar + 1) * 1024]
        for q in range(2):
            K.op("vector", lambda e, o=outq[:, q * 512:(q + 1) * 512],
                 w=wq32[:, q * 512:(q + 1) * 512], b=banks[q][:, 0:512]:
                 e.tensor_tensor(o, w, b, ALU.mult), inc=s_tt)
        tt_done.append(s_tt.cnt)
        K.wait("sync", s_tt)
        if qq >= 2:
            K.wait("sync", s_out[qpar])
        K.dma("sync", wm_ap[slab][:, jq * 1024:(jq + 1) * 1024], outq, s_out[qpar])
    K.wait("sync", s_out[0])
    K.wait("sync", s_out[1])
    return s_out


def emit_l0ab(K, c, maskT_ap, wst_ap, wm_ap, sel4_ap, ind_ap, ln_sems):
    sv, sa, sp = ln_sems
    s_bm = emit_l0ab_maskprep(K, c, maskT_ap, sel4_ap, wait_wbuf=(sp,))
    s_out = emit_l0ab_wm(K, c, wst_ap, wm_ap, ind_ap, s_bm)
    s_ev, s_mm, s_wf = emit_proj_tok(
        K, c, wm_ap, 8, dest_resid, name="pj", prequant=True,
        pe_waits=(sv,), dma_waits=tuple(s_out))
    return s_ev



def l0ffn_consts(nc):
    e8 = np.zeros((8, 128), np.float32)
    for er in range(128):
        e8[er // 16, er] = 2.0
    return nc.inline_tensor(e8.astype(BF16NP), "e8c").ap()


def emit_l0ffn(K, c, wg9_ap, ast_ap, wbst_ap, bias_ap, bthr_ap, e8_ap, ln_sems, tag=""):
    sv, sa, sp = ln_sems
    s_h = K.sem("s_ff_h" + tag)
    base_h = s_h.cnt
    s_ha = K.sem("s_ff_ha" + tag)
    s_hc = K.sem("s_ff_hc" + tag)
    s_g = K.sem("s_ff_g" + tag)
    s_gv = K.sem("s_ff_gv" + tag)
    s_gs = K.sem("s_ff_gs" + tag)
    s_w9 = K.sem("s_ff_w9" + tag)
    obf = c.ob[:].bitcast(F32)
    abf = c.rs[:, 0:128].bitcast(BF16)
    ast32 = obf[:, 0:128]
    col = lambda i: c.rs[:, 288 + i: 288 + i + 1]
    e9 = c.rs[:, 304:312]
    ad = c.rs[:, 312:320]
    wv = c.rs[:, 320:328]
    wnf = c.rs[:, 328:336]
    e8_sb = c.rs[:, 336:400].bitcast(BF16)
    wg9b = obf[:, 128:272].bitcast(BF16)
    K.wait("tensor", sv)
    cast_done = []
    for dc in range(NDC):
        par = dc % 2
        if dc >= 1:
            K.wait("sync", s_hc, cast_done[dc - 1])
        K.dma("sync", ast32, ast_ap[dc], s_ha)
        K.wait("vector", s_ha)
        if dc >= 2:
            K.wait("vector", s_h, base_h + dc - 1)
        K.op("vector", lambda e, o=abf[:, par * 128:(par + 1) * 128]:
             e.tensor_copy(o, ast32), inc=s_hc)
        cast_done.append(s_hc.cnt)
        K.wait("tensor", s_hc)
        for th in range(2):
            K.op("tensor", lambda e, b=c.ps[4 + th][:, 0:512],
                 l=abf[:, par * 128:(par + 1) * 128],
                 r=c.x_fm[:, dc * T + th * 512: dc * T + th * 512 + 512],
                 st=(dc == 0), spp=(dc == NDC - 1):
                 e.matmul(b, l, r, start=st, stop=spp),
                 inc=(s_h if th == 1 else None))
    K.wait("sync", s_h)
    K.dma("sync", c.rs[:, 0:288], wg9_ap, s_w9)
    K.dma("sync", e8_sb[0:8, 0:128], e8_ap, s_w9)
    K.dma("sync", c.rs[0:1, 298:299], bthr_ap, s_w9)
    K.wait("vector", s_w9)
    K.op("vector", lambda e: e.tensor_copy(wg9b, c.rs[:, 0:288]), inc=s_gv)
    K.wait("tensor", s_w9)
    K.op("tensor", lambda e: e.matmul(c.ps[7][:, 0:1], c.onesr[:], c.rs[0:1, 298:299],
         start=True, stop=True), inc=s_g)
    K.wait("vector", s_g)
    K.op("vector", lambda e: e.tensor_copy(col(15), c.ps[7][:, 0:1]), inc=s_gv)
    K.wait("scalar", s_gv)
    K.wait("vector", s_gv)
    K.wait("tensor", s_gv)
    for tt in range(NT):
        if tt > 0:
            K.wait("tensor", s_gs)
        for dc in range(NDC):
            K.op("tensor", lambda e, l=c.x_fm[:, dc * T + tt * 128: dc * T + (tt + 1) * 128],
                 r=wg9b[:, dc * 9: dc * 9 + 9], st=(dc == 0), spp=(dc == NDC - 1):
                 e.matmul(c.ps[6][:, 0:9], l, r, start=st, stop=spp),
                 inc=(s_g if dc == NDC - 1 else None))
        K.wait("vector", s_g)
        K.op("vector", lambda e: e.reduce_max(col(0), c.ps[6][:, 0:8], axis=AX.X), inc=s_gv)
        K.wait("vector", s_gv)
        K.op("vector", lambda e: e.tensor_scalar(col(1), col(0), -1.0, None, ALU.mult), inc=s_gv)
        K.wait("scalar", s_gv)
        K.op("scalar", lambda e: e.activation(e9, c.ps[6][:, 0:8], AF.Exp,
             bias=col(1), accum_out=col(2)), inc=s_gs)
        K.op("scalar", lambda e: e.activation(col(3), c.ps[6][:, 8:9], AF.Sigmoid,
             bias=col(15)), inc=s_gs)
        K.wait("vector", s_gs)
        K.op("vector", lambda e: e.reciprocal(col(4), col(2)), inc=s_gv)
        K.op("vector", lambda e: e.tensor_scalar(col(5), col(3), -0.125, None, ALU.mult), inc=s_gv)
        K.wait("vector", s_gv)
        K.op("vector", lambda e: e.tensor_scalar(ad, e9, col(4), col(5), ALU.mult, ALU.add), inc=s_gv)
        K.wait("vector", s_gv)
        K.op("vector", lambda e: e.tensor_scalar(wv, ad, 0.0, None, ALU.max), inc=s_gv)
        K.wait("vector", s_gv)
        K.op("vector", lambda e: e.reduce_sum(col(6), wv, axis=AX.X), inc=s_gv)
        K.wait("vector", s_gv)
        K.op("vector", lambda e: e.tensor_scalar(col(7), col(6), 0.0, None, ALU.is_equal), inc=s_gv)
        K.wait("vector", s_gv)
        K.op("vector", lambda e: e.tensor_add(col(8), col(6), col(7)), inc=s_gv)
        K.wait("vector", s_gv)
        K.op("vector", lambda e: e.reciprocal(col(9), col(8)), inc=s_gv)
        K.wait("vector", s_gv)
        K.op("vector", lambda e: e.tensor_scalar(wnf, wv, col(9), None, ALU.mult), inc=s_gv)
        K.wait("tensor", s_gv)
        K.op("tensor", lambda e: e.transpose(c.ps[7][0:8, 0:128], wnf, c.ident[:]), inc=s_g)
        K.wait("vector", s_g)
        K.op("vector", lambda e: e.tensor_copy(c.bmT01[0:8, 0:128], c.ps[7][0:8, 0:128]), inc=s_gv)
        K.wait("tensor", s_gv)
        K.op("tensor", lambda e: e.matmul(c.ps[7][:, 0:128], e8_sb[0:8, 0:128],
             c.bmT01[0:8, 0:128], start=True, stop=True), inc=s_g)
        K.wait("vector", s_g)
        K.op("vector", lambda e: e.tensor_copy(c.bmT01[:, 0:128], c.ps[7][:, 0:128]), inc=s_gv)
        K.wait("vector", s_gv)
        hb = c.ps[4 + tt // 4][:, (tt % 4) * 128:(tt % 4) * 128 + 128]
        K.op("vector", lambda e, h=hb, o=c.hw[:, tt * 128:(tt + 1) * 128]:
             e.tensor_tensor(o, h, c.bmT01[:, 0:128], ALU.mult), inc=s_gv)
        K.wait("vector", s_gv)
    def lhsT_fn(dc, tt):
        if dc < NDC:
            return c.x_fm[:, dc * T + tt * 128: dc * T + (tt + 1) * 128]
        return c.hw[:, tt * 128:(tt + 1) * 128]
    s_ev, s_mm, s_wf = emit_proj_tok(
        K, c, wbst_ap, 8, dest_resid, name="pj", bias_ap=bias_ap,
        n_dc=34, dcg=2, lhsT_fn=lhsT_fn, pe_waits=(s_gv,), dma_waits=(s_gv,))
    return s_ev


def l0ffn_host(W_base, W_gate, W_thr, A_w, B_w):
    Wg9 = np.concatenate([W_gate.T, W_thr[:, None]], axis=1)
    wg9 = np.ascontiguousarray(Wg9.reshape(32, 128, 9).transpose(1, 0, 2)
                               .reshape(128, 288)).astype(np.float32)
    A_all = A_w.transpose(2, 0, 1).reshape(4096, 128)
    ast = np.ascontiguousarray(A_all.reshape(32, 128, 128)).astype(np.float32)
    B_all = B_w.transpose(0, 2, 1).reshape(128, 4096)
    WT_aug = np.concatenate(
        [W_base.T, B_all, np.zeros((128, 4096), np.float32)], axis=0)
    wbst = w_stream(WT_aug, dcg=2)
    return wg9, ast, wbst


def l0a_host(Wq, bq, Wk, bk, Wv, bv, Wo, bo):
    WT_kv = np.concatenate([Wk.T, Wv.T], axis=1)
    wkvst = w_stream(WT_kv.astype(np.float32), dcg=4)
    kvbias = np.concatenate([bk, bv])[None, :].astype(np.float32)
    WqT = Wq.T.astype(np.float32)
    sl = []
    for h in range(32):
        wh = WqT[:, h * 128:(h + 1) * 128].reshape(32, 128, 128)
        for half in range(2):
            sl.append(np.ascontiguousarray(
                wh[half * 16:(half + 1) * 16].transpose(1, 0, 2).reshape(128, 2048)))
    wqst = np.stack(sl)
    bqa = np.ascontiguousarray(bq.reshape(32, 1, 128)).astype(np.float32)
    wost = w_stream(Wo.T.astype(np.float32), dcg=4)
    bo_ = bo[None, :].astype(np.float32)
    return wkvst, kvbias, wqst, bqa, wost, bo_


def emit_l0a_kvproj(K, c, wkvst_ap, kvbias_ap, kv_dram, ln_sems):
    sv, sa, sp = ln_sems
    s_kvs = K.sem("s_l0a_kvs")
    s_kvo = [K.sem("s_l0a_kvo0"), K.sem("s_l0a_kvo1")]
    hwf = c.hw[:].bitcast(F32)

    def dest(K, c, tt, os, psum, s_ev):
        idx = tt + os * NT
        par = idx % 2
        if idx >= 2:
            K.wait("vector", s_kvo[par])
        ob = c.ob[:, par * 512:(par + 1) * 512]
        if os < 2:
            K.wait("scalar", s_ev)
            K.op("scalar", lambda e, p=psum: e.activation(hwf, p, AF.Exp), inc=s_kvs)
            K.wait("vector", s_kvs)
            K.op("vector", lambda e: e.tensor_scalar(hwf, hwf, 1.0, None, ALU.min), inc=s_ev)
            K.wait("vector", s_ev)
            K.op("vector", lambda e, p=psum, o=ob: e.scalar_tensor_tensor(
                o, p, 0.0, hwf, ALU.max, ALU.add), inc=s_ev)
        else:
            K.op("vector", lambda e, p=psum, o=ob: e.tensor_copy(o, p), inc=s_ev)
        K.wait("sync", s_ev)
        K.dma("sync", kv_dram[tt][:, os * 512:(os + 1) * 512], ob, s_kvo[par])

    s_ev, s_mm, s_wf = emit_proj_tok(
        K, c, wkvst_ap, 4, dest, name="pj", bias_ap=kvbias_ap,
        pe_waits=(sv,), dma_waits=(sv,))
    return s_ev, s_kvo


def emit_l0a_kvstate(K, c, kv_dram, s_kvo, kvout_ap):
    s_t = [K.sem("s_l0a_t0"), K.sem("s_l0a_t1")]
    s_pe = K.sem("s_l0a_kvpe")
    s_kv = K.sem("s_l0a_kve")
    s_ko = K.sem("s_l0a_ko")
    hwf = c.hw[:].bitcast(F32)
    ks_acc = c.rs[:, 144:152]
    for w in s_kvo:
        K.wait("sync", w)
    K.op("vector", lambda e: e.memset(ks_acc, 0.0), inc=s_kv)
    K.wait("vector", s_kv)
    tld = 0
    for wave in range(2):
        for tt in range(NT):
            par = tld % 2
            sl = c.wbuf[:, par * 4096: par * 4096 + 2048]
            if tld >= 2:
                K.wait("sync", s_pe, s_pe.cnt - 1)
            K.dma("sync", sl, kv_dram[tt], s_t[par])
            K.wait("tensor", s_t[par])
            if tt > 0:
                K.wait("tensor", s_kv)
            for q in range(4):
                kvh = wave * 4 + q
                ksl = sl[:, kvh * 128:(kvh + 1) * 128]
                vsl = sl[:, 1024 + kvh * 128: 1024 + (kvh + 1) * 128]
                K.op("tensor", lambda e, k=ksl, v=vsl, b=c.ps[q][:, 0:128],
                     st=(tt == 0), spp=(tt == NT - 1):
                     e.matmul(b, k, v, start=st, stop=spp), inc=None)
                K.op("tensor", lambda e, k=ksl, b=c.ps[4][:, q:q + 1]:
                     e.matmul(b, k, c.onescol[:], start=True, stop=True),
                     inc=(s_pe if q == 3 else None))
            tld += 1
            K.wait("vector", s_pe)
            K.op("vector", lambda e, o=ks_acc[:, wave * 4:(wave + 1) * 4]:
                 e.tensor_tensor(o, o, c.ps[4][:, 0:4], ALU.add), inc=s_kv)
        K.wait("vector", s_pe)
        for q in range(4):
            if wave > 0 or q > 0:
                K.wait("vector", s_ko)
            K.op("vector", lambda e, b=c.ps[q][:, 0:128], o=hwf[:, 0:128]:
                 e.tensor_copy(o, b), inc=s_kv)
            K.wait("sync", s_kv)
            K.dma("sync", kvout_ap[:, (wave * 4 + q) * 128:(wave * 4 + q) * 128 + 128],
                  hwf[:, 0:128], s_ko)
    K.wait("vector", s_kv)
    K.op("vector", lambda e: e.engine_nop(), inc=s_kv)
    K.wait("sync", s_kv)
    K.dma("sync", kvout_ap[:, 1024:1032], ks_acc, s_ko)
    K.wait("sync", s_ko)
    return s_ko


def emit_l0a_kvreduce_in(K, c, kvparts_ap):
    s_l = K.sem("s_l0a_kl")
    s_v = K.sem("s_l0a_kvv")
    wb32 = c.wbuf[:].bitcast(F32)
    for piece in range(3):
        w = 512 if piece < 2 else 8
        if piece > 0:
            K.wait("sync", s_v)
        for r in range(4):
            K.dma("sync", wb32[:, r * 512: r * 512 + w],
                  kvparts_ap[r][:, piece * 512: piece * 512 + w], s_l)
        K.wait("vector", s_l)
        K.op("vector", lambda e, w=w: e.tensor_add(wb32[:, 0:w], wb32[:, 0:w], wb32[:, 512:512 + w]), inc=s_v)
        K.op("vector", lambda e, w=w: e.tensor_add(wb32[:, 1024:1024 + w], wb32[:, 1024:1024 + w], wb32[:, 1536:1536 + w]), inc=s_v)
        K.wait("vector", s_v)
        K.op("vector", lambda e, w=w: e.tensor_add(wb32[:, 0:w], wb32[:, 0:w], wb32[:, 1024:1024 + w]), inc=s_v)
        K.wait("vector", s_v)
        if piece < 2:
            K.op("vector", lambda e, piece=piece: e.tensor_copy(
                c.ob[:, piece * 512:(piece + 1) * 512], wb32[:, 0:512]), inc=s_v)
        else:
            K.op("vector", lambda e: e.tensor_scalar(wb32[:, 0:8], wb32[:, 0:8], 1e-10, None, ALU.add), inc=s_v)
            K.wait("vector", s_v)
            K.op("vector", lambda e: e.reciprocal(c.rs[:, 128:136], wb32[:, 0:8]), inc=s_v)
        K.wait("vector", s_v)
    K.op("vector", lambda e: e.engine_nop(), inc=s_v)
    return s_v


def emit_l0a_attn(K, c, wqst_ap, bqa_ap, attn_dram, s_kvready, ln_sems):
    sv, sa, sp = ln_sems
    s_w = [K.sem("s_l0a_qw0"), K.sem("s_l0a_qw1")]
    s_c = K.sem("s_l0a_qc")
    s_q = K.sem("s_l0a_qpe")
    s_qv = K.sem("s_l0a_qv")
    s_qs = K.sem("s_l0a_qs")
    s_b = K.sem("s_l0a_qb")
    s_ao = [K.sem("s_l0a_ao0"), K.sem("s_l0a_ao1")]
    wb32 = c.wbuf[:].bitcast(F32)
    hwf = c.hw[:].bitcast(F32)
    q_phi = c.wbuf[:, 6144:7168]
    bqrow = c.rs[0:1, 256:384]
    for e in ("vector", "tensor", "sync", "scalar"):
        K.wait(e, s_kvready)
    K.wait("tensor", sv)
    cast_done = []
    qdone = []
    for h in range(32):
        for half in range(2):
            hs = h * 2 + half
            for j in range(2):
                p = hs * 2 + j
                if p >= 1:
                    K.wait("sync", s_c, cast_done[p - 1])
                K.dma("sync", wb32[:, 2048:3072],
                      wqst_ap[hs][:, j * 1024:(j + 1) * 1024], s_w[0])
                if hs >= 2 and j == 0:
                    K.wait("vector", s_q, qdone[hs - 2])
                K.wait("vector", s_w[0])
                K.op("vector", lambda e,
                     o=c.wbuf[:, (hs % 2) * 2048 + j * 1024: (hs % 2) * 2048 + (j + 1) * 1024],
                     i=wb32[:, 2048:3072]:
                     e.tensor_copy(o, i), inc=s_c)
                cast_done.append(s_c.cnt)
        if h >= 1:
            K.wait("sync", s_q)
        K.dma("sync", bqrow, bqa_ap[h], s_b)
        if h > 0:
            K.wait("tensor", s_qv)
        K.wait("tensor", s_b)
        for half in range(2):
            hs = h * 2 + half
            K.wait("tensor", s_c, cast_done[hs * 2 + 1])
            for i in range(16):
                dc = half * 16 + i
                l = c.wbuf[:, (hs % 2) * 2048 + i * 128: (hs % 2) * 2048 + (i + 1) * 128]
                for ts in range(2):
                    K.op("tensor", lambda e, l=l, b=c.ps[5 + ts][:, 0:512],
                         r=c.x_fm[:, dc * T + ts * 512: dc * T + ts * 512 + 512],
                         st=(dc == 0), spp=(dc == NDC - 1):
                         e.matmul(b, l, r, start=st, stop=spp), inc=None)
            K.op("tensor", lambda e: e.drain(), inc=s_q)
            qdone.append(s_q.cnt)
        for ts in range(2):
            for seg in range(4):
                K.op("tensor", lambda e, b=c.ps[5 + ts][:, seg * 128:(seg + 1) * 128]:
                     e.matmul(b, bqrow, c.onesr[:], start=False, stop=True,
                              skip_group_check=True),
                     inc=(s_q if (ts == 1 and seg == 3) else None))
        for ts in range(2):
            K.wait("scalar", s_q)
            K.wait("scalar", s_qv)
            K.op("scalar", lambda e, p=c.ps[5 + ts][:, 0:512]: e.activation(hwf, p, AF.Exp), inc=s_qs)
            K.wait("vector", s_qs)
            K.op("vector", lambda e: e.tensor_scalar(hwf, hwf, 1.0, None, ALU.min), inc=s_qv)
            K.wait("vector", s_qv)
            K.op("vector", lambda e, p=c.ps[5 + ts][:, 0:512], o=q_phi[:, ts * 512:(ts + 1) * 512]:
                 e.scalar_tensor_tensor(o, p, 0.0, hwf, ALU.max, ALU.add), inc=s_qv)
            K.wait("vector", s_qv)
        kvh = h // 4
        K.wait("tensor", s_qv)
        for ts in range(2):
            apar = ts
            K.op("tensor", lambda e, b=c.ps[3 + apar][:, 0:512],
                 l=c.ob[:, kvh * 128:(kvh + 1) * 128],
                 r=q_phi[:, ts * 512:(ts + 1) * 512]:
                 e.matmul(b, l, r, start=True, stop=True), inc=s_q)
            K.wait("vector", s_q)
            idx = h * 2 + ts
            if idx >= 2:
                K.wait("vector", s_ao[apar])
            obf = c.wbuf[:, 7168 + apar * 512: 7168 + (apar + 1) * 512]
            K.op("vector", lambda e, b=c.ps[3 + apar][:, 0:512], o=obf, kvh=kvh:
                 e.tensor_scalar(o, b, c.rs[:, 128 + kvh:129 + kvh], None, ALU.mult), inc=s_qv)
            K.wait("sync", s_qv)
            K.dma("sync", attn_dram[0:128, h * 1024 + ts * 512: h * 1024 + (ts + 1) * 512],
                  obf, s_ao[apar])
    K.wait("sync", s_ao[0])
    K.wait("sync", s_ao[1])
    return s_ao


def emit_l0a_oproj(K, c, attn_dram, wost_ap, bo_ap, s_ao, prequant=False, name="l0ao",
                   attn_list=None):
    s_l = K.sem("s_al_" + name)
    for w in s_ao:
        K.wait("sync", w)
    if attn_list is not None:
        for dc in range(NDC):
            K.dma("sync", c.x_fm[:, dc * T:(dc + 1) * T], attn_list[dc], s_l)
    else:
        K.dma("sync", c.x_fm[:], attn_dram, s_l)
    s_ev, s_mm, s_wf = emit_proj_tok(
        K, c, wost_ap, 8, dest_resid, name=name, bias_ap=bo_ap, prequant=prequant,
        pe_waits=(s_l,), dma_waits=(s_l,))
    return s_ev


def l1ffn_host(W, A, B, router_W):
    rw8 = np.ascontiguousarray(router_W.T.reshape(32, 128, 8).transpose(1, 0, 2)
                               .reshape(128, 256)).astype(np.float32)
    A_all = A.transpose(2, 0, 1).reshape(4096, 128)
    ast = np.ascontiguousarray(A_all.reshape(32, 128, 128)).astype(np.float32)
    B_all = B.transpose(0, 2, 1).reshape(128, 4096)
    WT_aug = np.concatenate(
        [W.T, B_all, np.zeros((128, 4096), np.float32)], axis=0)
    wbst = w_stream(WT_aug, dcg=2)
    return rw8, ast, wbst


def sel_host(core):
    b, r = divmod(core, 4)
    pos = np.arange(r * T, (r + 1) * T)
    sel = ((pos >= 611) & (pos <= 4094)).astype(np.float32)
    return np.ascontiguousarray(sel.reshape(8, 128).T).astype(BF16NP)


def emit_l1ffn_router_partial(K, c, rw8_ap, selb_ap, rpart_out_ap, ln_sems, tag=""):
    sv, sa, sp = ln_sems
    s_r = K.sem("s_rt_r" + tag)
    s_rv = K.sem("s_rt_v" + tag)
    s_w = K.sem("s_rt_w" + tag)
    obf = c.ob[:].bitcast(F32)
    rw8b = obf[:, 256:384].bitcast(BF16)
    selb = obf[:, 384:388].bitcast(BF16)
    K.dma("sync", obf[:, 0:256], rw8_ap, s_w)
    K.dma("sync", selb, selb_ap, s_w)
    K.wait("vector", s_w)
    K.op("vector", lambda e: e.tensor_copy(rw8b, obf[:, 0:256]), inc=s_rv)
    K.wait("vector", s_rv)
    K.wait("tensor", s_rv)
    K.wait("tensor", sv)
    for tt in range(NT):
        if tt > 0:
            K.wait("tensor", s_rv)
        for dc in range(NDC):
            K.op("tensor", lambda e, l=c.x_fm[:, dc * T + tt * 128: dc * T + (tt + 1) * 128],
                 r=rw8b[:, dc * 8: dc * 8 + 8], st=(dc == 0), spp=(dc == NDC - 1):
                 e.matmul(c.ps[6][:, 0:8], l, r, start=st, stop=spp),
                 inc=(s_r if dc == NDC - 1 else None))
        K.wait("vector", s_r)
        K.op("vector", lambda e: e.tensor_copy(c.rs[:, 256:260].bitcast(BF16), c.ps[6][:, 0:8]), inc=s_rv)
        K.wait("vector", s_rv)
        K.op("vector", lambda e: e.engine_nop(), inc=s_rv)
        K.wait("tensor", s_rv)
        K.op("tensor", lambda e, l=selb[:, tt:tt + 1], st=(tt == 0), spp=(tt == NT - 1):
             e.matmul(c.ps[7][0:1, 0:8], l, c.rs[:, 256:260].bitcast(BF16), start=st, stop=spp),
             inc=s_r)
    K.wait("vector", s_r)
    if rpart_out_ap is not None:
        K.op("vector", lambda e: e.tensor_copy(c.rs[0:1, 264:272], c.ps[7][0:1, 0:8]), inc=s_rv)
        K.wait("sync", s_rv)
        s_o = K.sem("s_rt_o" + tag)
        K.dma("sync", rpart_out_ap, c.rs[0:1, 264:272], s_o)
        K.wait("sync", s_o)
        return s_o
    return s_r


def emit_l1ffn_routing(K, c, rparts_ap, rb_ap, e8_ap, tag=""):
    s_l = K.sem("s_ro_l" + tag)
    s_v = K.sem("s_ro_v" + tag)
    s_s = K.sem("s_ro_s" + tag)
    s_p = K.sem("s_ro_p" + tag)
    row = lambda i: c.rs[0:1, 280 + i * 8: 280 + (i + 1) * 8]
    e8_sb = c.rs[:, 336:400].bitcast(BF16)
    col1 = lambda i: c.rs[0:1, 408 + i: 409 + i]
    rowb = c.rs[0:1, 404:408].bitcast(BF16)[0:1, 0:8]
    colb = c.rs[:, 274:275].bitcast(BF16)[0:8, 0:1]
    for r in range(4):
        K.dma("sync", c.rs[0:1, 280 + 8 * r:288 + 8 * r], rparts_ap[r], s_l)
    K.dma("sync", c.rs[0:1, 312:320], rb_ap, s_l)
    K.dma("sync", e8_sb[0:8, 0:128], e8_ap, s_l)
    K.wait("vector", s_l)
    K.op("vector", lambda e: e.tensor_add(row(0), row(0), row(1)), inc=s_v)
    K.op("vector", lambda e: e.tensor_add(row(2), row(2), row(3)), inc=s_v)
    K.wait("vector", s_v)
    K.op("vector", lambda e: e.tensor_add(row(0), row(0), row(2)), inc=s_v)
    K.wait("vector", s_v)
    K.op("vector", lambda e: e.scalar_tensor_tensor(
        row(0), row(0), 1.0 / 3484.0, c.rs[0:1, 312:320], ALU.mult, ALU.add), inc=s_v)
    K.wait("vector", s_v)
    K.op("vector", lambda e: e.reduce_max(col1(0), row(0), axis=AX.X), inc=s_v)
    K.wait("vector", s_v)
    K.op("vector", lambda e: e.tensor_scalar(col1(1), col1(0), -1.0, None, ALU.mult), inc=s_v)
    K.wait("scalar", s_v)
    K.op("scalar", lambda e: e.activation(row(5), row(0), AF.Exp,
         bias=col1(1), accum_out=col1(2)), inc=s_s)
    K.wait("vector", s_s)
    K.op("vector", lambda e: e.reciprocal(col1(3), col1(2)), inc=s_v)
    K.wait("vector", s_v)
    K.op("vector", lambda e: e.tensor_scalar(row(6), row(5), col1(3), None, ALU.mult), inc=s_v)
    K.wait("vector", s_v)
    K.op("vector", lambda e: e.tensor_copy(rowb, row(6)), inc=s_v)
    K.wait("tensor", s_v)
    K.op("tensor", lambda e: e.transpose(c.ps[6][:].bitcast(BF16)[0:8, 0:1], rowb, c.identb[0:1, 0:1]), inc=s_p)
    K.wait("vector", s_p)
    K.op("vector", lambda e: e.tensor_copy(colb, c.ps[6][:].bitcast(BF16)[0:8, 0:1]), inc=s_v)
    K.wait("vector", s_v)
    K.op("vector", lambda e: e.engine_nop(), inc=s_v)
    K.wait("tensor", s_v)
    K.op("tensor", lambda e: e.matmul(c.ps[7][:, 0:1], e8_sb[0:8, 0:128],
         colb, start=True, stop=True), inc=s_p)
    K.wait("vector", s_p)
    K.op("vector", lambda e: e.tensor_copy(c.rs[:, 272:273], c.ps[7][:, 0:1]), inc=s_v)
    K.wait("vector", s_v)
    K.op("vector", lambda e: e.engine_nop(), inc=s_v)
    return s_v


def emit_l1ffn_main(K, c, ast_ap, wbst_ap, bias_ap, s_route, ln_sems, tag=""):
    sv, sa, sp = ln_sems
    s_h = K.sem("s_lf_h" + tag)
    base_h = s_h.cnt
    s_ha = K.sem("s_lf_ha" + tag)
    s_hc = K.sem("s_lf_hc" + tag)
    s_hv = K.sem("s_lf_hv" + tag)
    obf = c.ob[:].bitcast(F32)
    abf = c.rs[:, 0:128].bitcast(BF16)
    ast32 = obf[:, 0:128]
    K.wait("tensor", sv)
    K.wait("vector", s_route)
    cast_done = []
    for dc in range(NDC):
        par = dc % 2
        if dc >= 1:
            K.wait("sync", s_hc, cast_done[dc - 1])
        K.dma("sync", ast32, ast_ap[dc], s_ha)
        K.wait("vector", s_ha)
        if dc >= 2:
            K.wait("vector", s_h, base_h + dc - 1)
        K.op("vector", lambda e, o=abf[:, par * 128:(par + 1) * 128]:
             e.tensor_copy(o, ast32), inc=s_hc)
        cast_done.append(s_hc.cnt)
        K.wait("tensor", s_hc)
        for th in range(2):
            K.op("tensor", lambda e, b=c.ps[4 + th][:, 0:512],
                 l=abf[:, par * 128:(par + 1) * 128],
                 r=c.x_fm[:, dc * T + th * 512: dc * T + th * 512 + 512],
                 st=(dc == 0), spp=(dc == NDC - 1):
                 e.matmul(b, l, r, start=st, stop=spp),
                 inc=(s_h if th == 1 else None))
    K.wait("vector", s_h)
    for th in range(2):
        K.op("vector", lambda e, b=c.ps[4 + th][:, 0:512], o=c.hw[:, th * 512:(th + 1) * 512]:
             e.tensor_scalar(o, b, c.rs[:, 272:273], None, ALU.mult), inc=s_hv)
    K.wait("vector", s_hv)
    K.op("vector", lambda e: e.engine_nop(), inc=s_hv)

    def lhsT_fn(dc, tt):
        if dc < NDC:
            return c.x_fm[:, dc * T + tt * 128: dc * T + (tt + 1) * 128]
        return c.hw[:, tt * 128:(tt + 1) * 128]
    s_ev, s_mm, s_wf = emit_proj_tok(
        K, c, wbst_ap, 8, dest_resid, name="pj", bias_ap=bias_ap,
        n_dc=34, dcg=2, lhsT_fn=lhsT_fn, pe_waits=(s_hv,), dma_waits=(s_hv,))
    return s_ev


def l1a_host(Wq_d, bq_d, W_qk, b_qk, Wkv_d, bkv_d, Wv_u, bv_u, Wo, bo):
    Wc16 = np.concatenate([Wq_d, Wkv_d], axis=1)
    wc16 = np.ascontiguousarray(Wc16.reshape(32, 128, 16).transpose(1, 0, 2)
                                .reshape(128, 512)).astype(np.float32)
    cbias = np.concatenate([bq_d, bkv_d])[None, :].astype(np.float32)
    wqk = W_qk[None if False else slice(None), :].astype(np.float32)
    bqk4 = np.ascontiguousarray(b_qk.reshape(32, 4).T).astype(np.float32)
    wvu = Wv_u.astype(np.float32)
    bvu = np.ascontiguousarray(bv_u.reshape(32, 128, 1)).astype(np.float32)
    wost = w_stream(Wo.astype(np.float32), dcg=4)
    boo = bo[None, :].astype(np.float32)
    return wc16, cbias, wqk, bqk4, wvu, bvu, wost, boo


def emit_l1a_cproj(K, c, wc16_ap, cbias_ap, ckv_own_ap, ln_sems, want_cq=True, tag=""):
    sv, sa, sp = ln_sems
    s_w = K.sem("s_ca_w" + tag)
    s_pe = K.sem("s_ca_pe" + tag)
    s_v = K.sem("s_ca_v" + tag)
    s_o = K.sem("s_ca_o" + tag)
    wbf = c.wbuf[:].bitcast(F32)
    wc16b = c.ob[:, 0:512]
    K.wait("sync", sp)
    K.dma("sync", wbf[:, 0:512], wc16_ap, s_w)
    K.dma("sync", c.rs[0:1, 0:16], cbias_ap, s_w)
    K.wait("vector", s_w)
    K.op("vector", lambda e: e.tensor_copy(wc16b, wbf[:, 0:512]), inc=s_v)
    K.wait("vector", s_v)
    K.wait("tensor", s_v)
    K.wait("tensor", sv)
    cqb = c.rs[:, 16:24].bitcast(BF16)
    ckvb = c.rs[:, 24:26].bitcast(BF16)
    for tt in range(NT):
        if tt > 0:
            K.wait("tensor", s_v)
        for dc in range(NDC):
            K.op("tensor", lambda e, l=c.x_fm[:, dc * T + tt * 128: dc * T + (tt + 1) * 128],
                 r=wc16b[:, dc * 16: dc * 16 + 16], st=(dc == 0), spp=(dc == NDC - 1):
                 e.matmul(c.ps[6][:, 0:16], l, r, start=st, stop=spp), inc=None)
        K.op("tensor", lambda e: e.matmul(c.ps[6][:, 0:16], c.onesr[:], c.rs[0:1, 0:16],
             start=False, stop=True, skip_group_check=True), inc=s_pe)
        K.wait("vector", s_pe)
        if tt >= 1:
            K.wait("vector", s_o)
        K.op("vector", lambda e: e.tensor_copy(ckvb, c.ps[6][:, 12:16]), inc=s_v)
        if want_cq:
            K.op("vector", lambda e: e.tensor_copy(cqb[:, 0:12], c.ps[6][:, 0:12]), inc=s_v)
        K.wait("sync", s_v)
        if tt >= 1:
            K.wait("sync", s_o)
        K.dma("sync", ckv_own_ap[tt * 128:(tt + 1) * 128, :], ckvb, s_o)
        if want_cq:
            K.wait("tensor", s_v)
            K.op("tensor", lambda e: e.transpose(c.ps[7][:].bitcast(BF16)[0:12, 0:128],
                 cqb[:, 0:12], c.identb[:]), inc=s_pe)
            K.wait("vector", s_pe)
            K.op("vector", lambda e, tt=tt: e.tensor_copy(
                c.wbuf[0:12, 6144 + tt * 128: 6144 + (tt + 1) * 128],
                c.ps[7][:].bitcast(BF16)[0:12, 0:128]), inc=s_v)
        K.wait("vector", s_v)
        K.op("vector", lambda e: e.engine_nop(), inc=s_v)
    K.wait("sync", s_o)
    return s_v, s_o


def emit_l1a_attn(K, c, ckv_full_ap, wqk_ap, bqk4_ap, wvu_ap, bvu_ap, attn_dram,
                  s_cq, tag=""):
    s_w = K.sem("s_la_w" + tag)
    s_pe = K.sem("s_la_pe" + tag)
    s_v = K.sem("s_la_v" + tag)
    s_s = K.sem("s_la_s" + tag)
    s_b = K.sem("s_la_b" + tag)
    s_t = K.sem("s_la_t" + tag)
    s_ao = [K.sem("s_la_ao0" + tag), K.sem("s_la_ao1" + tag)]
    obf = c.ob[:].bitcast(F32)
    xff = c.x_fm[:].bitcast(F32)
    hwf = c.hw[:].bitcast(F32)
    ckv_fm = c.wbuf[0:4, 0:4096]
    q4h_sb = c.wbuf[0:4, 4096:5120]
    P = [c.wbuf[:, 5120:5632], c.wbuf[:, 5632:6144]]
    cq_fm = c.wbuf[0:12, 6144:7168]
    tn_sb = c.wbuf[0:4, 7168:7680]
    wqk_b = c.ob[0:12, 640:768]
    bqk4 = obf[0:4, 448:480]
    bvu_col = obf[:, 480:481]
    wvu_b = c.x_fm[0:4, 0:4096]
    aout = [c.ob[:, 128:640], c.wbuf[:, 7680:8192]]
    K.wait("sync", s_cq)
    for g in range(8):
        K.dma("sync", c.ob[:, g * 16:(g + 1) * 16].rearrange("p (a b) -> p a b", b=4),
              ckv_full_ap[g * 512:(g + 1) * 512, :].rearrange("(a p) b -> p a b", p=128),
              s_w, allow_slow_non_contiguous=True)
    K.dma("sync", xff[0:4, 4096:8192], wvu_ap, s_w)
    K.dma("sync", hwf[0:12, 0:128], wqk_ap, s_w)
    K.dma("sync", bqk4, bqk4_ap, s_w)
    K.wait("vector", s_w)
    K.op("vector", lambda e: e.tensor_copy(wvu_b, xff[0:4, 4096:8192]), inc=s_v)
    K.op("vector", lambda e: e.tensor_copy(wqk_b, hwf[0:12, 0:128]), inc=s_v)
    K.wait("vector", s_v)
    K.op("vector", lambda e: e.engine_nop(), inc=s_v)
    K.wait("tensor", s_v)
    for kc in range(NDC):
        if kc >= 2:
            K.wait("tensor", s_v, s_v.cnt)
        K.op("tensor", lambda e, kc=kc: e.transpose(
            c.ps[6 + kc % 2][:].bitcast(BF16)[0:4, 0:128],
            c.ob[:, kc * 4:(kc + 1) * 4], c.identb[:]), inc=s_pe)
        K.wait("vector", s_pe)
        K.op("vector", lambda e, kc=kc: e.tensor_copy(
            ckv_fm[0:4, kc * 128:(kc + 1) * 128],
            c.ps[6 + kc % 2][:].bitcast(BF16)[0:4, 0:128]), inc=s_v)
    K.wait("vector", s_v)
    K.op("vector", lambda e: e.engine_nop(), inc=s_v)
    K.wait("tensor", s_v)
    for h in range(32):
        if h >= 1:
            K.wait("sync", s_v)
        K.dma("sync", bvu_col, bvu_ap[h], s_b)
        K.wait("tensor", s_b)
        if h >= 1:
            K.wait("tensor", s_v)
        for ts in range(2):
            K.op("tensor", lambda e, ts=ts, h=h: e.matmul(
                c.ps[6 + ts][0:4, 0:512], wqk_b[0:12, 4 * h:4 * h + 4],
                cq_fm[0:12, ts * 512:(ts + 1) * 512], start=True, stop=True), inc=s_pe)
        K.wait("vector", s_pe)
        for ts in range(2):
            K.op("vector", lambda e, ts=ts, h=h: e.tensor_scalar(
                q4h_sb[0:4, ts * 512:(ts + 1) * 512], c.ps[6 + ts][0:4, 0:512],
                bqk4[0:4, h:h + 1], 0.5, ALU.add, ALU.mult), inc=s_v)
        K.wait("vector", s_v)
        K.op("vector", lambda e: e.engine_nop(), inc=s_v)
        K.wait("tensor", s_v)
        K.wait("scalar", s_v)
        for qs in range(2):
            K.wait("tensor", s_v)
            sread = {}
            tdone = {}
            SABANK = [0, 1, 4, 5]

            def tden(kc):
                p2 = kc % 2
                K.wait("tensor", s_s, sread[kc])
                K.op("tensor", lambda e, kc=kc, p2=p2: e.matmul(
                    c.ps[2][0:4, 0:512], c.ob[:, kc * 4:(kc + 1) * 4], P[p2],
                    start=(kc == 0), stop=(kc == NDC - 1)), inc=None)
                K.op("tensor", lambda e, kc=kc, p2=p2: e.matmul(
                    c.ps[3][0:1, 0:512], c.onescol[:], P[p2],
                    start=(kc == 0), stop=(kc == NDC - 1)), inc=s_t)
                tdone[kc] = s_t.cnt

            for kc in range(NDC):
                bk = SABANK[kc % 4]
                p2 = kc % 2
                if kc >= 4:
                    K.wait("tensor", s_s, sread[kc - 4])
                K.op("tensor", lambda e, kc=kc, bk=bk, qs=qs: e.matmul(
                    c.ps[bk][:, 0:512], ckv_fm[0:4, kc * 128:(kc + 1) * 128],
                    q4h_sb[0:4, qs * 512:(qs + 1) * 512], start=True, stop=True), inc=s_pe)
                mmv = s_pe.cnt
                if kc >= 1:
                    tden(kc - 1)
                if kc >= 2:
                    K.wait("scalar", s_t, tdone[kc - 2])
                K.wait("scalar", s_pe, mmv)
                K.op("scalar", lambda e, bk=bk, p2=p2: e.activation(
                    P[p2], c.ps[bk][:, 0:512], AF.Exp), inc=s_s)
                sread[kc] = s_s.cnt
            tden(NDC - 1)
            K.wait("vector", s_t)
            K.op("vector", lambda e: e.tensor_copy(c.rs[0:4, 0:512], c.ps[2][0:4, 0:512]), inc=s_v)
            K.op("vector", lambda e: e.reciprocal(hwf[0:1, 0:512], c.ps[3][0:1, 0:512]), inc=s_v)
            K.wait("vector", s_v)
            K.op("vector", lambda e: e.engine_nop(), inc=s_v)
            K.wait("tensor", s_v)
            K.op("tensor", lambda e: e.matmul(c.ps[4][0:4, 0:512], c.onesr[0:1, 0:4],
                 hwf[0:1, 0:512], start=True, stop=True), inc=s_pe)
            K.wait("vector", s_pe)
            K.op("vector", lambda e: e.tensor_tensor(tn_sb, c.rs[0:4, 0:512],
                 c.ps[4][0:4, 0:512], ALU.mult), inc=s_v)
            K.wait("vector", s_v)
            K.op("vector", lambda e: e.engine_nop(), inc=s_v)
            K.wait("tensor", s_v)
            K.op("tensor", lambda e, h=h: e.matmul(c.ps[5][:, 0:512],
                 wvu_b[0:4, h * 128:(h + 1) * 128], tn_sb, start=True, stop=True), inc=s_pe)
            K.wait("vector", s_pe)
            idx = h * 2 + qs
            if idx >= 2:
                K.wait("vector", s_ao[qs])
            K.op("vector", lambda e, qs=qs: e.tensor_scalar(
                aout[qs], c.ps[5][:, 0:512], bvu_col, None, ALU.add), inc=s_v)
            K.wait("sync", s_v)
            K.dma("sync", attn_dram[0:128, h * 1024 + qs * 512: h * 1024 + (qs + 1) * 512],
                  aout[qs], s_ao[qs])
    K.wait("sync", s_ao[0])
    K.wait("sync", s_ao[1])
    return s_ao


def l1b_host_one(codes, scales):
    cst = w_stream(codes.astype(np.int32), dcg=4)
    n_slab = cst.shape[0]
    ng = codes.shape[0] // 512
    rows = []
    for s in range(n_slab):
        os_, g = divmod(s, ng)
        for j in range(2):
            for q in range(2):
                dcs = g * 4 + j * 2 + q
                rows.append(scales[dcs, os_ * 512:(os_ + 1) * 512])
    srow = np.ascontiguousarray(np.stack(rows)).astype(np.float32)
    return cst, srow


def l1b_host(q_codes, q_scales, k_codes, k_scales, v_codes, v_scales, o_codes, o_scales):
    mk_stream = l1b_host_one
    qc, qs_ = mk_stream(q_codes, q_scales)
    kc_, ks_ = mk_stream(k_codes, k_scales)
    vc, vs_ = mk_stream(v_codes, v_scales)
    oc, os_ = mk_stream(o_codes, o_scales)
    return qc, qs_, kc_, ks_, vc, vs_, oc, os_


def rope_host(core):
    b, r = divmod(core, 4)
    pos = np.arange(r * T, (r + 1) * T, dtype=np.float32)
    inv = (1.0 / 10000.0 ** (np.arange(0, 128, 2, dtype=np.float32) / 128.0))
    f = pos[:, None] * inv[None, :]
    cos = np.cos(f).astype(np.float32)
    sin = np.sin(f).astype(np.float32)
    cosq = np.tile(cos, (1, 4)).reshape(8, 128, 256)
    sinq = np.tile(sin, (1, 4)).reshape(8, 128, 256)
    return np.ascontiguousarray(cosq), np.ascontiguousarray(sinq)


def emit_l1b_dequant(K, c, cst_ap, srow_ap, wdq_ap, tag=""):
    s_cd = [K.sem("s_dq_c0" + tag), K.sem("s_dq_c1" + tag)]
    s_sc = [K.sem("s_dq_s0" + tag), K.sem("s_dq_s1" + tag)]
    s_v = K.sem("s_dq_v" + tag)
    s_o = [K.sem("s_dq_o0" + tag), K.sem("s_dq_o1" + tag)]
    wbi = c.wbuf[:].bitcast(I32)
    cpiece = lambda p: wbi[:, 1024 + p * 1024: 2048 + p * 1024]
    scslab = [c.rs[:, 0:512], c.hw[:].bitcast(F32)[:, 0:512]]
    n_slab = cst_ap.shape[0]
    stt_done = []
    for qq in range(2 * n_slab):
        slab, j = divmod(qq, 2)
        cpar = qq % 2
        if qq >= 2:
            K.wait("sync", s_v, stt_done[2 * qq - 3])
        K.dma("sync", cpiece(cpar), cst_ap[slab][:, j * 1024:(j + 1) * 1024],
              s_cd[cpar])
        for q in range(2):
            r = qq * 2 + q
            p2 = r % 2
            if r >= 2:
                K.wait("scalar", s_v, stt_done[r - 2])
            K.dma("scalar", scslab[p2], srow_ap[r, :].partition_broadcast(128),
                  s_sc[p2])
            K.wait("vector", s_sc[p2])
            K.wait("vector", s_cd[cpar])
            if r >= 2:
                K.wait("vector", s_o[p2])
            K.op("vector", lambda e, q=q, p2=p2, cpar=cpar: e.scalar_tensor_tensor(
                c.wbuf[:, 6144 + p2 * 512: 6144 + (p2 + 1) * 512],
                cpiece(cpar)[:, q * 512:(q + 1) * 512], -8.0, scslab[p2],
                ALU.add, ALU.mult), inc=s_v)
            stt_done.append(s_v.cnt)
            K.wait("sync", s_v)
            K.dma("sync", wdq_ap[slab][:, j * 1024 + q * 512: j * 1024 + (q + 1) * 512],
                  c.wbuf[:, 6144 + p2 * 512: 6144 + (p2 + 1) * 512], s_o[p2])
    K.wait("sync", s_o[0])
    K.wait("sync", s_o[1])
    K.wait("scalar", s_v)
    return s_o


def emit_l1b_qkv(K, c, wq_dq_ap, wkv_dq_ap, cosq_ap, sinq_ap, q_own_ap, kv_own_ap,
                 ln_sems, tag=""):
    sv, sa, sp = ln_sems
    hwf = c.hw[:].bitcast(F32)
    s_cs = K.sem("s_qb_cs" + tag)
    s_oo = K.sem("s_qb_oo" + tag)
    cosb = c.ob[:, 0:256]
    sinb = c.ob[:, 256:512]
    ob_i = c.ob[:, 512:1024]

    def rope_dest(out_ap, do_rope):
        def dest(K, c, tt, os, psum, s_ev):
            K.wait("vector", s_ev)
            if do_rope(os):
                K.wait("sync", s_ev)
                K.dma("sync", cosb, cosq_ap[tt], s_cs)
                K.dma("sync", sinb, sinq_ap[tt], s_cs)
                pe_ = psum[:, 0:512:2]
                po_ = psum[:, 1:512:2]
                t1 = hwf[:, 0:256]
                t2 = hwf[:, 256:512]
                K.wait("vector", s_cs)
                K.op("vector", lambda e: e.tensor_tensor(t1, pe_, cosb, ALU.mult), inc=s_ev)
                K.op("vector", lambda e: e.tensor_tensor(t2, po_, sinb, ALU.mult), inc=s_ev)
                K.wait("vector", s_ev)
                K.op("vector", lambda e: e.tensor_sub(ob_i[:, 0:512:2], t1, t2), inc=s_ev)
                K.wait("vector", s_ev)
                K.op("vector", lambda e: e.tensor_tensor(t1, po_, cosb, ALU.mult), inc=s_ev)
                K.op("vector", lambda e: e.tensor_tensor(t2, pe_, sinb, ALU.mult), inc=s_ev)
                K.wait("vector", s_ev)
                K.op("vector", lambda e: e.tensor_add(ob_i[:, 1:512:2], t1, t2), inc=s_ev)
            else:
                K.op("vector", lambda e, p=psum: e.tensor_copy(ob_i, p), inc=s_ev)
            pb = psum.bitcast(BF16)
            stg = c.rs[:].bitcast(BF16)
            K.wait("tensor", s_ev)
            for j in range(4):
                K.op("tensor", lambda e, j=j: e.transpose(
                    pb[0:128, j * 128:(j + 1) * 128],
                    ob_i[:, j * 128:(j + 1) * 128], c.identb[:]), inc=s_ev)
            K.wait("vector", s_ev)
            K.wait("vector", s_oo)
            for j in range(4):
                K.op("vector", lambda e, j=j: e.tensor_copy(
                    stg[:, j * 128:(j + 1) * 128], pb[0:128, j * 128:(j + 1) * 128]),
                    inc=s_ev)
            K.wait("sync", s_ev)
            for j in range(4):
                K.dma("sync", out_ap[os * 512 + j * 128:os * 512 + (j + 1) * 128,
                                     tt * 128:(tt + 1) * 128],
                      stg[:, j * 128:(j + 1) * 128], s_oo)
        return dest

    s_ev1, _, _ = emit_proj_tok(K, c, wq_dq_ap, 8, rope_dest(q_own_ap, lambda os: True),
                                name="pj", prequant=True,
                                pe_waits=(sv,), dma_waits=(sv,))
    s_ev2, _, _ = emit_proj_tok(K, c, wkv_dq_ap, 4, rope_dest(kv_own_ap, lambda os: os < 2),
                                name="pj", prequant=True,
                                pe_waits=(s_ev1,), dma_waits=(s_ev1,))
    K.wait("sync", s_oo)
    return s_ev2, s_oo


def l1b_masks(nc):
    m = np.zeros((4, 128, 512), np.float32)
    for j in range(4):
        for kl in range(128):
            m[j, kl, :] = np.where(j * 128 + kl <= np.arange(512), 0.0, -1e30)
    mt = np.zeros((4, 128, 512), np.float32)
    for i in range(4):
        for ql in range(128):
            mt[i, ql, :] = np.where(np.arange(512) <= i * 128 + ql, 0.0, -1e30)
    return (nc.inline_tensor(m.astype(BF16NP), "l1bmasks").ap(),
            nc.inline_tensor(mt.astype(BF16NP), "l1bmaskst").ap())


def emit_l1b_attn(K, c, qloc_ap, kloc_ap, vloc_ap, masks_aps, attn_own_ap, tag=""):
    masks_ap, maskst_ap = masks_aps
    """Head-sharded causal attention, 2 units x 4 qheads. VARIANT B:
    pass1 rowmax (S_tok), pass2 S^T + exp + mask-add + den + attn@v; divide at evac.
    Inputs are feature-major per-core slices: qloc [4,1024,1024] (g, qdim-window,
    tok), kloc/vloc [4,256,1024] (g, unit*128+feat, tok). Output attn_own_b
    [4096, 1024]: rows (qs//2)*1024 + u*512 + qh*128 + feat, cols token-in-block.
    All big buffers alias the x_fm region."""
    xf = c.x_fm
    K_fm = xf[:, 0:4096]
    V_sb = xf[:, 4096:8192]
    q_fm = xf[:, 8192:12288]
    P = [xf[:, 12288:12800], xf[:, 12800:13312]]
    maskb = [xf[:, 13312 + j * 512: 13312 + (j + 1) * 512] for j in range(4)]
    masktb = [xf[:, 16896 + j * 512: 16896 + (j + 1) * 512] for j in range(4)]
    nm_sb = xf[0:1, 15360:15872]
    aout = [xf[:, 15872:16384], xf[:, 16384:16896]]
    hwf = c.hw[:].bitcast(F32)
    s_kv = K.sem("s_ab_kv" + tag)
    s_q = K.sem("s_ab_q" + tag)
    s_pe = K.sem("s_ab_pe" + tag)
    s_v = K.sem("s_ab_v" + tag)
    s_s = K.sem("s_ab_s" + tag)
    s_t = K.sem("s_ab_t" + tag)
    s_m = K.sem("s_ab_m" + tag)
    s_ao = [K.sem("s_ab_ao0" + tag), K.sem("s_ab_ao1" + tag)]
    s_mk = K.sem("s_ab_mk" + tag)
    s_kf = K.sem("s_ab_kf" + tag)
    s_ksc = K.sem("s_ab_ksc" + tag)
    RSQ = float(1.0 / np.sqrt(128.0))
    for j in range(4):
        K.dma("sync", maskb[j], masks_ap[j], s_mk)
        K.dma("sync", masktb[j], maskst_ap[j], s_mk)
    K.wait("vector", s_mk)
    s_vt = K.sem("s_ab_vt" + tag)
    for u in range(2):
        if u > 0:
            K.wait("sync", s_v)
            K.wait("sync", s_pe)
            K.wait("vector", s_pe)
            K.wait("tensor", s_v)
        for g in range(4):
            K.dma("sync", K_fm[:, g * 1024:(g + 1) * 1024],
                  kloc_ap[g][u * 128:(u + 1) * 128, :], s_kf)
            K.dma("sync", q_fm[:, g * 1024:(g + 1) * 1024],
                  vloc_ap[g][u * 128:(u + 1) * 128, :], s_kv)
        K.wait("vector", s_kf)
        for j in range(8):
            K.op("vector", lambda e, j=j: e.tensor_scalar(
                K_fm[:, j * 512:(j + 1) * 512], K_fm[:, j * 512:(j + 1) * 512],
                RSQ, None, ALU.mult), inc=s_ksc)
        K.wait("tensor", s_kv)
        for kc in range(NDC):
            par = kc % 2
            if kc >= 2:
                K.wait("tensor", s_vt)
            K.op("tensor", lambda e, kc=kc, par=par: e.transpose(
                c.ps[6 + par][:].bitcast(BF16)[0:128, 0:128],
                q_fm[:, kc * 128:(kc + 1) * 128], c.identb[:]), inc=s_pe)
            K.wait("vector", s_pe)
            K.op("vector", lambda e, kc=kc, par=par: e.tensor_copy(
                V_sb[:, kc * 128:(kc + 1) * 128],
                c.ps[6 + par][:].bitcast(BF16)[0:128, 0:128]), inc=s_vt)
        K.wait("tensor", s_vt)
        K.wait("sync", s_vt)
        for qh in range(4):
            if u > 0 or qh > 0:
                K.wait("sync", s_pe)
            for g in range(4):
                K.dma("sync", q_fm[:, g * 1024:(g + 1) * 1024],
                      qloc_ap[g][u * 512 + qh * 128:u * 512 + (qh + 1) * 128, :], s_q)
            K.wait("tensor", s_q)
            K.wait("tensor", s_kv)
            K.wait("tensor", s_ksc)
            K.wait("tensor", s_mk)
            for qs in range(8):
                K.wait("tensor", s_v)
                rd_done = {}
                mcol = lambda i: c.rs[:, 200 + i:201 + i]
                tmp = lambda i: c.rs[:, 204 + i:205 + i]
                for ck in range(qs + 1):
                    b0 = 4 * (ck % 2)
                    if ck >= 2:
                        K.wait("tensor", s_m, rd_done[ck - 2])
                    for i in range(4):
                        K.op("tensor", lambda e, i=i, ck=ck, qs=qs, b0=b0: e.matmul(
                            c.ps[b0 + i][:, 0:512],
                            q_fm[:, (qs * 4 + i) * 128:(qs * 4 + i + 1) * 128],
                            K_fm[:, ck * 512:(ck + 1) * 512], start=True, stop=True),
                            inc=(s_pe if i == 3 else None))
                    K.wait("vector", s_pe)
                    if ck == qs:
                        bs = 4 * ((ck + 1) % 2)
                        if ck >= 1:
                            K.wait("vector", s_m)
                        for i in range(4):
                            K.op("vector", lambda e, i=i, b0=b0, bs=bs: e.tensor_tensor(
                                c.ps[bs + i][:, 0:512], c.ps[b0 + i][:, 0:512],
                                masktb[i], ALU.add), inc=s_m)
                        K.wait("vector", s_m)
                        for i in range(4):
                            d = mcol(i) if ck == 0 else tmp(i)
                            K.op("vector", lambda e, d=d, i=i, bs=bs: e.reduce_max(
                                d, c.ps[bs + i][:, 0:512], axis=AX.X), inc=s_m)
                    else:
                        if ck >= 2:
                            K.wait("vector", s_m)
                        for i in range(4):
                            d = mcol(i) if ck == 0 else tmp(i)
                            K.op("vector", lambda e, d=d, i=i, b0=b0: e.reduce_max(
                                d, c.ps[b0 + i][:, 0:512], axis=AX.X), inc=s_m)
                    if ck > 0:
                        K.wait("vector", s_m)
                        for i in range(4):
                            K.op("vector", lambda e, m=mcol(i), t=tmp(i):
                                 e.tensor_max(m, m, t), inc=s_m)
                    rd_done[ck] = s_m.cnt
                K.wait("vector", s_m)
                nmb = c.rs[:, 208:210].bitcast(BF16)
                K.op("vector", lambda e, n=nmb: e.tensor_scalar(
                    n[:, 0:4], c.rs[:, 200:204], -1.0, None, ALU.mult), inc=s_m)
                K.wait("vector", s_m)
                K.wait("tensor", s_m)
                nm_done = {}
                for i in range(4):
                    if i >= 2:
                        K.wait("tensor", s_m, nm_done[i - 2])
                    K.op("tensor", lambda e, i=i, n=nmb: e.transpose(
                        c.ps[6 + i % 2][:].bitcast(BF16)[0:1, 0:128],
                        n[:, i:i + 1], c.identb[:]), inc=s_pe)
                    K.wait("vector", s_pe)
                    K.op("vector", lambda e, i=i: e.tensor_copy(
                        nm_sb[0:1, i * 128:(i + 1) * 128],
                        c.ps[6 + i % 2][:].bitcast(BF16)[0:1, 0:128]), inc=s_m)
                    nm_done[i] = s_m.cnt
                K.wait("tensor", s_m)
                sread = {}
                tdone = {}
                nkt = 4 * (qs + 1)

                def denpv(kt, nkt=nkt):
                    p2 = kt % 2
                    K.wait("tensor", s_s, sread[kt])
                    K.op("tensor", lambda e, p2=p2, st=(kt == 0), sp=(kt == nkt - 1):
                         e.matmul(c.ps[4][0:1, 0:512], c.onescol[:], P[p2],
                                  start=st, stop=sp), inc=None)
                    K.op("tensor", lambda e, kt=kt, p2=p2, st=(kt == 0),
                         sp=(kt == nkt - 1):
                         e.matmul(c.ps[5][:, 0:512], V_sb[:, kt * 128:(kt + 1) * 128],
                                  P[p2], start=st, stop=sp), inc=s_t)
                    tdone[kt] = s_t.cnt

                for kt in range(nkt):
                    par = kt % 4
                    p2 = kt % 2
                    if kt >= 4:
                        K.wait("tensor", s_s, sread[kt - 4])
                    K.op("tensor", lambda e, kt=kt, par=par, qs=qs: e.matmul(
                        c.ps[par][:, 0:512], K_fm[:, kt * 128:(kt + 1) * 128],
                        q_fm[:, qs * 512:(qs + 1) * 512], start=True, stop=True), inc=None)
                    K.op("tensor", lambda e, par=par: e.matmul(
                        c.ps[par][:, 0:512], c.onesrb[:], nm_sb,
                        start=False, stop=True, skip_group_check=True), inc=s_pe)
                    mmv = s_pe.cnt
                    if kt >= 1:
                        denpv(kt - 1)
                    if kt >= 2:
                        K.wait("scalar", s_t, tdone[kt - 2])
                    if kt >= 4 * qs:
                        j = kt - 4 * qs
                        K.wait("vector", s_pe, mmv)
                        K.wait("vector", s_s)
                        K.op("vector", lambda e, par=par, j=j: e.tensor_tensor(
                            hwf[:, 0:512], c.ps[par][:, 0:512], maskb[j], ALU.add),
                            inc=s_s)
                        K.wait("scalar", s_s)
                        K.op("scalar", lambda e, p2=p2: e.activation(
                            P[p2], hwf[:, 0:512], AF.Exp), inc=s_s)
                    else:
                        K.wait("scalar", s_pe, mmv)
                        K.op("scalar", lambda e, par=par, p2=p2: e.activation(
                            P[p2], c.ps[par][:, 0:512], AF.Exp), inc=s_s)
                    sread[kt] = s_s.cnt
                denpv(nkt - 1)
                K.wait("vector", s_t)
                K.op("vector", lambda e: e.reciprocal(c.rs[0:1, 216:728] if False else hwf[0:1, 0:512], c.ps[4][0:1, 0:512]), inc=s_v)
                K.wait("vector", s_v)
                K.op("vector", lambda e: e.engine_nop(), inc=s_v)
                K.wait("tensor", s_v)
                K.op("tensor", lambda e: e.matmul(c.ps[6][:, 0:512], c.onesr[:],
                     hwf[0:1, 0:512], start=True, stop=True), inc=s_pe)
                K.wait("vector", s_pe)
                K.op("vector", lambda e: e.tensor_copy(hwf[:, 0:512] if False else c.hw[:, 512:1024].bitcast(F32) if False else hwf[:, 0:512], c.ps[5][:, 0:512]), inc=s_v)
                K.wait("vector", s_v)
                apar = qs % 2
                idx = ((u * 4 + qh) * 8 + qs)
                if idx >= 2:
                    K.wait("vector", s_ao[apar])
                K.op("vector", lambda e, apar=apar: e.tensor_tensor(
                    aout[apar], hwf[:, 0:512], c.ps[6][:, 0:512], ALU.mult), inc=s_v)
                K.wait("sync", s_v)
                r0 = (qs // 2) * 1024 + u * 512 + qh * 128
                c0 = (qs % 2) * 512
                K.dma("sync", attn_own_ap[r0:r0 + 128, c0:c0 + 512],
                      aout[apar], s_ao[apar])
    K.wait("sync", s_ao[0])
    K.wait("sync", s_ao[1])
    K.wait("vector", s_ao[0], s_ao[0].cnt)
    K.wait("vector", s_ao[1], s_ao[1].cnt)
    return s_ao



def _pre(nc, K, x_in):
    c = alloc_common(nc, K)
    s_x = emit_load_x(K, c, x_in)
    for e in ("vector", "scalar", "tensor"):
        K.wait(e, s_x)
    ln = emit_ln_xfm(K, c)
    return c, ln


def _build_l0a_A(nc):
    K = MK(nc)
    x_in = nc.dram_tensor("x_in", [T, D], F32, kind="ExternalInput").ap()
    wkvst = nc.dram_tensor("wkvst", [32, 128, 2048], F32, kind="ExternalInput").ap()
    kvbias = nc.dram_tensor("kvbias", [1, 2048], F32, kind="ExternalInput").ap()
    kvout = nc.dram_tensor("kvout", [128, 1032], F32, kind="ExternalOutput").ap()
    kv_dram = nc.dram_tensor("kv_dram", [8, 128, 2048], BF16, kind="Internal").ap()
    c, ln = _pre(nc, K, x_in)
    K.scope("l0a_kvp")
    s_ev, s_kvo = emit_l0a_kvproj(K, c, wkvst, kvbias, kv_dram, ln)
    emit_l0a_kvstate(K, c, kv_dram, s_kvo, kvout)
    return K.build()


def _build_l0a_B(nc):
    K = MK(nc)
    x_in = nc.dram_tensor("x_in", [T, D], F32, kind="ExternalInput").ap()
    kvparts = nc.dram_tensor("kvparts", [4, 128, 1032], F32, kind="ExternalInput").ap()
    wqst = nc.dram_tensor("wqst", [64, 128, 2048], F32, kind="ExternalInput").ap()
    bqa = nc.dram_tensor("bqa", [32, 1, 128], F32, kind="ExternalInput").ap()
    wost = nc.dram_tensor("wost", [64, 128, 2048], F32, kind="ExternalInput").ap()
    bo = nc.dram_tensor("bo", [1, 4096], F32, kind="ExternalInput").ap()
    x_out = nc.dram_tensor("x_out", [T, D], F32, kind="ExternalOutput").ap()
    attn_dram = nc.dram_tensor("attn_dram", [128, 32768], BF16, kind="Internal").ap()
    c, ln = _pre(nc, K, x_in)
    sv, sa, sp = ln
    K.wait("sync", sp)
    K.scope("l0a_red")
    s_kv = emit_l0a_kvreduce_in(K, c, kvparts)
    K.scope("l0a_att")
    s_ao = emit_l0a_attn(K, c, wqst, bqa, attn_dram, s_kv, ln)
    s_ev = emit_l0a_oproj(K, c, attn_dram, wost, bo, s_ao)
    K.wait("sync", s_ev)
    emit_store_x(K, c, x_out)
    return K.build()


def _build_l0ffn(nc):
    K = MK(nc)
    x_in = nc.dram_tensor("x_in", [T, D], F32, kind="ExternalInput").ap()
    wg9 = nc.dram_tensor("wg9", [128, 288], F32, kind="ExternalInput").ap()
    ast = nc.dram_tensor("ast", [32, 128, 128], F32, kind="ExternalInput").ap()
    wbst = nc.dram_tensor("wbst", [136, 128, 1024], F32, kind="ExternalInput").ap()
    bias = nc.dram_tensor("bias", [1, 4096], F32, kind="ExternalInput").ap()
    bthr = nc.dram_tensor("bthr", [1, 1], F32, kind="ExternalInput").ap()
    x_out = nc.dram_tensor("x_out", [T, D], F32, kind="ExternalOutput").ap()
    c = alloc_common(nc, K)
    e8_ap = l0ffn_consts(nc)
    s_x = emit_load_x(K, c, x_in)
    for e in ("vector", "scalar", "tensor"):
        K.wait(e, s_x)
    ln = emit_ln_xfm(K, c)
    s_ev = emit_l0ffn(K, c, wg9, ast, wbst, bias, bthr, e8_ap, ln)
    K.wait("sync", s_ev)
    emit_store_x(K, c, x_out)
    return K.build()


def _build_l0ab(nc):
    K = MK(nc)
    x_in = nc.dram_tensor("x_in", [T, D], F32, kind="ExternalInput").ap()
    maskT = nc.dram_tensor("maskT", [D, D], I32, kind="ExternalInput").ap()
    wst = nc.dram_tensor("wst", [64, 128, 2048], F32, kind="ExternalInput").ap()
    x_out = nc.dram_tensor("x_out", [T, D], F32, kind="ExternalOutput").ap()
    wm = nc.dram_tensor("wm", [64, 128, 2048], BF16, kind="Internal").ap()
    c = alloc_common(nc, K)
    sel4_ap, ind_ap = l0ab_consts(nc)
    s_x = emit_load_x(K, c, x_in)
    for e in ("vector", "scalar", "tensor"):
        K.wait(e, s_x)
    ln_sems = emit_ln_xfm(K, c)
    s_ev = emit_l0ab(K, c, maskT, wst, wm, sel4_ap, ind_ap, ln_sems)
    K.wait("sync", s_ev)
    emit_store_x(K, c, x_out)
    return K.build()


def _build_l1a_A(nc):
    K = MK(nc)
    x_in = nc.dram_tensor("x_in", [T, D], F32, kind="ExternalInput").ap()
    wc16 = nc.dram_tensor("wc16", [128, 512], F32, kind="ExternalInput").ap()
    cbias = nc.dram_tensor("cbias", [1, 16], F32, kind="ExternalInput").ap()
    ckv_own = nc.dram_tensor("ckv_own", [T, 4], BF16, kind="ExternalOutput").ap()
    c, ln = _pre(nc, K, x_in)
    emit_l1a_cproj(K, c, wc16, cbias, ckv_own, ln, want_cq=False)
    return K.build()


def _build_l1a_B(nc):
    K = MK(nc)
    x_in = nc.dram_tensor("x_in", [T, D], F32, kind="ExternalInput").ap()
    ckv_full = nc.dram_tensor("ckv_full", [4096, 4], BF16, kind="ExternalInput").ap()
    wc16 = nc.dram_tensor("wc16", [128, 512], F32, kind="ExternalInput").ap()
    cbias = nc.dram_tensor("cbias", [1, 16], F32, kind="ExternalInput").ap()
    wqk = nc.dram_tensor("wqk", [12, 128], F32, kind="ExternalInput").ap()
    bqk4 = nc.dram_tensor("bqk4", [4, 32], F32, kind="ExternalInput").ap()
    wvu = nc.dram_tensor("wvu", [4, 4096], F32, kind="ExternalInput").ap()
    bvu = nc.dram_tensor("bvu", [32, 128, 1], F32, kind="ExternalInput").ap()
    wost = nc.dram_tensor("wost", [64, 128, 2048], F32, kind="ExternalInput").ap()
    bo = nc.dram_tensor("bo", [1, 4096], F32, kind="ExternalInput").ap()
    x_out = nc.dram_tensor("x_out", [T, D], F32, kind="ExternalOutput").ap()
    ckv_own2 = nc.dram_tensor("ckv_own2", [T, 4], BF16, kind="Internal").ap()
    attn_dram = nc.dram_tensor("attn_dram", [128, 32768], BF16, kind="Internal").ap()
    c, ln = _pre(nc, K, x_in)
    s_v, s_o = emit_l1a_cproj(K, c, wc16, cbias, ckv_own2, ln, want_cq=True)
    s_ao = emit_l1a_attn(K, c, ckv_full, wqk, bqk4, wvu, bvu, attn_dram, s_o)
    s_ev = emit_l0a_oproj(K, c, attn_dram, wost, bo, s_ao)
    K.wait("sync", s_ev)
    emit_store_x(K, c, x_out)
    return K.build()


def _build_l1ffn_A(nc):
    K = MK(nc)
    x_in = nc.dram_tensor("x_in", [T, D], F32, kind="ExternalInput").ap()
    rw8 = nc.dram_tensor("rw8", [128, 256], F32, kind="ExternalInput").ap()
    selb = nc.dram_tensor("selb", [128, 8], BF16, kind="ExternalInput").ap()
    rpart = nc.dram_tensor("rpart", [1, 8], F32, kind="ExternalOutput").ap()
    c, ln = _pre(nc, K, x_in)
    emit_l1ffn_router_partial(K, c, rw8, selb, rpart, ln)
    return K.build()


def _build_l1ffn_B(nc):
    K = MK(nc)
    x_in = nc.dram_tensor("x_in", [T, D], F32, kind="ExternalInput").ap()
    rparts = nc.dram_tensor("rparts", [4, 1, 8], F32, kind="ExternalInput").ap()
    rb = nc.dram_tensor("rb", [1, 8], F32, kind="ExternalInput").ap()
    ast = nc.dram_tensor("ast", [32, 128, 128], F32, kind="ExternalInput").ap()
    wbst = nc.dram_tensor("wbst", [136, 128, 1024], F32, kind="ExternalInput").ap()
    bias = nc.dram_tensor("bias", [1, 4096], F32, kind="ExternalInput").ap()
    x_out = nc.dram_tensor("x_out", [T, D], F32, kind="ExternalOutput").ap()
    c = alloc_common(nc, K)
    e8_ap = l0ffn_consts(nc)
    s_x = emit_load_x(K, c, x_in)
    for e in ("vector", "scalar", "tensor"):
        K.wait(e, s_x)
    ln = emit_ln_xfm(K, c)
    s_route = emit_l1ffn_routing(K, c, rparts, rb, e8_ap)
    s_ev = emit_l1ffn_main(K, c, ast, wbst, bias, s_route, ln)
    K.wait("sync", s_ev)
    emit_store_x(K, c, x_out)
    return K.build()


def _build_l1b_A(nc):
    K = MK(nc)
    x_in = nc.dram_tensor("x_in", [T, D], F32, kind="ExternalInput").ap()
    qcst = nc.dram_tensor("qcst", [64, 128, 2048], I32, kind="ExternalInput").ap()
    qsrow = nc.dram_tensor("qsrow", [256, 1, 512], F32, kind="ExternalInput").ap()
    kvcst = nc.dram_tensor("kvcst", [32, 128, 2048], I32, kind="ExternalInput").ap()
    kvsrow = nc.dram_tensor("kvsrow", [128, 1, 512], F32, kind="ExternalInput").ap()
    cosq = nc.dram_tensor("cosq", [8, 128, 256], BF16, kind="ExternalInput").ap()
    sinq = nc.dram_tensor("sinq", [8, 128, 256], BF16, kind="ExternalInput").ap()
    q_own = nc.dram_tensor("q_own", [T, 4096], BF16, kind="ExternalOutput").ap()
    kv_own = nc.dram_tensor("kv_own", [T, 2048], BF16, kind="ExternalOutput").ap()
    wq_dq = nc.dram_tensor("wq_dq", [64, 128, 2048], BF16, kind="Internal").ap()
    wkv_dq = nc.dram_tensor("wkv_dq", [32, 128, 2048], BF16, kind="Internal").ap()
    c, ln = _pre(nc, K, x_in)
    sv, sa, sp = ln
    K.wait("sync", sp)
    K.wait("vector", sp)
    K.scope("l1b_dq")
    s_o1 = emit_l1b_dequant(K, c, qcst, qsrow, wq_dq, tag="dq")
    s_o2 = emit_l1b_dequant(K, c, kvcst, kvsrow, wkv_dq, tag="dq")
    for w in s_o1 + s_o2:
        K.wait("sync", w)
        K.wait("vector", w)
    emit_l1b_qkv(K, c, wq_dq, wkv_dq, cosq, sinq, q_own, kv_own, ln)
    return K.build()


def _build_l1b_B(nc):
    K = MK(nc)
    qu = nc.dram_tensor("qu", [2, 4096, 512], BF16, kind="ExternalInput").ap()
    ku = nc.dram_tensor("ku", [2, 4096, 128], BF16, kind="ExternalInput").ap()
    vu = nc.dram_tensor("vu", [2, 4096, 128], BF16, kind="ExternalInput").ap()
    attn_own = nc.dram_tensor("attn_own", [1024, 4096], BF16, kind="ExternalOutput").ap()
    c = alloc_common(nc, K)
    masks_aps = l1b_masks(nc)
    K.wait("sync", c.s_const)
    K.wait("vector", c.s_const)
    K.wait("tensor", c.s_const)
    emit_l1b_attn(K, c, qu, ku, vu, masks_aps, attn_own)
    return K.build()


def _build_l1b_C(nc):
    K = MK(nc)
    x_in = nc.dram_tensor("x_in", [T, D], F32, kind="ExternalInput").ap()
    ocst = nc.dram_tensor("ocst", [64, 128, 2048], I32, kind="ExternalInput").ap()
    osrow = nc.dram_tensor("osrow", [256, 1, 512], F32, kind="ExternalInput").ap()
    attn_fm = nc.dram_tensor("attn_fm", [128, 32768], BF16, kind="ExternalInput").ap()
    x_out = nc.dram_tensor("x_out", [T, D], F32, kind="ExternalOutput").ap()
    wo_dq = nc.dram_tensor("wo_dq", [64, 128, 2048], BF16, kind="Internal").ap()
    c = alloc_common(nc, K)
    s_x = emit_load_x(K, c, x_in)
    s_o1 = emit_l1b_dequant(K, c, ocst, osrow, wo_dq, tag="dq")
    for w in s_o1:
        K.wait("sync", w)
        K.wait("vector", w)
    s_ev = emit_l0a_oproj(K, c, attn_fm, wo_dq, None, [], prequant=True, name="l1bo")
    K.wait("sync", s_ev)
    emit_store_x(K, c, x_out)
    return K.build()


def _bar(K, sems, engines=("sync", "scalar", "vector", "tensor", "gpsimd")):
    for e in engines:
        for s in sems:
            K.wait(e, s)


def _build_fused(nc):
    K = MK(nc)
    EI, INT = "ExternalInput", "Internal"
    dt = nc.dram_tensor
    x_in = dt("x_in", [T, D], F32, kind=EI).ap()
    x_out = dt("x_out", [T, D], F32, kind="ExternalOutput").ap()
    wkvst = dt("wkvst", [32, 128, 2048], F32, kind=EI).ap()
    kvbias = dt("kvbias", [1, 2048], F32, kind=EI).ap()
    wqst = dt("wqst", [64, 128, 2048], F32, kind=EI).ap()
    bqa = dt("bqa", [32, 1, 128], F32, kind=EI).ap()
    wost0 = dt("wost0", [64, 128, 2048], F32, kind=EI).ap()
    bo0 = dt("bo0", [1, 4096], F32, kind=EI).ap()
    wg9 = dt("wg9", [128, 288], F32, kind=EI).ap()
    ast0 = dt("ast0", [32, 128, 128], F32, kind=EI).ap()
    wbst0 = dt("wbst0", [136, 128, 1024], F32, kind=EI).ap()
    bias0 = dt("bias0", [1, 4096], F32, kind=EI).ap()
    bthr0 = dt("bthr0", [1, 1], F32, kind=EI).ap()
    maskT = dt("maskT", [D, D], I32, kind=EI).ap()
    wst_ab = dt("wst_ab", [64, 128, 2048], F32, kind=EI).ap()
    wc16 = dt("wc16", [128, 512], F32, kind=EI).ap()
    cbias = dt("cbias", [1, 16], F32, kind=EI).ap()
    wqk = dt("wqk", [12, 128], F32, kind=EI).ap()
    bqk4 = dt("bqk4", [4, 32], F32, kind=EI).ap()
    wvu = dt("wvu", [4, 4096], F32, kind=EI).ap()
    bvu = dt("bvu", [32, 128, 1], F32, kind=EI).ap()
    wost1 = dt("wost1", [64, 128, 2048], F32, kind=EI).ap()
    bo1 = dt("bo1", [1, 4096], F32, kind=EI).ap()
    rw8 = dt("rw8", [128, 256], F32, kind=EI).ap()
    selb = dt("selb", [128, 8], BF16, kind=EI).ap()
    rb = dt("rb", [1, 8], F32, kind=EI).ap()
    ast1 = dt("ast1", [32, 128, 128], F32, kind=EI).ap()
    wbst1 = dt("wbst1", [136, 128, 1024], F32, kind=EI).ap()
    bias1 = dt("bias1", [1, 4096], F32, kind=EI).ap()
    qcst = dt("qcst", [64, 128, 2048], I32, kind=EI).ap()
    qsrow = dt("qsrow", [256, 512], F32, kind=EI).ap()
    kvcst = dt("kvcst", [32, 128, 2048], I32, kind=EI).ap()
    kvsrow = dt("kvsrow", [128, 512], F32, kind=EI).ap()
    ocst = dt("ocst", [64, 128, 2048], I32, kind=EI).ap()
    osrow = dt("osrow", [256, 512], F32, kind=EI).ap()
    cosq = dt("cosq", [8, 128, 256], BF16, kind=EI).ap()
    sinq = dt("sinq", [8, 128, 256], BF16, kind=EI).ap()
    kv_dram = dt("kv_dram", [8, 128, 2048], BF16, kind=INT).ap()
    kvout_own = dt("kvout_own", [128, 1032], F32, kind=INT).ap()
    kvg = dt("kvg", [512, 1032], F32, kind=INT).ap()
    attn_dram = dt("attn_dram", [128, 32768], BF16, kind=INT).ap()
    ckv_own = dt("ckv_own", [T, 4], BF16, kind=INT).ap()
    ckvg = dt("ckvg", [4096, 4], BF16, kind=INT).ap()
    rpart1 = dt("rpart1", [1, 8], F32, kind=INT).ap()
    rg1 = dt("rg1", [4, 8], F32, kind=INT).ap()
    rpart2 = dt("rpart2", [1, 8], F32, kind=INT).ap()
    rg2 = dt("rg2", [4, 8], F32, kind=INT).ap()
    wq_dq = dt("wq_dq", [64, 128, 2048], BF16, kind=INT).ap()
    wkv_dq = dt("wkv_dq", [32, 128, 2048], BF16, kind=INT).ap()
    wo_dq = dt("wo_dq", [64, 128, 2048], BF16, kind=INT).ap()
    q_own = dt("q_own", [4096, 1024], BF16, kind=INT).ap()
    kv_own = dt("kv_own", [2048, 1024], BF16, kind=INT).ap()
    qg = dt("qg", [16384, 1024], BF16, kind=INT).ap()
    kvg2 = dt("kvg2", [8192, 1024], BF16, kind=INT).ap()
    qloc = dt("qloc", [4, 1024, 1024], BF16, kind=INT).ap()
    kloc = dt("kloc", [4, 256, 1024], BF16, kind=INT).ap()
    vloc = dt("vloc", [4, 256, 1024], BF16, kind=INT).ap()
    attn_own = dt("attn_own", [4096, 1024], BF16, kind=INT).ap()
    ag = dt("ag", [16384, 1024], BF16, kind=INT).ap()
    aloc = dt("aloc", [4, 1024, 1024], BF16, kind=INT).ap()

    c = alloc_common(nc, K)
    e8_ap = l0ffn_consts(nc)
    sel4_ap, ind_ap = l0ab_consts(nc)
    masks_aps = l1b_masks(nc)
    s_x = emit_load_x(K, c, x_in)
    for e in ("vector", "scalar", "tensor"):
        K.wait(e, s_x)

    K.scope("l0a")
    ln = emit_ln_xfm(K, c, tag="")
    K.scope("l0a_kvp")
    s_ev, s_kvo = emit_l0a_kvproj(K, c, wkvst, kvbias, kv_dram, ln)
    K.scope("l0a_kvs")
    s_ko = emit_l0a_kvstate(K, c, kv_dram, s_kvo, kvout_own)
    s_ag1 = emit_allgather(K, kvout_own, kvg, "ag", wait_sems=(s_ko,))
    _bar(K, [s_ev, s_ko])
    K.wait("sync", s_ag1)
    K.wait("vector", s_ag1)
    kvparts = [kvg[r * 128:(r + 1) * 128, :] for r in range(4)]
    K.scope("l0a_red")
    s_kv = emit_l0a_kvreduce_in(K, c, kvparts)
    K.scope("l0a_att")
    s_ao = emit_l0a_attn(K, c, wqst, bqa, attn_dram, s_kv, ln)
    K.scope("l0a_opj")
    s_ev0 = emit_l0a_oproj(K, c, attn_dram, wost0, bo0, s_ao, name="pj")
    _bar(K, [s_ev0])

    K.scope("l0f1")
    ln = emit_ln_xfm(K, c, tag="")
    s_ev1 = emit_l0ffn(K, c, wg9, ast0, wbst0, bias0, bthr0, e8_ap, ln, tag="")
    _bar(K, [s_ev1])

    K.scope("l0ab")
    ln = emit_ln_xfm(K, c, tag="")
    wm = dt("wm", [64, 128, 2048], BF16, kind=INT).ap()
    s_ev2 = emit_l0ab(K, c, maskT, wst_ab, wm, sel4_ap, ind_ap, ln)
    _bar(K, [s_ev2])

    K.scope("l0f2")
    ln = emit_ln_xfm(K, c, tag="")
    s_ev3 = emit_l0ffn(K, c, wg9, ast0, wbst0, bias0, bthr0, e8_ap, ln, tag="")
    _bar(K, [s_ev3])

    K.scope("l1a")
    ln = emit_ln_xfm(K, c, tag="")
    K.scope("l1a_cp")
    s_v, s_o = emit_l1a_cproj(K, c, wc16, cbias, ckv_own, ln, want_cq=True)
    s_ag2 = emit_allgather(K, ckv_own, ckvg, "ag", wait_sems=(s_o,))
    K.wait("sync", s_ag2)
    K.scope("l1a_att")
    s_ao = emit_l1a_attn(K, c, ckvg, wqk, bqk4, wvu, bvu, attn_dram, s_o)
    K.scope("l1a_opj")
    s_ev4 = emit_l0a_oproj(K, c, attn_dram, wost1, bo1, s_ao, name="pj")
    _bar(K, [s_ev4])

    K.scope("l1f1")
    ln = emit_ln_xfm(K, c, tag="")
    s_rp = emit_l1ffn_router_partial(K, c, rw8, selb, rpart1, ln, tag="")
    s_ag3 = emit_allgather(K, rpart1, rg1, "ag", wait_sems=(s_rp,))
    K.wait("sync", s_ag3)
    rg1l = [rg1[r:r + 1, :] for r in range(4)]
    s_route = emit_l1ffn_routing(K, c, rg1l, rb, e8_ap, tag="")
    s_ev5 = emit_l1ffn_main(K, c, ast1, wbst1, bias1, s_route, ln, tag="")
    _bar(K, [s_ev5])

    K.scope("l1b")
    ln = emit_ln_xfm(K, c, tag="")
    sv6, sa6, sp6 = ln
    K.wait("sync", sp6)
    K.wait("vector", sp6)
    K.scope("l1b_dq")
    s_o1 = emit_l1b_dequant(K, c, qcst, qsrow, wq_dq, tag="dq")
    s_o2 = emit_l1b_dequant(K, c, kvcst, kvsrow, wkv_dq, tag="dq")
    for w in s_o1 + s_o2:
        K.wait("sync", w)
        K.wait("vector", w)
    K.scope("l1b_qkv")
    s_ev6, s_oo = emit_l1b_qkv(K, c, wq_dq, wkv_dq, cosq, sinq, q_own, kv_own, ln)
    s_ag4 = emit_allgather(K, q_own, qg, "ag", wait_sems=(s_oo, s_ev6))
    s_ag5 = emit_allgather(K, kv_own, kvg2, "ag")
    _bar(K, [s_ev6, s_oo], engines=("sync", "vector", "scalar", "tensor"))
    K.scope("l1b_dqo")
    s_o3 = emit_l1b_dequant(K, c, ocst, osrow, wo_dq, tag="dq")
    K.wait("gpsimd", s_ag4)
    K.wait("gpsimd", s_ag5)
    s_sl = K.sem("s_slice")
    pid = nc.partition_id()
    off = pid % 4
    for g in range(4):
        K.dma("gpsimd", qloc[g], qg[ds(g * 4096 + off * 1024, 1024), 0:1024], s_sl)
        K.dma("gpsimd", kloc[g], kvg2[ds(g * 2048 + off * 256, 256), 0:1024], s_sl)
        K.dma("gpsimd", vloc[g], kvg2[ds(g * 2048 + 1024 + off * 256, 256), 0:1024], s_sl)
    for w in s_o3:
        K.wait("sync", w)
        K.wait("vector", w)
        K.wait("tensor", w)
    K.wait("sync", s_sl)
    K.wait("tensor", s_sl)
    K.scope("l1b_att")
    s_ao2 = emit_l1b_attn(K, c, qloc, kloc, vloc, masks_aps, attn_own)
    s_ag6 = emit_allgather(K, attn_own, ag, "ag", wait_sems=list(s_ao2))
    K.wait("gpsimd", s_ag6)
    for g in range(4):
        K.dma("gpsimd", aloc[g], ag[ds(g * 4096 + off * 1024, 1024), 0:1024], s_sl)
    K.wait("sync", s_sl)
    attn_list = [aloc[dc // 8][(dc % 8) * 128:(dc % 8) * 128 + 128, :]
                 for dc in range(NDC)]
    K.scope("l1b_opj")
    s_ev7 = emit_l0a_oproj(K, c, None, wo_dq, None, [], prequant=True,
                           name="pj", attn_list=attn_list)
    _bar(K, [s_ev7])

    K.scope("l1f2")
    ln = emit_ln_xfm(K, c, tag="")
    s_rp2 = emit_l1ffn_router_partial(K, c, rw8, selb, rpart2, ln, tag="")
    s_ag7 = emit_allgather(K, rpart2, rg2, "ag", wait_sems=(s_rp2,))
    K.wait("sync", s_ag7)
    rg2l = [rg2[r:r + 1, :] for r in range(4)]
    s_route2 = emit_l1ffn_routing(K, c, rg2l, rb, e8_ap, tag="")
    s_ev8 = emit_l1ffn_main(K, c, ast1, wbst1, bias1, s_route2, ln, tag="")
    K.wait("sync", s_ev8)
    emit_store_x(K, c, x_out)
    return K.build()


def kernel(**inputs):
    inp = {k: np.asarray(v) for k, v in inputs.items()}
    C = list(range(8))
    x_full = np.ascontiguousarray(inp["L0_attn_a_hidden_states"].astype(np.float32))
    shards = shard_tokens(x_full)

    wkvst, kvbias, wqst, bqa, wost0, bo0 = l0a_host(
        inp["L0_attn_a_Wq"], inp["L0_attn_a_bq"], inp["L0_attn_a_Wk"],
        inp["L0_attn_a_bk"], inp["L0_attn_a_Wv"], inp["L0_attn_a_bv"],
        inp["L0_attn_a_Wo"], inp["L0_attn_a_bo"])
    wg9, ast0, wbst0 = l0ffn_host(
        inp["L0_ffn_W_base"], inp["L0_ffn_W_gate"], inp["L0_ffn_W_thr"],
        inp["L0_ffn_A_w"], inp["L0_ffn_B_w"])
    bias0 = np.ascontiguousarray(inp["L0_ffn_b_base"][None, :]).astype(np.float32)
    bthr0 = np.ascontiguousarray(inp["L0_ffn_b_thr"][None, :]).astype(np.float32)
    wst_ab = w_stream(np.ascontiguousarray(
        inp["L0_attn_b_weight"].astype(np.float32).T), dcg=4)
    maskT = np.ascontiguousarray(inp["L0_attn_b_mask"].astype(np.int32).T)
    wc16, cbias, wqk, bqk4, wvu, bvu, wost1, bo1 = l1a_host(
        inp["L1_attn_a_Wq_d"], inp["L1_attn_a_bq_d"], inp["L1_attn_a_W_qk"],
        inp["L1_attn_a_b_qk"], inp["L1_attn_a_Wkv_d"], inp["L1_attn_a_bkv_d"],
        inp["L1_attn_a_Wv_u"], inp["L1_attn_a_bv_u"], inp["L1_attn_a_Wo"],
        inp["L1_attn_a_bo"])
    rw8, ast1, wbst1 = l1ffn_host(
        inp["L1_ffn_W"], inp["L1_ffn_A"], inp["L1_ffn_B"], inp["L1_ffn_router_W"])
    bias1 = np.ascontiguousarray(inp["L1_ffn_b"][None, :]).astype(np.float32)
    rbr = np.ascontiguousarray(inp["L1_ffn_router_b"][None, :]).astype(np.float32)
    kvc = np.concatenate([inp["L1_attn_b_k_codes"], inp["L1_attn_b_v_codes"]],
                         axis=1).astype(np.int32)
    kvs = np.concatenate([inp["L1_attn_b_k_scales"], inp["L1_attn_b_v_scales"]],
                         axis=1).astype(np.float32)
    qcst, qsrow = l1b_host_one(inp["L1_attn_b_q_codes"].astype(np.int32),
                               inp["L1_attn_b_q_scales"].astype(np.float32))
    kvcst, kvsrow = l1b_host_one(kvc, kvs)
    ocst, osrow = l1b_host_one(inp["L1_attn_b_o_codes"].astype(np.int32),
                               inp["L1_attn_b_o_scales"].astype(np.float32))
    ropes = [rope_host(cc) for cc in C]

    nc = make_nc("fused")
    _build_fused(nc)
    in_maps = []
    for cc in C:
        in_maps.append({
            "x_in": shards[cc],
            "wkvst": wkvst, "kvbias": kvbias, "wqst": wqst, "bqa": bqa,
            "wost0": wost0, "bo0": bo0,
            "wg9": wg9, "ast0": ast0, "wbst0": wbst0, "bias0": bias0,
            "bthr0": bthr0,
            "maskT": maskT, "wst_ab": wst_ab,
            "wc16": wc16, "cbias": cbias, "wqk": wqk, "bqk4": bqk4,
            "wvu": wvu, "bvu": bvu, "wost1": wost1, "bo1": bo1,
            "rw8": rw8, "selb": sel_host(cc), "rb": rbr,
            "ast1": ast1, "wbst1": wbst1, "bias1": bias1,
            "qcst": qcst, "qsrow": qsrow, "kvcst": kvcst, "kvsrow": kvsrow,
            "ocst": ocst, "osrow": osrow,
            "cosq": ropes[cc][0].astype(BF16NP),
            "sinq": ropes[cc][1].astype(BF16NP),
        })
    res = run_bass_kernel_spmd(nc, in_maps, core_ids=C)
    shards = [res.results[cc]["x_out"] for cc in C]
    return unshard_tokens(shards).astype(np.float32)
